# version 07
# baseline (speedup 1.0000x reference)
import os
import numpy as np
import ml_dtypes
import concourse.bass as bass
import concourse.mybir as mybir
from concourse.bass_utils import run_bass_kernel_spmd
from contextlib import ExitStack

F32 = mybir.dt.float32
BF16 = mybir.dt.bfloat16
I32 = mybir.dt.int32
F16 = mybir.dt.float16
U8 = mybir.dt.uint8
FP8 = mybir.dt.float8e4
AF = mybir.ActivationFunctionType
ALU = mybir.AluOpType
AX = mybir.AxisListType

D = 2048
T = 2048
TH = 1024
_LAST_PLAN = None
TOWN = 1024
C = 128
NCH = 16
NV = 16
KH = 8
DK = DV = 64
NEG = -30000.0
NLEV = 7
I_FF = 5632
HT = 44
LN_EPS = 1e-5
EPS_A = 1e-6


class Plan:
    def __init__(self, nc, es):
        self.marks = {}
        self.nc = nc
        self.items = []
        self.cnt = {e: 0 for e in ("te", "dve", "act", "gp", "sy")}
        self.sems = {e: es.enter_context(nc.semaphore("s_" + e)) for e in self.cnt}
        self.dma = es.enter_context(nc.semaphore("s_dma"))
        self.dma_cnt = 0
        self.dmag = es.enter_context(nc.semaphore("s_dmag"))
        self.dmag_cnt = 0

    def emit(self, eng, fn, waits=(), inc=True):
        waits = tuple((we, wv) for (we, wv) in waits if wv)
        if inc:
            self.cnt[eng] += 1
        self.items.append((eng, fn, waits, "c" if inc else None))
        return self.cnt[eng]

    def dma_emit(self, eng, fn, waits=()):
        waits = tuple((we, wv) for (we, wv) in waits if wv)
        if eng == "gp":
            self.dmag_cnt += 16
            self.items.append((eng, fn, waits, "dg"))
            return self.dmag_cnt
        self.dma_cnt += 16
        self.items.append((eng, fn, waits, "d"))
        return self.dma_cnt

    def now(self, eng):
        return self.cnt[eng]

    def mark(self, name):
        self.marks[name] = {e: self.cnt[e] for e in self.cnt}

    def barrier(self):
        return dict(te=("te", self.cnt["te"]), dve=("dve", self.cnt["dve"]),
                    act=("act", self.cnt["act"]), gp=("gp", self.cnt["gp"]),
                    dma=("dma", self.dma_cnt), dmag=("dmag", self.dmag_cnt))

    def bwaits(self, b, *names):
        return [b[n] for n in names]

    def replay(self, eng_name, engine):
        for e, fn, waits, inc in self.items:
            if e != eng_name:
                continue
            for (we, wv) in waits:
                sem = self.dma if we == "dma" else (self.dmag if we == "dmag" else self.sems[we])
                engine.wait_ge(sem, wv)
            inst = fn(engine)
            if inc == "d":
                inst.then_inc(self.dma, 16)
            elif inc == "dg":
                inst.then_inc(self.dmag, 16)
            elif inc == "c":
                inst.then_inc(self.sems[e], 1)


def build(debug=False):
    nc = bass.Bass("TRN2", target_bir_lowering=False, detect_race_conditions=False)
    dt_ = nc.dram_tensor
    io = {}
    io["x_scan_tm"] = dt_("x_scan_tm", [T, D], F32, kind="ExternalInput").ap()
    io["x_scan_cm"] = dt_("x_scan_cm", [D, T], F32, kind="ExternalInput").ap()
    io["x_own_tm"] = dt_("x_own_tm", [TOWN, D], F32, kind="ExternalInput").ap()
    io["x_own_cm"] = dt_("x_own_cm", [D, TOWN], F32, kind="ExternalInput").ap()
    io["wqkv_t"] = dt_("wqkv_t", [16, 128, 16, 128], F32, kind="ExternalInput").ap()
    io["wz_t"] = dt_("wz_t", [2, 16, 128, 512], F32, kind="ExternalInput").ap()
    io["wb_t"] = dt_("wb_t", [16, 128, 16], F32, kind="ExternalInput").ap()
    io["wa_t"] = dt_("wa_t", [16, 128, 16], F32, kind="ExternalInput").ap()
    io["wo_t"] = dt_("wo_t", [4, 8, 128, 512], F32, kind="ExternalInput").ap()
    io["convw"] = dt_("convw", [16, 128, 4], F32, kind="ExternalInput").ap()
    io["vecs"] = dt_("vecs", [64, 3], F32, kind="ExternalInput").ap()
    io["sel2"] = dt_("sel2", [2, 128], BF16, kind="ExternalInput").ap()
    io["gub"] = dt_("gub", [2, 11, 8, 128, 512], U8, kind="ExternalInput").ap()
    io["gus"] = dt_("gus", [2, 2, 11, 8, 512], F16, kind="ExternalInput").ap()
    io["dnb"] = dt_("dnb", [4, 22, 128, 512], U8, kind="ExternalInput").ap()
    io["dns"] = dt_("dns", [4, 2, 22, 512], F16, kind="ExternalInput").ap()
    io["out"] = dt_("out", [TOWN, D], F32, kind="ExternalOutput").ap()
    if debug:
        io["xn_dbg"] = dt_("xn_dbg", [128, 16, TOWN], BF16, kind="ExternalOutput").ap()
        io["yn_dbg"] = dt_("yn_dbg", [128, 8, TOWN], BF16, kind="ExternalOutput").ap()
        io["attn_dbg"] = dt_("attn_dbg", [TOWN, D], F32, kind="ExternalOutput").ap()
        io["h_dbg"] = dt_("h_dbg", [128, HT, TOWN], BF16, kind="ExternalOutput").ap()
        io["wdq_dbg"] = dt_("wdq_dbg", [128, 8, 2, 512], BF16, kind="ExternalOutput").ap()
        io["gsil_dbg"] = dt_("gsil_dbg", [128, 4, TOWN], F32, kind="ExternalOutput").ap()
        io["yt_dbg"] = dt_("yt_dbg", [128, 8, 2048], BF16, kind="ExternalOutput").ap()
    attn_sc = dt_("attn_sc", [TOWN, D], F32, kind="Internal").ap()

    es = ExitStack()
    sb = lambda n, s, d: es.enter_context(nc.sbuf_tensor(n, s, d))
    ps = lambda n, s, d: es.enter_context(nc.psum_tensor(n, s, d))

    with es:
        P = Plan(nc, es)
        global _LAST_PLAN
        _LAST_PLAN = P
        id128 = sb("id128", [128, 128], BF16)
        id64s = sb("id64s", [128, 64], BF16)
        idf32 = sb("idf32", [128, 128], F32)
        ones128 = sb("ones128", [128, 128], BF16)
        ones_f = sb("ones_f", [128, 128], F32)
        ones_qb = sb("ones_qb", [128, 64], BF16)
        ones2f = sb("ones2f", [2, 512], BF16)
        maskL = sb("maskL", [128, 128], F32)
        tricum = sb("tricum", [128, 128], F32)
        rowsel = sb("rowsel", [128, 128], F32)
        vecs = sb("vecs_s", [64, 3], F32)
        nalog = sb("nalog", [16, 1], F32)
        nwcol = sb("nwcol", [128, 1], F32)
        cst = sb("cst", [128, 4], F32)
        dtrow = sb("dtrow", [1, 16], F32)
        P.emit("dve", lambda v: v.memset(cst[:, 0:1], LN_EPS))
        P.emit("dve", lambda v: v.memset(cst[:, 1:2], EPS_A))
        P.emit("dve", lambda v: v.memset(cst[:, 2:3], 1.0))
        P.emit("dve", lambda v: v.memset(cst[:, 3:4], -1.0))

        obf = sb("obuf", [128, NV // 2, NCH // 2, C], BF16)

        pb = [ps(f"pb{i}", [128, 512], F32) for i in range(8)]
        pbh = [b.ap().bitcast(BF16) for b in pb]

        LD_VEC = P.dma_emit("sy", lambda sy: sy.dma_start(out=vecs[:], in_=io["vecs"][:]))
        ctx_iota = nc.sbuf_tensor("iota_r", [128, 128], I32)
        iota_r = ctx_iota.__enter__()
        ctx_iotac = nc.sbuf_tensor("iota_c", [128, 128], I32)
        iota_c = ctx_iotac.__enter__()
        ctx_low = nc.sbuf_tensor("low01", [128, 128], F32)
        low01 = ctx_low.__enter__()
        P.emit("gp", lambda gp: gp.iota(iota_r[:], pattern=[[0, 128]], channel_multiplier=1))
        P.emit("gp", lambda gp: gp.iota(iota_c[:], pattern=[[1, 128]], channel_multiplier=0))
        GPC = P.now("gp")
        P.emit("dve", lambda v: v.tensor_tensor(out=id128[:], in0=iota_r[:], in1=iota_c[:], op=ALU.is_equal),
               waits=[("gp", GPC)])
        P.emit("dve", lambda v: v.tensor_tensor(out=idf32[:], in0=iota_r[:], in1=iota_c[:], op=ALU.is_equal))
        P.emit("dve", lambda v: v.tensor_tensor(out=low01[:], in0=iota_r[:], in1=iota_c[:], op=ALU.is_gt))
        P.emit("dve", lambda v: v.tensor_tensor(out=tricum[:], in0=iota_r[:], in1=iota_c[:], op=ALU.is_le))
        P.emit("dve", lambda v: v.tensor_scalar(out=maskL[:], in0=low01[:], scalar1=1.0, scalar2=-NEG,
                                                op0=ALU.subtract, op1=ALU.mult),
               waits=[("dve", P.now("dve"))])
        P.emit("dve", lambda v: v.tensor_scalar(out=rowsel[:], in0=iota_r[:], scalar1=127, scalar2=None,
                                                op0=ALU.is_equal))
        P.emit("dve", lambda v: v.memset(ones128[:], 1.0))
        P.emit("dve", lambda v: v.memset(ones_f[:], 1.0))
        P.emit("dve", lambda v: v.memset(ones_qb[:], 1.0))
        LD_SEL = P.dma_emit("sy", lambda sy: sy.dma_start(out=ones2f[:, 0:128],
                                                          in_=io["sel2"][:]))
        P.emit("dve", lambda v: v.tensor_copy(out=id64s[0:64, :], in_=id128[0:64, 0:64]),
               waits=[("dve", P.now("dve"))])
        P.emit("dve", lambda v: v.tensor_copy(out=id64s[64:128, :], in_=id128[0:64, 0:64]),
               waits=[("dve", P.now("dve"))])
        ctx_low.__exit__(None, None, None)
        ctx_iotac.__exit__(None, None, None)
        ctx_iota.__exit__(None, None, None)
        P.emit("act", lambda s: s.activation(out=nalog[:], in_=vecs[0:16, 1:2], func=AF.Exp),
               waits=[("dma", LD_VEC)])
        ACT_AL = P.now("act")
        P.emit("dve", lambda v: v.tensor_scalar(out=nalog[:], in0=nalog[:], scalar1=-1.0, scalar2=None,
                                                op0=ALU.mult), waits=[("act", ACT_AL), ("dve", P.now("dve"))])
        P.emit("dve", lambda v: v.tensor_copy(out=nwcol[0:64, :], in_=vecs[0:64, 2:3]),
               waits=[("dve", P.now("dve"))])
        P.emit("te", lambda te: te.transpose(
            out=pb[6][0:1, 0:16], in_=vecs[0:16, 0:1], identity=idf32[0:16, 0:16]),
            waits=[("dma", LD_VEC), ("dve", P.now("dve"))])
        TE_DT = P.now("te")
        P.emit("dve", lambda v: v.tensor_copy(out=dtrow[:], in_=pb[6][0:1, 0:16]),
               waits=[("te", TE_DT)])
        P.emit("dve", lambda v: v.tensor_copy(out=nwcol[64:128, :], in_=vecs[0:64, 2:3]),
               waits=[("dve", P.now("dve"))])
        B0 = P.barrier()
        P.mark("B0")

        ostat = sb("ostat", [128, 8, 2], F32)
        sstat = sb("sstat", [128, 16, 2], F32)
        bnt = sb("bnt", [128, 2, 4, 6], F32)
        with nc.sbuf_tensor("xtile", [128, 2, D], F32) as xtile:
            lastd = [0, 0]
            for (xin, nt, stat) in ((io["x_own_tm"], 8, ostat), (io["x_scan_tm"], 16, sstat)):
                for i in range(nt):
                    par = i % 2
                    dm = P.dma_emit("sy", lambda sy, i=i, par=par, xin=xin: sy.dma_start(
                        out=xtile[:, par, :], in_=xin[i * 128:(i + 1) * 128, :]),
                        waits=[("dve", lastd[par])])
                    for c in range(4):
                        P.emit("dve", lambda v, par=par, c=c: v.bn_stats(
                            out=bnt[:, par, c, :], in_=xtile[:, par, c * 512:(c + 1) * 512]),
                            waits=[("dma", dm)] if c == 0 else [("dve", P.now("dve"))])
                    P.emit("dve", lambda v, i=i, par=par, stat=stat: v.bn_aggr(
                        out=stat[:, i, :], in_=bnt[:, par, :, :].rearrange("p a b -> p (a b)")),
                        waits=[("dve", P.now("dve"))])
                    lastd[par] = P.now("dve")
        STATS = P.now("dve")
        rstd_o = sb("rstd_o", [128, 8], F32)
        mr_o = sb("mr_o", [128, 8], F32)
        rstd_s = sb("rstd_s", [128, 16], F32)
        mr_s = sb("mr_s", [128, 16], F32)
        sq_t = sb("sq_t", [128, 16], F32)
        for (stat, nt, rstd, mr) in ((ostat, 8, rstd_o, mr_o), (sstat, 16, rstd_s, mr_s)):
            P.emit("act", lambda s, stat=stat, nt=nt: s.activation(
                out=sq_t[:, 0:nt], in_=stat.ap()[:, :, 1], func=AF.Sqrt, bias=cst[:, 0:1]),
                waits=[("dve", STATS), ("dve", P.now("dve"))])
            AC = P.now("act")
            P.emit("dve", lambda v, nt=nt, rstd=rstd: v.reciprocal(
                out=rstd[:], in_=sq_t[:, 0:nt]), waits=[("act", AC)])
            P.emit("dve", lambda v, stat=stat, rstd=rstd, mr=mr: v.tensor_tensor(
                out=mr[:], in0=stat.ap()[:, :, 0], in1=rstd[:], op=ALU.mult),
                waits=[("dve", P.now("dve"))])
        NORMS = P.now("dve")
        def emit_xn(xcm_d, nt512, rstd, mr, xn_dst, tag, prior=()):
            prior = list(prior)
            with nc.sbuf_tensor("rowscr" + tag, [1, 2, 512], F32) as rowscr, \
                 nc.sbuf_tensor("xcmt" + tag, [128, 2, 512], F32) as xcmt, \
                 nc.sbuf_tensor("rbt" + tag, [128, 2, 512], F32) as rbt, \
                 nc.sbuf_tensor("mbt" + tag, [128, 2, 512], F32) as mbt, \
                 nc.sbuf_tensor("xtmp" + tag, [128, 2, 512], F32) as xtmp:
                lastd = [0, 0]
                rowscr_te = 0
                for tc in range(nt512):
                    par = tc % 2
                    for ii in range(4):
                        i = tc * 4 + ii
                        for half, src_c in ((0, rstd), (1, mr)):
                            P.emit("te", lambda te, i=i, src_c=src_c: te.matmul(
                                out=pb[6][0:1, 0:128], lhsT=src_c[:, i:i + 1], rhs=idf32[:],
                                start=True, stop=True),
                                waits=[("dve", max(NORMS, P.now("dve"))), ("te", rowscr_te)]
                                      + (prior if (tc == 0 and ii == 0 and half == 0) else []))
                            TE_R = P.now("te")
                            P.emit("dve", lambda v, ii=ii, half=half: v.tensor_copy(
                                out=rowscr[:, half, ii * 128:(ii + 1) * 128], in_=pb[6][0:1, 0:128]),
                                waits=[("te", TE_R)])
                    DV_ROW = P.now("dve")
                    P.emit("te", lambda te: te.matmul(
                        out=pb[6][:, 0:512], lhsT=ones_f[0:1, :], rhs=rowscr[:, 0, :],
                        start=True, stop=True), waits=[("dve", max(DV_ROW, lastd[par]))])
                    TE_B1 = P.now("te")
                    P.emit("dve", lambda v, par=par: v.tensor_copy(out=rbt[:, par, :], in_=pb[6][:, 0:512]),
                           waits=[("te", TE_B1)])
                    DV_B1 = P.now("dve")
                    P.emit("te", lambda te: te.matmul(
                        out=pb[6][:, 0:512], lhsT=ones_f[0:1, :], rhs=rowscr[:, 1, :],
                        start=True, stop=True), waits=[("dve", DV_B1)])
                    TE_B2 = P.now("te")
                    rowscr_te = TE_B2
                    P.emit("dve", lambda v, par=par: v.tensor_copy(out=mbt[:, par, :], in_=pb[6][:, 0:512]),
                           waits=[("te", TE_B2)])
                    for dtile in range(16):
                        dm = P.dma_emit("sy", lambda sy, dtile=dtile, tc=tc, par=par, xcm_d=xcm_d: sy.dma_start(
                            out=xcmt[:, par, :],
                            in_=xcm_d[dtile * 128:(dtile + 1) * 128, tc * 512:(tc + 1) * 512]),
                            waits=[("dve", lastd[par])] if lastd[par] else prior)
                        P.emit("dve", lambda v, par=par: v.tensor_tensor(
                            out=xtmp[:, par, :], in0=xcmt[:, par, :], in1=rbt[:, par, :], op=ALU.mult),
                            waits=[("dma", dm)])
                        P.emit("dve", lambda v, dtile=dtile, tc=tc, par=par, xn_dst=xn_dst: v.tensor_tensor(
                            out=xn_dst[:, dtile, tc * 512:(tc + 1) * 512],
                            in0=xtmp[:, par, :], in1=mbt[:, par, :], op=ALU.subtract),
                            waits=[("dve", P.now("dve"))])
                        lastd[par] = P.now("dve")


        es2 = ExitStack()
        sb2 = lambda n, s, d: es2.enter_context(nc.sbuf_tensor(n, s, d))
        qf = sb2("qf", [128, 4, T], BF16)
        kf = sb2("kf", [128, 4, T], BF16)
        b_tm = sb2("b_tm", [128, NCH, NV], F32)
        g_tm = sb2("g_tm", [128, NCH, NV], F32)
        ktm_t = sb2("ktm_t", [128, NCH, KH, DK], BF16)
        vtm_t = sb2("vtm_t", [128, NCH, NV, DV], BF16)
        with nc.sbuf_tensor("xn_scan", [128, 16, T], BF16) as xn_scan:
            emit_xn(io["x_scan_cm"], 4, rstd_s, mr_s, xn_scan, "s", prior=[("dve", NORMS)])
            XN_SCAN = P.now("dve")
            XN_TE = P.now("te")

            with nc.sbuf_tensor("wkt", [128, 2, 16, 128], BF16) as wkt, \
                 nc.sbuf_tensor("cvt", [128, 3 + T], F32) as cvt_, \
                 nc.sbuf_tensor("cva", [128, TH], F32) as cva_, \
                 nc.sbuf_tensor("cvb", [128, TH], F32) as cvb_, \
                 nc.sbuf_tensor("convw_s", [128, 16, 4], F32) as convw_s, \
                 nc.sbuf_tensor("sil", [128, TH], F32) as sil_, \
                 nc.sbuf_tensor("ktf", [128, 2, 128], F32) as ktf, \
                 nc.sbuf_tensor("sqv", [128, 2, 128], F32) as sqv, \
                 nc.sbuf_tensor("rsq", [128, 2, 2], F32) as rsq, \
                 nc.sbuf_tensor("ktb", [128, 2, 128], BF16) as ktb:
                cvt = cvt_.ap().rearrange("p (a b) -> p a b", a=1)
                cva = cva_.ap().rearrange("p (a b) -> p a b", a=1)
                cvb = cvb_.ap().rearrange("p (a b) -> p a b", a=1)
                sil = sil_.ap().rearrange("p (a b) -> p a b", a=1)
                vsil = sil_.ap().bitcast(BF16).rearrange("p (a b) -> p a b", a=2)
                cwdm = 0
                for i in range(16):
                    cwdm = P.dma_emit("sy", lambda sy, i=i: sy.dma_start(
                        out=convw_s[:, i, :], in_=io["convw"][i, :, :]),
                        waits=[("dve", XN_SCAN), ("te", XN_TE)] if i == 0 else [])
                last_w = [0, 0]
                last_cv = [0, 0]
                last_sil = [0, 0]
                norm_free_d = 0
                norm_free_a = 0
                norm_free_te = 0
                for ot in range(16):
                    par = ot % 2
                    wdm = P.dma_emit("gp", lambda gp, ot=ot, par=par: gp.dma_start(
                        out=wkt[:, par, :, :], in_=io["wqkv_t"][ot, :, :, :]),
                        waits=[("te", last_w[par])] if ot >= 2
                        else [("dve", XN_SCAN), ("te", XN_TE)])
                    for tch in range(4):
                        pbank = pb[par]
                        for kt in range(16):
                            P.emit("te", lambda te, par=par, tch=tch, kt=kt, pbank=pbank: te.matmul(
                                out=pbank[:, 0:512], lhsT=wkt[:, par, kt, :],
                                rhs=xn_scan[:, kt, tch * 512:(tch + 1) * 512],
                                start=(kt == 0), stop=(kt == 15)),
                                waits=([("dmag", wdm), ("dve", max(XN_SCAN, last_cv[par]))]
                                       if kt == 0 else []))
                        TE_P = P.now("te")
                        P.emit("dve", lambda v, par=par, tch=tch, pbank=pbank: v.tensor_copy(
                            out=cvt[:, 0, 3 + tch * 512: 3 + (tch + 1) * 512], in_=pbank[:, 0:512]),
                            waits=[("te", TE_P)])
                        last_cv[par] = P.now("dve")
                    last_w[par] = P.now("te")
                    P.emit("dve", lambda v: v.memset(cvt[:, 0, 0:3], 0.0),
                           waits=[("dve", last_cv[par]), ("dma", cwdm)])
                    for h2 in range(2):
                        P.emit("dve", lambda v, ot=ot, h2=h2: v.tensor_scalar(
                            out=cva[:, 0, :], in0=cvt[:, 0, 3 + h2 * TH:3 + h2 * TH + TH],
                            scalar1=convw_s[:, ot, 3:4], scalar2=None, op0=ALU.mult),
                            waits=[("dve", P.now("dve")),
                                   ("act", max(last_sil[0], last_sil[1]))])
                        bufs = (cva, cvb)
                        for idx, j in enumerate((2, 1, 0)):
                            src_b, dst_b = bufs[idx % 2], bufs[(idx + 1) % 2]
                            P.emit("dve", lambda v, ot=ot, h2=h2, j=j, src_b=src_b, dst_b=dst_b:
                                   v.scalar_tensor_tensor(
                                       out=dst_b[:, 0, :],
                                       in0=cvt[:, 0, j + h2 * TH:j + h2 * TH + TH],
                                       scalar=convw_s[:, ot, j:j + 1], in1=src_b[:, 0, :],
                                       op0=ALU.mult, op1=ALU.add),
                                   waits=[("dve", P.now("dve"))])
                        conv_out = bufs[1]
                        CONV_D = P.now("dve")
                        if ot < 8:
                            P.emit("act", lambda s, conv_out=conv_out: s.activation(
                                out=sil[:, 0, :], in_=conv_out[:, 0, :], func=AF.Silu),
                                waits=[("dve", CONV_D), ("te", norm_free_te)])
                            ACT_S = P.now("act")
                            last_sil[par] = ACT_S
                            dst_fm = qf if ot < 4 else kf
                            fmslot = ot % 4
                            for tb8 in range(8):
                                tb = h2 * 8 + tb8
                                tpar = tb % 2
                                P.emit("te", lambda te, tb8=tb8: te.transpose(
                                    out=pb[4][:, 0:128], in_=sil[:, 0, tb8 * 128:(tb8 + 1) * 128],
                                    identity=idf32[:]),
                                    waits=[("act", ACT_S), ("dve", norm_free_d)])
                                TE_T1 = P.now("te")
                                P.emit("dve", lambda v, tpar=tpar: v.tensor_copy(
                                    out=ktf[:, tpar, :], in_=pb[4][:, 0:128]), waits=[("te", TE_T1)])
                                DV_T1 = P.now("dve")
                                P.emit("act", lambda s, tpar=tpar: s.activation(
                                    out=sqv[:, tpar, :], in_=ktf[:, tpar, :], func=AF.Square),
                                    waits=[("dve", DV_T1), ("act", norm_free_a)])
                                ACT_Q = P.now("act")
                                for hh in range(2):
                                    P.emit("dve", lambda v, tpar=tpar, hh=hh: v.tensor_reduce(
                                        out=rsq[:, tpar, hh:hh + 1],
                                        in_=sqv[:, tpar, hh * 64:(hh + 1) * 64],
                                        axis=AX.X, op=ALU.add),
                                        waits=[("act", ACT_Q)] if hh == 0 else [("dve", P.now("dve"))])
                                DV_RED = P.now("dve")
                                P.emit("act", lambda s, tpar=tpar: s.activation(
                                    out=sqv[:, tpar, 0:2], in_=rsq[:, tpar, :], func=AF.Sqrt, bias=cst[:, 1:2]),
                                    waits=[("dve", DV_RED)])
                                ACT_R = P.now("act")
                                norm_free_a = ACT_R
                                P.emit("dve", lambda v, tpar=tpar: v.reciprocal(
                                    out=rsq[:, tpar, :], in_=sqv[:, tpar, 0:2]), waits=[("act", ACT_R)])
                                if ot < 4:
                                    P.emit("dve", lambda v, tpar=tpar: v.tensor_scalar(
                                        out=rsq[:, tpar, :], in0=rsq[:, tpar, :], scalar1=DK ** -0.5,
                                        scalar2=None, op0=ALU.mult), waits=[("dve", P.now("dve"))])
                                DV_R2 = P.now("dve")
                                for hh in range(2):
                                    P.emit("dve", lambda v, tpar=tpar, hh=hh: v.tensor_scalar(
                                        out=ktb[:, tpar, hh * 64:(hh + 1) * 64],
                                        in0=ktf[:, tpar, hh * 64:(hh + 1) * 64],
                                        scalar1=rsq[:, tpar, hh:hh + 1], scalar2=None, op0=ALU.mult),
                                        waits=[("dve", DV_R2)])
                                DV_N = P.now("dve")
                                if ot >= 4:
                                    ch = tb
                                    kh0 = (ot - 4) * 2
                                    P.emit("dve", lambda v, tpar=tpar, ch=ch, kh0=kh0: v.tensor_copy(
                                        out=ktm_t[:, ch, kh0, :], in_=ktb[:, tpar, 0:64]),
                                        waits=[("dve", DV_N)])
                                    P.emit("dve", lambda v, tpar=tpar, ch=ch, kh0=kh0: v.tensor_copy(
                                        out=ktm_t[:, ch, kh0 + 1, :], in_=ktb[:, tpar, 64:128]),
                                        waits=[("dve", P.now("dve"))])
                                DV_K = P.now("dve")
                                P.emit("te", lambda te, tpar=tpar: te.transpose(
                                    out=pbh[4][:, 256:384], in_=ktb[:, tpar, :], identity=id128[:]),
                                    waits=[("dve", DV_K)])
                                TE_T2 = P.now("te")
                                P.emit("dve", lambda v, dst_fm=dst_fm, fmslot=fmslot, tb=tb: v.tensor_copy(
                                    out=dst_fm[:, fmslot, tb * 128:(tb + 1) * 128], in_=pbh[4][:, 256:384]),
                                    waits=[("te", TE_T2)])
                                norm_free_d = P.now("dve")
                            norm_free_te = P.now("te")
                        else:
                            P.emit("act", lambda s, conv_out=conv_out: s.activation(
                                out=vsil[:, 0, :], in_=conv_out[:, 0, :], func=AF.Silu),
                                waits=[("dve", CONV_D), ("te", norm_free_te)])
                            ACT_V = P.now("act")
                            last_sil[par] = ACT_V
                            vt = ot - 8
                            for tb8 in range(8):
                                tb = h2 * 8 + tb8
                                P.emit("te", lambda te, tb8=tb8: te.transpose(
                                    out=pbh[4][:, 0:128], in_=vsil[:, 0, tb8 * 128:(tb8 + 1) * 128],
                                    identity=id128[:]),
                                    waits=[("act", ACT_V), ("dve", norm_free_d)])
                                TE_TV = P.now("te")
                                P.emit("dve", lambda v, vt=vt, tb=tb: v.tensor_copy(
                                    out=vtm_t[:, tb, 2 * vt:2 * vt + 2, :].rearrange("p a b -> p (a b)"),
                                    in_=pbh[4][:, 0:128]), waits=[("te", TE_TV)])
                                norm_free_d = P.now("dve")
                            norm_free_te = P.now("te")
            QKV_B = P.barrier()
            P.mark("QKV")

            with nc.sbuf_tensor("wba", [128, 16, 32], BF16) as wba, \
                 nc.sbuf_tensor("braw", [16, T], F32) as braw, \
                 nc.sbuf_tensor("graw", [16, T], F32) as graw, \
                 nc.sbuf_tensor("gtm1", [16, T], F32) as gtm1, \
                 nc.sbuf_tensor("gtm2", [16, T], F32) as gtm2:
                wdm_b = P.dma_emit("gp", lambda gp: gp.dma_start(
                    out=wba[:, :, 0:16], in_=io["wb_t"].rearrange("a p c -> p a c")),
                    waits=P.bwaits(QKV_B, "te", "act", "dve"))
                wdm_a = P.dma_emit("gp", lambda gp: gp.dma_start(
                    out=wba[:, :, 16:32], in_=io["wa_t"].rearrange("a p c -> p a c")))
                for cofs, dst in ((0, braw), (16, graw)):
                    for tch in range(4):
                        for kt in range(16):
                            P.emit("te", lambda te, cofs=cofs, tch=tch, kt=kt: te.matmul(
                                out=pb[5][0:16, 0:512], lhsT=wba[:, kt, cofs:cofs + 16],
                                rhs=xn_scan[:, kt, tch * 512:(tch + 1) * 512],
                                start=(kt == 0), stop=(kt == 15 and cofs == 0)),
                                waits=[("dmag", wdm_a), ("dve", P.now("dve"))] if kt == 0 else [])
                        if cofs == 16:
                            for cc in range(4):
                                P.emit("te", lambda te, cc=cc: te.matmul(
                                    out=pb[5][0:16, cc * 128:(cc + 1) * 128], lhsT=dtrow[:],
                                    rhs=ones_f[0:1, :], start=False, stop=(cc == 3)))
                        TE_BA = P.now("te")
                        P.emit("dve", lambda v, dst=dst, tch=tch: v.tensor_copy(
                            out=dst[:, tch * 512:(tch + 1) * 512], in_=pb[5][0:16, 0:512]),
                            waits=[("te", TE_BA)])
                BA_D = P.now("dve")
                P.emit("act", lambda s: s.activation(out=gtm1[:], in_=braw[:], func=AF.Sigmoid),
                       waits=[("dve", BA_D)])
                SIG_A = P.now("act")
                P.emit("act", lambda s: s.activation(out=gtm2[:], in_=graw[:], func=AF.Abs),
                       waits=[("dve", BA_D), ("act", SIG_A)])
                D_AB = P.now("dve")
                A_ABS = P.now("act")
                P.emit("act", lambda s: s.activation(out=braw[:], in_=gtm2[:], func=AF.Exp,
                                                     scale=cst[0:16, 3:4]),
                       waits=[("act", A_ABS)])
                A_EX = P.now("act")
                P.emit("act", lambda s: s.activation(out=gtm2[:], in_=braw[:], func=AF.Ln,
                                                     bias=cst[0:16, 2:3]),
                       waits=[("act", A_EX)])
                A_LN = P.now("act")
                P.emit("act", lambda s: s.activation(out=braw[:], in_=graw[:], func=AF.Relu),
                       waits=[("act", A_LN)])
                A_RELU = P.now("act")
                P.emit("dve", lambda v: v.tensor_tensor(
                    out=graw[:], in0=braw[:], in1=gtm2[:], op=ALU.add),
                    waits=[("act", A_RELU), ("dve", P.now("dve"))])
                P.emit("dve", lambda v: v.tensor_scalar(
                    out=graw[:], in0=graw[:], scalar1=nalog[0:16, 0:1],
                    scalar2=None, op0=ALU.mult), waits=[("dve", P.now("dve"))])
                BG_D = P.now("dve")
                BG_A = P.now("act")
                lastbg = 0
                for ch in range(NCH):
                    P.emit("te", lambda te, ch=ch: te.transpose(
                        out=pb[4][:, 0:16], in_=gtm1[:, ch * 128:(ch + 1) * 128],
                        identity=idf32[0:16, 0:16]),
                        waits=[("dve", max(BG_D, lastbg)), ("act", BG_A)])
                    TE_TB = P.now("te")
                    P.emit("dve", lambda v, ch=ch: v.tensor_copy(out=b_tm[:, ch, :], in_=pb[4][:, 0:16]),
                           waits=[("te", TE_TB)])
                    DV_TB = P.now("dve")
                    P.emit("te", lambda te, ch=ch: te.transpose(
                        out=pb[4][:, 0:16], in_=graw[:, ch * 128:(ch + 1) * 128],
                        identity=idf32[0:16, 0:16]), waits=[("dve", DV_TB)])
                    TE_TG = P.now("te")
                    P.emit("dve", lambda v, ch=ch: v.tensor_copy(out=g_tm[:, ch, :], in_=pb[4][:, 0:16]),
                           waits=[("te", TE_TG)])
                    lastbg = P.now("dve")
            BG_B = P.barrier()
            P.mark("BG")

            BG2_B = P.barrier()
        es3 = ExitStack()
        emit_delta_phase(nc, P, es3, pb, pbh, dict(
            ktm=ktm_t, vtm=vtm_t, kfm=kf, qfm=qf, g_tm=g_tm, b_tm=b_tm,
            id128=id128, id64s=id64s, idf32=idf32, ones128=ones128, ones_f=ones_f,
            ones_qb=ones_qb, maskL=maskL, tricum=tricum, rowsel=rowsel, obuf=obf),
            BG2_B)
        DELTA_B = P.barrier()
        P.mark("DELTA")
        es3.close()
        es2.close()
        h_cm = sb("h_cm", [128, HT, TOWN], BF16)
        ctx_xn_own = nc.sbuf_tensor("xn_own", [128, 16, TOWN], BF16)
        xn_own = ctx_xn_own.__enter__()
        ctx_yn_cm = nc.sbuf_tensor("yn_cm", [128, 8, TOWN], BF16)
        yn_cm = ctx_yn_cm.__enter__()
        emit_xn(io["x_own_cm"], 2, rstd_o, mr_o, xn_own, "o",
                prior=P.bwaits(DELTA_B, "te", "act", "dve"))
        XN_OWN = P.now("dve")
        XNO_TE = P.now("te")
        if debug:
            P.dma_emit("sy", lambda sy: sy.dma_start(out=io["xn_dbg"][:], in_=xn_own[:]),
                       waits=[("dve", XN_OWN)])

        with nc.sbuf_tensor("ytm", [128, 2, 1024], F32) as ytm, \
             nc.sbuf_tensor("yss", [128, 2, 16], F32) as yss, \
             nc.sbuf_tensor("yrs", [128, 2, 16], F32) as yrs, \
             nc.sbuf_tensor("zsil", [128, 2, 1024], F32) as zsil, \
             nc.sbuf_tensor("wz", [128, 2, 16, 512], BF16) as wz, \
             nc.sbuf_tensor("ynt", [128, 1, 1024], BF16) as ynt:
            wzdm = [0, 0]
            for zc in range(2):
                wzdm[zc] = P.dma_emit("gp", lambda gp, zc=zc: gp.dma_start(
                    out=wz[:, zc, :, :], in_=io["wz_t"][zc].rearrange("a p c -> p a c")),
                    waits=[("te", XNO_TE), ("dve", XN_OWN), ("act", DELTA_B["act"][1])]
                    if zc == 0 else [])
            lasty = [0, 0]
            last_yn_p = [0, 0]
            red_done = 0
            last_ynt_te = 0
            for tt in range(8):
                par = tt % 2
                ch = 8 + tt
                for vp in range(8):
                    P.emit("te", lambda te, vp=vp, tt=tt: te.transpose(
                        out=pbh[4][:, 0:128], in_=obf[:, vp, tt, :], identity=id128[:]),
                        waits=([("dve", P.now("dve"))] + P.bwaits(DELTA_B, "act", "dve")) if vp == 0
                        else [("dve", P.now("dve"))])
                    TE_OT = P.now("te")
                    P.emit("dve", lambda v, vp=vp, par=par: v.tensor_copy(
                        out=ytm[:, par, vp * 128:(vp + 1) * 128], in_=pbh[4][:, 0:128]),
                        waits=[("te", TE_OT)])
                Y_D = P.now("dve")
                for zc in range(2):
                    for kt in range(16):
                        P.emit("te", lambda te, tt=tt, zc=zc, kt=kt: te.matmul(
                            out=pb[5][:, 0:512], lhsT=xn_own[:, kt, tt * 128:(tt + 1) * 128],
                            rhs=wz[:, zc, kt, :], start=(kt == 0), stop=(kt == 15)),
                            waits=[("dmag", wzdm[1]), ("act", P.now("act")), ("dve", XN_OWN)]
                            if kt == 0 else [])
                    TE_Z = P.now("te")
                    P.emit("act", lambda s, par=par, zc=zc: s.activation(
                        out=zsil[:, par, zc * 512:(zc + 1) * 512], in_=pb[5][:, 0:512], func=AF.Silu),
                        waits=[("te", TE_Z), ("dve", last_yn_p[par])])
                Z_A = P.now("act")
                for hq in range(2):
                    P.emit("act", lambda s, par=par, hq=hq: s.activation(
                        out=pb[6 + hq][:, 0:512], in_=ytm[:, par, hq * 512:(hq + 1) * 512],
                        func=AF.Square),
                        waits=[("dve", max(Y_D, red_done))])
                A_SQ = P.now("act")
                for vh in range(16):
                    P.emit("dve", lambda v, par=par, vh=vh: v.tensor_reduce(
                        out=yrs[:, par, vh:vh + 1],
                        in_=pb[6 + vh // 8][:, (vh % 8) * 64:(vh % 8 + 1) * 64],
                        axis=AX.X, op=ALU.add),
                        waits=[("act", A_SQ)] if vh == 0 else [("dve", P.now("dve"))])
                P.emit("dve", lambda v, par=par: v.tensor_scalar(
                    out=yrs[:, par, :], in0=yrs[:, par, :], scalar1=1.0 / DV, scalar2=None,
                    op0=ALU.mult), waits=[("dve", P.now("dve"))])
                red_done = P.now("dve")
                DV_RS = red_done
                P.emit("act", lambda s, par=par: s.activation(
                    out=yss[:, par, :], in_=yrs[:, par, :], func=AF.Sqrt, bias=cst[:, 1:2]),
                    waits=[("dve", DV_RS)])
                A_RS = P.now("act")
                P.emit("dve", lambda v, par=par: v.reciprocal(
                    out=yrs[:, par, :], in_=yss[:, par, :]), waits=[("act", A_RS)])
                DV_RC = P.now("dve")
                for vh in range(16):
                    P.emit("dve", lambda v, par=par, vh=vh: v.tensor_scalar(
                        out=ytm[:, par, vh * 64:(vh + 1) * 64], in0=ytm[:, par, vh * 64:(vh + 1) * 64],
                        scalar1=yrs[:, par, vh:vh + 1], scalar2=None, op0=ALU.mult),
                        waits=[("dve", DV_RC)])
                P.emit("dve", lambda v, par=par: v.tensor_tensor(
                    out=ynt[:, 0, :], in0=ytm[:, par, :], in1=zsil[:, par, :], op=ALU.mult),
                    waits=[("dve", P.now("dve")), ("act", Z_A), ("te", last_ynt_te)])
                YN_D = P.now("dve")
                last_yn_p[par] = YN_D
                for vt in range(8):
                    P.emit("te", lambda te, par=par, vt=vt: te.transpose(
                        out=pbh[4][:, 0:128], in_=ynt[:, 0, vt * 128:(vt + 1) * 128],
                        identity=id128[:]), waits=[("dve", max(YN_D, P.now("dve")))])
                    TE_YT = P.now("te")
                    P.emit("dve", lambda v, tt=tt, vt=vt: v.tensor_scalar(
                        out=yn_cm[:, vt, tt * 128:(tt + 1) * 128], in0=pbh[4][:, 0:128],
                        scalar1=nwcol[:, 0:1], scalar2=None, op0=ALU.mult),
                        waits=[("te", TE_YT)])
                last_ynt_te = P.now("te")
                lasty[par] = P.now("dve")
        GATE_B = P.barrier()
        P.mark("GATE")
        if debug:
            P.dma_emit("sy", lambda sy: sy.dma_start(out=io["yn_dbg"][:], in_=yn_cm[:]),
                       waits=[GATE_B["dve"]])

        with nc.sbuf_tensor("wo", [128, 2, 512], BF16) as wo, \
             nc.sbuf_tensor("aev", [128, 2, 512], F32) as aev:
            lastw = [0, 0]
            lasta = [0, 0]
            ev_prev = 0
            for dc in range(4):
                for kt in range(8):
                    par = kt % 2
                    wdm = P.dma_emit("gp", lambda gp, dc=dc, kt=kt, par=par: gp.dma_start(
                        out=wo[:, par, :], in_=io["wo_t"][dc, kt, :, :]),
                        waits=[("te", lastw[par])] if (dc, kt) >= (0, 2)
                        else P.bwaits(GATE_B, "te", "act", "dve"))
                    for tt in range(8):
                        w = []
                        if tt == 0:
                            w.append(("dmag", wdm))
                        if kt == 0:
                            w.append(("dve", max(ev_prev, GATE_B["dve"][1])))
                            if tt == 0 and dc == 0:
                                w.append(("act", GATE_B["act"][1]))
                        P.emit("te", lambda te, kt=kt, tt=tt, par=par: te.matmul(
                            out=pb[tt][:, 0:512], lhsT=yn_cm[:, kt, tt * 128:(tt + 1) * 128],
                            rhs=wo[:, par, :], start=(kt == 0), stop=(kt == 7)),
                            waits=w)
                    lastw[par] = P.now("te")
                TE_WO = P.now("te")
                for tt in range(8):
                    epar = tt % 2
                    P.emit("dve", lambda v, tt=tt, epar=epar: v.tensor_copy(
                        out=aev[:, epar, :], in_=pb[tt][:, 0:512]),
                        waits=[("te", TE_WO), ("dma", lasta[epar])])
                    AD = P.now("dve")
                    admc = P.dma_emit("sy", lambda sy, tt=tt, dc=dc, epar=epar: sy.dma_start(
                        out=attn_sc[tt * 128:(tt + 1) * 128, dc * 512:(dc + 1) * 512],
                        in_=aev[:, epar, :]), waits=[("dve", AD)])
                    lasta[epar] = admc
                ev_prev = P.now("dve")
        WOUT_B = P.barrier()
        P.mark("WOUT")
        if debug:
            P.dma_emit("sy", lambda sy: sy.dma_start(out=io["attn_dbg"][:], in_=attn_sc[:]),
                       waits=[WOUT_B["dma"]])
        ctx_yn_cm.__exit__(None, None, None)

        with nc.sbuf_tensor("bt", [128, 2, 512], U8) as bt, \
             nc.sbuf_tensor("sstrip", [2, 1, 8, 512], F16) as sstrip, \
             nc.sbuf_tensor("sstripb", [2, 1, 8, 512], BF16) as sstripb, \
             nc.sbuf_tensor("i1", [128, 1, 128], I32) as i1, \
             nc.sbuf_tensor("i2", [128, 1, 128], I32) as i2, \
             nc.sbuf_tensor("wdq", [128, 2, 8, 2, 512], BF16) as wdq, \
             nc.sbuf_tensor("gsil", [128, 4, TOWN], F32) as gsil:
            bti = bt.ap().bitcast(I32)

            def dequant_granule(srcap, scrow_ap, par, dqslot, pp, extra_w=()):
                dm = P.dma_emit("gp", lambda gp: gp.dma_start(out=bt[:, par, :], in_=srcap),
                                waits=list(extra_w) + [("dve", P.now("dve"))])
                P.emit("te", lambda te: te.matmul(
                    out=pb[6][:, 0:512], lhsT=ones2f[:, 0:128], rhs=scrow_ap,
                    start=True, stop=True), waits=[("dve", P.now("dve"))])
                TE_SC = P.now("te")
                P.emit("dve", lambda v: v.tensor_scalar(
                    out=i1[:, 0, :], in0=bti[:, par, :], scalar1=0x08080808, scalar2=4,
                    op0=ALU.bitwise_and, op1=ALU.logical_shift_left), waits=[("dmag", dm)])
                P.emit("dve", lambda v: v.tensor_scalar(
                    out=i2[:, 0, :], in0=bti[:, par, :], scalar1=0x07070707, scalar2=2,
                    op0=ALU.bitwise_and, op1=ALU.logical_shift_left))
                P.emit("dve", lambda v: v.tensor_tensor(
                    out=i1[:, 0, :], in0=i1[:, 0, :], in1=i2[:, 0, :], op=ALU.bitwise_or),
                    waits=[("dve", P.now("dve"))])
                P.emit("dve", lambda v: v.tensor_tensor(
                    out=wdq[:, pp, dqslot, 0, :], in0=i1.ap().bitcast(FP8)[:, 0, :],
                    in1=pb[6][:, 0:512], op=ALU.mult),
                    waits=[("te", TE_SC), ("dve", P.now("dve"))])
                P.emit("dve", lambda v: v.tensor_scalar(
                    out=i2[:, 0, :], in0=bti[:, par, :], scalar1=0x70707070, scalar2=2,
                    op0=ALU.bitwise_and, op1=ALU.logical_shift_right))
                P.emit("dve", lambda v: v.tensor_scalar(
                    out=i1[:, 0, :], in0=bti[:, par, :], scalar1=0x7F7F7F7F, scalar2=None,
                    op0=ALU.bitwise_and), waits=[("dve", P.now("dve"))])
                P.emit("dve", lambda v: v.tensor_tensor(
                    out=i1[:, 0, :], in0=bti[:, par, :], in1=i1[:, 0, :], op=ALU.bitwise_xor),
                    waits=[("dve", P.now("dve"))])
                P.emit("dve", lambda v: v.tensor_tensor(
                    out=i2[:, 0, :], in0=i1[:, 0, :], in1=i2[:, 0, :], op=ALU.bitwise_or),
                    waits=[("dve", P.now("dve"))])
                P.emit("dve", lambda v: v.tensor_tensor(
                    out=wdq[:, pp, dqslot, 1, :], in0=i2.ap().bitcast(FP8)[:, 0, :],
                    in1=pb[6][:, 0:512], op=ALU.mult), waits=[("dve", P.now("dve"))])
                return P.now("dve")

            NOFFN = bool(os.environ.get("DEVK_NOFFN"))
            sc_cast = 0
            bc_te = 0
            te_mm = [WOUT_B["te"][1], WOUT_B["te"][1]]
            gsil_free = [0] * 8
            bankev = [("dve", WOUT_B["dve"][1])] * 4
            for it in range(0 if NOFFN else 22):
                h512, part = it // 2, it % 2
                pp = it % 2
                spar = part
                sdm = P.dma_emit("sy", lambda sy, part=part, h512=h512: sy.dma_start(
                    out=sstrip[:, 0, :, :], in_=io["gus"][part, :, h512, :, :]),
                    waits=[("te", te_mm[pp]), ("dve", sc_cast)]
                          + (P.bwaits(WOUT_B, "te", "act", "dve") if it == 0 else []))
                P.emit("dve", lambda v: v.tensor_copy(
                    out=sstripb[:, 0, :, :], in_=sstrip[:, 0, :, :]),
                    waits=[("dma", sdm), ("te", bc_te)])
                sc_cast = P.now("dve")
                for ib in range(8):
                    xw = [("te", te_mm[pp])] if ib == 0 else []
                    if it == 0 and ib < 2:
                        xw = xw + P.bwaits(WOUT_B, "te", "act", "dve")
                    dequant_granule(io["gub"][part, h512, ib, :, :],
                                    sstripb[:, 0, ib, :], ib % 2, ib, pp,
                                    extra_w=xw)
                DQ_D = P.now("dve")
                bc_te = P.now("te")
                for o128 in range(4):
                    for tch in range(2):
                        g8 = o128 * 2 + tch
                        pbk = pb[g8 % 4]
                        P.emit("te", lambda te, o128=o128, tch=tch, pbk=pbk, pp=pp:
                               te.matmul(
                                   out=pbk[:, 0:512],
                                   lhsT=wdq[:, pp, 0, 0, o128 * 128:(o128 + 1) * 128],
                                   rhs=xn_own[:, 0, tch * 512:(tch + 1) * 512],
                                   start=True, stop=False),
                               waits=[("dve", DQ_D)] + [bankev[g8 % 4]])
                        for ib in range(8):
                            for lh in range(2):
                                if ib == 0 and lh == 0:
                                    continue
                                P.emit("te", lambda te, o128=o128, tch=tch, ib=ib, lh=lh,
                                       pbk=pbk, pp=pp:
                                       te.matmul(
                                           out=pbk[:, 0:512],
                                           lhsT=wdq[:, pp, ib, lh, o128 * 128:(o128 + 1) * 128],
                                           rhs=xn_own[:, ib * 2 + lh, tch * 512:(tch + 1) * 512],
                                           start=False, stop=(ib == 7 and lh == 1)))
                        TE_GU = P.now("te")
                        if part == 0:
                            P.emit("act", lambda s, o128=o128, tch=tch, pbk=pbk: s.activation(
                                out=gsil[:, o128, tch * 512:(tch + 1) * 512], in_=pbk[:, 0:512],
                                func=AF.Silu),
                                waits=[("te", TE_GU), ("dve", gsil_free[g8])])
                            bankev[g8 % 4] = ("act", P.now("act"))
                        else:
                            P.emit("dve", lambda v, h512=h512, o128=o128, tch=tch, pbk=pbk:
                                   v.tensor_tensor(
                                       out=h_cm[:, h512 * 4 + o128, tch * 512:(tch + 1) * 512],
                                       in0=pbk[:, 0:512], in1=gsil[:, o128, tch * 512:(tch + 1) * 512],
                                       op=ALU.mult),
                                   waits=[("te", TE_GU), ("act", P.now("act"))])
                            gsil_free[g8] = P.now("dve")
                            bankev[g8 % 4] = ("dve", P.now("dve"))
                te_mm[pp] = P.now("te")
            GU_B = P.barrier()
            P.mark("GU")
            if debug:
                P.dma_emit("sy", lambda sy: sy.dma_start(out=io["h_dbg"][:], in_=h_cm[:]),
                           waits=[GU_B["dve"], GU_B["act"]])
                P.dma_emit("sy", lambda sy: sy.dma_start(out=io["wdq_dbg"][:], in_=wdq[:]),
                           waits=[GU_B["dve"]])
                P.dma_emit("sy", lambda sy: sy.dma_start(out=io["gsil_dbg"][:], in_=gsil[:]),
                           waits=[GU_B["act"]])
        ctx_xn_own.__exit__(None, None, None)

        with nc.sbuf_tensor("btd", [128, 2, 512], U8) as btd, \
             nc.sbuf_tensor("i1d", [128, 2, 128], I32) as i1d, \
             nc.sbuf_tensor("i2d", [128, 2, 128], I32) as i2d, \
             nc.sbuf_tensor("wdd", [128, 22, 2, 512], BF16) as wdd, \
             nc.sbuf_tensor("dsp", [2, 1, 512], F16) as dsp, \
             nc.sbuf_tensor("dspb", [2, 1, 512], BF16) as dspb, \
             nc.sbuf_tensor("ytile", [128, 8, 2048], BF16) as ytile, \
             nc.sbuf_tensor("yp", [128, 4], F32) as yp, \
             nc.sbuf_tensor("yr", [128, 2, 1], F32) as yr, \
             nc.sbuf_tensor("xres", [128, 1, 2048], F32) as xres, \
             nc.sbuf_tensor("ares", [128, 2048], F32) as ares:
            btid = btd.ap().bitcast(I32)
            NOFFN = bool(os.environ.get("DEVK_NOFFN"))
            dbankev = [0] * 4
            scst = [0, 0]
            dnfree_h = [GU_B["te"][1], GU_B["te"][1]]

            def dq_down(d512, hb, first=False):
                par = hb % 2
                sdm = P.dma_emit("sy", lambda sy, d512=d512, hb=hb: sy.dma_start(
                    out=dsp[:, 0, :], in_=io["dns"][d512, :, hb, :]),
                    waits=[("dve", scst[0])]
                          + (P.bwaits(GU_B, "te", "act", "dve") if first else []))
                P.emit("dve", lambda v: v.tensor_copy(
                    out=dspb[:, 0, :], in_=dsp[:, 0, :]),
                    waits=[("dma", sdm), ("te", scst[1])]
                          + (P.bwaits(GU_B, "dve") if first else []))
                scst[0] = P.now("dve")
                dm = P.dma_emit("gp", lambda gp, d512=d512, hb=hb, par=par: gp.dma_start(
                    out=btd[:, par, :], in_=io["dnb"][d512, hb, :, :]),
                    waits=([("dve", P.now("dve"))]
                           + ([("te", dnfree_h[hb // 11])] if (hb % 11) < 2 else [])
                           + (P.bwaits(GU_B, "te", "act", "dve") if first else [])))
                P.emit("te", lambda te: te.matmul(
                    out=pb[6][:, 0:512], lhsT=ones2f[:, 0:128], rhs=dspb[:, 0, :],
                    start=True, stop=True), waits=[("dve", P.now("dve"))])
                TE_SC = P.now("te")
                scst[1] = TE_SC
                P.emit("dve", lambda v, par=par: v.tensor_scalar(
                    out=i1d[:, par, :], in0=btid[:, par, :], scalar1=0x08080808, scalar2=4,
                    op0=ALU.bitwise_and, op1=ALU.logical_shift_left), waits=[("dmag", dm)])
                P.emit("dve", lambda v, par=par: v.tensor_scalar(
                    out=i2d[:, par, :], in0=btid[:, par, :], scalar1=0x07070707, scalar2=2,
                    op0=ALU.bitwise_and, op1=ALU.logical_shift_left))
                P.emit("dve", lambda v, par=par: v.tensor_tensor(
                    out=i1d[:, par, :], in0=i1d[:, par, :], in1=i2d[:, par, :], op=ALU.bitwise_or),
                    waits=[("dve", P.now("dve"))])
                P.emit("dve", lambda v, par=par, hb=hb: v.tensor_tensor(
                    out=wdd[:, hb, 0, :], in0=i1d.ap().bitcast(FP8)[:, par, :],
                    in1=pb[6][:, 0:512], op=ALU.mult),
                    waits=[("te", TE_SC), ("dve", P.now("dve"))])
                P.emit("dve", lambda v, par=par: v.tensor_scalar(
                    out=i2d[:, par, :], in0=btid[:, par, :], scalar1=0x70707070, scalar2=2,
                    op0=ALU.bitwise_and, op1=ALU.logical_shift_right))
                P.emit("dve", lambda v, par=par: v.tensor_scalar(
                    out=i1d[:, par, :], in0=btid[:, par, :], scalar1=0x7F7F7F7F, scalar2=None,
                    op0=ALU.bitwise_and), waits=[("dve", P.now("dve"))])
                P.emit("dve", lambda v, par=par: v.tensor_tensor(
                    out=i1d[:, par, :], in0=btid[:, par, :], in1=i1d[:, par, :],
                    op=ALU.bitwise_xor), waits=[("dve", P.now("dve"))])
                P.emit("dve", lambda v, par=par: v.tensor_tensor(
                    out=i2d[:, par, :], in0=i1d[:, par, :], in1=i2d[:, par, :],
                    op=ALU.bitwise_or), waits=[("dve", P.now("dve"))])
                P.emit("dve", lambda v, par=par, hb=hb: v.tensor_tensor(
                    out=wdd[:, hb, 1, :], in0=i2d.ap().bitcast(FP8)[:, par, :],
                    in1=pb[6][:, 0:512], op=ALU.mult), waits=[("dve", P.now("dve"))])

            def dn_mms(t128, hlo, hhi, dq_cnt, openg, closeg, d512):
                pbk = pb[t128 % 4]
                for hb in range(hlo, hhi):
                    for lh in range(2):
                        w = []
                        if hb == hlo and lh == 0:
                            w = [("dve", max(dq_cnt, dbankev[t128 % 4]))] if openg \
                                else [("dve", dq_cnt)]
                        P.emit("te", lambda te, t128=t128, hb=hb, lh=lh, pbk=pbk: te.matmul(
                            out=pbk[:, 0:512], lhsT=h_cm[:, hb * 2 + lh, t128 * 128:(t128 + 1) * 128],
                            rhs=wdd[:, hb, lh, :], start=(openg and hb == hlo and lh == 0),
                            stop=(closeg and hb == hhi - 1 and lh == 1)), waits=w)
                if closeg:
                    TE_DN = P.now("te")
                    P.emit("dve", lambda v, t128=t128, d512=d512, pbk=pbk: v.tensor_copy(
                        out=ytile[:, t128, d512 * 512:(d512 + 1) * 512], in_=pbk[:, 0:512]),
                        waits=[("te", TE_DN)])
                    dbankev[t128 % 4] = P.now("dve")

            if not NOFFN:
                for hb in range(11):
                    dq_down(0, hb, first=(hb == 0))
                DQ_H0 = P.now("dve")
                for d512 in range(4):
                    for t128 in range(4):
                        dn_mms(t128, 0, 11, DQ_H0, True, False, d512)
                    for hb in range(11, 22):
                        dq_down(d512, hb)
                    DQ_H1 = P.now("dve")
                    for t128 in range(4):
                        dn_mms(t128, 11, 22, DQ_H1, False, True, d512)
                    for t128 in range(4, 8):
                        dn_mms(t128, 0, 11, DQ_H0, True, False, d512)
                    dnfree_h[0] = P.now("te")
                    if d512 < 3:
                        for hb in range(11):
                            dq_down(d512 + 1, hb)
                    DQ_H0 = P.now("dve")
                    for t128 in range(4, 8):
                        dn_mms(t128, 11, 22, DQ_H1, False, True, d512)
                    dnfree_h[1] = P.now("te")
            Y_D2 = P.now("dve")
            if debug:
                P.dma_emit("sy", lambda sy: sy.dma_start(out=io["yt_dbg"][:], in_=ytile[:]),
                           waits=[("dve", Y_D2)])
            lastx = 0
            last_odm = 0
            sqfree = [0, 0]
            for t128 in range(0 if NOFFN else 8):
                par = t128 % 2
                xdm = P.dma_emit("sy", lambda sy, t128=t128: sy.dma_start(
                    out=xres[:, 0, :], in_=io["x_own_tm"][t128 * 128:(t128 + 1) * 128, :]),
                    waits=[("dve", lastx)])
                adm = P.dma_emit("sy", lambda sy, t128=t128: sy.dma_start(
                    out=ares[:], in_=attn_sc[t128 * 128:(t128 + 1) * 128, :]),
                    waits=[("dve", lastx),
                           ("dma", max(last_odm, WOUT_B["dma"][1]))])
                for g in range(4):
                    sbk = pb[5 if g % 2 == 0 else 7]
                    P.emit("act", lambda s, t128=t128, g=g, sbk=sbk: s.activation(
                        out=sbk[:, 0:512], in_=ytile[:, t128, g * 512:(g + 1) * 512],
                        func=AF.Square),
                        waits=[("dve", max(Y_D2, sqfree[g % 2]))])
                    A_S2 = P.now("act")
                    P.emit("dve", lambda v, g=g, sbk=sbk: v.tensor_reduce(
                        out=yp[:, g:g + 1], in_=sbk[:, 0:512], axis=AX.X, op=ALU.add),
                        waits=[("act", A_S2)])
                    sqfree[g % 2] = P.now("dve")
                P.emit("dve", lambda v, par=par: v.tensor_reduce(
                    out=yr[:, par, :], in_=yp[:], axis=AX.X, op=ALU.add),
                    waits=[("dve", P.now("dve"))])
                P.emit("dve", lambda v, par=par: v.tensor_scalar(
                    out=yr[:, par, :], in0=yr[:, par, :], scalar1=1.0 / D, scalar2=None,
                    op0=ALU.mult), waits=[("dve", P.now("dve"))])
                DV_Y = P.now("dve")
                P.emit("act", lambda s, par=par: s.activation(
                    out=yp[:, 0:1], in_=yr[:, par, :], func=AF.Sqrt, bias=cst[:, 1:2]),
                    waits=[("dve", DV_Y)])
                A_Y = P.now("act")
                P.emit("dve", lambda v, par=par: v.reciprocal(
                    out=yr[:, par, :], in_=yp[:, 0:1]), waits=[("act", A_Y)])
                DV_YR = P.now("dve")
                P.emit("dve", lambda v: v.tensor_tensor(
                    out=ares[:], in0=xres[:, 0, :], in1=ares[:], op=ALU.add),
                    waits=[("dma", adm), ("dma", xdm)])
                P.emit("dve", lambda v, t128=t128, par=par: v.scalar_tensor_tensor(
                    out=ares[:], in0=ytile[:, t128, :], scalar=yr[:, par, 0:1],
                    in1=ares[:], op0=ALU.mult, op1=ALU.add),
                    waits=[("dve", P.now("dve")), ("dve", DV_YR)])
                OD = P.now("dve")
                odm = P.dma_emit("sy", lambda sy, t128=t128: sy.dma_start(
                    out=io["out"][t128 * 128:(t128 + 1) * 128, :], in_=ares[:]),
                    waits=[("dve", OD)])
                last_odm = odm
                lastx = OD
            FIN_DMA = P.dma_cnt
            P.emit("sy", lambda sy: sy.nop(), waits=[("dma", FIN_DMA)], inc=False)

        blk = es.enter_context(nc.Block())

        @blk.sync
        def _(sy):
            P.replay("sy", sy)

        @blk.tensor
        def _(te):
            P.replay("te", te)

        @blk.vector
        def _(v):
            P.replay("dve", v)

        @blk.scalar
        def _(s):
            P.replay("act", s)

        @blk.gpsimd
        def _(g):
            P.replay("gp", g)

    return nc, io



def emit_delta_phase(nc, P, es, pb, pbh, tn, B):
    sb = lambda n, s, d: es.enter_context(nc.sbuf_tensor(n, s, d))
    ktm, vtm, kfm, qfm = tn["ktm"], tn["vtm"], tn["kfm"], tn["qfm"]
    g_tm, b_tm = tn["g_tm"], tn["b_tm"]
    id128, id64s, idf32 = tn["id128"], tn["id64s"], tn["idf32"]
    ones128, ones_f, ones_qb = tn["ones128"], tn["ones_f"], tn["ones_qb"]
    maskL, tricum, rowsel = tn["maskL"], tn["tricum"], tn["rowsel"]
    obuf = tn["obuf"]
    NIC = NCH * NV

    gc_tm = sb("gc_tm", [128, NIC], F32)
    gcHn = sb("gcHn", [128, NIC], F32)
    egc = sb("egc", [128, NIC], F32)
    einv_t = sb("einv_t", [128, NIC], F32)
    egcC = sb("egcC", [128, NIC], F32)
    begc = sb("begc", [128, NIC], F32)
    bneg = sb("bneg", [128, NIC], F32)
    dtmp = sb("dtmp", [128, NIC], F32)

    NROT = 2
    diagH_t = sb("d_diagH", [128, NROT, 128], F32)
    diagHn_t = sb("d_diagHn", [128, NROT, 128], F32)
    diagE_t = sb("d_diagE", [128, NROT, 128], BF16)
    expGs_t = sb("d_expGs", [128, NROT, 128], F32)
    expGi_t = sb("d_expGi", [128, NROT, 128], F32)
    abuf_t = sb("d_abuf", [128, NROT, 128], BF16)
    qbart_t = sb("d_qbart", [64, NV, 128], F32)

    Bm_t = sb("d_Bm", [128, NV, 128], BF16)
    Ct_t = sb("d_Ct", [128, NV, 128], BF16)
    Bk_t = sb("d_Bk", [128, NV, 2, 128], BF16)
    Ck_t = sb("d_Ck", [128, NV, 2, 128], BF16)
    At_t = sb("d_At", [128, NV, 128], BF16)
    Yb_t = sb("d_Yb", [128, NV, 2, 128], BF16)
    khat_t = sb("d_khat", [128, NV, DK], BF16)
    AUs_t = sb("d_AUs", [64, NV, 128], BF16)
    QmAW_t = sb("d_QmAW", [64, NV, 128], BF16)
    Ptp_t = sb("d_Ptp", [64, NV, DK], BF16)
    Zs_t = sb("d_Zs", [64, NV, DV], BF16)
    IegcC_t = sb("d_IegcC", [64, NV, 64], BF16)
    Svh_t = sb("d_Svh", [64, NV, DV], BF16)

    diagH = lambda vh: diagH_t[:, vh % NROT, :]
    diagHn = lambda vh: diagHn_t[:, vh % NROT, :]
    diagE = lambda vh: diagE_t[:, vh % NROT, :]
    expGs = lambda vh: expGs_t[:, vh % NROT, :]
    expGi = lambda vh: expGi_t[:, vh % NROT, :]
    abuf = lambda vh: abuf_t[:, vh % NROT, :]
    qbart = lambda vh: qbart_t[:, vh, :]
    Bm = lambda vh: Bm_t[:, vh, :]
    Ct = lambda vh: Ct_t[:, vh, :]
    At = lambda vh: At_t[:, vh, :]
    khat = lambda vh: khat_t[:, vh, :]
    AUs = lambda vh: AUs_t[:, vh, :]
    QmAW = lambda vh: QmAW_t[:, vh, :]
    Ptp = lambda vh: Ptp_t[:, vh, :]
    Zs = lambda vh: Zs_t[:, vh, :]
    IegcC = lambda vh: IegcC_t[:, vh, :]
    Svh = lambda vh: Svh_t[:, vh, :]

    def grel_v(i):
        return pb[i][:, 0:128]

    def gram_kk(i):
        return pb[2 + i][:, 0:128]

    def gram_qk(i):
        return pb[2 + i][:, 128:256]

    def sq_v(i):
        return pb[2 + i][:, 256:384]

    BT_V = pbh[4][:, 0:128]
    AT_V = pbh[4][:, 256:384]
    QB_V = pb[5][0:64, 0:128]
    GC_V = pb[6][:, 0:NIC]
    PT_V = pb[7][0:64, 0:64]
    Z_V = pb[7][0:64, 64:128]
    O_V = pb[6][0:64, 256:384]
    S_V = pb[6][0:64, 384:448]

    def col(t, ch, vh):
        i = ch * NV + vh
        return t[:, i:i + 1]

    bw = [B["te"], B["dve"], B["act"], B["gp"], B["dma"], B["dmag"]]

    P.emit("dve", lambda v: v.memset(Svh_t[:], 0.0), waits=bw)
    CONST = P.now("dve")
    gflat = g_tm.ap().rearrange("p a b -> p (a b)")
    bflat = b_tm.ap().rearrange("p a b -> p (a b)")
    P.emit("te", lambda te: te.matmul(out=GC_V, lhsT=tricum[:], rhs=gflat,
                                      start=True, stop=True), waits=bw + [("dve", CONST)])
    TE_GC = P.now("te")
    P.emit("dve", lambda v: v.tensor_copy(out=gc_tm[:], in_=GC_V), waits=[("te", TE_GC)])
    GC_BASE = P.now("dve")
    P.emit("act", lambda s: s.activation(out=gcHn[:], in_=gc_tm[:], func=AF.Copy, scale=-1.0),
           waits=[("dve", GC_BASE)] + bw)
    P.emit("act", lambda s: s.activation(out=egc[:], in_=gc_tm[:], func=AF.Exp))
    ACT_E = P.now("act")
    P.emit("te", lambda te: te.matmul(out=GC_V, lhsT=rowsel[:], rhs=gc_tm[:],
                                      start=True, stop=True),
           waits=[("dve", GC_BASE), ("act", ACT_E)])
    TE_GCC = P.now("te")
    P.emit("dve", lambda v: v.tensor_tensor(out=dtmp[:], in0=GC_V, in1=gc_tm[:], op=ALU.subtract),
           waits=[("te", TE_GCC)])
    DVE_EINV = P.now("dve")
    P.emit("act", lambda s: s.activation(out=einv_t[:], in_=dtmp[:], func=AF.Exp),
           waits=[("dve", DVE_EINV)])
    P.emit("act", lambda s: s.activation(out=egcC[:], in_=GC_V, func=AF.Exp))
    P.emit("dve", lambda v: v.tensor_tensor(out=begc[:], in0=bflat, in1=egc[:], op=ALU.mult),
           waits=[("act", ACT_E)])
    P.emit("act", lambda s: s.activation(out=bneg[:], in_=bflat, func=AF.Copy, scale=-1.0))
    PREP_DVE = P.now("dve")
    PREP_ACT = P.now("act")

    grel_free = [0, 0]
    grel_free_a = [0, 0]
    gram_free = [0, 0]
    sq_free = [0, 0]
    sqf_act = [0, 0]
    awu_free_act = [0, 0]
    diag_free = [0, 0]
    expgs_free = [0, 0]
    abuf_free = [0, 0]
    iegc_free = [0] * NV
    b4_free_act = 0
    qb_free_dve = 0
    b5_free = dict(dve=0, act=0)
    o_free_act = 0
    s_free_act = 0
    sact = [0] * NV

    first = True
    for ch in range(NCH):
        te_prev = P.now("te")
        gp_diag = [0] * NV
        dve_b = [0] * NV
        act_b = [0] * NV
        for vh in range(NV):
            w = [("act", PREP_ACT), ("dve", PREP_DVE)] if first else []
            first = False
            w = w + [("te", max(diag_free[vh % 2], te_prev))]
            P.emit("dve", lambda v, ch=ch, vh=vh: v.tensor_scalar(
                out=diagH(vh), in0=idf32[:], scalar1=col(gc_tm, ch, vh), scalar2=None, op0=ALU.mult),
                waits=w)
            P.emit("dve", lambda v, ch=ch, vh=vh: v.tensor_scalar(
                out=diagHn(vh), in0=idf32[:], scalar1=col(gcHn, ch, vh), scalar2=None, op0=ALU.mult))
            P.emit("dve", lambda v, ch=ch, vh=vh: v.tensor_scalar(
                out=diagE(vh), in0=id128[:], scalar1=col(egc, ch, vh), scalar2=None, op0=ALU.mult))
            P.emit("dve", lambda v, ch=ch, vh=vh: v.tensor_scalar(
                out=IegcC(vh), in0=id64s[0:64, :], scalar1=col(egcC, ch, vh)[0:64, :],
                scalar2=None, op0=ALU.mult), waits=[("dve", iegc_free[vh])])
            gp_diag[vh] = P.now("dve")
            kh = vh // 2
            khs = (kh % 2) * 64
            kht = kh // 2
            par = vh % 2
            kslice = kfm[khs:khs + 64, kht, ch * C:(ch + 1) * C]
            qslice = qfm[khs:khs + 64, kht, ch * C:(ch + 1) * C]
            P.emit("te", lambda te, vh=vh, par=par: te.matmul(out=grel_v(par), lhsT=diagH(vh), rhs=ones_f[:],
                                                              start=True, stop=False),
                   waits=[("dve", max(gp_diag[vh], grel_free[par])),
                          ("act", max(awu_free_act[par], grel_free_a[par]))])
            P.emit("te", lambda te, vh=vh, par=par: te.matmul(out=grel_v(par), lhsT=ones_f[:], rhs=diagHn(vh),
                                                              start=False, stop=False))
            P.emit("te", lambda te, vh=vh, par=par: te.matmul(out=grel_v(par), lhsT=idf32[:], rhs=maskL[:],
                                                              start=False, stop=True))
            TE_GREL = P.now("te")
            P.emit("te", lambda te, ks=kslice, par=par: te.matmul(
                out=gram_kk(par), lhsT=ks, rhs=ks, start=True, stop=True),
                waits=[("dve", gram_free[par]), ("act", sqf_act[par])])
            P.emit("te", lambda te, ks=kslice, qs=qslice, par=par: te.matmul(
                out=gram_qk(par), lhsT=qs, rhs=ks, start=True, stop=True))
            P.emit("te", lambda te, vh=vh: te.matmul(out=QB_V, lhsT=ones_qb[:], rhs=diagE(vh),
                                                     start=True, stop=True),
                   waits=[("dve", max(qb_free_dve, gp_diag[vh]))])
            TE_G = P.now("te")
            diag_free[par] = TE_G
            P.emit("act", lambda s, vh=vh, par=par: s.activation(out=expGs(vh), in_=grel_v(par), func=AF.Exp),
                   waits=[("te", TE_GREL), ("dve", max(expgs_free[vh % 2], grel_free[par]))])
            grel_free_a[par] = P.now("act")
            ACT_X = P.now("act")
            P.emit("dve", lambda v, vh=vh, ch=ch, par=par: v.scalar_tensor_tensor(
                out=Bm(vh), in0=gram_kk(par), scalar=col(bneg, ch, vh), in1=expGs(vh),
                op0=ALU.mult, op1=ALU.mult), waits=[("act", ACT_X), ("te", te_prev)])
            P.emit("dve", lambda v, vh=vh: v.tensor_tensor(
                out=expGi(vh), in0=expGs(vh), in1=idf32[:], op=ALU.add))
            expgs_free[vh % 2] = P.now("dve")
            P.emit("dve", lambda v, vh=vh, ch=ch: v.tensor_scalar(
                out=khat(vh), in0=ktm[:, ch, vh // 2, :], scalar1=col(einv_t, ch, vh),
                scalar2=None, op0=ALU.mult), waits=[("te", te_prev)])
            P.emit("dve", lambda v, vh=vh, par=par: v.tensor_tensor(
                out=abuf(vh), in0=gram_qk(par), in1=expGi(vh), op=ALU.mult),
                waits=[("te", max(abuf_free[vh % 2], te_prev))])
            gram_free[par] = P.now("dve")
            DVE_BA = P.now("dve")
            P.emit("te", lambda te, vh=vh: te.transpose(out=BT_V, in_=Bm(vh), identity=id128[:]),
                   waits=[("dve", DVE_BA), ("act", b4_free_act)])
            TE_BT = P.now("te")
            P.emit("act", lambda s, vh=vh: s.activation(out=Ct(vh), in_=BT_V, func=AF.Copy),
                   waits=[("te", TE_BT)])
            b4a = P.now("act")
            P.emit("te", lambda te, vh=vh: te.transpose(out=AT_V, in_=abuf(vh), identity=id128[:]),
                   waits=[("act", b4a)])
            TE_AT = P.now("te")
            abuf_free[vh % 2] = TE_AT
            P.emit("act", lambda s, vh=vh: s.activation(out=At(vh), in_=AT_V, func=AF.Copy),
                   waits=[("te", TE_AT)])
            b4_free_act = P.now("act")
            P.emit("dve", lambda v, vh=vh, ch=ch: v.tensor_scalar(
                out=Yb_t[:, vh, 0, 0:64], in0=ktm[:, ch, vh // 2, :], scalar1=col(begc, ch, vh),
                scalar2=None, op0=ALU.mult), waits=[("te", te_prev)])
            P.emit("dve", lambda v, vh=vh, ch=ch: v.tensor_scalar(
                out=Yb_t[:, vh, 0, 64:128], in0=vtm[:, ch, vh, :],
                scalar1=bflat[:, ch * NV + vh: ch * NV + vh + 1],
                scalar2=None, op0=ALU.mult))
            P.emit("dve", lambda v, vh=vh, ch=ch, qs=qslice: v.tensor_tensor(
                out=qbart(vh), in0=qs, in1=QB_V, op=ALU.mult), waits=[("te", TE_G)])
            qb_free_dve = P.now("dve")
            dve_b[vh] = P.now("dve")
            act_b[vh] = P.now("act")

        bprev = [Bm(vh) for vh in range(NV)]
        cprev = [Ct(vh) for vh in range(NV)]
        yprev = [Yb_t[:, vh, 0, :] for vh in range(NV)]
        ydve = list(dve_b)
        bcrdy = [0] * NV
        bcrdy_act = list(act_b)
        ap_free = [0, 0]
        sqf = [max(s, dve_b[NV - 1]) for s in sq_free]
        sqa = list(sqf_act)
        for lev in range(NLEV):
            for vh in range(NV):
                par = vh % 2
                P.emit("te", lambda te, c=cprev[vh], y=yprev[vh], par=par: te.matmul(
                    out=grel_v(par), lhsT=c, rhs=y, start=True, stop=True),
                    waits=[("dve", max(ydve[vh], grel_free[par], ap_free[par])),
                           ("act", max(bcrdy_act[vh], grel_free_a[par]))])
                TE_AP = P.now("te")
                ynew = Yb_t[:, vh, (lev + 1) % 2, :]
                P.emit("dve", lambda v, y=yprev[vh], yn=ynew, par=par: v.tensor_tensor(
                    out=yn, in0=grel_v(par), in1=y, op=ALU.add), waits=[("te", TE_AP)])
                yprev[vh] = ynew
                ydve[vh] = P.now("dve")
                ap_free[par] = P.now("dve")
                grel_free[par] = P.now("dve")
                if lev < NLEV - 1:
                    bnew = Bk_t[:, vh, lev % 2, :]
                    cnew = Ck_t[:, vh, lev % 2, :]
                    P.emit("te", lambda te, c=cprev[vh], b=bprev[vh], par=par: te.matmul(
                        out=sq_v(par), lhsT=c, rhs=b, start=True, stop=True),
                        waits=[("dve", sqf[par]), ("act", max(bcrdy_act[vh], sqa[par]))])
                    TE_B = P.now("te")
                    P.emit("act", lambda s, bn=bnew, par=par: s.activation(out=bn, in_=sq_v(par),
                                                                           func=AF.Copy),
                           waits=[("te", TE_B)])
                    A_BN = P.now("act")
                    P.emit("te", lambda te, b=bprev[vh], c=cprev[vh], par=par: te.matmul(
                        out=sq_v(par), lhsT=b, rhs=c, start=True, stop=True),
                        waits=[("act", A_BN)])
                    TE_C = P.now("te")
                    P.emit("act", lambda s, cn=cnew, par=par: s.activation(out=cn, in_=sq_v(par),
                                                                           func=AF.Copy),
                           waits=[("te", TE_C)])
                    sqa[par] = P.now("act")
                    bcrdy_act[vh] = P.now("act")
                    bprev[vh], cprev[vh] = bnew, cnew
                    sqf[par] = 0
        sq_free[0], sq_free[1] = sqf[0], sqf[1]
        sqf_act[0], sqf_act[1] = sqa[0], sqa[1]

        dve_fin = [0] * NV
        act_fin = [0] * NV
        for vh in range(NV):
            par = vh % 2
            P.emit("te", lambda te, y=yprev[vh], vh=vh, par=par: te.matmul(
                out=grel_v(par), lhsT=y, rhs=At(vh), start=True, stop=True),
                waits=[("dve", max(ydve[vh], grel_free[par])),
                       ("act", max(awu_free_act[par], grel_free_a[par]))])
            TE_AWU = P.now("te")
            P.emit("dve", lambda v, vh=vh, par=par: v.tensor_tensor(
                out=QmAW(vh), in0=qbart(vh), in1=grel_v(par)[0:64, :], op=ALU.subtract),
                waits=[("te", TE_AWU)])
            DVE_QM = P.now("dve")
            grel_free[par] = DVE_QM
            P.emit("act", lambda s, vh=vh, par=par: s.activation(
                out=AUs(vh), in_=grel_v(par)[64:128, :], func=AF.Copy),
                waits=[("te", TE_AWU), ("dve", DVE_QM)])
            awu_free_act[par] = P.now("act")
            P.emit("te", lambda te, y=yprev[vh], vh=vh: te.matmul(
                out=PT_V, lhsT=y[:, 0:64], rhs=khat(vh), start=True, stop=True),
                waits=[("dve", b5_free["dve"]), ("act", b5_free["act"])])
            P.emit("te", lambda te, y=yprev[vh], vh=vh: te.matmul(
                out=Z_V, lhsT=khat(vh), rhs=y[:, 64:128], start=True, stop=True))
            TE_PZ = P.now("te")
            P.emit("dve", lambda v, vh=vh: v.scalar_tensor_tensor(
                out=Ptp(vh), in0=PT_V, scalar=-1.0, in1=IegcC(vh), op0=ALU.mult, op1=ALU.add),
                waits=[("te", TE_PZ)])
            P.emit("dve", lambda v, vh=vh: v.tensor_copy(
                out=Zs(vh), in_=Z_V), waits=[("te", TE_PZ), ("dve", P.now("dve"))])
            b5_free["dve"] = P.now("dve")
            iegc_free[vh] = P.now("dve")
            dve_fin[vh] = P.now("dve")
            act_fin[vh] = P.now("act")

        for vh in range(NV):
            vp = vh // 2
            half = (vh % 2) * 64
            if ch >= 8:
                P.emit("te", lambda te, vh=vh: te.matmul(
                    out=O_V, lhsT=Svh(vh), rhs=QmAW(vh), start=True, stop=False),
                    waits=[("dve", dve_fin[vh]),
                           ("act", max(act_fin[vh], sact[vh], o_free_act, s_free_act))])
                P.emit("te", lambda te, vh=vh: te.matmul(
                    out=O_V, lhsT=id64s[0:64, :], rhs=AUs(vh), start=False, stop=True))
                TE_O = P.now("te")
                P.emit("act", lambda s, vh=vh, ch=ch, vp=vp, half=half: s.activation(
                    out=obuf[half:half + 64, vp, ch - 8, :], in_=O_V, func=AF.Copy),
                    waits=[("te", TE_O)])
                o_free_act = P.now("act")
            P.emit("te", lambda te, vh=vh: te.matmul(
                out=S_V, lhsT=Ptp(vh), rhs=Svh(vh), start=True, stop=False),
                waits=[("act", max(sact[vh], o_free_act, s_free_act))])
            P.emit("te", lambda te, vh=vh: te.matmul(
                out=S_V, lhsT=id64s[0:64, :], rhs=Zs(vh), start=False, stop=True))
            TE_S = P.now("te")
            P.emit("act", lambda s, vh=vh: s.activation(
                out=Svh(vh), in_=S_V, func=AF.Copy), waits=[("te", TE_S)])
            sact[vh] = P.now("act")
            s_free_act = sact[vh]



def _make_pi(n):
    pi = np.empty(n, np.int64)
    for b0 in range(n // 256):
        pi[256 * b0:256 * b0 + 128] = 256 * b0 + 2 * np.arange(128)
        pi[256 * b0 + 128:256 * b0 + 256] = 256 * b0 + 2 * np.arange(128) + 1
    return pi


PI = _make_pi(2048)
SIG = _make_pi(5632)


def _prep_weights(inp):
    W_qkv = np.asarray(inp["L0_attn_W_qkv"], np.float32)
    W_z = np.asarray(inp["L0_attn_W_z"], np.float32)
    W_b = np.asarray(inp["L0_attn_W_b"], np.float32)
    W_a = np.asarray(inp["L0_attn_W_a"], np.float32)
    W_out = np.asarray(inp["L0_attn_W_out"], np.float32)
    conv_w = np.asarray(inp["L0_attn_conv_w"], np.float32)
    dt_bias = np.asarray(inp["L0_attn_dt_bias"], np.float32)
    A_log = np.asarray(inp["L0_attn_A_log"], np.float32)
    norm_w = np.asarray(inp["L0_attn_norm_weight"], np.float32)
    gup = np.asarray(inp["L0_ffn_gate_up_packed"])
    gus = np.asarray(inp["L0_ffn_gate_up_scales"])
    dnp_ = np.asarray(inp["L0_ffn_down_packed"])
    dns = np.asarray(inp["L0_ffn_down_scales"])

    w = {}
    wq = W_qkv[:, PI].T.reshape(16, 128, 16, 128)
    w["wqkv_t"] = np.ascontiguousarray(wq.transpose(2, 1, 0, 3))
    wz = W_z[:, PI].T.reshape(16, 128, 2, 512)
    w["wz_t"] = np.ascontiguousarray(wz.transpose(2, 0, 1, 3))
    w["wb_t"] = np.ascontiguousarray(W_b[:, PI].T.reshape(16, 128, 16))
    w["wa_t"] = np.ascontiguousarray(W_a[:, PI].T.reshape(16, 128, 16))
    wo = W_out[PI, :].T.reshape(8, 128, 4, 512)
    w["wo_t"] = np.ascontiguousarray(wo.transpose(2, 0, 1, 3))
    w["convw"] = np.ascontiguousarray(conv_w[:, 0, :].reshape(16, 128, 4))
    vecs = np.zeros((64, 3), np.float32)
    vecs[0:16, 0] = dt_bias
    vecs[0:16, 1] = A_log
    vecs[0:64, 2] = norm_w
    w["vecs"] = vecs
    sel2 = np.zeros((2, 128), ml_dtypes.bfloat16)
    sel2[0, 0:64] = 64.0
    sel2[1, 64:128] = 64.0
    w["sel2"] = sel2
    I = 5632
    gub = np.empty((2, 11, 8, 128, 512), np.uint8)
    guss = np.empty((2, 2, 11, 8, 512), np.float16)
    for part in range(2):
        rows = gup[part * I + SIG]
        bT = rows.view(np.uint8).reshape(I, 1024).T
        gub[part] = bT.reshape(8, 128, 11, 512).transpose(2, 0, 1, 3)
        sc = gus[:, part * I + SIG]
        guss[part] = sc.reshape(8, 2, 11, 512).transpose(1, 2, 0, 3)
    w["gub"] = np.ascontiguousarray(gub)
    w["gus"] = np.ascontiguousarray(guss)
    rows = dnp_[PI]
    bT2 = rows.view(np.uint8).reshape(2048, 2816).T
    w["dnb"] = np.ascontiguousarray(bT2.reshape(22, 128, 4, 512).transpose(2, 0, 1, 3))
    w["dns"] = np.ascontiguousarray(dns[:, PI].reshape(22, 2, 4, 512).transpose(2, 1, 0, 3))
    return w


def _prep_core_x(core, x_full):
    b, hf = core // 2, core % 2
    x = np.asarray(x_full, np.float32)[b]
    m = {}
    if hf == 0:
        xs = np.concatenate([np.zeros((1024, 2048), np.float32), x[0:1024]], 0)
    else:
        xs = x
    xs = np.ascontiguousarray(xs[:, PI])
    m["x_scan_tm"] = xs
    m["x_scan_cm"] = np.ascontiguousarray(xs.T)
    xo = np.ascontiguousarray(x[hf * 1024:(hf + 1) * 1024][:, PI])
    m["x_own_tm"] = xo
    m["x_own_cm"] = np.ascontiguousarray(xo.T)
    return m


def kernel(**inputs):
    w = _prep_weights(inputs)
    in_maps = []
    for core in range(8):
        m = dict(w)
        m.update(_prep_core_x(core, inputs["L0_attn_x"]))
        in_maps.append(m)
    nc, io = build()
    res = run_bass_kernel_spmd(nc, in_maps, core_ids=list(range(8)))
    out = np.empty((4, 2048, 2048), np.float32)
    for core in range(8):
        b, hf = core // 2, core % 2
        shard = np.asarray(res.results[core]["out"]).reshape(1024, 2048)
        out[b, hf * 1024:(hf + 1) * 1024][:, PI] = shard
    return out


# version 09
# speedup vs baseline: 1.1002x; 1.1002x over previous
import os
import numpy as np
import ml_dtypes
import concourse.bass as bass
import concourse.mybir as mybir
from concourse.bass_utils import run_bass_kernel_spmd
from contextlib import ExitStack

F32 = mybir.dt.float32
BF16 = mybir.dt.bfloat16
I32 = mybir.dt.int32
F16 = mybir.dt.float16
U8 = mybir.dt.uint8
FP8 = mybir.dt.float8e4
AF = mybir.ActivationFunctionType
ALU = mybir.AluOpType
AX = mybir.AxisListType

D = 2048
T = 2048
TH = 1024
_LAST_PLAN = None
TOWN = 1024
C = 128
NCH = 16
NV = 16
KH = 8
DK = DV = 64
NEG = -30000.0
NLEV = 7
I_FF = 5632
HT = 44
LN_EPS = 1e-5
EPS_A = 1e-6


class Plan:
    def __init__(self, nc, es):
        self.marks = {}
        self.nc = nc
        self.items = []
        self.cnt = {e: 0 for e in ("te", "dve", "act", "gp", "sy")}
        self.sems = {e: es.enter_context(nc.semaphore("s_" + e)) for e in self.cnt}
        self.dma = es.enter_context(nc.semaphore("s_dma"))
        self.dma_cnt = 0
        self.dmag = es.enter_context(nc.semaphore("s_dmag"))
        self.dmag_cnt = 0

    def emit(self, eng, fn, waits=(), inc=True):
        waits = tuple((we, wv) for (we, wv) in waits if wv)
        if inc:
            self.cnt[eng] += 1
        self.items.append((eng, fn, waits, "c" if inc else None))
        return self.cnt[eng]

    def dma_emit(self, eng, fn, waits=()):
        waits = tuple((we, wv) for (we, wv) in waits if wv)
        if eng == "gp":
            self.dmag_cnt += 16
            self.items.append((eng, fn, waits, "dg"))
            return self.dmag_cnt
        self.dma_cnt += 16
        self.items.append((eng, fn, waits, "d"))
        return self.dma_cnt

    def now(self, eng):
        return self.cnt[eng]

    def mark(self, name):
        self.marks[name] = {e: self.cnt[e] for e in self.cnt}

    def barrier(self):
        return dict(te=("te", self.cnt["te"]), dve=("dve", self.cnt["dve"]),
                    act=("act", self.cnt["act"]), gp=("gp", self.cnt["gp"]),
                    dma=("dma", self.dma_cnt), dmag=("dmag", self.dmag_cnt))

    def bwaits(self, b, *names):
        return [b[n] for n in names]

    def replay(self, eng_name, engine):
        for e, fn, waits, inc in self.items:
            if e != eng_name:
                continue
            for (we, wv) in waits:
                sem = self.dma if we == "dma" else (self.dmag if we == "dmag" else self.sems[we])
                engine.wait_ge(sem, wv)
            inst = fn(engine)
            if inc == "d":
                inst.then_inc(self.dma, 16)
            elif inc == "dg":
                inst.then_inc(self.dmag, 16)
            elif inc == "c":
                inst.then_inc(self.sems[e], 1)


def build(debug=False):
    nc = bass.Bass("TRN2", target_bir_lowering=False, detect_race_conditions=False)
    dt_ = nc.dram_tensor
    io = {}
    io["x_scan_tm"] = dt_("x_scan_tm", [T, D], F32, kind="ExternalInput").ap()
    io["x_scan_cm"] = dt_("x_scan_cm", [D, T], F32, kind="ExternalInput").ap()
    io["x_own_tm"] = dt_("x_own_tm", [TOWN, D], F32, kind="ExternalInput").ap()
    io["x_own_cm"] = dt_("x_own_cm", [D, TOWN], F32, kind="ExternalInput").ap()
    io["wqkv_t"] = dt_("wqkv_t", [16, 128, 16, 128], F32, kind="ExternalInput").ap()
    io["wz_t"] = dt_("wz_t", [2, 16, 128, 512], F32, kind="ExternalInput").ap()
    io["wb_t"] = dt_("wb_t", [16, 128, 16], F32, kind="ExternalInput").ap()
    io["wa_t"] = dt_("wa_t", [16, 128, 16], F32, kind="ExternalInput").ap()
    io["wo_t"] = dt_("wo_t", [4, 8, 128, 512], F32, kind="ExternalInput").ap()
    io["convw"] = dt_("convw", [16, 128, 4], F32, kind="ExternalInput").ap()
    io["vecs"] = dt_("vecs", [64, 3], F32, kind="ExternalInput").ap()
    io["sel2"] = dt_("sel2", [2, 128], BF16, kind="ExternalInput").ap()
    io["gub"] = dt_("gub", [2, 11, 8, 128, 512], U8, kind="ExternalInput").ap()
    io["gus"] = dt_("gus", [2, 2, 11, 8, 512], F16, kind="ExternalInput").ap()
    io["dnb"] = dt_("dnb", [4, 22, 128, 512], U8, kind="ExternalInput").ap()
    io["dns"] = dt_("dns", [4, 2, 22, 512], F16, kind="ExternalInput").ap()
    io["out"] = dt_("out", [TOWN, D], F32, kind="ExternalOutput").ap()
    if debug:
        io["xn_dbg"] = dt_("xn_dbg", [128, 16, TOWN], BF16, kind="ExternalOutput").ap()
        io["yn_dbg"] = dt_("yn_dbg", [128, 8, TOWN], BF16, kind="ExternalOutput").ap()
        io["attn_dbg"] = dt_("attn_dbg", [TOWN, D], F32, kind="ExternalOutput").ap()
        io["h_dbg"] = dt_("h_dbg", [128, HT, TOWN], BF16, kind="ExternalOutput").ap()
        io["wdq_dbg"] = dt_("wdq_dbg", [128, 8, 2, 512], BF16, kind="ExternalOutput").ap()
        io["gsil_dbg"] = dt_("gsil_dbg", [128, 4, TOWN], F32, kind="ExternalOutput").ap()
        io["yt_dbg"] = dt_("yt_dbg", [128, 8, 2048], BF16, kind="ExternalOutput").ap()
    attn_sc = dt_("attn_sc", [TOWN, D], F32, kind="Internal").ap()

    es = ExitStack()
    sb = lambda n, s, d: es.enter_context(nc.sbuf_tensor(n, s, d))
    ps = lambda n, s, d: es.enter_context(nc.psum_tensor(n, s, d))

    with es:
        P = Plan(nc, es)
        global _LAST_PLAN
        _LAST_PLAN = P
        id128 = sb("id128", [128, 128], BF16)
        id64s = sb("id64s", [128, 64], BF16)
        idf32 = sb("idf32", [128, 128], F32)
        ones128 = sb("ones128", [128, 128], BF16)
        ones_f = sb("ones_f", [128, 128], F32)
        ones_qb = sb("ones_qb", [128, 64], BF16)
        ones2f = sb("ones2f", [2, 512], BF16)
        maskL = sb("maskL", [128, 128], F32)
        tricum = sb("tricum", [128, 128], F32)
        rowsel = sb("rowsel", [128, 128], F32)
        vecs = sb("vecs_s", [64, 3], F32)
        nalog = sb("nalog", [16, 1], F32)
        nwcol = sb("nwcol", [128, 1], F32)
        cst = sb("cst", [128, 4], F32)
        dtrow = sb("dtrow", [1, 16], F32)
        P.emit("dve", lambda v: v.memset(cst[:, 0:1], LN_EPS))
        P.emit("dve", lambda v: v.memset(cst[:, 1:2], EPS_A))
        P.emit("dve", lambda v: v.memset(cst[:, 2:3], 1.0))
        P.emit("dve", lambda v: v.memset(cst[:, 3:4], -1.0))

        obf = sb("obuf", [128, NV // 2, NCH // 2, C], BF16)

        pb = [ps(f"pb{i}", [128, 512], F32) for i in range(8)]
        pbh = [b.ap().bitcast(BF16) for b in pb]

        LD_VEC = P.dma_emit("sy", lambda sy: sy.dma_start(out=vecs[:], in_=io["vecs"][:]))
        ctx_iota = nc.sbuf_tensor("iota_r", [128, 128], I32)
        iota_r = ctx_iota.__enter__()
        ctx_iotac = nc.sbuf_tensor("iota_c", [128, 128], I32)
        iota_c = ctx_iotac.__enter__()
        ctx_low = nc.sbuf_tensor("low01", [128, 128], F32)
        low01 = ctx_low.__enter__()
        P.emit("gp", lambda gp: gp.iota(iota_r[:], pattern=[[0, 128]], channel_multiplier=1))
        P.emit("gp", lambda gp: gp.iota(iota_c[:], pattern=[[1, 128]], channel_multiplier=0))
        GPC = P.now("gp")
        P.emit("dve", lambda v: v.tensor_tensor(out=id128[:], in0=iota_r[:], in1=iota_c[:], op=ALU.is_equal),
               waits=[("gp", GPC)])
        P.emit("dve", lambda v: v.tensor_tensor(out=idf32[:], in0=iota_r[:], in1=iota_c[:], op=ALU.is_equal))
        P.emit("dve", lambda v: v.tensor_tensor(out=low01[:], in0=iota_r[:], in1=iota_c[:], op=ALU.is_gt))
        P.emit("dve", lambda v: v.tensor_tensor(out=tricum[:], in0=iota_r[:], in1=iota_c[:], op=ALU.is_le))
        P.emit("dve", lambda v: v.tensor_scalar(out=maskL[:], in0=low01[:], scalar1=1.0, scalar2=-NEG,
                                                op0=ALU.subtract, op1=ALU.mult),
               waits=[("dve", P.now("dve"))])
        P.emit("dve", lambda v: v.tensor_scalar(out=rowsel[:], in0=iota_r[:], scalar1=127, scalar2=None,
                                                op0=ALU.is_equal))
        P.emit("dve", lambda v: v.memset(ones128[:], 1.0))
        P.emit("dve", lambda v: v.memset(ones_f[:], 1.0))
        P.emit("dve", lambda v: v.memset(ones_qb[:], 1.0))
        LD_SEL = P.dma_emit("sy", lambda sy: sy.dma_start(out=ones2f[:, 0:128],
                                                          in_=io["sel2"][:]))
        P.emit("dve", lambda v: v.tensor_copy(out=id64s[0:64, :], in_=id128[0:64, 0:64]),
               waits=[("dve", P.now("dve"))])
        P.emit("dve", lambda v: v.tensor_copy(out=id64s[64:128, :], in_=id128[0:64, 0:64]),
               waits=[("dve", P.now("dve"))])
        ctx_low.__exit__(None, None, None)
        ctx_iotac.__exit__(None, None, None)
        ctx_iota.__exit__(None, None, None)
        P.emit("act", lambda s: s.activation(out=nalog[:], in_=vecs[0:16, 1:2], func=AF.Exp),
               waits=[("dma", LD_VEC)])
        ACT_AL = P.now("act")
        P.emit("dve", lambda v: v.tensor_scalar(out=nalog[:], in0=nalog[:], scalar1=-1.0, scalar2=None,
                                                op0=ALU.mult), waits=[("act", ACT_AL), ("dve", P.now("dve"))])
        P.emit("dve", lambda v: v.tensor_copy(out=nwcol[0:64, :], in_=vecs[0:64, 2:3]),
               waits=[("dve", P.now("dve"))])
        P.emit("te", lambda te: te.transpose(
            out=pb[6][0:1, 0:16], in_=vecs[0:16, 0:1], identity=idf32[0:16, 0:16]),
            waits=[("dma", LD_VEC), ("dve", P.now("dve"))])
        TE_DT = P.now("te")
        P.emit("dve", lambda v: v.tensor_copy(out=dtrow[:], in_=pb[6][0:1, 0:16]),
               waits=[("te", TE_DT)])
        P.emit("dve", lambda v: v.tensor_copy(out=nwcol[64:128, :], in_=vecs[0:64, 2:3]),
               waits=[("dve", P.now("dve"))])
        B0 = P.barrier()
        P.mark("B0")

        ostat = sb("ostat", [128, 8, 2], F32)
        sstat = sb("sstat", [128, 16, 2], F32)
        bnt = sb("bnt", [128, 2, 4, 6], F32)
        with nc.sbuf_tensor("xtile", [128, 2, D], F32) as xtile:
            lastd = [0, 0]
            for (xin, nt, stat) in ((io["x_own_tm"], 8, ostat), (io["x_scan_tm"], 16, sstat)):
                for i in range(nt):
                    par = i % 2
                    dm = P.dma_emit("sy", lambda sy, i=i, par=par, xin=xin: sy.dma_start(
                        out=xtile[:, par, :], in_=xin[i * 128:(i + 1) * 128, :]),
                        waits=[("dve", lastd[par])])
                    for c in range(4):
                        P.emit("dve", lambda v, par=par, c=c: v.bn_stats(
                            out=bnt[:, par, c, :], in_=xtile[:, par, c * 512:(c + 1) * 512]),
                            waits=[("dma", dm)] if c == 0 else [("dve", P.now("dve"))])
                    P.emit("dve", lambda v, i=i, par=par, stat=stat: v.bn_aggr(
                        out=stat[:, i, :], in_=bnt[:, par, :, :].rearrange("p a b -> p (a b)")),
                        waits=[("dve", P.now("dve"))])
                    lastd[par] = P.now("dve")
        STATS = P.now("dve")
        rstd_o = sb("rstd_o", [128, 8], F32)
        mr_o = sb("mr_o", [128, 8], F32)
        rstd_s = sb("rstd_s", [128, 16], F32)
        mr_s = sb("mr_s", [128, 16], F32)
        sq_t = sb("sq_t", [128, 16], F32)
        for (stat, nt, rstd, mr) in ((ostat, 8, rstd_o, mr_o), (sstat, 16, rstd_s, mr_s)):
            P.emit("act", lambda s, stat=stat, nt=nt: s.activation(
                out=sq_t[:, 0:nt], in_=stat.ap()[:, :, 1], func=AF.Sqrt, bias=cst[:, 0:1]),
                waits=[("dve", STATS), ("dve", P.now("dve"))])
            AC = P.now("act")
            P.emit("dve", lambda v, nt=nt, rstd=rstd: v.reciprocal(
                out=rstd[:], in_=sq_t[:, 0:nt]), waits=[("act", AC)])
            P.emit("dve", lambda v, stat=stat, rstd=rstd, mr=mr: v.tensor_tensor(
                out=mr[:], in0=stat.ap()[:, :, 0], in1=rstd[:], op=ALU.mult),
                waits=[("dve", P.now("dve"))])
        NORMS = P.now("dve")
        def emit_xn(xcm_d, nt512, rstd, mr, xn_dst, tag, prior=()):
            prior = list(prior)
            with nc.sbuf_tensor("rowscr" + tag, [1, 2, 512], F32) as rowscr, \
                 nc.sbuf_tensor("xcmt" + tag, [128, 2, 512], F32) as xcmt, \
                 nc.sbuf_tensor("rbt" + tag, [128, 2, 512], F32) as rbt, \
                 nc.sbuf_tensor("mbt" + tag, [128, 2, 512], F32) as mbt, \
                 nc.sbuf_tensor("xtmp" + tag, [128, 2, 512], F32) as xtmp:
                lastd = [0, 0]
                rowscr_te = 0
                for tc in range(nt512):
                    par = tc % 2
                    for ii in range(4):
                        i = tc * 4 + ii
                        for half, src_c in ((0, rstd), (1, mr)):
                            P.emit("te", lambda te, i=i, src_c=src_c: te.matmul(
                                out=pb[6][0:1, 0:128], lhsT=src_c[:, i:i + 1], rhs=idf32[:],
                                start=True, stop=True),
                                waits=[("dve", max(NORMS, P.now("dve"))), ("te", rowscr_te)]
                                      + (prior if (tc == 0 and ii == 0 and half == 0) else []))
                            TE_R = P.now("te")
                            P.emit("dve", lambda v, ii=ii, half=half: v.tensor_copy(
                                out=rowscr[:, half, ii * 128:(ii + 1) * 128], in_=pb[6][0:1, 0:128]),
                                waits=[("te", TE_R)])
                    DV_ROW = P.now("dve")
                    P.emit("te", lambda te: te.matmul(
                        out=pb[6][:, 0:512], lhsT=ones_f[0:1, :], rhs=rowscr[:, 0, :],
                        start=True, stop=True), waits=[("dve", max(DV_ROW, lastd[par]))])
                    TE_B1 = P.now("te")
                    P.emit("dve", lambda v, par=par: v.tensor_copy(out=rbt[:, par, :], in_=pb[6][:, 0:512]),
                           waits=[("te", TE_B1)])
                    DV_B1 = P.now("dve")
                    P.emit("te", lambda te: te.matmul(
                        out=pb[6][:, 0:512], lhsT=ones_f[0:1, :], rhs=rowscr[:, 1, :],
                        start=True, stop=True), waits=[("dve", DV_B1)])
                    TE_B2 = P.now("te")
                    rowscr_te = TE_B2
                    P.emit("dve", lambda v, par=par: v.tensor_copy(out=mbt[:, par, :], in_=pb[6][:, 0:512]),
                           waits=[("te", TE_B2)])
                    for dtile in range(16):
                        dm = P.dma_emit("sy", lambda sy, dtile=dtile, tc=tc, par=par, xcm_d=xcm_d: sy.dma_start(
                            out=xcmt[:, par, :],
                            in_=xcm_d[dtile * 128:(dtile + 1) * 128, tc * 512:(tc + 1) * 512]),
                            waits=[("dve", lastd[par])] if lastd[par] else prior)
                        P.emit("dve", lambda v, par=par: v.tensor_tensor(
                            out=xtmp[:, par, :], in0=xcmt[:, par, :], in1=rbt[:, par, :], op=ALU.mult),
                            waits=[("dma", dm)])
                        P.emit("dve", lambda v, dtile=dtile, tc=tc, par=par, xn_dst=xn_dst: v.tensor_tensor(
                            out=xn_dst[:, dtile, tc * 512:(tc + 1) * 512],
                            in0=xtmp[:, par, :], in1=mbt[:, par, :], op=ALU.subtract),
                            waits=[("dve", P.now("dve"))])
                        lastd[par] = P.now("dve")


        es2 = ExitStack()
        sb2 = lambda n, s, d: es2.enter_context(nc.sbuf_tensor(n, s, d))
        qf = sb2("qf", [128, 4, T], BF16)
        kf = sb2("kf", [128, 4, T], BF16)
        b_tm = sb2("b_tm", [128, NCH, NV], F32)
        g_tm = sb2("g_tm", [128, NCH, NV], F32)
        ktm_t = sb2("ktm_t", [128, NCH, KH, DK], BF16)
        vtm_t = sb2("vtm_t", [128, NCH, NV, DV], BF16)
        with nc.sbuf_tensor("xn_scan", [128, 16, T], BF16) as xn_scan:
            emit_xn(io["x_scan_cm"], 4, rstd_s, mr_s, xn_scan, "s", prior=[("dve", NORMS)])
            XN_SCAN = P.now("dve")
            XN_TE = P.now("te")

            with nc.sbuf_tensor("wkt", [128, 2, 16, 128], BF16) as wkt, \
                 nc.sbuf_tensor("cvt", [128, 3 + T], F32) as cvt_, \
                 nc.sbuf_tensor("cva", [128, TH], F32) as cva_, \
                 nc.sbuf_tensor("cvb", [128, TH], F32) as cvb_, \
                 nc.sbuf_tensor("convw_s", [128, 16, 4], F32) as convw_s, \
                 nc.sbuf_tensor("sil", [128, TH], F32) as sil_, \
                 nc.sbuf_tensor("ktf", [128, 2, 128], F32) as ktf, \
                 nc.sbuf_tensor("sqv", [128, 2, 128], F32) as sqv, \
                 nc.sbuf_tensor("rsq", [128, 2, 2], F32) as rsq, \
                 nc.sbuf_tensor("ktb", [128, 2, 128], BF16) as ktb:
                cvt = cvt_.ap().rearrange("p (a b) -> p a b", a=1)
                cva = cva_.ap().rearrange("p (a b) -> p a b", a=1)
                cvb = cvb_.ap().rearrange("p (a b) -> p a b", a=1)
                sil = sil_.ap().rearrange("p (a b) -> p a b", a=1)
                vsil = sil_.ap().bitcast(BF16).rearrange("p (a b) -> p a b", a=2)
                cwdm = 0
                for i in range(16):
                    cwdm = P.dma_emit("sy", lambda sy, i=i: sy.dma_start(
                        out=convw_s[:, i, :], in_=io["convw"][i, :, :]),
                        waits=[("dve", XN_SCAN), ("te", XN_TE)] if i == 0 else [])
                last_w = [0, 0]
                last_cv = [0, 0]
                last_sil = [0, 0]
                norm_free_d = 0
                norm_free_a = 0
                norm_free_te = 0
                for ot in range(16):
                    par = ot % 2
                    wdm = P.dma_emit("gp", lambda gp, ot=ot, par=par: gp.dma_start(
                        out=wkt[:, par, :, :], in_=io["wqkv_t"][ot, :, :, :]),
                        waits=[("te", last_w[par])] if ot >= 2
                        else [("dve", XN_SCAN), ("te", XN_TE)])
                    for tch in range(4):
                        pbank = pb[par]
                        for kt in range(16):
                            P.emit("te", lambda te, par=par, tch=tch, kt=kt, pbank=pbank: te.matmul(
                                out=pbank[:, 0:512], lhsT=wkt[:, par, kt, :],
                                rhs=xn_scan[:, kt, tch * 512:(tch + 1) * 512],
                                start=(kt == 0), stop=(kt == 15)),
                                waits=([("dmag", wdm), ("dve", max(XN_SCAN, last_cv[par]))]
                                       if kt == 0 else []))
                        TE_P = P.now("te")
                        P.emit("dve", lambda v, par=par, tch=tch, pbank=pbank: v.tensor_copy(
                            out=cvt[:, 0, 3 + tch * 512: 3 + (tch + 1) * 512], in_=pbank[:, 0:512]),
                            waits=[("te", TE_P)])
                        last_cv[par] = P.now("dve")
                    last_w[par] = P.now("te")
                    P.emit("dve", lambda v: v.memset(cvt[:, 0, 0:3], 0.0),
                           waits=[("dve", last_cv[par]), ("dma", cwdm)])
                    for h2 in range(2):
                        P.emit("dve", lambda v, ot=ot, h2=h2: v.tensor_scalar(
                            out=cva[:, 0, :], in0=cvt[:, 0, 3 + h2 * TH:3 + h2 * TH + TH],
                            scalar1=convw_s[:, ot, 3:4], scalar2=None, op0=ALU.mult),
                            waits=[("dve", P.now("dve")),
                                   ("act", max(last_sil[0], last_sil[1]))])
                        bufs = (cva, cvb)
                        for idx, j in enumerate((2, 1, 0)):
                            src_b, dst_b = bufs[idx % 2], bufs[(idx + 1) % 2]
                            P.emit("dve", lambda v, ot=ot, h2=h2, j=j, src_b=src_b, dst_b=dst_b:
                                   v.scalar_tensor_tensor(
                                       out=dst_b[:, 0, :],
                                       in0=cvt[:, 0, j + h2 * TH:j + h2 * TH + TH],
                                       scalar=convw_s[:, ot, j:j + 1], in1=src_b[:, 0, :],
                                       op0=ALU.mult, op1=ALU.add),
                                   waits=[("dve", P.now("dve"))])
                        conv_out = bufs[1]
                        CONV_D = P.now("dve")
                        if ot < 8:
                            P.emit("act", lambda s, conv_out=conv_out: s.activation(
                                out=sil[:, 0, :], in_=conv_out[:, 0, :], func=AF.Silu),
                                waits=[("dve", CONV_D), ("te", norm_free_te)])
                            ACT_S = P.now("act")
                            last_sil[par] = ACT_S
                            dst_fm = qf if ot < 4 else kf
                            fmslot = ot % 4
                            for tb8 in range(8):
                                tb = h2 * 8 + tb8
                                tpar = tb % 2
                                P.emit("te", lambda te, tb8=tb8: te.transpose(
                                    out=pb[4][:, 0:128], in_=sil[:, 0, tb8 * 128:(tb8 + 1) * 128],
                                    identity=idf32[:]),
                                    waits=[("act", ACT_S), ("dve", norm_free_d)])
                                TE_T1 = P.now("te")
                                P.emit("dve", lambda v, tpar=tpar: v.tensor_copy(
                                    out=ktf[:, tpar, :], in_=pb[4][:, 0:128]), waits=[("te", TE_T1)])
                                DV_T1 = P.now("dve")
                                P.emit("act", lambda s, tpar=tpar: s.activation(
                                    out=sqv[:, tpar, :], in_=ktf[:, tpar, :], func=AF.Square),
                                    waits=[("dve", DV_T1), ("act", norm_free_a)])
                                ACT_Q = P.now("act")
                                for hh in range(2):
                                    P.emit("dve", lambda v, tpar=tpar, hh=hh: v.tensor_reduce(
                                        out=rsq[:, tpar, hh:hh + 1],
                                        in_=sqv[:, tpar, hh * 64:(hh + 1) * 64],
                                        axis=AX.X, op=ALU.add),
                                        waits=[("act", ACT_Q)] if hh == 0 else [("dve", P.now("dve"))])
                                DV_RED = P.now("dve")
                                P.emit("act", lambda s, tpar=tpar: s.activation(
                                    out=sqv[:, tpar, 0:2], in_=rsq[:, tpar, :], func=AF.Sqrt, bias=cst[:, 1:2]),
                                    waits=[("dve", DV_RED)])
                                ACT_R = P.now("act")
                                norm_free_a = ACT_R
                                P.emit("dve", lambda v, tpar=tpar: v.reciprocal(
                                    out=rsq[:, tpar, :], in_=sqv[:, tpar, 0:2]), waits=[("act", ACT_R)])
                                if ot < 4:
                                    P.emit("dve", lambda v, tpar=tpar: v.tensor_scalar(
                                        out=rsq[:, tpar, :], in0=rsq[:, tpar, :], scalar1=DK ** -0.5,
                                        scalar2=None, op0=ALU.mult), waits=[("dve", P.now("dve"))])
                                DV_R2 = P.now("dve")
                                for hh in range(2):
                                    P.emit("dve", lambda v, tpar=tpar, hh=hh: v.tensor_scalar(
                                        out=ktb[:, tpar, hh * 64:(hh + 1) * 64],
                                        in0=ktf[:, tpar, hh * 64:(hh + 1) * 64],
                                        scalar1=rsq[:, tpar, hh:hh + 1], scalar2=None, op0=ALU.mult),
                                        waits=[("dve", DV_R2)])
                                DV_N = P.now("dve")
                                if ot >= 4:
                                    ch = tb
                                    kh0 = (ot - 4) * 2
                                    P.emit("dve", lambda v, tpar=tpar, ch=ch, kh0=kh0: v.tensor_copy(
                                        out=ktm_t[:, ch, kh0, :], in_=ktb[:, tpar, 0:64]),
                                        waits=[("dve", DV_N)])
                                    P.emit("dve", lambda v, tpar=tpar, ch=ch, kh0=kh0: v.tensor_copy(
                                        out=ktm_t[:, ch, kh0 + 1, :], in_=ktb[:, tpar, 64:128]),
                                        waits=[("dve", P.now("dve"))])
                                DV_K = P.now("dve")
                                P.emit("te", lambda te, tpar=tpar: te.transpose(
                                    out=pbh[4][:, 256:384], in_=ktb[:, tpar, :], identity=id128[:]),
                                    waits=[("dve", DV_K)])
                                TE_T2 = P.now("te")
                                P.emit("dve", lambda v, dst_fm=dst_fm, fmslot=fmslot, tb=tb: v.tensor_copy(
                                    out=dst_fm[:, fmslot, tb * 128:(tb + 1) * 128], in_=pbh[4][:, 256:384]),
                                    waits=[("te", TE_T2)])
                                norm_free_d = P.now("dve")
                            norm_free_te = P.now("te")
                        else:
                            P.emit("act", lambda s, conv_out=conv_out: s.activation(
                                out=vsil[:, 0, :], in_=conv_out[:, 0, :], func=AF.Silu),
                                waits=[("dve", CONV_D), ("te", norm_free_te)])
                            ACT_V = P.now("act")
                            last_sil[par] = ACT_V
                            vt = ot - 8
                            for tb8 in range(8):
                                tb = h2 * 8 + tb8
                                P.emit("te", lambda te, tb8=tb8: te.transpose(
                                    out=pbh[4][:, 0:128], in_=vsil[:, 0, tb8 * 128:(tb8 + 1) * 128],
                                    identity=id128[:]),
                                    waits=[("act", ACT_V), ("dve", norm_free_d)])
                                TE_TV = P.now("te")
                                P.emit("dve", lambda v, vt=vt, tb=tb: v.tensor_copy(
                                    out=vtm_t[:, tb, 2 * vt:2 * vt + 2, :].rearrange("p a b -> p (a b)"),
                                    in_=pbh[4][:, 0:128]), waits=[("te", TE_TV)])
                                norm_free_d = P.now("dve")
                            norm_free_te = P.now("te")
            QKV_B = P.barrier()
            P.mark("QKV")

            with nc.sbuf_tensor("wba", [128, 16, 32], BF16) as wba, \
                 nc.sbuf_tensor("braw", [16, T], F32) as braw, \
                 nc.sbuf_tensor("graw", [16, T], F32) as graw, \
                 nc.sbuf_tensor("gtm1", [16, T], F32) as gtm1, \
                 nc.sbuf_tensor("gtm2", [16, T], F32) as gtm2:
                wdm_b = P.dma_emit("gp", lambda gp: gp.dma_start(
                    out=wba[:, :, 0:16], in_=io["wb_t"].rearrange("a p c -> p a c")),
                    waits=P.bwaits(QKV_B, "te", "act", "dve"))
                wdm_a = P.dma_emit("gp", lambda gp: gp.dma_start(
                    out=wba[:, :, 16:32], in_=io["wa_t"].rearrange("a p c -> p a c")))
                for cofs, dst in ((0, braw), (16, graw)):
                    for tch in range(4):
                        for kt in range(16):
                            P.emit("te", lambda te, cofs=cofs, tch=tch, kt=kt: te.matmul(
                                out=pb[5][0:16, 0:512], lhsT=wba[:, kt, cofs:cofs + 16],
                                rhs=xn_scan[:, kt, tch * 512:(tch + 1) * 512],
                                start=(kt == 0), stop=(kt == 15 and cofs == 0)),
                                waits=[("dmag", wdm_a), ("dve", P.now("dve"))] if kt == 0 else [])
                        if cofs == 16:
                            for cc in range(4):
                                P.emit("te", lambda te, cc=cc: te.matmul(
                                    out=pb[5][0:16, cc * 128:(cc + 1) * 128], lhsT=dtrow[:],
                                    rhs=ones_f[0:1, :], start=False, stop=(cc == 3)))
                        TE_BA = P.now("te")
                        P.emit("dve", lambda v, dst=dst, tch=tch: v.tensor_copy(
                            out=dst[:, tch * 512:(tch + 1) * 512], in_=pb[5][0:16, 0:512]),
                            waits=[("te", TE_BA)])
                BA_D = P.now("dve")
                P.emit("act", lambda s: s.activation(out=gtm1[:], in_=braw[:], func=AF.Sigmoid),
                       waits=[("dve", BA_D)])
                SIG_A = P.now("act")
                P.emit("act", lambda s: s.activation(out=gtm2[:], in_=graw[:], func=AF.Abs),
                       waits=[("dve", BA_D), ("act", SIG_A)])
                D_AB = P.now("dve")
                A_ABS = P.now("act")
                P.emit("act", lambda s: s.activation(out=braw[:], in_=gtm2[:], func=AF.Exp,
                                                     scale=cst[0:16, 3:4]),
                       waits=[("act", A_ABS)])
                A_EX = P.now("act")
                P.emit("act", lambda s: s.activation(out=gtm2[:], in_=braw[:], func=AF.Ln,
                                                     bias=cst[0:16, 2:3]),
                       waits=[("act", A_EX)])
                A_LN = P.now("act")
                P.emit("act", lambda s: s.activation(out=braw[:], in_=graw[:], func=AF.Relu),
                       waits=[("act", A_LN)])
                A_RELU = P.now("act")
                P.emit("dve", lambda v: v.tensor_tensor(
                    out=graw[:], in0=braw[:], in1=gtm2[:], op=ALU.add),
                    waits=[("act", A_RELU), ("dve", P.now("dve"))])
                P.emit("dve", lambda v: v.tensor_scalar(
                    out=graw[:], in0=graw[:], scalar1=nalog[0:16, 0:1],
                    scalar2=None, op0=ALU.mult), waits=[("dve", P.now("dve"))])
                BG_D = P.now("dve")
                BG_A = P.now("act")
                lastbg = 0
                for ch in range(NCH):
                    P.emit("te", lambda te, ch=ch: te.transpose(
                        out=pb[4][:, 0:16], in_=gtm1[:, ch * 128:(ch + 1) * 128],
                        identity=idf32[0:16, 0:16]),
                        waits=[("dve", max(BG_D, lastbg)), ("act", BG_A)])
                    TE_TB = P.now("te")
                    P.emit("dve", lambda v, ch=ch: v.tensor_copy(out=b_tm[:, ch, :], in_=pb[4][:, 0:16]),
                           waits=[("te", TE_TB)])
                    DV_TB = P.now("dve")
                    P.emit("te", lambda te, ch=ch: te.transpose(
                        out=pb[4][:, 0:16], in_=graw[:, ch * 128:(ch + 1) * 128],
                        identity=idf32[0:16, 0:16]), waits=[("dve", DV_TB)])
                    TE_TG = P.now("te")
                    P.emit("dve", lambda v, ch=ch: v.tensor_copy(out=g_tm[:, ch, :], in_=pb[4][:, 0:16]),
                           waits=[("te", TE_TG)])
                    lastbg = P.now("dve")
            BG_B = P.barrier()
            P.mark("BG")

            BG2_B = P.barrier()
        es3 = ExitStack()
        emit_delta_phase(nc, P, es3, pb, pbh, dict(
            ktm=ktm_t, vtm=vtm_t, kfm=kf, qfm=qf, g_tm=g_tm, b_tm=b_tm,
            id128=id128, id64s=id64s, idf32=idf32, ones128=ones128, ones_f=ones_f,
            ones_qb=ones_qb, maskL=maskL, tricum=tricum, rowsel=rowsel, obuf=obf),
            BG2_B)
        DELTA_B = P.barrier()
        P.mark("DELTA")
        es3.close()
        es2.close()
        h_cm = sb("h_cm", [128, HT, TOWN], BF16)
        ctx_xn_own = nc.sbuf_tensor("xn_own", [128, 16, TOWN], BF16)
        xn_own = ctx_xn_own.__enter__()
        ctx_yn_cm = nc.sbuf_tensor("yn_cm", [128, 8, TOWN], BF16)
        yn_cm = ctx_yn_cm.__enter__()
        emit_xn(io["x_own_cm"], 2, rstd_o, mr_o, xn_own, "o",
                prior=P.bwaits(DELTA_B, "te", "act", "dve"))
        XN_OWN = P.now("dve")
        XNO_TE = P.now("te")
        if debug:
            P.dma_emit("sy", lambda sy: sy.dma_start(out=io["xn_dbg"][:], in_=xn_own[:]),
                       waits=[("dve", XN_OWN)])

        with nc.sbuf_tensor("ytm", [128, 2, 1024], F32) as ytm, \
             nc.sbuf_tensor("yss", [128, 2, 16], F32) as yss, \
             nc.sbuf_tensor("yrs", [128, 2, 16], F32) as yrs, \
             nc.sbuf_tensor("zsil", [128, 2, 1024], F32) as zsil, \
             nc.sbuf_tensor("wz", [128, 2, 16, 512], BF16) as wz, \
             nc.sbuf_tensor("ynt", [128, 1, 1024], BF16) as ynt:
            wzdm = [0, 0]
            for zc in range(2):
                wzdm[zc] = P.dma_emit("gp", lambda gp, zc=zc: gp.dma_start(
                    out=wz[:, zc, :, :], in_=io["wz_t"][zc].rearrange("a p c -> p a c")),
                    waits=[("te", XNO_TE), ("dve", XN_OWN), ("act", DELTA_B["act"][1])]
                    if zc == 0 else [])
            lasty = [0, 0]
            last_yn_p = [0, 0]
            red_done = 0
            last_ynt_te = 0
            for tt in range(8):
                par = tt % 2
                ch = 8 + tt
                for vp in range(8):
                    P.emit("te", lambda te, vp=vp, tt=tt: te.transpose(
                        out=pbh[4][:, 0:128], in_=obf[:, vp, tt, :], identity=id128[:]),
                        waits=([("dve", P.now("dve"))] + P.bwaits(DELTA_B, "act", "dve")) if vp == 0
                        else [("dve", P.now("dve"))])
                    TE_OT = P.now("te")
                    P.emit("dve", lambda v, vp=vp, par=par: v.tensor_copy(
                        out=ytm[:, par, vp * 128:(vp + 1) * 128], in_=pbh[4][:, 0:128]),
                        waits=[("te", TE_OT)])
                Y_D = P.now("dve")
                for zc in range(2):
                    for kt in range(16):
                        P.emit("te", lambda te, tt=tt, zc=zc, kt=kt: te.matmul(
                            out=pb[5][:, 0:512], lhsT=xn_own[:, kt, tt * 128:(tt + 1) * 128],
                            rhs=wz[:, zc, kt, :], start=(kt == 0), stop=(kt == 15)),
                            waits=[("dmag", wzdm[1]), ("act", P.now("act")), ("dve", XN_OWN)]
                            if kt == 0 else [])
                    TE_Z = P.now("te")
                    P.emit("act", lambda s, par=par, zc=zc: s.activation(
                        out=zsil[:, par, zc * 512:(zc + 1) * 512], in_=pb[5][:, 0:512], func=AF.Silu),
                        waits=[("te", TE_Z), ("dve", last_yn_p[par])])
                Z_A = P.now("act")
                for hq in range(2):
                    P.emit("act", lambda s, par=par, hq=hq: s.activation(
                        out=pb[6 + hq][:, 0:512], in_=ytm[:, par, hq * 512:(hq + 1) * 512],
                        func=AF.Square),
                        waits=[("dve", max(Y_D, red_done))])
                A_SQ = P.now("act")
                for vh in range(16):
                    P.emit("dve", lambda v, par=par, vh=vh: v.tensor_reduce(
                        out=yrs[:, par, vh:vh + 1],
                        in_=pb[6 + vh // 8][:, (vh % 8) * 64:(vh % 8 + 1) * 64],
                        axis=AX.X, op=ALU.add),
                        waits=[("act", A_SQ)] if vh == 0 else [("dve", P.now("dve"))])
                P.emit("dve", lambda v, par=par: v.tensor_scalar(
                    out=yrs[:, par, :], in0=yrs[:, par, :], scalar1=1.0 / DV, scalar2=None,
                    op0=ALU.mult), waits=[("dve", P.now("dve"))])
                red_done = P.now("dve")
                DV_RS = red_done
                P.emit("act", lambda s, par=par: s.activation(
                    out=yss[:, par, :], in_=yrs[:, par, :], func=AF.Sqrt, bias=cst[:, 1:2]),
                    waits=[("dve", DV_RS)])
                A_RS = P.now("act")
                P.emit("dve", lambda v, par=par: v.reciprocal(
                    out=yrs[:, par, :], in_=yss[:, par, :]), waits=[("act", A_RS)])
                DV_RC = P.now("dve")
                for vh in range(16):
                    P.emit("dve", lambda v, par=par, vh=vh: v.tensor_scalar(
                        out=ytm[:, par, vh * 64:(vh + 1) * 64], in0=ytm[:, par, vh * 64:(vh + 1) * 64],
                        scalar1=yrs[:, par, vh:vh + 1], scalar2=None, op0=ALU.mult),
                        waits=[("dve", DV_RC)])
                P.emit("dve", lambda v, par=par: v.tensor_tensor(
                    out=ynt[:, 0, :], in0=ytm[:, par, :], in1=zsil[:, par, :], op=ALU.mult),
                    waits=[("dve", P.now("dve")), ("act", Z_A), ("te", last_ynt_te)])
                YN_D = P.now("dve")
                last_yn_p[par] = YN_D
                for vt in range(8):
                    P.emit("te", lambda te, par=par, vt=vt: te.transpose(
                        out=pbh[4][:, 0:128], in_=ynt[:, 0, vt * 128:(vt + 1) * 128],
                        identity=id128[:]), waits=[("dve", max(YN_D, P.now("dve")))])
                    TE_YT = P.now("te")
                    P.emit("dve", lambda v, tt=tt, vt=vt: v.tensor_scalar(
                        out=yn_cm[:, vt, tt * 128:(tt + 1) * 128], in0=pbh[4][:, 0:128],
                        scalar1=nwcol[:, 0:1], scalar2=None, op0=ALU.mult),
                        waits=[("te", TE_YT)])
                last_ynt_te = P.now("te")
                lasty[par] = P.now("dve")
        GATE_B = P.barrier()
        P.mark("GATE")
        if debug:
            P.dma_emit("sy", lambda sy: sy.dma_start(out=io["yn_dbg"][:], in_=yn_cm[:]),
                       waits=[GATE_B["dve"]])

        with nc.sbuf_tensor("wo", [128, 2, 512], BF16) as wo, \
             nc.sbuf_tensor("aev", [128, 2, 512], F32) as aev:
            lastw = [0, 0]
            lasta = [0, 0]
            ev_prev = 0
            for dc in range(4):
                for kt in range(8):
                    par = kt % 2
                    wdm = P.dma_emit("gp", lambda gp, dc=dc, kt=kt, par=par: gp.dma_start(
                        out=wo[:, par, :], in_=io["wo_t"][dc, kt, :, :]),
                        waits=[("te", lastw[par])] if (dc, kt) >= (0, 2)
                        else P.bwaits(GATE_B, "te", "act", "dve"))
                    for tt in range(8):
                        w = []
                        if tt == 0:
                            w.append(("dmag", wdm))
                        if kt == 0:
                            w.append(("dve", max(ev_prev, GATE_B["dve"][1])))
                            if tt == 0 and dc == 0:
                                w.append(("act", GATE_B["act"][1]))
                        P.emit("te", lambda te, kt=kt, tt=tt, par=par: te.matmul(
                            out=pb[tt][:, 0:512], lhsT=yn_cm[:, kt, tt * 128:(tt + 1) * 128],
                            rhs=wo[:, par, :], start=(kt == 0), stop=(kt == 7)),
                            waits=w)
                    lastw[par] = P.now("te")
                TE_WO = P.now("te")
                for tt in range(8):
                    epar = tt % 2
                    P.emit("dve", lambda v, tt=tt, epar=epar: v.tensor_copy(
                        out=aev[:, epar, :], in_=pb[tt][:, 0:512]),
                        waits=[("te", TE_WO), ("dma", lasta[epar])])
                    AD = P.now("dve")
                    admc = P.dma_emit("sy", lambda sy, tt=tt, dc=dc, epar=epar: sy.dma_start(
                        out=attn_sc[tt * 128:(tt + 1) * 128, dc * 512:(dc + 1) * 512],
                        in_=aev[:, epar, :]), waits=[("dve", AD)])
                    lasta[epar] = admc
                ev_prev = P.now("dve")
        WOUT_B = P.barrier()
        P.mark("WOUT")
        if debug:
            P.dma_emit("sy", lambda sy: sy.dma_start(out=io["attn_dbg"][:], in_=attn_sc[:]),
                       waits=[WOUT_B["dma"]])
        ctx_yn_cm.__exit__(None, None, None)

        with nc.sbuf_tensor("bt", [128, 2, 512], U8) as bt, \
             nc.sbuf_tensor("sstrip", [2, 1, 8, 512], F16) as sstrip, \
             nc.sbuf_tensor("sstripb", [2, 1, 8, 512], BF16) as sstripb, \
             nc.sbuf_tensor("i1", [128, 1, 128], I32) as i1, \
             nc.sbuf_tensor("i2", [128, 1, 128], I32) as i2, \
             nc.sbuf_tensor("wdq", [128, 2, 8, 2, 512], BF16) as wdq, \
             nc.sbuf_tensor("gsil", [128, 4, TOWN], F32) as gsil:
            bti = bt.ap().bitcast(I32)

            def dequant_granule(srcap, scrow_ap, par, dqslot, pp, extra_w=()):
                dm = P.dma_emit("gp", lambda gp: gp.dma_start(out=bt[:, par, :], in_=srcap),
                                waits=list(extra_w) + [("dve", P.now("dve"))])
                P.emit("te", lambda te: te.matmul(
                    out=pb[6][:, 0:512], lhsT=ones2f[:, 0:128], rhs=scrow_ap,
                    start=True, stop=True), waits=[("dve", P.now("dve"))])
                TE_SC = P.now("te")
                P.emit("dve", lambda v: v.tensor_scalar(
                    out=i1[:, 0, :], in0=bti[:, par, :], scalar1=0x08080808, scalar2=4,
                    op0=ALU.bitwise_and, op1=ALU.logical_shift_left), waits=[("dmag", dm)])
                P.emit("dve", lambda v: v.tensor_scalar(
                    out=i2[:, 0, :], in0=bti[:, par, :], scalar1=0x07070707, scalar2=2,
                    op0=ALU.bitwise_and, op1=ALU.logical_shift_left))
                P.emit("dve", lambda v: v.tensor_tensor(
                    out=i1[:, 0, :], in0=i1[:, 0, :], in1=i2[:, 0, :], op=ALU.bitwise_or),
                    waits=[("dve", P.now("dve"))])
                P.emit("dve", lambda v: v.tensor_tensor(
                    out=wdq[:, pp, dqslot, 0, :], in0=i1.ap().bitcast(FP8)[:, 0, :],
                    in1=pb[6][:, 0:512], op=ALU.mult),
                    waits=[("te", TE_SC), ("dve", P.now("dve"))])
                P.emit("dve", lambda v: v.tensor_scalar(
                    out=i2[:, 0, :], in0=bti[:, par, :], scalar1=0x70707070, scalar2=2,
                    op0=ALU.bitwise_and, op1=ALU.logical_shift_right))
                P.emit("dve", lambda v: v.tensor_scalar(
                    out=i1[:, 0, :], in0=bti[:, par, :], scalar1=0x7F7F7F7F, scalar2=None,
                    op0=ALU.bitwise_and), waits=[("dve", P.now("dve"))])
                P.emit("dve", lambda v: v.tensor_tensor(
                    out=i1[:, 0, :], in0=bti[:, par, :], in1=i1[:, 0, :], op=ALU.bitwise_xor),
                    waits=[("dve", P.now("dve"))])
                P.emit("dve", lambda v: v.tensor_tensor(
                    out=i2[:, 0, :], in0=i1[:, 0, :], in1=i2[:, 0, :], op=ALU.bitwise_or),
                    waits=[("dve", P.now("dve"))])
                P.emit("dve", lambda v: v.tensor_tensor(
                    out=wdq[:, pp, dqslot, 1, :], in0=i2.ap().bitcast(FP8)[:, 0, :],
                    in1=pb[6][:, 0:512], op=ALU.mult), waits=[("dve", P.now("dve"))])
                return P.now("dve")

            NOFFN = bool(os.environ.get("DEVK_NOFFN"))
            sc_cast = 0
            bc_te = 0
            te_mm = [WOUT_B["te"][1], WOUT_B["te"][1]]
            gsil_free = [0] * 8
            bankev = [("dve", WOUT_B["dve"][1])] * 4
            for it in range(0 if NOFFN else 22):
                h512, part = it // 2, it % 2
                pp = it % 2
                spar = part
                sdm = P.dma_emit("sy", lambda sy, part=part, h512=h512: sy.dma_start(
                    out=sstrip[:, 0, :, :], in_=io["gus"][part, :, h512, :, :]),
                    waits=[("te", te_mm[pp]), ("dve", sc_cast)]
                          + (P.bwaits(WOUT_B, "te", "act", "dve") if it == 0 else []))
                P.emit("dve", lambda v: v.tensor_copy(
                    out=sstripb[:, 0, :, :], in_=sstrip[:, 0, :, :]),
                    waits=[("dma", sdm), ("te", bc_te)])
                sc_cast = P.now("dve")
                for ib in range(8):
                    xw = [("te", te_mm[pp])] if ib == 0 else []
                    if it == 0 and ib < 2:
                        xw = xw + P.bwaits(WOUT_B, "te", "act", "dve")
                    dequant_granule(io["gub"][part, h512, ib, :, :],
                                    sstripb[:, 0, ib, :], ib % 2, ib, pp,
                                    extra_w=xw)
                DQ_D = P.now("dve")
                bc_te = P.now("te")
                for o128 in range(4):
                    for tch in range(2):
                        g8 = o128 * 2 + tch
                        pbk = pb[g8 % 4]
                        P.emit("te", lambda te, o128=o128, tch=tch, pbk=pbk, pp=pp:
                               te.matmul(
                                   out=pbk[:, 0:512],
                                   lhsT=wdq[:, pp, 0, 0, o128 * 128:(o128 + 1) * 128],
                                   rhs=xn_own[:, 0, tch * 512:(tch + 1) * 512],
                                   start=True, stop=False),
                               waits=[("dve", DQ_D)] + [bankev[g8 % 4]])
                        for ib in range(8):
                            for lh in range(2):
                                if ib == 0 and lh == 0:
                                    continue
                                P.emit("te", lambda te, o128=o128, tch=tch, ib=ib, lh=lh,
                                       pbk=pbk, pp=pp:
                                       te.matmul(
                                           out=pbk[:, 0:512],
                                           lhsT=wdq[:, pp, ib, lh, o128 * 128:(o128 + 1) * 128],
                                           rhs=xn_own[:, ib * 2 + lh, tch * 512:(tch + 1) * 512],
                                           start=False, stop=(ib == 7 and lh == 1)))
                        TE_GU = P.now("te")
                        if part == 0:
                            P.emit("act", lambda s, o128=o128, tch=tch, pbk=pbk: s.activation(
                                out=gsil[:, o128, tch * 512:(tch + 1) * 512], in_=pbk[:, 0:512],
                                func=AF.Silu),
                                waits=[("te", TE_GU), ("dve", gsil_free[g8])])
                            bankev[g8 % 4] = ("act", P.now("act"))
                        else:
                            P.emit("dve", lambda v, h512=h512, o128=o128, tch=tch, pbk=pbk:
                                   v.tensor_tensor(
                                       out=h_cm[:, h512 * 4 + o128, tch * 512:(tch + 1) * 512],
                                       in0=pbk[:, 0:512], in1=gsil[:, o128, tch * 512:(tch + 1) * 512],
                                       op=ALU.mult),
                                   waits=[("te", TE_GU), ("act", P.now("act"))])
                            gsil_free[g8] = P.now("dve")
                            bankev[g8 % 4] = ("dve", P.now("dve"))
                te_mm[pp] = P.now("te")
            GU_B = P.barrier()
            P.mark("GU")
            if debug:
                P.dma_emit("sy", lambda sy: sy.dma_start(out=io["h_dbg"][:], in_=h_cm[:]),
                           waits=[GU_B["dve"], GU_B["act"]])
                P.dma_emit("sy", lambda sy: sy.dma_start(out=io["wdq_dbg"][:], in_=wdq[:]),
                           waits=[GU_B["dve"]])
                P.dma_emit("sy", lambda sy: sy.dma_start(out=io["gsil_dbg"][:], in_=gsil[:]),
                           waits=[GU_B["act"]])
        ctx_xn_own.__exit__(None, None, None)

        with nc.sbuf_tensor("btd", [128, 2, 512], U8) as btd, \
             nc.sbuf_tensor("i1d", [128, 2, 128], I32) as i1d, \
             nc.sbuf_tensor("i2d", [128, 2, 128], I32) as i2d, \
             nc.sbuf_tensor("wdd", [128, 22, 2, 512], BF16) as wdd, \
             nc.sbuf_tensor("dsp", [2, 1, 512], F16) as dsp, \
             nc.sbuf_tensor("dspb", [2, 1, 512], BF16) as dspb, \
             nc.sbuf_tensor("ytile", [128, 8, 2048], BF16) as ytile, \
             nc.sbuf_tensor("yp", [128, 4], F32) as yp, \
             nc.sbuf_tensor("yr", [128, 2, 1], F32) as yr, \
             nc.sbuf_tensor("xres", [128, 1, 2048], F32) as xres, \
             nc.sbuf_tensor("ares", [128, 2048], F32) as ares:
            btid = btd.ap().bitcast(I32)
            NOFFN = bool(os.environ.get("DEVK_NOFFN"))
            dbankev = [0] * 4
            scst = [0, 0]
            dnfree_h = [GU_B["te"][1], GU_B["te"][1]]

            def dq_down(d512, hb, first=False):
                par = hb % 2
                sdm = P.dma_emit("sy", lambda sy, d512=d512, hb=hb: sy.dma_start(
                    out=dsp[:, 0, :], in_=io["dns"][d512, :, hb, :]),
                    waits=[("dve", scst[0])]
                          + (P.bwaits(GU_B, "te", "act", "dve") if first else []))
                P.emit("dve", lambda v: v.tensor_copy(
                    out=dspb[:, 0, :], in_=dsp[:, 0, :]),
                    waits=[("dma", sdm), ("te", scst[1])]
                          + (P.bwaits(GU_B, "dve") if first else []))
                scst[0] = P.now("dve")
                dm = P.dma_emit("gp", lambda gp, d512=d512, hb=hb, par=par: gp.dma_start(
                    out=btd[:, par, :], in_=io["dnb"][d512, hb, :, :]),
                    waits=([("dve", P.now("dve"))]
                           + ([("te", dnfree_h[hb // 11])] if (hb % 11) < 2 else [])
                           + (P.bwaits(GU_B, "te", "act", "dve") if first else [])))
                P.emit("te", lambda te: te.matmul(
                    out=pb[6][:, 0:512], lhsT=ones2f[:, 0:128], rhs=dspb[:, 0, :],
                    start=True, stop=True), waits=[("dve", P.now("dve"))])
                TE_SC = P.now("te")
                scst[1] = TE_SC
                P.emit("dve", lambda v, par=par: v.tensor_scalar(
                    out=i1d[:, par, :], in0=btid[:, par, :], scalar1=0x08080808, scalar2=4,
                    op0=ALU.bitwise_and, op1=ALU.logical_shift_left), waits=[("dmag", dm)])
                P.emit("dve", lambda v, par=par: v.tensor_scalar(
                    out=i2d[:, par, :], in0=btid[:, par, :], scalar1=0x07070707, scalar2=2,
                    op0=ALU.bitwise_and, op1=ALU.logical_shift_left))
                P.emit("dve", lambda v, par=par: v.tensor_tensor(
                    out=i1d[:, par, :], in0=i1d[:, par, :], in1=i2d[:, par, :], op=ALU.bitwise_or),
                    waits=[("dve", P.now("dve"))])
                P.emit("dve", lambda v, par=par, hb=hb: v.tensor_tensor(
                    out=wdd[:, hb, 0, :], in0=i1d.ap().bitcast(FP8)[:, par, :],
                    in1=pb[6][:, 0:512], op=ALU.mult),
                    waits=[("te", TE_SC), ("dve", P.now("dve"))])
                P.emit("dve", lambda v, par=par: v.tensor_scalar(
                    out=i2d[:, par, :], in0=btid[:, par, :], scalar1=0x70707070, scalar2=2,
                    op0=ALU.bitwise_and, op1=ALU.logical_shift_right))
                P.emit("dve", lambda v, par=par: v.tensor_scalar(
                    out=i1d[:, par, :], in0=btid[:, par, :], scalar1=0x7F7F7F7F, scalar2=None,
                    op0=ALU.bitwise_and), waits=[("dve", P.now("dve"))])
                P.emit("dve", lambda v, par=par: v.tensor_tensor(
                    out=i1d[:, par, :], in0=btid[:, par, :], in1=i1d[:, par, :],
                    op=ALU.bitwise_xor), waits=[("dve", P.now("dve"))])
                P.emit("dve", lambda v, par=par: v.tensor_tensor(
                    out=i2d[:, par, :], in0=i1d[:, par, :], in1=i2d[:, par, :],
                    op=ALU.bitwise_or), waits=[("dve", P.now("dve"))])
                P.emit("dve", lambda v, par=par, hb=hb: v.tensor_tensor(
                    out=wdd[:, hb, 1, :], in0=i2d.ap().bitcast(FP8)[:, par, :],
                    in1=pb[6][:, 0:512], op=ALU.mult), waits=[("dve", P.now("dve"))])

            def dn_mms(t128, hlo, hhi, dq_cnt, openg, closeg, d512):
                pbk = pb[t128 % 4]
                for hb in range(hlo, hhi):
                    for lh in range(2):
                        w = []
                        if hb == hlo and lh == 0:
                            w = [("dve", max(dq_cnt, dbankev[t128 % 4]))] if openg \
                                else [("dve", dq_cnt)]
                        P.emit("te", lambda te, t128=t128, hb=hb, lh=lh, pbk=pbk: te.matmul(
                            out=pbk[:, 0:512], lhsT=h_cm[:, hb * 2 + lh, t128 * 128:(t128 + 1) * 128],
                            rhs=wdd[:, hb, lh, :], start=(openg and hb == hlo and lh == 0),
                            stop=(closeg and hb == hhi - 1 and lh == 1)), waits=w)
                if closeg:
                    TE_DN = P.now("te")
                    P.emit("dve", lambda v, t128=t128, d512=d512, pbk=pbk: v.tensor_copy(
                        out=ytile[:, t128, d512 * 512:(d512 + 1) * 512], in_=pbk[:, 0:512]),
                        waits=[("te", TE_DN)])
                    dbankev[t128 % 4] = P.now("dve")

            if not NOFFN:
                for hb in range(11):
                    dq_down(0, hb, first=(hb == 0))
                DQ_H0 = P.now("dve")
                for d512 in range(4):
                    for t128 in range(4):
                        dn_mms(t128, 0, 11, DQ_H0, True, False, d512)
                    for hb in range(11, 22):
                        dq_down(d512, hb)
                    DQ_H1 = P.now("dve")
                    for t128 in range(4):
                        dn_mms(t128, 11, 22, DQ_H1, False, True, d512)
                    for t128 in range(4, 8):
                        dn_mms(t128, 0, 11, DQ_H0, True, False, d512)
                    dnfree_h[0] = P.now("te")
                    if d512 < 3:
                        for hb in range(11):
                            dq_down(d512 + 1, hb)
                    DQ_H0 = P.now("dve")
                    for t128 in range(4, 8):
                        dn_mms(t128, 11, 22, DQ_H1, False, True, d512)
                    dnfree_h[1] = P.now("te")
            Y_D2 = P.now("dve")
            if debug:
                P.dma_emit("sy", lambda sy: sy.dma_start(out=io["yt_dbg"][:], in_=ytile[:]),
                           waits=[("dve", Y_D2)])
            lastx = 0
            last_odm = 0
            sqfree = [0, 0]
            for t128 in range(0 if NOFFN else 8):
                par = t128 % 2
                xdm = P.dma_emit("sy", lambda sy, t128=t128: sy.dma_start(
                    out=xres[:, 0, :], in_=io["x_own_tm"][t128 * 128:(t128 + 1) * 128, :]),
                    waits=[("dve", lastx)])
                adm = P.dma_emit("sy", lambda sy, t128=t128: sy.dma_start(
                    out=ares[:], in_=attn_sc[t128 * 128:(t128 + 1) * 128, :]),
                    waits=[("dve", lastx),
                           ("dma", max(last_odm, WOUT_B["dma"][1]))])
                for g in range(4):
                    sbk = pb[5 if g % 2 == 0 else 7]
                    P.emit("act", lambda s, t128=t128, g=g, sbk=sbk: s.activation(
                        out=sbk[:, 0:512], in_=ytile[:, t128, g * 512:(g + 1) * 512],
                        func=AF.Square),
                        waits=[("dve", max(Y_D2, sqfree[g % 2]))])
                    A_S2 = P.now("act")
                    P.emit("dve", lambda v, g=g, sbk=sbk: v.tensor_reduce(
                        out=yp[:, g:g + 1], in_=sbk[:, 0:512], axis=AX.X, op=ALU.add),
                        waits=[("act", A_S2)])
                    sqfree[g % 2] = P.now("dve")
                P.emit("dve", lambda v, par=par: v.tensor_reduce(
                    out=yr[:, par, :], in_=yp[:], axis=AX.X, op=ALU.add),
                    waits=[("dve", P.now("dve"))])
                P.emit("dve", lambda v, par=par: v.tensor_scalar(
                    out=yr[:, par, :], in0=yr[:, par, :], scalar1=1.0 / D, scalar2=None,
                    op0=ALU.mult), waits=[("dve", P.now("dve"))])
                DV_Y = P.now("dve")
                P.emit("act", lambda s, par=par: s.activation(
                    out=yp[:, 0:1], in_=yr[:, par, :], func=AF.Sqrt, bias=cst[:, 1:2]),
                    waits=[("dve", DV_Y)])
                A_Y = P.now("act")
                P.emit("dve", lambda v, par=par: v.reciprocal(
                    out=yr[:, par, :], in_=yp[:, 0:1]), waits=[("act", A_Y)])
                DV_YR = P.now("dve")
                P.emit("dve", lambda v: v.tensor_tensor(
                    out=ares[:], in0=xres[:, 0, :], in1=ares[:], op=ALU.add),
                    waits=[("dma", adm), ("dma", xdm)])
                P.emit("dve", lambda v, t128=t128, par=par: v.scalar_tensor_tensor(
                    out=ares[:], in0=ytile[:, t128, :], scalar=yr[:, par, 0:1],
                    in1=ares[:], op0=ALU.mult, op1=ALU.add),
                    waits=[("dve", P.now("dve")), ("dve", DV_YR)])
                OD = P.now("dve")
                odm = P.dma_emit("sy", lambda sy, t128=t128: sy.dma_start(
                    out=io["out"][t128 * 128:(t128 + 1) * 128, :], in_=ares[:]),
                    waits=[("dve", OD)])
                last_odm = odm
                lastx = OD
            FIN_DMA = P.dma_cnt
            P.emit("sy", lambda sy: sy.nop(), waits=[("dma", FIN_DMA)], inc=False)

        blk = es.enter_context(nc.Block())

        @blk.sync
        def _(sy):
            P.replay("sy", sy)

        @blk.tensor
        def _(te):
            P.replay("te", te)

        @blk.vector
        def _(v):
            P.replay("dve", v)

        @blk.scalar
        def _(s):
            P.replay("act", s)

        @blk.gpsimd
        def _(g):
            P.replay("gp", g)

    return nc, io



def emit_delta_phase(nc, P, es, pb, pbh, tn, B):
    sb = lambda n, s, d: es.enter_context(nc.sbuf_tensor(n, s, d))
    ktm, vtm, kfm, qfm = tn["ktm"], tn["vtm"], tn["kfm"], tn["qfm"]
    g_tm, b_tm = tn["g_tm"], tn["b_tm"]
    id128, id64s, idf32 = tn["id128"], tn["id64s"], tn["idf32"]
    ones128, ones_f, ones_qb = tn["ones128"], tn["ones_f"], tn["ones_qb"]
    maskL, tricum, rowsel = tn["maskL"], tn["tricum"], tn["rowsel"]
    obuf = tn["obuf"]
    NIC = NCH * NV

    gc_tm = sb("gc_tm", [128, NIC], F32)
    gcHn = sb("gcHn", [128, NIC], F32)
    egc = sb("egc", [128, NIC], F32)
    einv_t = sb("einv_t", [128, NIC], F32)
    egcC = sb("egcC", [128, NIC], F32)
    begc = sb("begc", [128, NIC], F32)
    bneg = sb("bneg", [128, NIC], F32)
    dtmp = sb("dtmp", [128, NIC], F32)

    NROT = 2
    diagH_t = sb("d_diagH", [128, NROT, 128], F32)
    diagHn_t = sb("d_diagHn", [128, NROT, 128], F32)
    diagE_t = sb("d_diagE", [128, NROT, 128], BF16)
    expGs_t = sb("d_expGs", [128, NROT, 128], F32)
    expGi_t = sb("d_expGi", [128, NROT, 128], F32)
    abuf_t = sb("d_abuf", [128, NROT, 128], BF16)
    qbart_t = sb("d_qbart", [64, NV, 128], F32)

    YB_t = sb("d_YB", [128, NV, 2, 256], BF16)
    Ct_t = sb("d_Ct", [128, NV, 128], BF16)
    Ck_t = sb("d_Ck", [128, NV, 2, 128], BF16)
    At_t = sb("d_At", [128, NV, 128], BF16)
    khat_t = sb("d_khat", [128, NV, DK], BF16)
    AUs_t = sb("d_AUs", [64, NV, 128], BF16)
    QmAW_t = sb("d_QmAW", [64, NV, 128], BF16)
    Ptp_t = sb("d_Ptp", [64, NV, DK], BF16)
    Zs_t = sb("d_Zs", [64, NV, DV], BF16)
    IegcC_t = sb("d_IegcC", [64, NV, 64], BF16)
    Svh_t = sb("d_Svh", [64, NV, DV], BF16)

    diagH = lambda vh: diagH_t[:, vh % NROT, :]
    diagHn = lambda vh: diagHn_t[:, vh % NROT, :]
    diagE = lambda vh: diagE_t[:, vh % NROT, :]
    expGs = lambda vh: expGs_t[:, vh % NROT, :]
    expGi = lambda vh: expGi_t[:, vh % NROT, :]
    abuf = lambda vh: abuf_t[:, vh % NROT, :]
    qbart = lambda vh: qbart_t[:, vh, :]
    Bm = lambda vh: YB_t[:, vh, 0, 128:256]
    Ct = lambda vh: Ct_t[:, vh, :]
    At = lambda vh: At_t[:, vh, :]
    khat = lambda vh: khat_t[:, vh, :]
    AUs = lambda vh: AUs_t[:, vh, :]
    QmAW = lambda vh: QmAW_t[:, vh, :]
    Ptp = lambda vh: Ptp_t[:, vh, :]
    Zs = lambda vh: Zs_t[:, vh, :]
    IegcC = lambda vh: IegcC_t[:, vh, :]
    Svh = lambda vh: Svh_t[:, vh, :]

    def grel_v(i):
        return pb[i][:, 0:128]

    def gram_kk(i):
        return pb[2 + i][:, 0:128]

    def gram_qk(i):
        return pb[2 + i][:, 128:256]

    def sq_v(i):
        return pb[2 + i][:, 256:384]

    BT_V = pbh[4][:, 0:128]
    AT_V = pbh[4][:, 256:384]
    QB_V = pb[5][0:64, 0:128]
    GC_V = pb[6][:, 0:NIC]
    PT_V = pb[7][0:64, 0:64]
    Z_V = pb[7][0:64, 64:128]
    O_V = pb[6][0:64, 256:384]
    S_V = pb[6][0:64, 384:448]

    def col(t, ch, vh):
        i = ch * NV + vh
        return t[:, i:i + 1]

    bw = [B["te"], B["dve"], B["act"], B["gp"], B["dma"], B["dmag"]]

    P.emit("dve", lambda v: v.memset(Svh_t[:], 0.0), waits=bw)
    CONST = P.now("dve")
    gflat = g_tm.ap().rearrange("p a b -> p (a b)")
    bflat = b_tm.ap().rearrange("p a b -> p (a b)")
    P.emit("te", lambda te: te.matmul(out=GC_V, lhsT=tricum[:], rhs=gflat,
                                      start=True, stop=True), waits=bw + [("dve", CONST)])
    TE_GC = P.now("te")
    P.emit("dve", lambda v: v.tensor_copy(out=gc_tm[:], in_=GC_V), waits=[("te", TE_GC)])
    GC_BASE = P.now("dve")
    P.emit("act", lambda s: s.activation(out=gcHn[:], in_=gc_tm[:], func=AF.Copy, scale=-1.0),
           waits=[("dve", GC_BASE)] + bw)
    P.emit("act", lambda s: s.activation(out=egc[:], in_=gc_tm[:], func=AF.Exp))
    ACT_E = P.now("act")
    P.emit("te", lambda te: te.matmul(out=GC_V, lhsT=rowsel[:], rhs=gc_tm[:],
                                      start=True, stop=True),
           waits=[("dve", GC_BASE), ("act", ACT_E)])
    TE_GCC = P.now("te")
    P.emit("dve", lambda v: v.tensor_tensor(out=dtmp[:], in0=GC_V, in1=gc_tm[:], op=ALU.subtract),
           waits=[("te", TE_GCC)])
    DVE_EINV = P.now("dve")
    P.emit("act", lambda s: s.activation(out=einv_t[:], in_=dtmp[:], func=AF.Exp),
           waits=[("dve", DVE_EINV)])
    P.emit("act", lambda s: s.activation(out=egcC[:], in_=GC_V, func=AF.Exp))
    P.emit("dve", lambda v: v.tensor_tensor(out=begc[:], in0=bflat, in1=egc[:], op=ALU.mult),
           waits=[("act", ACT_E)])
    P.emit("act", lambda s: s.activation(out=bneg[:], in_=bflat, func=AF.Copy, scale=-1.0))
    PREP_DVE = P.now("dve")
    PREP_ACT = P.now("act")

    grel_free = [0, 0]
    grel_free_a = [0, 0]
    gram_free = [0, 0]
    sq_free = [0, 0]
    sq_free_d = [0, 0]
    sqf_act = [0, 0]
    awu_free_act = [0, 0]
    diag_free = [0, 0]
    expgs_free = [0, 0]
    abuf_free = [0, 0]
    iegc_free = [0] * NV
    b4_free_act = 0
    qb_free_dve = 0
    b5_free = dict(dve=0, act=0)
    o_free_act = 0
    s_free_act = 0
    sact = [0] * NV

    first = True
    for ch in range(NCH):
        te_prev = P.now("te")
        gp_diag = [0] * NV
        dve_b = [0] * NV
        act_b = [0] * NV
        for vh in range(NV):
            w = [("act", PREP_ACT), ("dve", PREP_DVE)] if first else []
            first = False
            w = w + [("te", max(diag_free[vh % 2], te_prev))]
            P.emit("dve", lambda v, ch=ch, vh=vh: v.tensor_scalar(
                out=diagH(vh), in0=idf32[:], scalar1=col(gc_tm, ch, vh), scalar2=None, op0=ALU.mult),
                waits=w)
            P.emit("dve", lambda v, ch=ch, vh=vh: v.tensor_scalar(
                out=diagHn(vh), in0=idf32[:], scalar1=col(gcHn, ch, vh), scalar2=None, op0=ALU.mult))
            P.emit("dve", lambda v, ch=ch, vh=vh: v.tensor_scalar(
                out=diagE(vh), in0=id128[:], scalar1=col(egc, ch, vh), scalar2=None, op0=ALU.mult))
            P.emit("dve", lambda v, ch=ch, vh=vh: v.tensor_scalar(
                out=IegcC(vh), in0=id64s[0:64, :], scalar1=col(egcC, ch, vh)[0:64, :],
                scalar2=None, op0=ALU.mult), waits=[("dve", iegc_free[vh])])
            gp_diag[vh] = P.now("dve")
            kh = vh // 2
            khs = (kh % 2) * 64
            kht = kh // 2
            par = vh % 2
            kslice = kfm[khs:khs + 64, kht, ch * C:(ch + 1) * C]
            qslice = qfm[khs:khs + 64, kht, ch * C:(ch + 1) * C]
            P.emit("te", lambda te, vh=vh, par=par: te.matmul(out=grel_v(par), lhsT=diagH(vh), rhs=ones_f[:],
                                                              start=True, stop=False),
                   waits=[("dve", max(gp_diag[vh], grel_free[par])),
                          ("act", max(awu_free_act[par], grel_free_a[par]))])
            P.emit("te", lambda te, vh=vh, par=par: te.matmul(out=grel_v(par), lhsT=ones_f[:], rhs=diagHn(vh),
                                                              start=False, stop=False))
            P.emit("te", lambda te, vh=vh, par=par: te.matmul(out=grel_v(par), lhsT=idf32[:], rhs=maskL[:],
                                                              start=False, stop=True))
            TE_GREL = P.now("te")
            P.emit("te", lambda te, ks=kslice, par=par: te.matmul(
                out=gram_kk(par), lhsT=ks, rhs=ks, start=True, stop=True),
                waits=[("dve", max(gram_free[par], sq_free_d[par])), ("act", sqf_act[par])])
            P.emit("te", lambda te, ks=kslice, qs=qslice, par=par: te.matmul(
                out=gram_qk(par), lhsT=qs, rhs=ks, start=True, stop=True))
            P.emit("te", lambda te, vh=vh: te.matmul(out=QB_V, lhsT=ones_qb[:], rhs=diagE(vh),
                                                     start=True, stop=True),
                   waits=[("dve", max(qb_free_dve, gp_diag[vh]))])
            TE_G = P.now("te")
            diag_free[par] = TE_G
            P.emit("act", lambda s, vh=vh, par=par: s.activation(out=expGs(vh), in_=grel_v(par), func=AF.Exp),
                   waits=[("te", TE_GREL), ("dve", max(expgs_free[vh % 2], grel_free[par]))])
            grel_free_a[par] = P.now("act")
            ACT_X = P.now("act")
            P.emit("dve", lambda v, vh=vh, ch=ch, par=par: v.scalar_tensor_tensor(
                out=Bm(vh), in0=gram_kk(par), scalar=col(bneg, ch, vh), in1=expGs(vh),
                op0=ALU.mult, op1=ALU.mult), waits=[("act", ACT_X), ("te", te_prev)])
            P.emit("dve", lambda v, vh=vh: v.tensor_tensor(
                out=expGi(vh), in0=expGs(vh), in1=idf32[:], op=ALU.add))
            expgs_free[vh % 2] = P.now("dve")
            P.emit("dve", lambda v, vh=vh, ch=ch: v.tensor_scalar(
                out=khat(vh), in0=ktm[:, ch, vh // 2, :], scalar1=col(einv_t, ch, vh),
                scalar2=None, op0=ALU.mult), waits=[("te", te_prev)])
            P.emit("dve", lambda v, vh=vh, par=par: v.tensor_tensor(
                out=abuf(vh), in0=gram_qk(par), in1=expGi(vh), op=ALU.mult),
                waits=[("te", max(abuf_free[vh % 2], te_prev))])
            gram_free[par] = P.now("dve")
            DVE_BA = P.now("dve")
            P.emit("te", lambda te, vh=vh: te.transpose(out=BT_V, in_=Bm(vh), identity=id128[:]),
                   waits=[("dve", DVE_BA), ("act", b4_free_act)])
            TE_BT = P.now("te")
            P.emit("act", lambda s, vh=vh: s.activation(out=Ct(vh), in_=BT_V, func=AF.Copy),
                   waits=[("te", TE_BT)])
            b4a = P.now("act")
            P.emit("te", lambda te, vh=vh: te.transpose(out=AT_V, in_=abuf(vh), identity=id128[:]),
                   waits=[("act", b4a)])
            TE_AT = P.now("te")
            abuf_free[vh % 2] = TE_AT
            P.emit("act", lambda s, vh=vh: s.activation(out=At(vh), in_=AT_V, func=AF.Copy),
                   waits=[("te", TE_AT)])
            b4_free_act = P.now("act")
            P.emit("dve", lambda v, vh=vh, ch=ch: v.tensor_scalar(
                out=YB_t[:, vh, 0, 0:64], in0=ktm[:, ch, vh // 2, :], scalar1=col(begc, ch, vh),
                scalar2=None, op0=ALU.mult), waits=[("te", te_prev)])
            P.emit("dve", lambda v, vh=vh, ch=ch: v.tensor_scalar(
                out=YB_t[:, vh, 0, 64:128], in0=vtm[:, ch, vh, :],
                scalar1=bflat[:, ch * NV + vh: ch * NV + vh + 1],
                scalar2=None, op0=ALU.mult))
            P.emit("dve", lambda v, vh=vh, ch=ch, qs=qslice: v.tensor_tensor(
                out=qbart(vh), in0=qs, in1=QB_V, op=ALU.mult), waits=[("te", TE_G)])
            qb_free_dve = P.now("dve")
            dve_b[vh] = P.now("dve")
            act_b[vh] = P.now("act")

        ybprev = [YB_t[:, vh, 0, :] for vh in range(NV)]
        cprev = [Ct(vh) for vh in range(NV)]
        ydve = list(dve_b)
        b_act = [0] * NV
        c_dve = [0] * NV
        c_act = list(act_b)
        ap_free = [0, 0]
        apb_act = [0, 0]
        sqf = [max(s, dve_b[NV - 1]) for s in sq_free]
        sqd = list(sq_free_d)
        for lev in range(NLEV):
            last = lev == NLEV - 1
            for vh in range(NV):
                par = vh % 2
                if not last:
                    P.emit("te", lambda te, c=cprev[vh], yb=ybprev[vh], par=par: te.matmul(
                        out=pb[par][:, 0:256], lhsT=c, rhs=yb, start=True, stop=True),
                        waits=[("dve", max(ydve[vh], ap_free[par], c_dve[vh])),
                               ("act", max(c_act[vh], b_act[vh], grel_free_a[par], apb_act[par]))])
                    TE_AP = P.now("te")
                    s_new = (lev + 1) % 2
                    P.emit("dve", lambda v, vh=vh, s_new=s_new, yb=ybprev[vh], par=par: v.tensor_tensor(
                        out=YB_t[:, vh, s_new, 0:128], in0=pb[par][:, 0:128],
                        in1=yb[:, 0:128], op=ALU.add), waits=[("te", TE_AP)])
                    ap_free[par] = P.now("dve")
                    grel_free[par] = P.now("dve")
                    bold_act = b_act[vh]
                    P.emit("act", lambda s, vh=vh, s_new=s_new, par=par: s.activation(
                        out=YB_t[:, vh, s_new, 128:256], in_=pb[par][:, 128:256], func=AF.Copy),
                        waits=[("te", TE_AP), ("dve", ap_free[par])])
                    apb_act[par] = P.now("act")
                    b_act[vh] = P.now("act")
                    P.emit("te", lambda te, b=ybprev[vh][:, 128:256], c=cprev[vh], par=par: te.matmul(
                        out=sq_v(par), lhsT=b, rhs=c, start=True, stop=True),
                        waits=[("dve", max(sqf[par], sqd[par], c_dve[vh])),
                               ("act", max(c_act[vh], bold_act))])
                    TE_C = P.now("te")
                    cnew = Ck_t[:, vh, lev % 2, :]
                    P.emit("dve", lambda v, cn=cnew, par=par: v.tensor_copy(out=cn, in_=sq_v(par)),
                           waits=[("te", TE_C)])
                    sqd[par] = P.now("dve")
                    c_dve[vh] = P.now("dve")
                    ybprev[vh] = YB_t[:, vh, s_new, :]
                    ydve[vh] = ap_free[par]
                    cprev[vh] = cnew
                    sqf[par] = 0
                else:
                    P.emit("te", lambda te, c=cprev[vh], yb=ybprev[vh], par=par: te.matmul(
                        out=pb[par][:, 0:128], lhsT=c, rhs=yb[:, 0:128], start=True, stop=True),
                        waits=[("dve", max(ydve[vh], ap_free[par], c_dve[vh])),
                               ("act", max(c_act[vh], grel_free_a[par], apb_act[par]))])
                    TE_AP = P.now("te")
                    s_new = (lev + 1) % 2
                    P.emit("dve", lambda v, vh=vh, s_new=s_new, yb=ybprev[vh], par=par: v.tensor_tensor(
                        out=YB_t[:, vh, s_new, 0:128], in0=pb[par][:, 0:128],
                        in1=yb[:, 0:128], op=ALU.add), waits=[("te", TE_AP)])
                    ap_free[par] = P.now("dve")
                    grel_free[par] = P.now("dve")
                    ybprev[vh] = YB_t[:, vh, s_new, :]
                    ydve[vh] = ap_free[par]
        yprev = ybprev
        sq_free[0], sq_free[1] = sqf[0], sqf[1]
        sq_free_d[0], sq_free_d[1] = sqd[0], sqd[1]
        grel_free_a[0] = max(grel_free_a[0], apb_act[0])
        grel_free_a[1] = max(grel_free_a[1], apb_act[1])

        dve_fin = [0] * NV
        act_fin = [0] * NV
        for vh in range(NV):
            par = vh % 2
            P.emit("te", lambda te, y=yprev[vh][:, 0:128], vh=vh, par=par: te.matmul(
                out=grel_v(par), lhsT=y, rhs=At(vh), start=True, stop=True),
                waits=[("dve", max(ydve[vh], grel_free[par])),
                       ("act", max(awu_free_act[par], grel_free_a[par]))])
            TE_AWU = P.now("te")
            P.emit("dve", lambda v, vh=vh, par=par: v.tensor_tensor(
                out=QmAW(vh), in0=qbart(vh), in1=grel_v(par)[0:64, :], op=ALU.subtract),
                waits=[("te", TE_AWU)])
            DVE_QM = P.now("dve")
            grel_free[par] = DVE_QM
            P.emit("act", lambda s, vh=vh, par=par: s.activation(
                out=AUs(vh), in_=grel_v(par)[64:128, :], func=AF.Copy),
                waits=[("te", TE_AWU), ("dve", DVE_QM)])
            awu_free_act[par] = P.now("act")
            P.emit("te", lambda te, y=yprev[vh], vh=vh: te.matmul(
                out=PT_V, lhsT=y[:, 0:64], rhs=khat(vh), start=True, stop=True),
                waits=[("dve", b5_free["dve"]), ("act", b5_free["act"])])
            P.emit("te", lambda te, y=yprev[vh], vh=vh: te.matmul(
                out=Z_V, lhsT=khat(vh), rhs=y[:, 64:128], start=True, stop=True))
            TE_PZ = P.now("te")
            P.emit("dve", lambda v, vh=vh: v.scalar_tensor_tensor(
                out=Ptp(vh), in0=PT_V, scalar=-1.0, in1=IegcC(vh), op0=ALU.mult, op1=ALU.add),
                waits=[("te", TE_PZ)])
            P.emit("dve", lambda v, vh=vh: v.tensor_copy(
                out=Zs(vh), in_=Z_V), waits=[("te", TE_PZ), ("dve", P.now("dve"))])
            b5_free["dve"] = P.now("dve")
            iegc_free[vh] = P.now("dve")
            dve_fin[vh] = P.now("dve")
            act_fin[vh] = P.now("act")

        for vh in range(NV):
            vp = vh // 2
            half = (vh % 2) * 64
            if ch >= 8:
                P.emit("te", lambda te, vh=vh: te.matmul(
                    out=O_V, lhsT=Svh(vh), rhs=QmAW(vh), start=True, stop=False),
                    waits=[("dve", dve_fin[vh]),
                           ("act", max(act_fin[vh], sact[vh], o_free_act, s_free_act))])
                P.emit("te", lambda te, vh=vh: te.matmul(
                    out=O_V, lhsT=id64s[0:64, :], rhs=AUs(vh), start=False, stop=True))
                TE_O = P.now("te")
                P.emit("act", lambda s, vh=vh, ch=ch, vp=vp, half=half: s.activation(
                    out=obuf[half:half + 64, vp, ch - 8, :], in_=O_V, func=AF.Copy),
                    waits=[("te", TE_O)])
                o_free_act = P.now("act")
            P.emit("te", lambda te, vh=vh: te.matmul(
                out=S_V, lhsT=Ptp(vh), rhs=Svh(vh), start=True, stop=False),
                waits=[("act", max(sact[vh], o_free_act, s_free_act))])
            P.emit("te", lambda te, vh=vh: te.matmul(
                out=S_V, lhsT=id64s[0:64, :], rhs=Zs(vh), start=False, stop=True))
            TE_S = P.now("te")
            P.emit("act", lambda s, vh=vh: s.activation(
                out=Svh(vh), in_=S_V, func=AF.Copy), waits=[("te", TE_S)])
            sact[vh] = P.now("act")
            s_free_act = sact[vh]



def _make_pi(n):
    pi = np.empty(n, np.int64)
    for b0 in range(n // 256):
        pi[256 * b0:256 * b0 + 128] = 256 * b0 + 2 * np.arange(128)
        pi[256 * b0 + 128:256 * b0 + 256] = 256 * b0 + 2 * np.arange(128) + 1
    return pi


PI = _make_pi(2048)
SIG = _make_pi(5632)


def _prep_weights(inp):
    W_qkv = np.asarray(inp["L0_attn_W_qkv"], np.float32)
    W_z = np.asarray(inp["L0_attn_W_z"], np.float32)
    W_b = np.asarray(inp["L0_attn_W_b"], np.float32)
    W_a = np.asarray(inp["L0_attn_W_a"], np.float32)
    W_out = np.asarray(inp["L0_attn_W_out"], np.float32)
    conv_w = np.asarray(inp["L0_attn_conv_w"], np.float32)
    dt_bias = np.asarray(inp["L0_attn_dt_bias"], np.float32)
    A_log = np.asarray(inp["L0_attn_A_log"], np.float32)
    norm_w = np.asarray(inp["L0_attn_norm_weight"], np.float32)
    gup = np.asarray(inp["L0_ffn_gate_up_packed"])
    gus = np.asarray(inp["L0_ffn_gate_up_scales"])
    dnp_ = np.asarray(inp["L0_ffn_down_packed"])
    dns = np.asarray(inp["L0_ffn_down_scales"])

    w = {}
    wq = W_qkv[:, PI].T.reshape(16, 128, 16, 128)
    w["wqkv_t"] = np.ascontiguousarray(wq.transpose(2, 1, 0, 3))
    wz = W_z[:, PI].T.reshape(16, 128, 2, 512)
    w["wz_t"] = np.ascontiguousarray(wz.transpose(2, 0, 1, 3))
    w["wb_t"] = np.ascontiguousarray(W_b[:, PI].T.reshape(16, 128, 16))
    w["wa_t"] = np.ascontiguousarray(W_a[:, PI].T.reshape(16, 128, 16))
    wo = W_out[PI, :].T.reshape(8, 128, 4, 512)
    w["wo_t"] = np.ascontiguousarray(wo.transpose(2, 0, 1, 3))
    w["convw"] = np.ascontiguousarray(conv_w[:, 0, :].reshape(16, 128, 4))
    vecs = np.zeros((64, 3), np.float32)
    vecs[0:16, 0] = dt_bias
    vecs[0:16, 1] = A_log
    vecs[0:64, 2] = norm_w
    w["vecs"] = vecs
    sel2 = np.zeros((2, 128), ml_dtypes.bfloat16)
    sel2[0, 0:64] = 64.0
    sel2[1, 64:128] = 64.0
    w["sel2"] = sel2
    I = 5632
    gub = np.empty((2, 11, 8, 128, 512), np.uint8)
    guss = np.empty((2, 2, 11, 8, 512), np.float16)
    for part in range(2):
        rows = gup[part * I + SIG]
        bT = rows.view(np.uint8).reshape(I, 1024).T
        gub[part] = bT.reshape(8, 128, 11, 512).transpose(2, 0, 1, 3)
        sc = gus[:, part * I + SIG]
        guss[part] = sc.reshape(8, 2, 11, 512).transpose(1, 2, 0, 3)
    w["gub"] = np.ascontiguousarray(gub)
    w["gus"] = np.ascontiguousarray(guss)
    rows = dnp_[PI]
    bT2 = rows.view(np.uint8).reshape(2048, 2816).T
    w["dnb"] = np.ascontiguousarray(bT2.reshape(22, 128, 4, 512).transpose(2, 0, 1, 3))
    w["dns"] = np.ascontiguousarray(dns[:, PI].reshape(22, 2, 4, 512).transpose(2, 1, 0, 3))
    return w


def _prep_core_x(core, x_full):
    b, hf = core // 2, core % 2
    x = np.asarray(x_full, np.float32)[b]
    m = {}
    if hf == 0:
        xs = np.concatenate([np.zeros((1024, 2048), np.float32), x[0:1024]], 0)
    else:
        xs = x
    xs = np.ascontiguousarray(xs[:, PI])
    m["x_scan_tm"] = xs
    m["x_scan_cm"] = np.ascontiguousarray(xs.T)
    xo = np.ascontiguousarray(x[hf * 1024:(hf + 1) * 1024][:, PI])
    m["x_own_tm"] = xo
    m["x_own_cm"] = np.ascontiguousarray(xo.T)
    return m


def kernel(**inputs):
    w = _prep_weights(inputs)
    in_maps = []
    for core in range(8):
        m = dict(w)
        m.update(_prep_core_x(core, inputs["L0_attn_x"]))
        in_maps.append(m)
    nc, io = build()
    res = run_bass_kernel_spmd(nc, in_maps, core_ids=list(range(8)))
    out = np.empty((4, 2048, 2048), np.float32)
    for core in range(8):
        b, hf = core // 2, core % 2
        shard = np.asarray(res.results[core]["out"]).reshape(1024, 2048)
        out[b, hf * 1024:(hf + 1) * 1024][:, PI] = shard
    return out


# version 12
# speedup vs baseline: 1.2200x; 1.1089x over previous
import os
import numpy as np
import ml_dtypes
import concourse.bass as bass
import concourse.mybir as mybir
from concourse.bass_utils import run_bass_kernel_spmd
from contextlib import ExitStack

F32 = mybir.dt.float32
BF16 = mybir.dt.bfloat16
I32 = mybir.dt.int32
F16 = mybir.dt.float16
U8 = mybir.dt.uint8
FP8 = mybir.dt.float8e4
AF = mybir.ActivationFunctionType
ALU = mybir.AluOpType
AX = mybir.AxisListType

D = 2048
T = 2048
TH = 1024
_LAST_PLAN = None
TOWN = 1024
C = 128
NCH = 16
NV = 16
KH = 8
DK = DV = 64
NEG = -30000.0
NLEV = 7
I_FF = 5632
HT = 44
LN_EPS = 1e-5
EPS_A = 1e-6


class Plan:
    def __init__(self, nc, es):
        self.marks = {}
        self.nc = nc
        self.items = []
        self.cnt = {e: 0 for e in ("te", "dve", "act", "gp", "sy")}
        self.sems = {e: es.enter_context(nc.semaphore("s_" + e)) for e in self.cnt}
        self.dma = es.enter_context(nc.semaphore("s_dma"))
        self.dma_cnt = 0
        self.dmag = es.enter_context(nc.semaphore("s_dmag"))
        self.dmag_cnt = 0

    def emit(self, eng, fn, waits=(), inc=True):
        waits = tuple((we, wv) for (we, wv) in waits if wv)
        if inc:
            self.cnt[eng] += 1
        self.items.append((eng, fn, waits, "c" if inc else None))
        return self.cnt[eng]

    def dma_emit(self, eng, fn, waits=()):
        waits = tuple((we, wv) for (we, wv) in waits if wv)
        if eng == "gp":
            self.dmag_cnt += 16
            self.items.append((eng, fn, waits, "dg"))
            return self.dmag_cnt
        self.dma_cnt += 16
        self.items.append((eng, fn, waits, "d"))
        return self.dma_cnt

    def now(self, eng):
        return self.cnt[eng]

    def mark(self, name):
        self.marks[name] = {e: self.cnt[e] for e in self.cnt}

    def barrier(self):
        return dict(te=("te", self.cnt["te"]), dve=("dve", self.cnt["dve"]),
                    act=("act", self.cnt["act"]), gp=("gp", self.cnt["gp"]),
                    dma=("dma", self.dma_cnt), dmag=("dmag", self.dmag_cnt))

    def bwaits(self, b, *names):
        return [b[n] for n in names]

    def replay(self, eng_name, engine):
        for e, fn, waits, inc in self.items:
            if e != eng_name:
                continue
            for (we, wv) in waits:
                sem = self.dma if we == "dma" else (self.dmag if we == "dmag" else self.sems[we])
                engine.wait_ge(sem, wv)
            inst = fn(engine)
            if inc == "d":
                inst.then_inc(self.dma, 16)
            elif inc == "dg":
                inst.then_inc(self.dmag, 16)
            elif inc == "c":
                inst.then_inc(self.sems[e], 1)


def build(debug=False):
    nc = bass.Bass("TRN2", target_bir_lowering=False, detect_race_conditions=False)
    dt_ = nc.dram_tensor
    io = {}
    io["x_scan_tm"] = dt_("x_scan_tm", [T, D], F32, kind="ExternalInput").ap()
    io["x_scan_cm"] = dt_("x_scan_cm", [D, T], F32, kind="ExternalInput").ap()
    io["x_own_tm"] = dt_("x_own_tm", [TOWN, D], F32, kind="ExternalInput").ap()
    io["x_own_cm"] = dt_("x_own_cm", [D, TOWN], F32, kind="ExternalInput").ap()
    io["wqkv_t"] = dt_("wqkv_t", [16, 128, 16, 128], F32, kind="ExternalInput").ap()
    io["wz_t"] = dt_("wz_t", [2, 16, 128, 512], F32, kind="ExternalInput").ap()
    io["wb_t"] = dt_("wb_t", [16, 128, 16], F32, kind="ExternalInput").ap()
    io["wa_t"] = dt_("wa_t", [16, 128, 16], F32, kind="ExternalInput").ap()
    io["wo_t"] = dt_("wo_t", [4, 8, 128, 512], F32, kind="ExternalInput").ap()
    io["convw"] = dt_("convw", [16, 128, 4], F32, kind="ExternalInput").ap()
    io["vecs"] = dt_("vecs", [64, 3], F32, kind="ExternalInput").ap()
    io["sel2"] = dt_("sel2", [2, 128], BF16, kind="ExternalInput").ap()
    io["gub"] = dt_("gub", [2, 11, 8, 128, 512], U8, kind="ExternalInput").ap()
    io["gus"] = dt_("gus", [2, 2, 11, 8, 512], F16, kind="ExternalInput").ap()
    io["dnb"] = dt_("dnb", [4, 22, 128, 512], U8, kind="ExternalInput").ap()
    io["dns"] = dt_("dns", [4, 2, 22, 512], F16, kind="ExternalInput").ap()
    io["out"] = dt_("out", [TOWN, D], F32, kind="ExternalOutput").ap()
    if debug:
        io["xn_dbg"] = dt_("xn_dbg", [128, 16, TOWN], BF16, kind="ExternalOutput").ap()
        io["yn_dbg"] = dt_("yn_dbg", [128, 8, TOWN], BF16, kind="ExternalOutput").ap()
        io["attn_dbg"] = dt_("attn_dbg", [TOWN, D], F32, kind="ExternalOutput").ap()
        io["h_dbg"] = dt_("h_dbg", [128, HT, TOWN], BF16, kind="ExternalOutput").ap()
        io["wdq_dbg"] = dt_("wdq_dbg", [128, 8, 2, 512], BF16, kind="ExternalOutput").ap()
        io["gsil_dbg"] = dt_("gsil_dbg", [128, 4, TOWN], F32, kind="ExternalOutput").ap()
        io["yt_dbg"] = dt_("yt_dbg", [128, 8, 2048], BF16, kind="ExternalOutput").ap()
    attn_sc = dt_("attn_sc", [TOWN, D], F32, kind="Internal").ap()

    es = ExitStack()
    sb = lambda n, s, d: es.enter_context(nc.sbuf_tensor(n, s, d))
    ps = lambda n, s, d: es.enter_context(nc.psum_tensor(n, s, d))

    with es:
        P = Plan(nc, es)
        global _LAST_PLAN
        _LAST_PLAN = P
        id128 = sb("id128", [128, 128], BF16)
        id64s = sb("id64s", [128, 64], BF16)
        idf32 = sb("idf32", [128, 128], F32)
        ones128 = sb("ones128", [128, 128], BF16)
        ones_f = sb("ones_f", [128, 128], F32)
        ones_qb = sb("ones_qb", [128, 64], BF16)
        ones2f = sb("ones2f", [2, 512], BF16)
        maskL = sb("maskL", [128, 128], F32)
        tricum = sb("tricum", [128, 128], F32)
        rowsel = sb("rowsel", [128, 128], F32)
        vecs = sb("vecs_s", [64, 3], F32)
        nalog = sb("nalog", [16, 1], F32)
        nwcol = sb("nwcol", [128, 1], F32)
        cst = sb("cst", [128, 4], F32)
        dtrow = sb("dtrow", [1, 16], F32)
        P.emit("dve", lambda v: v.memset(cst[:, 0:1], LN_EPS))
        P.emit("dve", lambda v: v.memset(cst[:, 1:2], EPS_A))
        P.emit("dve", lambda v: v.memset(cst[:, 2:3], 1.0))
        P.emit("dve", lambda v: v.memset(cst[:, 3:4], -1.0))

        obf = sb("obuf", [128, NV // 2, NCH // 2, C], BF16)

        pb = [ps(f"pb{i}", [128, 512], F32) for i in range(8)]
        pbh = [b.ap().bitcast(BF16) for b in pb]

        LD_VEC = P.dma_emit("sy", lambda sy: sy.dma_start(out=vecs[:], in_=io["vecs"][:]))
        ctx_iota = nc.sbuf_tensor("iota_r", [128, 128], I32)
        iota_r = ctx_iota.__enter__()
        ctx_iotac = nc.sbuf_tensor("iota_c", [128, 128], I32)
        iota_c = ctx_iotac.__enter__()
        ctx_low = nc.sbuf_tensor("low01", [128, 128], F32)
        low01 = ctx_low.__enter__()
        P.emit("gp", lambda gp: gp.iota(iota_r[:], pattern=[[0, 128]], channel_multiplier=1))
        P.emit("gp", lambda gp: gp.iota(iota_c[:], pattern=[[1, 128]], channel_multiplier=0))
        GPC = P.now("gp")
        P.emit("dve", lambda v: v.tensor_tensor(out=id128[:], in0=iota_r[:], in1=iota_c[:], op=ALU.is_equal),
               waits=[("gp", GPC)])
        P.emit("dve", lambda v: v.tensor_tensor(out=idf32[:], in0=iota_r[:], in1=iota_c[:], op=ALU.is_equal))
        P.emit("dve", lambda v: v.tensor_tensor(out=low01[:], in0=iota_r[:], in1=iota_c[:], op=ALU.is_gt))
        P.emit("dve", lambda v: v.tensor_tensor(out=tricum[:], in0=iota_r[:], in1=iota_c[:], op=ALU.is_le))
        P.emit("dve", lambda v: v.tensor_scalar(out=maskL[:], in0=low01[:], scalar1=1.0, scalar2=-NEG,
                                                op0=ALU.subtract, op1=ALU.mult),
               waits=[("dve", P.now("dve"))])
        P.emit("dve", lambda v: v.tensor_scalar(out=rowsel[:], in0=iota_r[:], scalar1=127, scalar2=None,
                                                op0=ALU.is_equal))
        P.emit("dve", lambda v: v.memset(ones128[:], 1.0))
        P.emit("dve", lambda v: v.memset(ones_f[:], 1.0))
        P.emit("dve", lambda v: v.memset(ones_qb[:], 1.0))
        LD_SEL = P.dma_emit("sy", lambda sy: sy.dma_start(out=ones2f[:, 0:128],
                                                          in_=io["sel2"][:]))
        P.emit("dve", lambda v: v.tensor_copy(out=id64s[0:64, :], in_=id128[0:64, 0:64]),
               waits=[("dve", P.now("dve"))])
        P.emit("dve", lambda v: v.tensor_copy(out=id64s[64:128, :], in_=id128[0:64, 0:64]),
               waits=[("dve", P.now("dve"))])
        ctx_low.__exit__(None, None, None)
        ctx_iotac.__exit__(None, None, None)
        ctx_iota.__exit__(None, None, None)
        P.emit("act", lambda s: s.activation(out=nalog[:], in_=vecs[0:16, 1:2], func=AF.Exp),
               waits=[("dma", LD_VEC)])
        ACT_AL = P.now("act")
        P.emit("dve", lambda v: v.tensor_scalar(out=nalog[:], in0=nalog[:], scalar1=-1.0, scalar2=None,
                                                op0=ALU.mult), waits=[("act", ACT_AL), ("dve", P.now("dve"))])
        P.emit("dve", lambda v: v.tensor_copy(out=nwcol[0:64, :], in_=vecs[0:64, 2:3]),
               waits=[("dve", P.now("dve"))])
        P.emit("te", lambda te: te.transpose(
            out=pb[6][0:1, 0:16], in_=vecs[0:16, 0:1], identity=idf32[0:16, 0:16]),
            waits=[("dma", LD_VEC), ("dve", P.now("dve"))])
        TE_DT = P.now("te")
        P.emit("dve", lambda v: v.tensor_copy(out=dtrow[:], in_=pb[6][0:1, 0:16]),
               waits=[("te", TE_DT)])
        P.emit("dve", lambda v: v.tensor_copy(out=nwcol[64:128, :], in_=vecs[0:64, 2:3]),
               waits=[("dve", P.now("dve"))])
        B0 = P.barrier()
        P.mark("B0")

        ostat = sb("ostat", [128, 8, 2], F32)
        sstat = sb("sstat", [128, 16, 2], F32)
        bnt = sb("bnt", [128, 2, 4, 6], F32)
        with nc.sbuf_tensor("xtile", [128, 2, D], F32) as xtile:
            lastd = [0, 0]
            for (xin, nt, stat) in ((io["x_scan_tm"], 16, sstat), (io["x_own_tm"], 8, ostat)):
                for i in range(nt):
                    par = i % 2
                    dm = P.dma_emit("sy", lambda sy, i=i, par=par, xin=xin: sy.dma_start(
                        out=xtile[:, par, :], in_=xin[i * 128:(i + 1) * 128, :]),
                        waits=[("dve", lastd[par])])
                    for c in range(4):
                        P.emit("dve", lambda v, par=par, c=c: v.bn_stats(
                            out=bnt[:, par, c, :], in_=xtile[:, par, c * 512:(c + 1) * 512]),
                            waits=[("dma", dm)] if c == 0 else [("dve", P.now("dve"))])
                    P.emit("dve", lambda v, i=i, par=par, stat=stat: v.bn_aggr(
                        out=stat[:, i, :], in_=bnt[:, par, :, :].rearrange("p a b -> p (a b)")),
                        waits=[("dve", P.now("dve"))])
                    lastd[par] = P.now("dve")
        STATS = P.now("dve")
        rstd_o = sb("rstd_o", [128, 8], F32)
        mr_o = sb("mr_o", [128, 8], F32)
        rstd_s = sb("rstd_s", [128, 16], F32)
        mr_s = sb("mr_s", [128, 16], F32)
        sq_t = sb("sq_t", [128, 16], F32)
        for (stat, nt, rstd, mr) in ((ostat, 8, rstd_o, mr_o), (sstat, 16, rstd_s, mr_s)):
            P.emit("act", lambda s, stat=stat, nt=nt: s.activation(
                out=sq_t[:, 0:nt], in_=stat.ap()[:, :, 1], func=AF.Sqrt, bias=cst[:, 0:1]),
                waits=[("dve", STATS), ("dve", P.now("dve"))])
            AC = P.now("act")
            P.emit("dve", lambda v, nt=nt, rstd=rstd: v.reciprocal(
                out=rstd[:], in_=sq_t[:, 0:nt]), waits=[("act", AC)])
            P.emit("dve", lambda v, stat=stat, rstd=rstd, mr=mr: v.tensor_tensor(
                out=mr[:], in0=stat.ap()[:, :, 0], in1=rstd[:], op=ALU.mult),
                waits=[("dve", P.now("dve"))])
        NORMS = P.now("dve")
        def emit_xn(xcm_d, nt512, rstd, mr, xn_dst, tag, prior=()):
            prior = list(prior)
            tc_marks = []
            with nc.sbuf_tensor("rowscr" + tag, [1, 2, 512], F32) as rowscr, \
                 nc.sbuf_tensor("xcmt" + tag, [128, 2, 512], F32) as xcmt, \
                 nc.sbuf_tensor("rbt" + tag, [128, 2, 512], F32) as rbt, \
                 nc.sbuf_tensor("mbt" + tag, [128, 2, 512], F32) as mbt, \
                 nc.sbuf_tensor("xtmp" + tag, [128, 2, 512], F32) as xtmp:
                lastd = [0, 0]
                rowscr_te = 0
                for tc in range(nt512):
                    par = tc % 2
                    for ii in range(4):
                        i = tc * 4 + ii
                        for half, src_c in ((0, rstd), (1, mr)):
                            P.emit("te", lambda te, i=i, src_c=src_c: te.matmul(
                                out=pb[6][0:1, 0:128], lhsT=src_c[:, i:i + 1], rhs=idf32[:],
                                start=True, stop=True),
                                waits=[("dve", max(NORMS, P.now("dve"))), ("te", rowscr_te)]
                                      + (prior if (tc == 0 and ii == 0 and half == 0) else []))
                            TE_R = P.now("te")
                            P.emit("dve", lambda v, ii=ii, half=half: v.tensor_copy(
                                out=rowscr[:, half, ii * 128:(ii + 1) * 128], in_=pb[6][0:1, 0:128]),
                                waits=[("te", TE_R)])
                    DV_ROW = P.now("dve")
                    P.emit("te", lambda te: te.matmul(
                        out=pb[6][:, 0:512], lhsT=ones_f[0:1, :], rhs=rowscr[:, 0, :],
                        start=True, stop=True), waits=[("dve", max(DV_ROW, lastd[par]))])
                    TE_B1 = P.now("te")
                    P.emit("dve", lambda v, par=par: v.tensor_copy(out=rbt[:, par, :], in_=pb[6][:, 0:512]),
                           waits=[("te", TE_B1)])
                    DV_B1 = P.now("dve")
                    P.emit("te", lambda te: te.matmul(
                        out=pb[6][:, 0:512], lhsT=ones_f[0:1, :], rhs=rowscr[:, 1, :],
                        start=True, stop=True), waits=[("dve", DV_B1)])
                    TE_B2 = P.now("te")
                    rowscr_te = TE_B2
                    P.emit("dve", lambda v, par=par: v.tensor_copy(out=mbt[:, par, :], in_=pb[6][:, 0:512]),
                           waits=[("te", TE_B2)])
                    for dtile in range(16):
                        dm = P.dma_emit("sy", lambda sy, dtile=dtile, tc=tc, par=par, xcm_d=xcm_d: sy.dma_start(
                            out=xcmt[:, par, :],
                            in_=xcm_d[dtile * 128:(dtile + 1) * 128, tc * 512:(tc + 1) * 512]),
                            waits=[("dve", lastd[par])] if lastd[par] else prior)
                        P.emit("dve", lambda v, par=par: v.tensor_tensor(
                            out=xtmp[:, par, :], in0=xcmt[:, par, :], in1=rbt[:, par, :], op=ALU.mult),
                            waits=[("dma", dm)])
                        P.emit("dve", lambda v, dtile=dtile, tc=tc, par=par, xn_dst=xn_dst: v.tensor_tensor(
                            out=xn_dst[:, dtile, tc * 512:(tc + 1) * 512],
                            in0=xtmp[:, par, :], in1=mbt[:, par, :], op=ALU.subtract),
                            waits=[("dve", P.now("dve"))])
                        lastd[par] = P.now("dve")
                    tc_marks.append(P.now("dve"))
            return tc_marks


        es2 = ExitStack()
        sb2 = lambda n, s, d: es2.enter_context(nc.sbuf_tensor(n, s, d))
        qf = sb2("qf", [128, 4, T], BF16)
        kf = sb2("kf", [128, 4, T], BF16)
        b_tm = sb2("b_tm", [128, NCH, NV], F32)
        g_tm = sb2("g_tm", [128, NCH, NV], F32)
        ktm_t = sb2("ktm_t", [128, NCH, KH, DK], BF16)
        vtm_t = sb2("vtm_t", [128, NCH, NV, DV], BF16)
        with nc.sbuf_tensor("xn_scan", [128, 16, T], BF16) as xn_scan:
            XN_TCM = emit_xn(io["x_scan_cm"], 4, rstd_s, mr_s, xn_scan, "s", prior=[("dve", NORMS)])
            XN_SCAN = P.now("dve")
            XN_TE = P.now("te")

            with nc.sbuf_tensor("wkt", [128, 2, 16, 128], BF16) as wkt, \
                 nc.sbuf_tensor("cvt", [128, 3 + T], F32) as cvt_, \
                 nc.sbuf_tensor("cva", [128, TH], F32) as cva_, \
                 nc.sbuf_tensor("cvb", [128, TH], F32) as cvb_, \
                 nc.sbuf_tensor("convw_s", [128, 16, 4], F32) as convw_s, \
                 nc.sbuf_tensor("sil", [128, TH], F32) as sil_, \
                 nc.sbuf_tensor("ktf", [128, 2, 128], F32) as ktf, \
                 nc.sbuf_tensor("sqv", [128, 2, 128], F32) as sqv, \
                 nc.sbuf_tensor("rsq", [128, 2, 2], F32) as rsq, \
                 nc.sbuf_tensor("ktb", [128, 2, 128], BF16) as ktb:
                cvt = cvt_.ap().rearrange("p (a b) -> p a b", a=1)
                cva = cva_.ap().rearrange("p (a b) -> p a b", a=1)
                cvb = cvb_.ap().rearrange("p (a b) -> p a b", a=1)
                sil = sil_.ap().rearrange("p (a b) -> p a b", a=1)
                vsil = sil_.ap().bitcast(BF16).rearrange("p (a b) -> p a b", a=2)
                cwdm = 0
                for i in range(16):
                    cwdm = P.dma_emit("sy", lambda sy, i=i: sy.dma_start(
                        out=convw_s[:, i, :], in_=io["convw"][i, :, :]),
                        waits=[("dve", XN_SCAN), ("te", XN_TE)] if i == 0 else [])
                last_w = [0, 0]
                last_cv = [0, 0]
                last_sil = [0, 0]
                norm_free_d = 0
                norm_free_a = 0
                norm_free_te = 0
                for ot in range(16):
                    par = ot % 2
                    wdm = P.dma_emit("gp", lambda gp, ot=ot, par=par: gp.dma_start(
                        out=wkt[:, par, :, :], in_=io["wqkv_t"][ot, :, :, :]),
                        waits=[("te", last_w[par])] if ot >= 2
                        else [("dve", XN_SCAN), ("te", XN_TE)])
                    for tch in range(4):
                        pbank = pb[par]
                        for kt in range(16):
                            P.emit("te", lambda te, par=par, tch=tch, kt=kt, pbank=pbank: te.matmul(
                                out=pbank[:, 0:512], lhsT=wkt[:, par, kt, :],
                                rhs=xn_scan[:, kt, tch * 512:(tch + 1) * 512],
                                start=(kt == 0), stop=(kt == 15)),
                                waits=([("dmag", wdm), ("dve", max(XN_TCM[tch], last_cv[par]))]
                                       if kt == 0 else []))
                        TE_P = P.now("te")
                        P.emit("dve", lambda v, par=par, tch=tch, pbank=pbank: v.tensor_copy(
                            out=cvt[:, 0, 3 + tch * 512: 3 + (tch + 1) * 512], in_=pbank[:, 0:512]),
                            waits=[("te", TE_P)])
                        last_cv[par] = P.now("dve")
                    last_w[par] = P.now("te")
                    P.emit("dve", lambda v: v.memset(cvt[:, 0, 0:3], 0.0),
                           waits=[("dve", last_cv[par]), ("dma", cwdm)])
                    for h2 in range(2):
                        P.emit("dve", lambda v, ot=ot, h2=h2: v.tensor_scalar(
                            out=cva[:, 0, :], in0=cvt[:, 0, 3 + h2 * TH:3 + h2 * TH + TH],
                            scalar1=convw_s[:, ot, 3:4], scalar2=None, op0=ALU.mult),
                            waits=[("dve", P.now("dve")),
                                   ("act", max(last_sil[0], last_sil[1]))])
                        bufs = (cva, cvb)
                        for idx, j in enumerate((2, 1, 0)):
                            src_b, dst_b = bufs[idx % 2], bufs[(idx + 1) % 2]
                            P.emit("dve", lambda v, ot=ot, h2=h2, j=j, src_b=src_b, dst_b=dst_b:
                                   v.scalar_tensor_tensor(
                                       out=dst_b[:, 0, :],
                                       in0=cvt[:, 0, j + h2 * TH:j + h2 * TH + TH],
                                       scalar=convw_s[:, ot, j:j + 1], in1=src_b[:, 0, :],
                                       op0=ALU.mult, op1=ALU.add),
                                   waits=[("dve", P.now("dve"))])
                        conv_out = bufs[1]
                        CONV_D = P.now("dve")
                        if ot < 8:
                            P.emit("act", lambda s, conv_out=conv_out: s.activation(
                                out=sil[:, 0, :], in_=conv_out[:, 0, :], func=AF.Silu),
                                waits=[("dve", CONV_D), ("te", norm_free_te)])
                            ACT_S = P.now("act")
                            last_sil[par] = ACT_S
                            dst_fm = qf if ot < 4 else kf
                            fmslot = ot % 4
                            for tb8 in range(8):
                                tb = h2 * 8 + tb8
                                tpar = tb % 2
                                P.emit("te", lambda te, tb8=tb8: te.transpose(
                                    out=pb[4][:, 0:128], in_=sil[:, 0, tb8 * 128:(tb8 + 1) * 128],
                                    identity=idf32[:]),
                                    waits=[("act", ACT_S), ("dve", norm_free_d)])
                                TE_T1 = P.now("te")
                                P.emit("dve", lambda v, tpar=tpar: v.tensor_copy(
                                    out=ktf[:, tpar, :], in_=pb[4][:, 0:128]), waits=[("te", TE_T1)])
                                DV_T1 = P.now("dve")
                                P.emit("act", lambda s, tpar=tpar: s.activation(
                                    out=sqv[:, tpar, :], in_=ktf[:, tpar, :], func=AF.Square),
                                    waits=[("dve", DV_T1), ("act", norm_free_a)])
                                ACT_Q = P.now("act")
                                P.emit("dve", lambda v, tpar=tpar: v.tensor_reduce(
                                    out=rsq[:, tpar, :].rearrange("p (a b) -> p a b", b=1),
                                    in_=sqv[:, tpar, :].rearrange("p (a b) -> p a b", a=2),
                                    axis=AX.X, op=ALU.add),
                                    waits=[("act", ACT_Q)])
                                DV_RED = P.now("dve")
                                P.emit("act", lambda s, tpar=tpar: s.activation(
                                    out=sqv[:, tpar, 0:2], in_=rsq[:, tpar, :], func=AF.Sqrt, bias=cst[:, 1:2]),
                                    waits=[("dve", DV_RED)])
                                ACT_R = P.now("act")
                                norm_free_a = ACT_R
                                P.emit("dve", lambda v, tpar=tpar: v.reciprocal(
                                    out=rsq[:, tpar, :], in_=sqv[:, tpar, 0:2]), waits=[("act", ACT_R)])
                                if ot < 4:
                                    P.emit("dve", lambda v, tpar=tpar: v.tensor_scalar(
                                        out=rsq[:, tpar, :], in0=rsq[:, tpar, :], scalar1=DK ** -0.5,
                                        scalar2=None, op0=ALU.mult), waits=[("dve", P.now("dve"))])
                                DV_R2 = P.now("dve")
                                P.emit("dve", lambda v, tpar=tpar: v.tensor_tensor(
                                    out=ktb[:, tpar, :].rearrange("p (a b) -> p a b", a=2),
                                    in0=ktf[:, tpar, :].rearrange("p (a b) -> p a b", a=2),
                                    in1=rsq[:, tpar, :].rearrange("p (a b) -> p a b", b=1)
                                        .broadcast_to((128, 2, 64)),
                                    op=ALU.mult),
                                    waits=[("dve", DV_R2)])
                                DV_N = P.now("dve")
                                if ot >= 4:
                                    ch = tb
                                    kh0 = (ot - 4) * 2
                                    P.emit("dve", lambda v, tpar=tpar, ch=ch, kh0=kh0: v.tensor_copy(
                                        out=ktm_t[:, ch, kh0, :], in_=ktb[:, tpar, 0:64]),
                                        waits=[("dve", DV_N)])
                                    P.emit("dve", lambda v, tpar=tpar, ch=ch, kh0=kh0: v.tensor_copy(
                                        out=ktm_t[:, ch, kh0 + 1, :], in_=ktb[:, tpar, 64:128]),
                                        waits=[("dve", P.now("dve"))])
                                DV_K = P.now("dve")
                                P.emit("te", lambda te, tpar=tpar: te.transpose(
                                    out=pbh[4][:, 256:384], in_=ktb[:, tpar, :], identity=id128[:]),
                                    waits=[("dve", DV_K)])
                                TE_T2 = P.now("te")
                                P.emit("dve", lambda v, dst_fm=dst_fm, fmslot=fmslot, tb=tb: v.tensor_copy(
                                    out=dst_fm[:, fmslot, tb * 128:(tb + 1) * 128], in_=pbh[4][:, 256:384]),
                                    waits=[("te", TE_T2)])
                                norm_free_d = P.now("dve")
                            norm_free_te = P.now("te")
                        else:
                            P.emit("act", lambda s, conv_out=conv_out: s.activation(
                                out=vsil[:, 0, :], in_=conv_out[:, 0, :], func=AF.Silu),
                                waits=[("dve", CONV_D), ("te", norm_free_te)])
                            ACT_V = P.now("act")
                            last_sil[par] = ACT_V
                            vt = ot - 8
                            for tb8 in range(8):
                                tb = h2 * 8 + tb8
                                P.emit("te", lambda te, tb8=tb8: te.transpose(
                                    out=pbh[4][:, 0:128], in_=vsil[:, 0, tb8 * 128:(tb8 + 1) * 128],
                                    identity=id128[:]),
                                    waits=[("act", ACT_V), ("dve", norm_free_d)])
                                TE_TV = P.now("te")
                                P.emit("dve", lambda v, vt=vt, tb=tb: v.tensor_copy(
                                    out=vtm_t[:, tb, 2 * vt:2 * vt + 2, :].rearrange("p a b -> p (a b)"),
                                    in_=pbh[4][:, 0:128]), waits=[("te", TE_TV)])
                                norm_free_d = P.now("dve")
                            norm_free_te = P.now("te")
            QKV_B = P.barrier()
            P.mark("QKV")

            with nc.sbuf_tensor("wba", [128, 16, 32], BF16) as wba, \
                 nc.sbuf_tensor("braw", [16, T], F32) as braw, \
                 nc.sbuf_tensor("graw", [16, T], F32) as graw, \
                 nc.sbuf_tensor("gtm1", [16, T], F32) as gtm1, \
                 nc.sbuf_tensor("gtm2", [16, T], F32) as gtm2:
                wdm_b = P.dma_emit("gp", lambda gp: gp.dma_start(
                    out=wba[:, :, 0:16], in_=io["wb_t"].rearrange("a p c -> p a c")),
                    waits=P.bwaits(QKV_B, "te", "act", "dve"))
                wdm_a = P.dma_emit("gp", lambda gp: gp.dma_start(
                    out=wba[:, :, 16:32], in_=io["wa_t"].rearrange("a p c -> p a c")))
                for cofs, dst in ((0, braw), (16, graw)):
                    for tch in range(4):
                        for kt in range(16):
                            P.emit("te", lambda te, cofs=cofs, tch=tch, kt=kt: te.matmul(
                                out=pb[5][0:16, 0:512], lhsT=wba[:, kt, cofs:cofs + 16],
                                rhs=xn_scan[:, kt, tch * 512:(tch + 1) * 512],
                                start=(kt == 0), stop=(kt == 15 and cofs == 0)),
                                waits=[("dmag", wdm_a), ("dve", P.now("dve"))] if kt == 0 else [])
                        if cofs == 16:
                            for cc in range(4):
                                P.emit("te", lambda te, cc=cc: te.matmul(
                                    out=pb[5][0:16, cc * 128:(cc + 1) * 128], lhsT=dtrow[:],
                                    rhs=ones_f[0:1, :], start=False, stop=(cc == 3)))
                        TE_BA = P.now("te")
                        P.emit("dve", lambda v, dst=dst, tch=tch: v.tensor_copy(
                            out=dst[:, tch * 512:(tch + 1) * 512], in_=pb[5][0:16, 0:512]),
                            waits=[("te", TE_BA)])
                BA_D = P.now("dve")
                P.emit("act", lambda s: s.activation(out=gtm1[:], in_=braw[:], func=AF.Sigmoid),
                       waits=[("dve", BA_D)])
                SIG_A = P.now("act")
                P.emit("act", lambda s: s.activation(out=gtm2[:], in_=graw[:], func=AF.Abs),
                       waits=[("dve", BA_D), ("act", SIG_A)])
                D_AB = P.now("dve")
                A_ABS = P.now("act")
                P.emit("act", lambda s: s.activation(out=braw[:], in_=gtm2[:], func=AF.Exp,
                                                     scale=cst[0:16, 3:4]),
                       waits=[("act", A_ABS)])
                A_EX = P.now("act")
                P.emit("act", lambda s: s.activation(out=gtm2[:], in_=braw[:], func=AF.Ln,
                                                     bias=cst[0:16, 2:3]),
                       waits=[("act", A_EX)])
                A_LN = P.now("act")
                P.emit("act", lambda s: s.activation(out=braw[:], in_=graw[:], func=AF.Relu),
                       waits=[("act", A_LN)])
                A_RELU = P.now("act")
                P.emit("dve", lambda v: v.tensor_tensor(
                    out=graw[:], in0=braw[:], in1=gtm2[:], op=ALU.add),
                    waits=[("act", A_RELU), ("dve", P.now("dve"))])
                P.emit("dve", lambda v: v.tensor_scalar(
                    out=graw[:], in0=graw[:], scalar1=nalog[0:16, 0:1],
                    scalar2=None, op0=ALU.mult), waits=[("dve", P.now("dve"))])
                BG_D = P.now("dve")
                BG_A = P.now("act")
                lastbg = 0
                for ch in range(NCH):
                    P.emit("te", lambda te, ch=ch: te.transpose(
                        out=pb[4][:, 0:16], in_=gtm1[:, ch * 128:(ch + 1) * 128],
                        identity=idf32[0:16, 0:16]),
                        waits=[("dve", max(BG_D, lastbg)), ("act", BG_A)])
                    TE_TB = P.now("te")
                    P.emit("dve", lambda v, ch=ch: v.tensor_copy(out=b_tm[:, ch, :], in_=pb[4][:, 0:16]),
                           waits=[("te", TE_TB)])
                    DV_TB = P.now("dve")
                    P.emit("te", lambda te, ch=ch: te.transpose(
                        out=pb[4][:, 0:16], in_=graw[:, ch * 128:(ch + 1) * 128],
                        identity=idf32[0:16, 0:16]), waits=[("dve", DV_TB)])
                    TE_TG = P.now("te")
                    P.emit("dve", lambda v, ch=ch: v.tensor_copy(out=g_tm[:, ch, :], in_=pb[4][:, 0:16]),
                           waits=[("te", TE_TG)])
                    lastbg = P.now("dve")
            BG_B = P.barrier()
            P.mark("BG")

            BG2_B = P.barrier()
        es3 = ExitStack()
        emit_delta_phase(nc, P, es3, pb, pbh, dict(
            ktm=ktm_t, vtm=vtm_t, kfm=kf, qfm=qf, g_tm=g_tm, b_tm=b_tm,
            id128=id128, id64s=id64s, idf32=idf32, ones128=ones128, ones_f=ones_f,
            ones_qb=ones_qb, maskL=maskL, tricum=tricum, rowsel=rowsel, obuf=obf),
            BG2_B)
        DELTA_B = P.barrier()
        P.mark("DELTA")
        es3.close()
        es2.close()
        h_cm = sb("h_cm", [128, HT, TOWN], BF16)
        ctx_xn_own = nc.sbuf_tensor("xn_own", [128, 16, TOWN], BF16)
        xn_own = ctx_xn_own.__enter__()
        ctx_yn_cm = nc.sbuf_tensor("yn_cm", [128, 8, TOWN], BF16)
        yn_cm = ctx_yn_cm.__enter__()
        emit_xn(io["x_own_cm"], 2, rstd_o, mr_o, xn_own, "o",
                prior=P.bwaits(DELTA_B, "te", "act", "dve"))
        XN_OWN = P.now("dve")
        XNO_TE = P.now("te")
        if debug:
            P.dma_emit("sy", lambda sy: sy.dma_start(out=io["xn_dbg"][:], in_=xn_own[:]),
                       waits=[("dve", XN_OWN)])

        with nc.sbuf_tensor("ytm", [128, 2, 1024], F32) as ytm, \
             nc.sbuf_tensor("yss", [128, 2, 16], F32) as yss, \
             nc.sbuf_tensor("yrs", [128, 2, 16], F32) as yrs, \
             nc.sbuf_tensor("zsil", [128, 2, 1024], F32) as zsil, \
             nc.sbuf_tensor("wz", [128, 2, 16, 512], BF16) as wz, \
             nc.sbuf_tensor("ynt", [128, 1, 1024], BF16) as ynt:
            wzdm = [0, 0]
            for zc in range(2):
                wzdm[zc] = P.dma_emit("gp", lambda gp, zc=zc: gp.dma_start(
                    out=wz[:, zc, :, :], in_=io["wz_t"][zc].rearrange("a p c -> p a c")),
                    waits=[("te", XNO_TE), ("dve", XN_OWN), ("act", DELTA_B["act"][1])]
                    if zc == 0 else [])
            lasty = [0, 0]
            last_yn_p = [0, 0]
            red_done = 0
            last_ynt_te = 0
            for tt in range(8):
                par = tt % 2
                ch = 8 + tt
                for vp in range(8):
                    P.emit("te", lambda te, vp=vp, tt=tt: te.transpose(
                        out=pbh[4][:, 0:128], in_=obf[:, vp, tt, :], identity=id128[:]),
                        waits=([("dve", P.now("dve"))] + P.bwaits(DELTA_B, "act", "dve")) if vp == 0
                        else [("dve", P.now("dve"))])
                    TE_OT = P.now("te")
                    P.emit("dve", lambda v, vp=vp, par=par: v.tensor_copy(
                        out=ytm[:, par, vp * 128:(vp + 1) * 128], in_=pbh[4][:, 0:128]),
                        waits=[("te", TE_OT)])
                Y_D = P.now("dve")
                for zc in range(2):
                    for kt in range(16):
                        P.emit("te", lambda te, tt=tt, zc=zc, kt=kt: te.matmul(
                            out=pb[5][:, 0:512], lhsT=xn_own[:, kt, tt * 128:(tt + 1) * 128],
                            rhs=wz[:, zc, kt, :], start=(kt == 0), stop=(kt == 15)),
                            waits=[("dmag", wzdm[1]), ("act", P.now("act")), ("dve", XN_OWN)]
                            if kt == 0 else [])
                    TE_Z = P.now("te")
                    P.emit("act", lambda s, par=par, zc=zc: s.activation(
                        out=zsil[:, par, zc * 512:(zc + 1) * 512], in_=pb[5][:, 0:512], func=AF.Silu),
                        waits=[("te", TE_Z), ("dve", last_yn_p[par])])
                Z_A = P.now("act")
                for hq in range(2):
                    P.emit("act", lambda s, par=par, hq=hq: s.activation(
                        out=pb[6 + hq][:, 0:512], in_=ytm[:, par, hq * 512:(hq + 1) * 512],
                        func=AF.Square),
                        waits=[("dve", max(Y_D, red_done))])
                A_SQ = P.now("act")
                for hq in range(2):
                    P.emit("dve", lambda v, par=par, hq=hq: v.tensor_reduce(
                        out=yrs[:, par, hq * 8:(hq + 1) * 8].rearrange("p (a b) -> p a b", b=1),
                        in_=pb[6 + hq][:, 0:512].rearrange("p (a b) -> p a b", a=8),
                        axis=AX.X, op=ALU.add),
                        waits=[("act", A_SQ)] if hq == 0 else [("dve", P.now("dve"))])
                P.emit("dve", lambda v, par=par: v.tensor_scalar(
                    out=yrs[:, par, :], in0=yrs[:, par, :], scalar1=1.0 / DV, scalar2=None,
                    op0=ALU.mult), waits=[("dve", P.now("dve"))])
                red_done = P.now("dve")
                DV_RS = red_done
                P.emit("act", lambda s, par=par: s.activation(
                    out=yss[:, par, :], in_=yrs[:, par, :], func=AF.Sqrt, bias=cst[:, 1:2]),
                    waits=[("dve", DV_RS)])
                A_RS = P.now("act")
                P.emit("dve", lambda v, par=par: v.reciprocal(
                    out=yrs[:, par, :], in_=yss[:, par, :]), waits=[("act", A_RS)])
                DV_RC = P.now("dve")
                P.emit("dve", lambda v, par=par: v.tensor_tensor(
                    out=ytm[:, par, :].rearrange("p (a b) -> p a b", a=16),
                    in0=ytm[:, par, :].rearrange("p (a b) -> p a b", a=16),
                    in1=yrs[:, par, :].rearrange("p (a b) -> p a b", b=1).broadcast_to((128, 16, 64)),
                    op=ALU.mult),
                    waits=[("dve", DV_RC)])
                P.emit("dve", lambda v, par=par: v.tensor_tensor(
                    out=ynt[:, 0, :], in0=ytm[:, par, :], in1=zsil[:, par, :], op=ALU.mult),
                    waits=[("dve", P.now("dve")), ("act", Z_A), ("te", last_ynt_te)])
                YN_D = P.now("dve")
                last_yn_p[par] = YN_D
                for vt in range(8):
                    P.emit("te", lambda te, par=par, vt=vt: te.transpose(
                        out=pbh[4][:, 0:128], in_=ynt[:, 0, vt * 128:(vt + 1) * 128],
                        identity=id128[:]), waits=[("dve", max(YN_D, P.now("dve")))])
                    TE_YT = P.now("te")
                    P.emit("dve", lambda v, tt=tt, vt=vt: v.tensor_scalar(
                        out=yn_cm[:, vt, tt * 128:(tt + 1) * 128], in0=pbh[4][:, 0:128],
                        scalar1=nwcol[:, 0:1], scalar2=None, op0=ALU.mult),
                        waits=[("te", TE_YT)])
                last_ynt_te = P.now("te")
                lasty[par] = P.now("dve")
        GATE_B = P.barrier()
        P.mark("GATE")
        if debug:
            P.dma_emit("sy", lambda sy: sy.dma_start(out=io["yn_dbg"][:], in_=yn_cm[:]),
                       waits=[GATE_B["dve"]])

        with nc.sbuf_tensor("wo", [128, 2, 512], BF16) as wo, \
             nc.sbuf_tensor("aev", [128, 2, 512], F32) as aev:
            lastw = [0, 0]
            lasta = [0, 0]
            ev_prev = 0
            for dc in range(4):
                for kt in range(8):
                    par = kt % 2
                    wdm = P.dma_emit("gp", lambda gp, dc=dc, kt=kt, par=par: gp.dma_start(
                        out=wo[:, par, :], in_=io["wo_t"][dc, kt, :, :]),
                        waits=[("te", lastw[par])] if (dc, kt) >= (0, 2)
                        else P.bwaits(GATE_B, "te", "act", "dve"))
                    for tt in range(8):
                        w = []
                        if tt == 0:
                            w.append(("dmag", wdm))
                        if kt == 0:
                            w.append(("dve", max(ev_prev, GATE_B["dve"][1])))
                            if tt == 0 and dc == 0:
                                w.append(("act", GATE_B["act"][1]))
                        P.emit("te", lambda te, kt=kt, tt=tt, par=par: te.matmul(
                            out=pb[tt][:, 0:512], lhsT=yn_cm[:, kt, tt * 128:(tt + 1) * 128],
                            rhs=wo[:, par, :], start=(kt == 0), stop=(kt == 7)),
                            waits=w)
                    lastw[par] = P.now("te")
                TE_WO = P.now("te")
                for tt in range(8):
                    epar = tt % 2
                    P.emit("dve", lambda v, tt=tt, epar=epar: v.tensor_copy(
                        out=aev[:, epar, :], in_=pb[tt][:, 0:512]),
                        waits=[("te", TE_WO), ("dma", lasta[epar])])
                    AD = P.now("dve")
                    admc = P.dma_emit("sy", lambda sy, tt=tt, dc=dc, epar=epar: sy.dma_start(
                        out=attn_sc[tt * 128:(tt + 1) * 128, dc * 512:(dc + 1) * 512],
                        in_=aev[:, epar, :]), waits=[("dve", AD)])
                    lasta[epar] = admc
                ev_prev = P.now("dve")
        WOUT_B = P.barrier()
        P.mark("WOUT")
        if debug:
            P.dma_emit("sy", lambda sy: sy.dma_start(out=io["attn_dbg"][:], in_=attn_sc[:]),
                       waits=[WOUT_B["dma"]])
        ctx_yn_cm.__exit__(None, None, None)

        with nc.sbuf_tensor("bt", [128, 3, 512], U8) as bt, \
             nc.sbuf_tensor("sstrip", [2, 1, 8, 512], F16) as sstrip, \
             nc.sbuf_tensor("sstripb", [2, 1, 8, 512], BF16) as sstripb, \
             nc.sbuf_tensor("i1", [128, 1, 128], I32) as i1, \
             nc.sbuf_tensor("i2", [128, 1, 128], I32) as i2, \
             nc.sbuf_tensor("wdq", [128, 2, 8, 2, 512], BF16) as wdq, \
             nc.sbuf_tensor("gsil", [128, 4, TOWN], F32) as gsil:
            bti = bt.ap().bitcast(I32)

            bt_free = [0, 0, 0]
            bc_free = [0, 0]
            gdq_n = [0]

            def dequant_granule(srcap, scrow_ap, par, dqslot, pp, cast_cnt, extra_w=()):
                dm = P.dma_emit("gp", lambda gp: gp.dma_start(out=bt[:, par, :], in_=srcap),
                                waits=list(extra_w) + [("dve", bt_free[par])])
                bcb = gdq_n[0] % 2
                gdq_n[0] += 1
                pbc = pb[6] if bcb == 0 else pb[4]
                P.emit("te", lambda te, pbc=pbc: te.matmul(
                    out=pbc[:, 0:512], lhsT=ones2f[:, 0:128], rhs=scrow_ap,
                    start=True, stop=True), waits=[("dve", max(bc_free[bcb], cast_cnt))])
                TE_SC = P.now("te")
                P.emit("dve", lambda v: v.tensor_scalar(
                    out=i1[:, 0, :], in0=bti[:, par, :], scalar1=0x08080808, scalar2=4,
                    op0=ALU.bitwise_and, op1=ALU.logical_shift_left), waits=[("dmag", dm)])
                P.emit("dve", lambda v: v.tensor_scalar(
                    out=i2[:, 0, :], in0=bti[:, par, :], scalar1=0x07070707, scalar2=2,
                    op0=ALU.bitwise_and, op1=ALU.logical_shift_left))
                P.emit("dve", lambda v: v.tensor_tensor(
                    out=i1[:, 0, :], in0=i1[:, 0, :], in1=i2[:, 0, :], op=ALU.bitwise_or),
                    waits=[("dve", P.now("dve"))])
                P.emit("dve", lambda v, pbc=pbc: v.tensor_tensor(
                    out=wdq[:, pp, dqslot, 0, :], in0=i1.ap().bitcast(FP8)[:, 0, :],
                    in1=pbc[:, 0:512], op=ALU.mult),
                    waits=[("te", TE_SC), ("dve", P.now("dve"))])
                P.emit("dve", lambda v: v.tensor_scalar(
                    out=i2[:, 0, :], in0=bti[:, par, :], scalar1=0x70707070, scalar2=2,
                    op0=ALU.bitwise_and, op1=ALU.logical_shift_right))
                P.emit("dve", lambda v: v.tensor_scalar(
                    out=i1[:, 0, :], in0=bti[:, par, :], scalar1=0x7F7F7F7F, scalar2=None,
                    op0=ALU.bitwise_and), waits=[("dve", P.now("dve"))])
                P.emit("dve", lambda v: v.tensor_tensor(
                    out=i1[:, 0, :], in0=bti[:, par, :], in1=i1[:, 0, :], op=ALU.bitwise_xor),
                    waits=[("dve", P.now("dve"))])
                bt_free[par] = P.now("dve")
                P.emit("dve", lambda v: v.tensor_tensor(
                    out=i2[:, 0, :], in0=i1[:, 0, :], in1=i2[:, 0, :], op=ALU.bitwise_or),
                    waits=[("dve", P.now("dve"))])
                P.emit("dve", lambda v, pbc=pbc: v.tensor_tensor(
                    out=wdq[:, pp, dqslot, 1, :], in0=i2.ap().bitcast(FP8)[:, 0, :],
                    in1=pbc[:, 0:512], op=ALU.mult), waits=[("dve", P.now("dve"))])
                bc_free[bcb] = P.now("dve")
                return P.now("dve")

            NOFFN = bool(os.environ.get("DEVK_NOFFN"))
            sc_cast = 0
            bc_te = 0
            te_mm = [WOUT_B["te"][1], WOUT_B["te"][1]]
            gsil_free = [0] * 8
            bankev = [("dve", WOUT_B["dve"][1])] * 4
            for it in range(0 if NOFFN else 22):
                h512, part = it // 2, it % 2
                pp = it % 2
                spar = part
                sdm = P.dma_emit("sy", lambda sy, part=part, h512=h512: sy.dma_start(
                    out=sstrip[:, 0, :, :], in_=io["gus"][part, :, h512, :, :]),
                    waits=[("te", te_mm[pp]), ("dve", sc_cast)]
                          + (P.bwaits(WOUT_B, "te", "act", "dve") if it == 0 else []))
                P.emit("dve", lambda v: v.tensor_copy(
                    out=sstripb[:, 0, :, :], in_=sstrip[:, 0, :, :]),
                    waits=[("dma", sdm), ("te", bc_te)])
                sc_cast = P.now("dve")
                for ib in range(8):
                    xw = [("te", te_mm[pp])] if ib == 0 else []
                    if it == 0 and ib < 2:
                        xw = xw + P.bwaits(WOUT_B, "te", "act", "dve")
                    if it == 0 and ib < 2:
                        bc_free[ib % 2] = max(bc_free[ib % 2], WOUT_B["dve"][1])
                    dequant_granule(io["gub"][part, h512, ib, :, :],
                                    sstripb[:, 0, ib, :], ib % 3, ib, pp, sc_cast,
                                    extra_w=xw)
                DQ_D = P.now("dve")
                bc_te = P.now("te")
                for o128 in range(4):
                    for tch in range(2):
                        g8 = o128 * 2 + tch
                        pbk = pb[g8 % 4]
                        P.emit("te", lambda te, o128=o128, tch=tch, pbk=pbk, pp=pp:
                               te.matmul(
                                   out=pbk[:, 0:512],
                                   lhsT=wdq[:, pp, 0, 0, o128 * 128:(o128 + 1) * 128],
                                   rhs=xn_own[:, 0, tch * 512:(tch + 1) * 512],
                                   start=True, stop=False),
                               waits=[("dve", DQ_D)] + [bankev[g8 % 4]])
                        for ib in range(8):
                            for lh in range(2):
                                if ib == 0 and lh == 0:
                                    continue
                                P.emit("te", lambda te, o128=o128, tch=tch, ib=ib, lh=lh,
                                       pbk=pbk, pp=pp:
                                       te.matmul(
                                           out=pbk[:, 0:512],
                                           lhsT=wdq[:, pp, ib, lh, o128 * 128:(o128 + 1) * 128],
                                           rhs=xn_own[:, ib * 2 + lh, tch * 512:(tch + 1) * 512],
                                           start=False, stop=(ib == 7 and lh == 1)))
                        TE_GU = P.now("te")
                        if part == 0:
                            P.emit("act", lambda s, o128=o128, tch=tch, pbk=pbk: s.activation(
                                out=gsil[:, o128, tch * 512:(tch + 1) * 512], in_=pbk[:, 0:512],
                                func=AF.Silu),
                                waits=[("te", TE_GU), ("dve", gsil_free[g8])])
                            bankev[g8 % 4] = ("act", P.now("act"))
                        else:
                            P.emit("dve", lambda v, h512=h512, o128=o128, tch=tch, pbk=pbk:
                                   v.tensor_tensor(
                                       out=h_cm[:, h512 * 4 + o128, tch * 512:(tch + 1) * 512],
                                       in0=pbk[:, 0:512], in1=gsil[:, o128, tch * 512:(tch + 1) * 512],
                                       op=ALU.mult),
                                   waits=[("te", TE_GU), ("act", P.now("act"))])
                            gsil_free[g8] = P.now("dve")
                            bankev[g8 % 4] = ("dve", P.now("dve"))
                te_mm[pp] = P.now("te")
            GU_B = P.barrier()
            P.mark("GU")
            if debug:
                P.dma_emit("sy", lambda sy: sy.dma_start(out=io["h_dbg"][:], in_=h_cm[:]),
                           waits=[GU_B["dve"], GU_B["act"]])
                P.dma_emit("sy", lambda sy: sy.dma_start(out=io["wdq_dbg"][:], in_=wdq[:]),
                           waits=[GU_B["dve"]])
                P.dma_emit("sy", lambda sy: sy.dma_start(out=io["gsil_dbg"][:], in_=gsil[:]),
                           waits=[GU_B["act"]])
        ctx_xn_own.__exit__(None, None, None)

        with nc.sbuf_tensor("btd", [128, 4, 512], U8) as btd, \
             nc.sbuf_tensor("i1d", [128, 1, 128], I32) as i1d, \
             nc.sbuf_tensor("i2d", [128, 1, 128], I32) as i2d, \
             nc.sbuf_tensor("wdd", [128, 22, 2, 512], BF16) as wdd, \
             nc.sbuf_tensor("dsp", [2, 1, 512], F16) as dsp, \
             nc.sbuf_tensor("dspb", [2, 1, 512], BF16) as dspb, \
             nc.sbuf_tensor("ytile", [128, 8, 2048], BF16) as ytile, \
             nc.sbuf_tensor("yp", [128, 4], F32) as yp, \
             nc.sbuf_tensor("yr", [128, 2, 1], F32) as yr, \
             nc.sbuf_tensor("xres", [128, 1, 2048], F32) as xres, \
             nc.sbuf_tensor("ares", [128, 2048], F32) as ares:
            btid = btd.ap().bitcast(I32)
            NOFFN = bool(os.environ.get("DEVK_NOFFN"))
            dbankev = [0] * 4
            scst = [0, 0]
            dnfree_h = [GU_B["te"][1], GU_B["te"][1]]

            btd_free = [0, 0, 0, 0]
            dbc_free = [GU_B["dve"][1], GU_B["dve"][1]]
            ddq_n = [0]

            def dq_down(d512, hb, first=False):
                par = hb % 4
                sdm = P.dma_emit("sy", lambda sy, d512=d512, hb=hb: sy.dma_start(
                    out=dsp[:, 0, :], in_=io["dns"][d512, :, hb, :]),
                    waits=[("dve", scst[0])]
                          + (P.bwaits(GU_B, "te", "act", "dve") if first else []))
                P.emit("dve", lambda v: v.tensor_copy(
                    out=dspb[:, 0, :], in_=dsp[:, 0, :]),
                    waits=[("dma", sdm), ("te", scst[1])]
                          + (P.bwaits(GU_B, "dve") if first else []))
                scst[0] = P.now("dve")
                dm = P.dma_emit("gp", lambda gp, d512=d512, hb=hb, par=par: gp.dma_start(
                    out=btd[:, par, :], in_=io["dnb"][d512, hb, :, :]),
                    waits=([("dve", btd_free[par])]
                           + ([("te", dnfree_h[hb // 11])] if (hb % 11) < 4 else [])
                           + (P.bwaits(GU_B, "te", "act", "dve") if first else [])))
                bcb = ddq_n[0] % 2
                ddq_n[0] += 1
                pbc = pb[6] if bcb == 0 else pb[4]
                P.emit("te", lambda te, pbc=pbc: te.matmul(
                    out=pbc[:, 0:512], lhsT=ones2f[:, 0:128], rhs=dspb[:, 0, :],
                    start=True, stop=True),
                    waits=[("dve", max(dbc_free[bcb], scst[0]))]
                          + (P.bwaits(GU_B, "dve", "act") if first else []))
                TE_SC = P.now("te")
                scst[1] = TE_SC
                P.emit("dve", lambda v, par=par: v.tensor_scalar(
                    out=i1d[:, 0, :], in0=btid[:, par, :], scalar1=0x08080808, scalar2=4,
                    op0=ALU.bitwise_and, op1=ALU.logical_shift_left), waits=[("dmag", dm)])
                P.emit("dve", lambda v, par=par: v.tensor_scalar(
                    out=i2d[:, 0, :], in0=btid[:, par, :], scalar1=0x07070707, scalar2=2,
                    op0=ALU.bitwise_and, op1=ALU.logical_shift_left))
                P.emit("dve", lambda v, par=par: v.tensor_tensor(
                    out=i1d[:, 0, :], in0=i1d[:, 0, :], in1=i2d[:, 0, :], op=ALU.bitwise_or),
                    waits=[("dve", P.now("dve"))])
                P.emit("dve", lambda v, par=par, hb=hb, pbc=pbc: v.tensor_tensor(
                    out=wdd[:, hb, 0, :], in0=i1d.ap().bitcast(FP8)[:, 0, :],
                    in1=pbc[:, 0:512], op=ALU.mult),
                    waits=[("te", TE_SC), ("dve", P.now("dve"))])
                P.emit("dve", lambda v, par=par: v.tensor_scalar(
                    out=i2d[:, 0, :], in0=btid[:, par, :], scalar1=0x70707070, scalar2=2,
                    op0=ALU.bitwise_and, op1=ALU.logical_shift_right))
                P.emit("dve", lambda v, par=par: v.tensor_scalar(
                    out=i1d[:, 0, :], in0=btid[:, par, :], scalar1=0x7F7F7F7F, scalar2=None,
                    op0=ALU.bitwise_and), waits=[("dve", P.now("dve"))])
                P.emit("dve", lambda v, par=par: v.tensor_tensor(
                    out=i1d[:, 0, :], in0=btid[:, par, :], in1=i1d[:, 0, :],
                    op=ALU.bitwise_xor), waits=[("dve", P.now("dve"))])
                btd_free[par] = P.now("dve")
                P.emit("dve", lambda v, par=par: v.tensor_tensor(
                    out=i2d[:, 0, :], in0=i1d[:, 0, :], in1=i2d[:, 0, :],
                    op=ALU.bitwise_or), waits=[("dve", P.now("dve"))])
                P.emit("dve", lambda v, par=par, hb=hb, pbc=pbc: v.tensor_tensor(
                    out=wdd[:, hb, 1, :], in0=i2d.ap().bitcast(FP8)[:, 0, :],
                    in1=pbc[:, 0:512], op=ALU.mult), waits=[("dve", P.now("dve"))])
                dbc_free[bcb] = P.now("dve")

            def dn_mms(t128, hlo, hhi, dq_cnt, openg, closeg, d512):
                pbk = pb[t128 % 4]
                for hb in range(hlo, hhi):
                    for lh in range(2):
                        w = []
                        if hb == hlo and lh == 0:
                            w = [("dve", max(dq_cnt, dbankev[t128 % 4]))] if openg \
                                else [("dve", dq_cnt)]
                        P.emit("te", lambda te, t128=t128, hb=hb, lh=lh, pbk=pbk: te.matmul(
                            out=pbk[:, 0:512], lhsT=h_cm[:, hb * 2 + lh, t128 * 128:(t128 + 1) * 128],
                            rhs=wdd[:, hb, lh, :], start=(openg and hb == hlo and lh == 0),
                            stop=(closeg and hb == hhi - 1 and lh == 1)), waits=w)
                if closeg:
                    TE_DN = P.now("te")
                    P.emit("dve", lambda v, t128=t128, d512=d512, pbk=pbk: v.tensor_copy(
                        out=ytile[:, t128, d512 * 512:(d512 + 1) * 512], in_=pbk[:, 0:512]),
                        waits=[("te", TE_DN)])
                    dbankev[t128 % 4] = P.now("dve")

            if not NOFFN:
                for hb in range(11):
                    dq_down(0, hb, first=(hb == 0))
                DQ_H0 = P.now("dve")
                for d512 in range(4):
                    for t128 in range(4):
                        dn_mms(t128, 0, 11, DQ_H0, True, False, d512)
                    for hb in range(11, 22):
                        dq_down(d512, hb)
                    DQ_H1 = P.now("dve")
                    for t128 in range(4):
                        dn_mms(t128, 11, 22, DQ_H1, False, True, d512)
                    for t128 in range(4, 8):
                        dn_mms(t128, 0, 11, DQ_H0, True, False, d512)
                    dnfree_h[0] = P.now("te")
                    if d512 < 3:
                        for hb in range(11):
                            dq_down(d512 + 1, hb)
                    DQ_H0 = P.now("dve")
                    for t128 in range(4, 8):
                        dn_mms(t128, 11, 22, DQ_H1, False, True, d512)
                    dnfree_h[1] = P.now("te")
            Y_D2 = P.now("dve")
            if debug:
                P.dma_emit("sy", lambda sy: sy.dma_start(out=io["yt_dbg"][:], in_=ytile[:]),
                           waits=[("dve", Y_D2)])
            lastx = 0
            last_odm = 0
            sqfree = [0, 0]
            for t128 in range(0 if NOFFN else 8):
                par = t128 % 2
                xdm = P.dma_emit("sy", lambda sy, t128=t128: sy.dma_start(
                    out=xres[:, 0, :], in_=io["x_own_tm"][t128 * 128:(t128 + 1) * 128, :]),
                    waits=[("dve", lastx)])
                adm = P.dma_emit("sy", lambda sy, t128=t128: sy.dma_start(
                    out=ares[:], in_=attn_sc[t128 * 128:(t128 + 1) * 128, :]),
                    waits=[("dve", lastx),
                           ("dma", max(last_odm, WOUT_B["dma"][1]))])
                for g in range(4):
                    sbk = pb[5 if g % 2 == 0 else 7]
                    P.emit("act", lambda s, t128=t128, g=g, sbk=sbk: s.activation(
                        out=sbk[:, 0:512], in_=ytile[:, t128, g * 512:(g + 1) * 512],
                        func=AF.Square),
                        waits=[("dve", max(Y_D2, sqfree[g % 2]))])
                    A_S2 = P.now("act")
                    P.emit("dve", lambda v, g=g, sbk=sbk: v.tensor_reduce(
                        out=yp[:, g:g + 1], in_=sbk[:, 0:512], axis=AX.X, op=ALU.add),
                        waits=[("act", A_S2)])
                    sqfree[g % 2] = P.now("dve")
                P.emit("dve", lambda v, par=par: v.tensor_reduce(
                    out=yr[:, par, :], in_=yp[:], axis=AX.X, op=ALU.add),
                    waits=[("dve", P.now("dve"))])
                P.emit("dve", lambda v, par=par: v.tensor_scalar(
                    out=yr[:, par, :], in0=yr[:, par, :], scalar1=1.0 / D, scalar2=None,
                    op0=ALU.mult), waits=[("dve", P.now("dve"))])
                DV_Y = P.now("dve")
                P.emit("act", lambda s, par=par: s.activation(
                    out=yp[:, 0:1], in_=yr[:, par, :], func=AF.Sqrt, bias=cst[:, 1:2]),
                    waits=[("dve", DV_Y)])
                A_Y = P.now("act")
                P.emit("dve", lambda v, par=par: v.reciprocal(
                    out=yr[:, par, :], in_=yp[:, 0:1]), waits=[("act", A_Y)])
                DV_YR = P.now("dve")
                P.emit("dve", lambda v: v.tensor_tensor(
                    out=ares[:], in0=xres[:, 0, :], in1=ares[:], op=ALU.add),
                    waits=[("dma", adm), ("dma", xdm)])
                P.emit("dve", lambda v, t128=t128, par=par: v.scalar_tensor_tensor(
                    out=ares[:], in0=ytile[:, t128, :], scalar=yr[:, par, 0:1],
                    in1=ares[:], op0=ALU.mult, op1=ALU.add),
                    waits=[("dve", P.now("dve")), ("dve", DV_YR)])
                OD = P.now("dve")
                odm = P.dma_emit("sy", lambda sy, t128=t128: sy.dma_start(
                    out=io["out"][t128 * 128:(t128 + 1) * 128, :], in_=ares[:]),
                    waits=[("dve", OD)])
                last_odm = odm
                lastx = OD
            FIN_DMA = P.dma_cnt
            P.emit("sy", lambda sy: sy.nop(), waits=[("dma", FIN_DMA)], inc=False)

        blk = es.enter_context(nc.Block())

        @blk.sync
        def _(sy):
            P.replay("sy", sy)

        @blk.tensor
        def _(te):
            P.replay("te", te)

        @blk.vector
        def _(v):
            P.replay("dve", v)

        @blk.scalar
        def _(s):
            P.replay("act", s)

        @blk.gpsimd
        def _(g):
            P.replay("gp", g)

    return nc, io



def emit_delta_phase(nc, P, es, pb, pbh, tn, B):
    sb = lambda n, s, d: es.enter_context(nc.sbuf_tensor(n, s, d))
    ktm, vtm, kfm, qfm = tn["ktm"], tn["vtm"], tn["kfm"], tn["qfm"]
    g_tm, b_tm = tn["g_tm"], tn["b_tm"]
    id128, id64s, idf32 = tn["id128"], tn["id64s"], tn["idf32"]
    ones128, ones_f, ones_qb = tn["ones128"], tn["ones_f"], tn["ones_qb"]
    maskL, tricum, rowsel = tn["maskL"], tn["tricum"], tn["rowsel"]
    obuf = tn["obuf"]
    NIC = NCH * NV

    gc_tm = sb("gc_tm", [128, NIC], F32)
    gcHn = sb("gcHn", [128, NIC], F32)
    egc = sb("egc", [128, NIC], F32)
    einv_t = sb("einv_t", [128, NIC], F32)
    egcC = sb("egcC", [128, NIC], F32)
    begc = sb("begc", [128, NIC], F32)
    bneg = sb("bneg", [128, NIC], F32)
    dtmp = sb("dtmp", [128, NIC], F32)

    NROT = 2
    diagH_t = sb("d_diagH", [128, NROT, 128], F32)
    diagHn_t = sb("d_diagHn", [128, NROT, 128], F32)
    diagE_t = sb("d_diagE", [128, NROT, 128], BF16)
    expGs_t = sb("d_expGs", [128, NROT, 128], F32)
    expGi_t = sb("d_expGi", [128, NROT, 128], F32)
    abuf_t = sb("d_abuf", [128, NROT, 128], BF16)
    qbart_t = sb("d_qbart", [64, NV, 128], F32)

    YB_t = sb("d_YB", [128, NV, 2, 256], BF16)
    Ct_t = sb("d_Ct", [128, NV, 128], BF16)
    Ck_t = sb("d_Ck", [128, NV, 2, 128], BF16)
    At_t = sb("d_At", [128, NV, 128], BF16)
    khat_t = sb("d_khat", [128, NV, DK], BF16)
    AUs_t = sb("d_AUs", [64, NV, 128], BF16)
    QmAW_t = sb("d_QmAW", [64, NV, 128], BF16)
    Ptp_t = sb("d_Ptp", [64, NV, DK], BF16)
    Zs_t = sb("d_Zs", [64, NV, DV], BF16)
    IegcC_t = sb("d_IegcC", [64, NV, 64], BF16)
    Svh_t = sb("d_Svh", [64, NV, DV], BF16)

    diagH = lambda vh: diagH_t[:, vh % NROT, :]
    diagHn = lambda vh: diagHn_t[:, vh % NROT, :]
    diagE = lambda vh: diagE_t[:, vh % NROT, :]
    expGs = lambda vh: expGs_t[:, vh % NROT, :]
    expGi = lambda vh: expGi_t[:, vh % NROT, :]
    abuf = lambda vh: abuf_t[:, vh % NROT, :]
    qbart = lambda vh: qbart_t[:, vh, :]
    Bm = lambda vh: YB_t[:, vh, 0, 128:256]
    Ct = lambda vh: Ct_t[:, vh, :]
    At = lambda vh: At_t[:, vh, :]
    khat = lambda vh: khat_t[:, vh, :]
    AUs = lambda vh: AUs_t[:, vh, :]
    QmAW = lambda vh: QmAW_t[:, vh, :]
    Ptp = lambda vh: Ptp_t[:, vh, :]
    Zs = lambda vh: Zs_t[:, vh, :]
    IegcC = lambda vh: IegcC_t[:, vh, :]
    Svh = lambda vh: Svh_t[:, vh, :]

    def grel_v(i):
        return pb[i][:, 0:128]

    def gram_kk(i):
        return pb[2 + i][:, 0:128]

    def gram_qk(i):
        return pb[2 + i][:, 128:256]

    def sq_v(i):
        return pb[2 + i][:, 256:384]

    BT_V = pbh[4][:, 0:128]
    AT_V = pbh[4][:, 256:384]
    QB_V = pb[5][0:64, 0:128]
    GC_V = pb[6][:, 0:NIC]
    PT_V = pb[7][0:64, 0:64]
    Z_V = pb[7][0:64, 64:128]
    O_V = pb[6][0:64, 256:384]
    S_V = pb[6][0:64, 384:448]

    def col(t, ch, vh):
        i = ch * NV + vh
        return t[:, i:i + 1]

    bw = [B["te"], B["dve"], B["act"], B["gp"], B["dma"], B["dmag"]]

    P.emit("dve", lambda v: v.memset(Svh_t[:], 0.0), waits=bw)
    CONST = P.now("dve")
    gflat = g_tm.ap().rearrange("p a b -> p (a b)")
    bflat = b_tm.ap().rearrange("p a b -> p (a b)")
    P.emit("te", lambda te: te.matmul(out=GC_V, lhsT=tricum[:], rhs=gflat,
                                      start=True, stop=True), waits=bw + [("dve", CONST)])
    TE_GC = P.now("te")
    P.emit("dve", lambda v: v.tensor_copy(out=gc_tm[:], in_=GC_V), waits=[("te", TE_GC)])
    GC_BASE = P.now("dve")
    P.emit("act", lambda s: s.activation(out=gcHn[:], in_=gc_tm[:], func=AF.Copy, scale=-1.0),
           waits=[("dve", GC_BASE)] + bw)
    P.emit("act", lambda s: s.activation(out=egc[:], in_=gc_tm[:], func=AF.Exp))
    ACT_E = P.now("act")
    P.emit("te", lambda te: te.matmul(out=GC_V, lhsT=rowsel[:], rhs=gc_tm[:],
                                      start=True, stop=True),
           waits=[("dve", GC_BASE), ("act", ACT_E)])
    TE_GCC = P.now("te")
    P.emit("dve", lambda v: v.tensor_tensor(out=dtmp[:], in0=GC_V, in1=gc_tm[:], op=ALU.subtract),
           waits=[("te", TE_GCC)])
    DVE_EINV = P.now("dve")
    P.emit("act", lambda s: s.activation(out=einv_t[:], in_=dtmp[:], func=AF.Exp),
           waits=[("dve", DVE_EINV)])
    P.emit("act", lambda s: s.activation(out=egcC[:], in_=GC_V, func=AF.Exp))
    P.emit("dve", lambda v: v.tensor_tensor(out=begc[:], in0=bflat, in1=egc[:], op=ALU.mult),
           waits=[("act", ACT_E)])
    P.emit("act", lambda s: s.activation(out=bneg[:], in_=bflat, func=AF.Copy, scale=-1.0))
    PREP_DVE = P.now("dve")
    PREP_ACT = P.now("act")

    grel_free = [0, 0]
    grel_free_a = [0, 0]
    gram_free = [0, 0]
    sq_free = [0, 0]
    sq_free_d = [0, 0]
    sqf_act = [0, 0]
    awu_free_act = [0, 0]
    diag_free = [0, 0]
    expgs_free = [0, 0]
    abuf_free = [0, 0]
    iegc_free = [0] * NV
    b4_free_act = 0
    qb_free_dve = 0
    b5_free = dict(dve=0, act=0)
    o_free_act = 0
    s_free_act = 0
    sact = [0] * NV

    first = True
    for ch in range(NCH):
        te_prev = P.now("te")
        gp_diag = [0] * NV
        dve_b = [0] * NV
        act_b = [0] * NV
        for vh in range(NV):
            w = [("act", PREP_ACT), ("dve", PREP_DVE)] if first else []
            first = False
            w = w + [("te", max(diag_free[vh % 2], te_prev))]
            P.emit("dve", lambda v, ch=ch, vh=vh: v.tensor_scalar(
                out=diagH(vh), in0=idf32[:], scalar1=col(gc_tm, ch, vh), scalar2=None, op0=ALU.mult),
                waits=w)
            P.emit("dve", lambda v, ch=ch, vh=vh: v.tensor_scalar(
                out=diagHn(vh), in0=idf32[:], scalar1=col(gcHn, ch, vh), scalar2=None, op0=ALU.mult))
            P.emit("dve", lambda v, ch=ch, vh=vh: v.tensor_scalar(
                out=diagE(vh), in0=id128[:], scalar1=col(egc, ch, vh), scalar2=None, op0=ALU.mult))
            P.emit("dve", lambda v, ch=ch, vh=vh: v.tensor_scalar(
                out=IegcC(vh), in0=id64s[0:64, :], scalar1=col(egcC, ch, vh)[0:64, :],
                scalar2=None, op0=ALU.mult), waits=[("dve", iegc_free[vh])])
            gp_diag[vh] = P.now("dve")
            kh = vh // 2
            khs = (kh % 2) * 64
            kht = kh // 2
            par = vh % 2
            kslice = kfm[khs:khs + 64, kht, ch * C:(ch + 1) * C]
            qslice = qfm[khs:khs + 64, kht, ch * C:(ch + 1) * C]
            P.emit("te", lambda te, vh=vh, par=par: te.matmul(out=grel_v(par), lhsT=diagH(vh), rhs=ones_f[:],
                                                              start=True, stop=False),
                   waits=[("dve", max(gp_diag[vh], grel_free[par])),
                          ("act", max(awu_free_act[par], grel_free_a[par]))])
            P.emit("te", lambda te, vh=vh, par=par: te.matmul(out=grel_v(par), lhsT=ones_f[:], rhs=diagHn(vh),
                                                              start=False, stop=False))
            P.emit("te", lambda te, vh=vh, par=par: te.matmul(out=grel_v(par), lhsT=idf32[:], rhs=maskL[:],
                                                              start=False, stop=True))
            TE_GREL = P.now("te")
            P.emit("te", lambda te, ks=kslice, par=par: te.matmul(
                out=gram_kk(par), lhsT=ks, rhs=ks, start=True, stop=True),
                waits=[("dve", max(gram_free[par], sq_free_d[par])), ("act", sqf_act[par])])
            P.emit("te", lambda te, ks=kslice, qs=qslice, par=par: te.matmul(
                out=gram_qk(par), lhsT=qs, rhs=ks, start=True, stop=True))
            P.emit("te", lambda te, vh=vh: te.matmul(out=QB_V, lhsT=ones_qb[:], rhs=diagE(vh),
                                                     start=True, stop=True),
                   waits=[("dve", max(qb_free_dve, gp_diag[vh]))])
            TE_G = P.now("te")
            diag_free[par] = TE_G
            P.emit("act", lambda s, vh=vh, par=par: s.activation(out=expGs(vh), in_=grel_v(par), func=AF.Exp),
                   waits=[("te", TE_GREL), ("dve", max(expgs_free[vh % 2], grel_free[par]))])
            grel_free_a[par] = P.now("act")
            ACT_X = P.now("act")
            P.emit("dve", lambda v, vh=vh, ch=ch, par=par: v.scalar_tensor_tensor(
                out=Bm(vh), in0=gram_kk(par), scalar=col(bneg, ch, vh), in1=expGs(vh),
                op0=ALU.mult, op1=ALU.mult), waits=[("act", ACT_X), ("te", te_prev)])
            P.emit("dve", lambda v, vh=vh: v.tensor_tensor(
                out=expGi(vh), in0=expGs(vh), in1=idf32[:], op=ALU.add))
            expgs_free[vh % 2] = P.now("dve")
            P.emit("dve", lambda v, vh=vh, ch=ch: v.tensor_scalar(
                out=khat(vh), in0=ktm[:, ch, vh // 2, :], scalar1=col(einv_t, ch, vh),
                scalar2=None, op0=ALU.mult), waits=[("te", te_prev)])
            P.emit("dve", lambda v, vh=vh, par=par: v.tensor_tensor(
                out=abuf(vh), in0=gram_qk(par), in1=expGi(vh), op=ALU.mult),
                waits=[("te", max(abuf_free[vh % 2], te_prev))])
            gram_free[par] = P.now("dve")
            DVE_BA = P.now("dve")
            P.emit("te", lambda te, vh=vh: te.transpose(out=BT_V, in_=Bm(vh), identity=id128[:]),
                   waits=[("dve", DVE_BA), ("act", b4_free_act)])
            TE_BT = P.now("te")
            P.emit("act", lambda s, vh=vh: s.activation(out=Ct(vh), in_=BT_V, func=AF.Copy),
                   waits=[("te", TE_BT)])
            b4a = P.now("act")
            P.emit("te", lambda te, vh=vh: te.transpose(out=AT_V, in_=abuf(vh), identity=id128[:]),
                   waits=[("act", b4a)])
            TE_AT = P.now("te")
            abuf_free[vh % 2] = TE_AT
            P.emit("act", lambda s, vh=vh: s.activation(out=At(vh), in_=AT_V, func=AF.Copy),
                   waits=[("te", TE_AT)])
            b4_free_act = P.now("act")
            P.emit("dve", lambda v, vh=vh, ch=ch: v.tensor_scalar(
                out=YB_t[:, vh, 0, 0:64], in0=ktm[:, ch, vh // 2, :], scalar1=col(begc, ch, vh),
                scalar2=None, op0=ALU.mult), waits=[("te", te_prev)])
            P.emit("dve", lambda v, vh=vh, ch=ch: v.tensor_scalar(
                out=YB_t[:, vh, 0, 64:128], in0=vtm[:, ch, vh, :],
                scalar1=bflat[:, ch * NV + vh: ch * NV + vh + 1],
                scalar2=None, op0=ALU.mult))
            P.emit("dve", lambda v, vh=vh, ch=ch, qs=qslice: v.tensor_tensor(
                out=qbart(vh), in0=qs, in1=QB_V, op=ALU.mult), waits=[("te", TE_G)])
            qb_free_dve = P.now("dve")
            dve_b[vh] = P.now("dve")
            act_b[vh] = P.now("act")

        ybprev = [YB_t[:, vh, 0, :] for vh in range(NV)]
        cprev = [Ct(vh) for vh in range(NV)]
        ydve = list(dve_b)
        b_act = [0] * NV
        c_dve = [0] * NV
        c_act = list(act_b)
        ap_free = [0, 0]
        apb_act = [0, 0]
        sqf = [max(s, dve_b[NV - 1]) for s in sq_free]
        sqd = list(sq_free_d)
        for lev in range(NLEV):
            last = lev == NLEV - 1
            for vh in range(NV):
                par = vh % 2
                if not last:
                    P.emit("te", lambda te, c=cprev[vh], yb=ybprev[vh], par=par: te.matmul(
                        out=pb[par][:, 0:256], lhsT=c, rhs=yb, start=True, stop=True),
                        waits=[("dve", max(ydve[vh], ap_free[par], c_dve[vh])),
                               ("act", max(c_act[vh], b_act[vh], grel_free_a[par], apb_act[par]))])
                    TE_AP = P.now("te")
                    s_new = (lev + 1) % 2
                    P.emit("dve", lambda v, vh=vh, s_new=s_new, yb=ybprev[vh], par=par: v.tensor_tensor(
                        out=YB_t[:, vh, s_new, 0:128], in0=pb[par][:, 0:128],
                        in1=yb[:, 0:128], op=ALU.add), waits=[("te", TE_AP)])
                    ap_free[par] = P.now("dve")
                    grel_free[par] = P.now("dve")
                    bold_act = b_act[vh]
                    P.emit("act", lambda s, vh=vh, s_new=s_new, par=par: s.activation(
                        out=YB_t[:, vh, s_new, 128:256], in_=pb[par][:, 128:256], func=AF.Copy),
                        waits=[("te", TE_AP), ("dve", ap_free[par])])
                    apb_act[par] = P.now("act")
                    b_act[vh] = P.now("act")
                    P.emit("te", lambda te, b=ybprev[vh][:, 128:256], c=cprev[vh], par=par: te.matmul(
                        out=sq_v(par), lhsT=b, rhs=c, start=True, stop=True),
                        waits=[("dve", max(sqf[par], sqd[par], c_dve[vh])),
                               ("act", max(c_act[vh], bold_act))])
                    TE_C = P.now("te")
                    cnew = Ck_t[:, vh, lev % 2, :]
                    P.emit("dve", lambda v, cn=cnew, par=par: v.tensor_copy(out=cn, in_=sq_v(par)),
                           waits=[("te", TE_C)])
                    sqd[par] = P.now("dve")
                    c_dve[vh] = P.now("dve")
                    ybprev[vh] = YB_t[:, vh, s_new, :]
                    ydve[vh] = ap_free[par]
                    cprev[vh] = cnew
                    sqf[par] = 0
                else:
                    P.emit("te", lambda te, c=cprev[vh], yb=ybprev[vh], par=par: te.matmul(
                        out=pb[par][:, 0:128], lhsT=c, rhs=yb[:, 0:128], start=True, stop=True),
                        waits=[("dve", max(ydve[vh], ap_free[par], c_dve[vh])),
                               ("act", max(c_act[vh], grel_free_a[par], apb_act[par]))])
                    TE_AP = P.now("te")
                    s_new = (lev + 1) % 2
                    P.emit("dve", lambda v, vh=vh, s_new=s_new, yb=ybprev[vh], par=par: v.tensor_tensor(
                        out=YB_t[:, vh, s_new, 0:128], in0=pb[par][:, 0:128],
                        in1=yb[:, 0:128], op=ALU.add), waits=[("te", TE_AP)])
                    ap_free[par] = P.now("dve")
                    grel_free[par] = P.now("dve")
                    ybprev[vh] = YB_t[:, vh, s_new, :]
                    ydve[vh] = ap_free[par]
        yprev = ybprev
        sq_free[0], sq_free[1] = sqf[0], sqf[1]
        sq_free_d[0], sq_free_d[1] = sqd[0], sqd[1]
        grel_free_a[0] = max(grel_free_a[0], apb_act[0])
        grel_free_a[1] = max(grel_free_a[1], apb_act[1])

        dve_fin = [0] * NV
        act_fin = [0] * NV
        for vh in range(NV):
            par = vh % 2
            P.emit("te", lambda te, y=yprev[vh][:, 0:128], vh=vh, par=par: te.matmul(
                out=grel_v(par), lhsT=y, rhs=At(vh), start=True, stop=True),
                waits=[("dve", max(ydve[vh], grel_free[par])),
                       ("act", max(awu_free_act[par], grel_free_a[par]))])
            TE_AWU = P.now("te")
            P.emit("dve", lambda v, vh=vh, par=par: v.tensor_tensor(
                out=QmAW(vh), in0=qbart(vh), in1=grel_v(par)[0:64, :], op=ALU.subtract),
                waits=[("te", TE_AWU)])
            DVE_QM = P.now("dve")
            grel_free[par] = DVE_QM
            P.emit("act", lambda s, vh=vh, par=par: s.activation(
                out=AUs(vh), in_=grel_v(par)[64:128, :], func=AF.Copy),
                waits=[("te", TE_AWU), ("dve", DVE_QM)])
            awu_free_act[par] = P.now("act")
            P.emit("te", lambda te, y=yprev[vh], vh=vh: te.matmul(
                out=PT_V, lhsT=y[:, 0:64], rhs=khat(vh), start=True, stop=True),
                waits=[("dve", b5_free["dve"]), ("act", b5_free["act"])])
            P.emit("te", lambda te, y=yprev[vh], vh=vh: te.matmul(
                out=Z_V, lhsT=khat(vh), rhs=y[:, 64:128], start=True, stop=True))
            TE_PZ = P.now("te")
            P.emit("dve", lambda v, vh=vh: v.scalar_tensor_tensor(
                out=Ptp(vh), in0=PT_V, scalar=-1.0, in1=IegcC(vh), op0=ALU.mult, op1=ALU.add),
                waits=[("te", TE_PZ)])
            P.emit("dve", lambda v, vh=vh: v.tensor_copy(
                out=Zs(vh), in_=Z_V), waits=[("te", TE_PZ), ("dve", P.now("dve"))])
            b5_free["dve"] = P.now("dve")
            iegc_free[vh] = P.now("dve")
            dve_fin[vh] = P.now("dve")
            act_fin[vh] = P.now("act")

        for vh in range(NV):
            vp = vh // 2
            half = (vh % 2) * 64
            if ch >= 8:
                P.emit("te", lambda te, vh=vh: te.matmul(
                    out=O_V, lhsT=Svh(vh), rhs=QmAW(vh), start=True, stop=False),
                    waits=[("dve", dve_fin[vh]),
                           ("act", max(act_fin[vh], sact[vh], o_free_act, s_free_act))])
                P.emit("te", lambda te, vh=vh: te.matmul(
                    out=O_V, lhsT=id64s[0:64, :], rhs=AUs(vh), start=False, stop=True))
                TE_O = P.now("te")
                P.emit("act", lambda s, vh=vh, ch=ch, vp=vp, half=half: s.activation(
                    out=obuf[half:half + 64, vp, ch - 8, :], in_=O_V, func=AF.Copy),
                    waits=[("te", TE_O)])
                o_free_act = P.now("act")
            P.emit("te", lambda te, vh=vh: te.matmul(
                out=S_V, lhsT=Ptp(vh), rhs=Svh(vh), start=True, stop=False),
                waits=[("act", max(sact[vh], o_free_act, s_free_act))])
            P.emit("te", lambda te, vh=vh: te.matmul(
                out=S_V, lhsT=id64s[0:64, :], rhs=Zs(vh), start=False, stop=True))
            TE_S = P.now("te")
            P.emit("act", lambda s, vh=vh: s.activation(
                out=Svh(vh), in_=S_V, func=AF.Copy), waits=[("te", TE_S)])
            sact[vh] = P.now("act")
            s_free_act = sact[vh]



def _make_pi(n):
    pi = np.empty(n, np.int64)
    for b0 in range(n // 256):
        pi[256 * b0:256 * b0 + 128] = 256 * b0 + 2 * np.arange(128)
        pi[256 * b0 + 128:256 * b0 + 256] = 256 * b0 + 2 * np.arange(128) + 1
    return pi


PI = _make_pi(2048)
SIG = _make_pi(5632)


def _prep_weights(inp):
    W_qkv = np.asarray(inp["L0_attn_W_qkv"], np.float32)
    W_z = np.asarray(inp["L0_attn_W_z"], np.float32)
    W_b = np.asarray(inp["L0_attn_W_b"], np.float32)
    W_a = np.asarray(inp["L0_attn_W_a"], np.float32)
    W_out = np.asarray(inp["L0_attn_W_out"], np.float32)
    conv_w = np.asarray(inp["L0_attn_conv_w"], np.float32)
    dt_bias = np.asarray(inp["L0_attn_dt_bias"], np.float32)
    A_log = np.asarray(inp["L0_attn_A_log"], np.float32)
    norm_w = np.asarray(inp["L0_attn_norm_weight"], np.float32)
    gup = np.asarray(inp["L0_ffn_gate_up_packed"])
    gus = np.asarray(inp["L0_ffn_gate_up_scales"])
    dnp_ = np.asarray(inp["L0_ffn_down_packed"])
    dns = np.asarray(inp["L0_ffn_down_scales"])

    w = {}
    wq = W_qkv[:, PI].T.reshape(16, 128, 16, 128)
    w["wqkv_t"] = np.ascontiguousarray(wq.transpose(2, 1, 0, 3))
    wz = W_z[:, PI].T.reshape(16, 128, 2, 512)
    w["wz_t"] = np.ascontiguousarray(wz.transpose(2, 0, 1, 3))
    w["wb_t"] = np.ascontiguousarray(W_b[:, PI].T.reshape(16, 128, 16))
    w["wa_t"] = np.ascontiguousarray(W_a[:, PI].T.reshape(16, 128, 16))
    wo = W_out[PI, :].T.reshape(8, 128, 4, 512)
    w["wo_t"] = np.ascontiguousarray(wo.transpose(2, 0, 1, 3))
    w["convw"] = np.ascontiguousarray(conv_w[:, 0, :].reshape(16, 128, 4))
    vecs = np.zeros((64, 3), np.float32)
    vecs[0:16, 0] = dt_bias
    vecs[0:16, 1] = A_log
    vecs[0:64, 2] = norm_w
    w["vecs"] = vecs
    sel2 = np.zeros((2, 128), ml_dtypes.bfloat16)
    sel2[0, 0:64] = 64.0
    sel2[1, 64:128] = 64.0
    w["sel2"] = sel2
    I = 5632
    gub = np.empty((2, 11, 8, 128, 512), np.uint8)
    guss = np.empty((2, 2, 11, 8, 512), np.float16)
    for part in range(2):
        rows = gup[part * I + SIG]
        bT = rows.view(np.uint8).reshape(I, 1024).T
        gub[part] = bT.reshape(8, 128, 11, 512).transpose(2, 0, 1, 3)
        sc = gus[:, part * I + SIG]
        guss[part] = sc.reshape(8, 2, 11, 512).transpose(1, 2, 0, 3)
    w["gub"] = np.ascontiguousarray(gub)
    w["gus"] = np.ascontiguousarray(guss)
    rows = dnp_[PI]
    bT2 = rows.view(np.uint8).reshape(2048, 2816).T
    w["dnb"] = np.ascontiguousarray(bT2.reshape(22, 128, 4, 512).transpose(2, 0, 1, 3))
    w["dns"] = np.ascontiguousarray(dns[:, PI].reshape(22, 2, 4, 512).transpose(2, 1, 0, 3))
    return w


def _prep_core_x(core, x_full):
    b, hf = core // 2, core % 2
    x = np.asarray(x_full, np.float32)[b]
    m = {}
    if hf == 0:
        xs = np.concatenate([np.zeros((1024, 2048), np.float32), x[0:1024]], 0)
    else:
        xs = x
    xs = np.ascontiguousarray(xs[:, PI])
    m["x_scan_tm"] = xs
    m["x_scan_cm"] = np.ascontiguousarray(xs.T)
    xo = np.ascontiguousarray(x[hf * 1024:(hf + 1) * 1024][:, PI])
    m["x_own_tm"] = xo
    m["x_own_cm"] = np.ascontiguousarray(xo.T)
    return m


def kernel(**inputs):
    w = _prep_weights(inputs)
    in_maps = []
    for core in range(8):
        m = dict(w)
        m.update(_prep_core_x(core, inputs["L0_attn_x"]))
        in_maps.append(m)
    nc, io = build()
    res = run_bass_kernel_spmd(nc, in_maps, core_ids=list(range(8)))
    out = np.empty((4, 2048, 2048), np.float32)
    for core in range(8):
        b, hf = core // 2, core % 2
        shard = np.asarray(res.results[core]["out"]).reshape(1024, 2048)
        out[b, hf * 1024:(hf + 1) * 1024][:, PI] = shard
    return out


# version 14
# speedup vs baseline: 1.2320x; 1.0099x over previous
import os
import numpy as np
import ml_dtypes
import concourse.bass as bass
import concourse.mybir as mybir
from concourse.bass_utils import run_bass_kernel_spmd
from contextlib import ExitStack

F32 = mybir.dt.float32
BF16 = mybir.dt.bfloat16
I32 = mybir.dt.int32
F16 = mybir.dt.float16
U8 = mybir.dt.uint8
FP8 = mybir.dt.float8e4
AF = mybir.ActivationFunctionType
ALU = mybir.AluOpType
AX = mybir.AxisListType

D = 2048
T = 2048
TH = 1024
_LAST_PLAN = None
TOWN = 1024
C = 128
NCH = 16
NV = 16
KH = 8
DK = DV = 64
NEG = -30000.0
NLEV = 7
I_FF = 5632
HT = 44
LN_EPS = 1e-5
EPS_A = 1e-6


class Plan:
    def __init__(self, nc, es):
        self.marks = {}
        self.nc = nc
        self.items = []
        self.cnt = {e: 0 for e in ("te", "dve", "act", "gp", "sy")}
        self.sems = {e: es.enter_context(nc.semaphore("s_" + e)) for e in self.cnt}
        self.dma = es.enter_context(nc.semaphore("s_dma"))
        self.dma_cnt = 0
        self.dmag = es.enter_context(nc.semaphore("s_dmag"))
        self.dmag_cnt = 0

    def emit(self, eng, fn, waits=(), inc=True):
        waits = tuple((we, wv) for (we, wv) in waits if wv)
        if inc:
            self.cnt[eng] += 1
        self.items.append((eng, fn, waits, "c" if inc else None))
        return self.cnt[eng]

    def dma_emit(self, eng, fn, waits=()):
        waits = tuple((we, wv) for (we, wv) in waits if wv)
        if eng == "gp":
            self.dmag_cnt += 16
            self.items.append((eng, fn, waits, "dg"))
            return self.dmag_cnt
        self.dma_cnt += 16
        self.items.append((eng, fn, waits, "d"))
        return self.dma_cnt

    def now(self, eng):
        return self.cnt[eng]

    def mark(self, name):
        self.marks[name] = {e: self.cnt[e] for e in self.cnt}

    def barrier(self):
        return dict(te=("te", self.cnt["te"]), dve=("dve", self.cnt["dve"]),
                    act=("act", self.cnt["act"]), gp=("gp", self.cnt["gp"]),
                    dma=("dma", self.dma_cnt), dmag=("dmag", self.dmag_cnt))

    def bwaits(self, b, *names):
        return [b[n] for n in names]

    def replay(self, eng_name, engine):
        for e, fn, waits, inc in self.items:
            if e != eng_name:
                continue
            for (we, wv) in waits:
                sem = self.dma if we == "dma" else (self.dmag if we == "dmag" else self.sems[we])
                engine.wait_ge(sem, wv)
            inst = fn(engine)
            if inc == "d":
                inst.then_inc(self.dma, 16)
            elif inc == "dg":
                inst.then_inc(self.dmag, 16)
            elif inc == "c":
                inst.then_inc(self.sems[e], 1)


def build(debug=False):
    nc = bass.Bass("TRN2", target_bir_lowering=False, detect_race_conditions=False)
    dt_ = nc.dram_tensor
    io = {}
    io["x_scan_tm"] = dt_("x_scan_tm", [T, D], F32, kind="ExternalInput").ap()
    io["x_scan_cm"] = dt_("x_scan_cm", [D, T], F32, kind="ExternalInput").ap()
    io["x_own_tm"] = dt_("x_own_tm", [TOWN, D], F32, kind="ExternalInput").ap()
    io["x_own_cm"] = dt_("x_own_cm", [D, TOWN], F32, kind="ExternalInput").ap()
    io["wqkv_t"] = dt_("wqkv_t", [16, 128, 16, 128], F32, kind="ExternalInput").ap()
    io["wz_t"] = dt_("wz_t", [2, 16, 128, 512], F32, kind="ExternalInput").ap()
    io["wb_t"] = dt_("wb_t", [16, 128, 16], F32, kind="ExternalInput").ap()
    io["wa_t"] = dt_("wa_t", [16, 128, 16], F32, kind="ExternalInput").ap()
    io["wo_t"] = dt_("wo_t", [4, 8, 128, 512], F32, kind="ExternalInput").ap()
    io["convw"] = dt_("convw", [16, 128, 4], F32, kind="ExternalInput").ap()
    io["vecs"] = dt_("vecs", [64, 3], F32, kind="ExternalInput").ap()
    io["sel2"] = dt_("sel2", [2, 128], BF16, kind="ExternalInput").ap()
    io["gub"] = dt_("gub", [2, 11, 8, 128, 512], U8, kind="ExternalInput").ap()
    io["gus"] = dt_("gus", [2, 2, 11, 8, 512], F16, kind="ExternalInput").ap()
    io["dnb"] = dt_("dnb", [4, 22, 128, 512], U8, kind="ExternalInput").ap()
    io["dns"] = dt_("dns", [4, 2, 22, 512], F16, kind="ExternalInput").ap()
    io["out"] = dt_("out", [TOWN, D], F32, kind="ExternalOutput").ap()
    if debug:
        io["xn_dbg"] = dt_("xn_dbg", [128, 16, TOWN], BF16, kind="ExternalOutput").ap()
        io["yn_dbg"] = dt_("yn_dbg", [128, 8, TOWN], BF16, kind="ExternalOutput").ap()
        io["attn_dbg"] = dt_("attn_dbg", [TOWN, D], F32, kind="ExternalOutput").ap()
        io["h_dbg"] = dt_("h_dbg", [128, HT, TOWN], BF16, kind="ExternalOutput").ap()
        io["wdq_dbg"] = dt_("wdq_dbg", [128, 8, 2, 512], BF16, kind="ExternalOutput").ap()
        io["gsil_dbg"] = dt_("gsil_dbg", [128, 4, TOWN], F32, kind="ExternalOutput").ap()
        io["yt_dbg"] = dt_("yt_dbg", [128, 8, 2048], BF16, kind="ExternalOutput").ap()
    attn_sc = dt_("attn_sc", [TOWN, D], F32, kind="Internal").ap()

    es = ExitStack()
    sb = lambda n, s, d: es.enter_context(nc.sbuf_tensor(n, s, d))
    ps = lambda n, s, d: es.enter_context(nc.psum_tensor(n, s, d))

    with es:
        P = Plan(nc, es)
        global _LAST_PLAN
        _LAST_PLAN = P
        id128 = sb("id128", [128, 128], BF16)
        id64s = sb("id64s", [128, 64], BF16)
        idf32 = sb("idf32", [128, 128], F32)
        ones128 = sb("ones128", [128, 128], BF16)
        ones_f = sb("ones_f", [128, 128], F32)
        ones_qb = sb("ones_qb", [128, 64], BF16)
        ones2f = sb("ones2f", [2, 512], BF16)
        maskL = sb("maskL", [128, 128], F32)
        tricum = sb("tricum", [128, 128], F32)
        rowsel = sb("rowsel", [128, 128], F32)
        vecs = sb("vecs_s", [64, 3], F32)
        nalog = sb("nalog", [16, 1], F32)
        nwcol = sb("nwcol", [128, 1], F32)
        cst = sb("cst", [128, 4], F32)
        dtrow = sb("dtrow", [1, 16], F32)
        P.emit("dve", lambda v: v.memset(cst[:, 0:1], LN_EPS))
        P.emit("dve", lambda v: v.memset(cst[:, 1:2], EPS_A))
        P.emit("dve", lambda v: v.memset(cst[:, 2:3], 1.0))
        P.emit("dve", lambda v: v.memset(cst[:, 3:4], -1.0))

        obf = sb("obuf", [128, NV // 2, NCH // 2, C], BF16)

        pb = [ps(f"pb{i}", [128, 512], F32) for i in range(8)]
        pbh = [b.ap().bitcast(BF16) for b in pb]

        LD_VEC = P.dma_emit("sy", lambda sy: sy.dma_start(out=vecs[:], in_=io["vecs"][:]))
        ctx_iota = nc.sbuf_tensor("iota_r", [128, 128], I32)
        iota_r = ctx_iota.__enter__()
        ctx_iotac = nc.sbuf_tensor("iota_c", [128, 128], I32)
        iota_c = ctx_iotac.__enter__()
        ctx_low = nc.sbuf_tensor("low01", [128, 128], F32)
        low01 = ctx_low.__enter__()
        P.emit("gp", lambda gp: gp.iota(iota_r[:], pattern=[[0, 128]], channel_multiplier=1))
        P.emit("gp", lambda gp: gp.iota(iota_c[:], pattern=[[1, 128]], channel_multiplier=0))
        GPC = P.now("gp")
        P.emit("dve", lambda v: v.tensor_tensor(out=id128[:], in0=iota_r[:], in1=iota_c[:], op=ALU.is_equal),
               waits=[("gp", GPC)])
        P.emit("dve", lambda v: v.tensor_tensor(out=idf32[:], in0=iota_r[:], in1=iota_c[:], op=ALU.is_equal))
        P.emit("dve", lambda v: v.tensor_tensor(out=low01[:], in0=iota_r[:], in1=iota_c[:], op=ALU.is_gt))
        P.emit("dve", lambda v: v.tensor_tensor(out=tricum[:], in0=iota_r[:], in1=iota_c[:], op=ALU.is_le))
        P.emit("dve", lambda v: v.tensor_scalar(out=maskL[:], in0=low01[:], scalar1=1.0, scalar2=-NEG,
                                                op0=ALU.subtract, op1=ALU.mult),
               waits=[("dve", P.now("dve"))])
        P.emit("dve", lambda v: v.tensor_scalar(out=rowsel[:], in0=iota_r[:], scalar1=127, scalar2=None,
                                                op0=ALU.is_equal))
        P.emit("dve", lambda v: v.memset(ones128[:], 1.0))
        P.emit("dve", lambda v: v.memset(ones_f[:], 1.0))
        P.emit("dve", lambda v: v.memset(ones_qb[:], 1.0))
        LD_SEL = P.dma_emit("sy", lambda sy: sy.dma_start(out=ones2f[:, 0:128],
                                                          in_=io["sel2"][:]))
        P.emit("dve", lambda v: v.tensor_copy(out=id64s[0:64, :], in_=id128[0:64, 0:64]),
               waits=[("dve", P.now("dve"))])
        P.emit("dve", lambda v: v.tensor_copy(out=id64s[64:128, :], in_=id128[0:64, 0:64]),
               waits=[("dve", P.now("dve"))])
        ctx_low.__exit__(None, None, None)
        ctx_iotac.__exit__(None, None, None)
        ctx_iota.__exit__(None, None, None)
        P.emit("act", lambda s: s.activation(out=nalog[:], in_=vecs[0:16, 1:2], func=AF.Exp),
               waits=[("dma", LD_VEC)])
        ACT_AL = P.now("act")
        P.emit("dve", lambda v: v.tensor_scalar(out=nalog[:], in0=nalog[:], scalar1=-1.0, scalar2=None,
                                                op0=ALU.mult), waits=[("act", ACT_AL), ("dve", P.now("dve"))])
        P.emit("dve", lambda v: v.tensor_copy(out=nwcol[0:64, :], in_=vecs[0:64, 2:3]),
               waits=[("dve", P.now("dve"))])
        P.emit("te", lambda te: te.transpose(
            out=pb[6][0:1, 0:16], in_=vecs[0:16, 0:1], identity=idf32[0:16, 0:16]),
            waits=[("dma", LD_VEC), ("dve", P.now("dve"))])
        TE_DT = P.now("te")
        P.emit("dve", lambda v: v.tensor_copy(out=dtrow[:], in_=pb[6][0:1, 0:16]),
               waits=[("te", TE_DT)])
        P.emit("dve", lambda v: v.tensor_copy(out=nwcol[64:128, :], in_=vecs[0:64, 2:3]),
               waits=[("dve", P.now("dve"))])
        B0 = P.barrier()
        P.mark("B0")

        ostat = sb("ostat", [128, 8, 2], F32)
        sstat = sb("sstat", [128, 16, 2], F32)
        bnt = sb("bnt", [128, 2, 4, 6], F32)
        with nc.sbuf_tensor("xtile", [128, 2, D], F32) as xtile:
            lastd = [0, 0]
            for (xin, nt, stat) in ((io["x_scan_tm"], 16, sstat), (io["x_own_tm"], 8, ostat)):
                for i in range(nt):
                    par = i % 2
                    dm = P.dma_emit("sy", lambda sy, i=i, par=par, xin=xin: sy.dma_start(
                        out=xtile[:, par, :], in_=xin[i * 128:(i + 1) * 128, :]),
                        waits=[("dve", lastd[par])])
                    for c in range(4):
                        P.emit("dve", lambda v, par=par, c=c: v.bn_stats(
                            out=bnt[:, par, c, :], in_=xtile[:, par, c * 512:(c + 1) * 512]),
                            waits=[("dma", dm)] if c == 0 else [("dve", P.now("dve"))])
                    P.emit("dve", lambda v, i=i, par=par, stat=stat: v.bn_aggr(
                        out=stat[:, i, :], in_=bnt[:, par, :, :].rearrange("p a b -> p (a b)")),
                        waits=[("dve", P.now("dve"))])
                    lastd[par] = P.now("dve")
        STATS = P.now("dve")
        rstd_o = sb("rstd_o", [128, 8], F32)
        mr_o = sb("mr_o", [128, 8], F32)
        rstd_s = sb("rstd_s", [128, 16], F32)
        mr_s = sb("mr_s", [128, 16], F32)
        sq_t = sb("sq_t", [128, 16], F32)
        for (stat, nt, rstd, mr) in ((ostat, 8, rstd_o, mr_o), (sstat, 16, rstd_s, mr_s)):
            P.emit("act", lambda s, stat=stat, nt=nt: s.activation(
                out=sq_t[:, 0:nt], in_=stat.ap()[:, :, 1], func=AF.Sqrt, bias=cst[:, 0:1]),
                waits=[("dve", STATS), ("dve", P.now("dve"))])
            AC = P.now("act")
            P.emit("dve", lambda v, nt=nt, rstd=rstd: v.reciprocal(
                out=rstd[:], in_=sq_t[:, 0:nt]), waits=[("act", AC)])
            P.emit("dve", lambda v, stat=stat, rstd=rstd, mr=mr: v.tensor_tensor(
                out=mr[:], in0=stat.ap()[:, :, 0], in1=rstd[:], op=ALU.mult),
                waits=[("dve", P.now("dve"))])
        NORMS = P.now("dve")
        def emit_xn(xcm_d, nt512, rstd, mr, xn_dst, tag, prior=()):
            prior = list(prior)
            tc_marks = []
            with nc.sbuf_tensor("rowscr" + tag, [1, 2, 512], F32) as rowscr, \
                 nc.sbuf_tensor("xcmt" + tag, [128, 2, 512], F32) as xcmt, \
                 nc.sbuf_tensor("rbt" + tag, [128, 2, 512], F32) as rbt, \
                 nc.sbuf_tensor("mbt" + tag, [128, 2, 512], F32) as mbt, \
                 nc.sbuf_tensor("xtmp" + tag, [128, 2, 512], F32) as xtmp:
                lastd = [0, 0]
                rowscr_te = 0
                for tc in range(nt512):
                    par = tc % 2
                    for ii in range(4):
                        i = tc * 4 + ii
                        for half, src_c in ((0, rstd), (1, mr)):
                            P.emit("te", lambda te, i=i, src_c=src_c: te.matmul(
                                out=pb[6][0:1, 0:128], lhsT=src_c[:, i:i + 1], rhs=idf32[:],
                                start=True, stop=True),
                                waits=[("dve", max(NORMS, P.now("dve"))), ("te", rowscr_te)]
                                      + (prior if (tc == 0 and ii == 0 and half == 0) else []))
                            TE_R = P.now("te")
                            P.emit("dve", lambda v, ii=ii, half=half: v.tensor_copy(
                                out=rowscr[:, half, ii * 128:(ii + 1) * 128], in_=pb[6][0:1, 0:128]),
                                waits=[("te", TE_R)])
                    DV_ROW = P.now("dve")
                    P.emit("te", lambda te: te.matmul(
                        out=pb[6][:, 0:512], lhsT=ones_f[0:1, :], rhs=rowscr[:, 0, :],
                        start=True, stop=True), waits=[("dve", max(DV_ROW, lastd[par]))])
                    TE_B1 = P.now("te")
                    P.emit("dve", lambda v, par=par: v.tensor_copy(out=rbt[:, par, :], in_=pb[6][:, 0:512]),
                           waits=[("te", TE_B1)])
                    DV_B1 = P.now("dve")
                    P.emit("te", lambda te: te.matmul(
                        out=pb[6][:, 0:512], lhsT=ones_f[0:1, :], rhs=rowscr[:, 1, :],
                        start=True, stop=True), waits=[("dve", DV_B1)])
                    TE_B2 = P.now("te")
                    rowscr_te = TE_B2
                    P.emit("dve", lambda v, par=par: v.tensor_copy(out=mbt[:, par, :], in_=pb[6][:, 0:512]),
                           waits=[("te", TE_B2)])
                    for dtile in range(16):
                        dm = P.dma_emit("sy", lambda sy, dtile=dtile, tc=tc, par=par, xcm_d=xcm_d: sy.dma_start(
                            out=xcmt[:, par, :],
                            in_=xcm_d[dtile * 128:(dtile + 1) * 128, tc * 512:(tc + 1) * 512]),
                            waits=[("dve", lastd[par])] if lastd[par] else prior)
                        P.emit("dve", lambda v, par=par: v.tensor_tensor(
                            out=xtmp[:, par, :], in0=xcmt[:, par, :], in1=rbt[:, par, :], op=ALU.mult),
                            waits=[("dma", dm)])
                        P.emit("dve", lambda v, dtile=dtile, tc=tc, par=par, xn_dst=xn_dst: v.tensor_tensor(
                            out=xn_dst[:, dtile, tc * 512:(tc + 1) * 512],
                            in0=xtmp[:, par, :], in1=mbt[:, par, :], op=ALU.subtract),
                            waits=[("dve", P.now("dve"))])
                        lastd[par] = P.now("dve")
                    tc_marks.append(P.now("dve"))
            return tc_marks


        es2 = ExitStack()
        sb2 = lambda n, s, d: es2.enter_context(nc.sbuf_tensor(n, s, d))
        qf = sb2("qf", [128, 4, T], BF16)
        kf = sb2("kf", [128, 4, T], BF16)
        b_tm = sb2("b_tm", [128, NCH, NV], F32)
        g_tm = sb2("g_tm", [128, NCH, NV], F32)
        ktm_t = sb2("ktm_t", [128, NCH, KH, DK], BF16)
        vtm_t = sb2("vtm_t", [128, NCH, NV, DV], BF16)
        with nc.sbuf_tensor("xn_scan", [128, 16, T], BF16) as xn_scan:
            XN_TCM = emit_xn(io["x_scan_cm"], 4, rstd_s, mr_s, xn_scan, "s", prior=[("dve", NORMS)])
            XN_SCAN = P.now("dve")
            XN_TE = P.now("te")

            with nc.sbuf_tensor("wkt", [128, 2, 16, 128], BF16) as wkt, \
                 nc.sbuf_tensor("cvt", [128, 3 + T], F32) as cvt_, \
                 nc.sbuf_tensor("cva", [128, TH], F32) as cva_, \
                 nc.sbuf_tensor("cvb", [128, TH], F32) as cvb_, \
                 nc.sbuf_tensor("convw_s", [128, 16, 4], F32) as convw_s, \
                 nc.sbuf_tensor("sil", [128, TH], F32) as sil_, \
                 nc.sbuf_tensor("ktf", [128, 2, 128], F32) as ktf, \
                 nc.sbuf_tensor("sqv", [128, 2, 128], F32) as sqv, \
                 nc.sbuf_tensor("rsq", [128, 2, 2], F32) as rsq, \
                 nc.sbuf_tensor("ktb", [128, 2, 128], BF16) as ktb:
                cvt = cvt_.ap().rearrange("p (a b) -> p a b", a=1)
                cva = cva_.ap().rearrange("p (a b) -> p a b", a=1)
                cvb = cvb_.ap().rearrange("p (a b) -> p a b", a=1)
                sil = sil_.ap().rearrange("p (a b) -> p a b", a=1)
                vsil = sil_.ap().bitcast(BF16).rearrange("p (a b) -> p a b", a=2)
                cwdm = 0
                for i in range(16):
                    cwdm = P.dma_emit("sy", lambda sy, i=i: sy.dma_start(
                        out=convw_s[:, i, :], in_=io["convw"][i, :, :]),
                        waits=[("dve", XN_SCAN), ("te", XN_TE)] if i == 0 else [])
                last_w = [0, 0]
                last_cv = [0, 0]
                last_sil = [0, 0]
                norm_free_d = 0
                norm_free_a = 0
                norm_free_te = 0
                for ot in range(16):
                    par = ot % 2
                    wdm = P.dma_emit("gp", lambda gp, ot=ot, par=par: gp.dma_start(
                        out=wkt[:, par, :, :], in_=io["wqkv_t"][ot, :, :, :]),
                        waits=[("te", last_w[par])] if ot >= 2
                        else [("dve", XN_SCAN), ("te", XN_TE)])
                    for tch in range(4):
                        pbank = pb[par]
                        for kt in range(16):
                            P.emit("te", lambda te, par=par, tch=tch, kt=kt, pbank=pbank: te.matmul(
                                out=pbank[:, 0:512], lhsT=wkt[:, par, kt, :],
                                rhs=xn_scan[:, kt, tch * 512:(tch + 1) * 512],
                                start=(kt == 0), stop=(kt == 15)),
                                waits=([("dmag", wdm), ("dve", max(XN_TCM[tch], last_cv[par]))]
                                       if kt == 0 else []))
                        TE_P = P.now("te")
                        P.emit("dve", lambda v, par=par, tch=tch, pbank=pbank: v.tensor_copy(
                            out=cvt[:, 0, 3 + tch * 512: 3 + (tch + 1) * 512], in_=pbank[:, 0:512]),
                            waits=[("te", TE_P)])
                        last_cv[par] = P.now("dve")
                    last_w[par] = P.now("te")
                    P.emit("dve", lambda v: v.memset(cvt[:, 0, 0:3], 0.0),
                           waits=[("dve", last_cv[par]), ("dma", cwdm)])
                    for h2 in range(2):
                        P.emit("dve", lambda v, ot=ot, h2=h2: v.tensor_scalar(
                            out=cva[:, 0, :], in0=cvt[:, 0, 3 + h2 * TH:3 + h2 * TH + TH],
                            scalar1=convw_s[:, ot, 3:4], scalar2=None, op0=ALU.mult),
                            waits=[("dve", P.now("dve")),
                                   ("act", max(last_sil[0], last_sil[1]))])
                        bufs = (cva, cvb)
                        for idx, j in enumerate((2, 1, 0)):
                            src_b, dst_b = bufs[idx % 2], bufs[(idx + 1) % 2]
                            P.emit("dve", lambda v, ot=ot, h2=h2, j=j, src_b=src_b, dst_b=dst_b:
                                   v.scalar_tensor_tensor(
                                       out=dst_b[:, 0, :],
                                       in0=cvt[:, 0, j + h2 * TH:j + h2 * TH + TH],
                                       scalar=convw_s[:, ot, j:j + 1], in1=src_b[:, 0, :],
                                       op0=ALU.mult, op1=ALU.add),
                                   waits=[("dve", P.now("dve"))])
                        conv_out = bufs[1]
                        CONV_D = P.now("dve")
                        if ot < 8:
                            P.emit("act", lambda s, conv_out=conv_out: s.activation(
                                out=sil[:, 0, :], in_=conv_out[:, 0, :], func=AF.Silu),
                                waits=[("dve", CONV_D), ("te", norm_free_te)])
                            ACT_S = P.now("act")
                            last_sil[par] = ACT_S
                            dst_fm = qf if ot < 4 else kf
                            fmslot = ot % 4
                            for tb8 in range(8):
                                tb = h2 * 8 + tb8
                                tpar = tb % 2
                                P.emit("te", lambda te, tb8=tb8: te.transpose(
                                    out=pb[4][:, 0:128], in_=sil[:, 0, tb8 * 128:(tb8 + 1) * 128],
                                    identity=idf32[:]),
                                    waits=[("act", ACT_S), ("dve", norm_free_d)])
                                TE_T1 = P.now("te")
                                P.emit("dve", lambda v, tpar=tpar: v.tensor_copy(
                                    out=ktf[:, tpar, :], in_=pb[4][:, 0:128]), waits=[("te", TE_T1)])
                                DV_T1 = P.now("dve")
                                P.emit("act", lambda s, tpar=tpar: s.activation(
                                    out=sqv[:, tpar, :], in_=ktf[:, tpar, :], func=AF.Square),
                                    waits=[("dve", DV_T1), ("act", norm_free_a)])
                                ACT_Q = P.now("act")
                                P.emit("dve", lambda v, tpar=tpar: v.tensor_reduce(
                                    out=rsq[:, tpar, :].rearrange("p (a b) -> p a b", b=1),
                                    in_=sqv[:, tpar, :].rearrange("p (a b) -> p a b", a=2),
                                    axis=AX.X, op=ALU.add),
                                    waits=[("act", ACT_Q)])
                                DV_RED = P.now("dve")
                                P.emit("act", lambda s, tpar=tpar: s.activation(
                                    out=sqv[:, tpar, 0:2], in_=rsq[:, tpar, :], func=AF.Sqrt, bias=cst[:, 1:2]),
                                    waits=[("dve", DV_RED)])
                                ACT_R = P.now("act")
                                norm_free_a = ACT_R
                                P.emit("dve", lambda v, tpar=tpar: v.reciprocal(
                                    out=rsq[:, tpar, :], in_=sqv[:, tpar, 0:2]), waits=[("act", ACT_R)])
                                if ot < 4:
                                    P.emit("dve", lambda v, tpar=tpar: v.tensor_scalar(
                                        out=rsq[:, tpar, :], in0=rsq[:, tpar, :], scalar1=DK ** -0.5,
                                        scalar2=None, op0=ALU.mult), waits=[("dve", P.now("dve"))])
                                DV_R2 = P.now("dve")
                                P.emit("dve", lambda v, tpar=tpar: v.tensor_tensor(
                                    out=ktb[:, tpar, :].rearrange("p (a b) -> p a b", a=2),
                                    in0=ktf[:, tpar, :].rearrange("p (a b) -> p a b", a=2),
                                    in1=rsq[:, tpar, :].rearrange("p (a b) -> p a b", b=1)
                                        .broadcast_to((128, 2, 64)),
                                    op=ALU.mult),
                                    waits=[("dve", DV_R2)])
                                DV_N = P.now("dve")
                                if ot >= 4:
                                    ch = tb
                                    kh0 = (ot - 4) * 2
                                    P.emit("dve", lambda v, tpar=tpar, ch=ch, kh0=kh0: v.tensor_copy(
                                        out=ktm_t[:, ch, kh0, :], in_=ktb[:, tpar, 0:64]),
                                        waits=[("dve", DV_N)])
                                    P.emit("dve", lambda v, tpar=tpar, ch=ch, kh0=kh0: v.tensor_copy(
                                        out=ktm_t[:, ch, kh0 + 1, :], in_=ktb[:, tpar, 64:128]),
                                        waits=[("dve", P.now("dve"))])
                                DV_K = P.now("dve")
                                P.emit("te", lambda te, tpar=tpar: te.transpose(
                                    out=pbh[4][:, 256:384], in_=ktb[:, tpar, :], identity=id128[:]),
                                    waits=[("dve", DV_K)])
                                TE_T2 = P.now("te")
                                P.emit("dve", lambda v, dst_fm=dst_fm, fmslot=fmslot, tb=tb: v.tensor_copy(
                                    out=dst_fm[:, fmslot, tb * 128:(tb + 1) * 128], in_=pbh[4][:, 256:384]),
                                    waits=[("te", TE_T2)])
                                norm_free_d = P.now("dve")
                            norm_free_te = P.now("te")
                        else:
                            P.emit("act", lambda s, conv_out=conv_out: s.activation(
                                out=vsil[:, 0, :], in_=conv_out[:, 0, :], func=AF.Silu),
                                waits=[("dve", CONV_D), ("te", norm_free_te)])
                            ACT_V = P.now("act")
                            last_sil[par] = ACT_V
                            vt = ot - 8
                            for tb8 in range(8):
                                tb = h2 * 8 + tb8
                                P.emit("te", lambda te, tb8=tb8: te.transpose(
                                    out=pbh[4][:, 0:128], in_=vsil[:, 0, tb8 * 128:(tb8 + 1) * 128],
                                    identity=id128[:]),
                                    waits=[("act", ACT_V), ("dve", norm_free_d)])
                                TE_TV = P.now("te")
                                P.emit("dve", lambda v, vt=vt, tb=tb: v.tensor_copy(
                                    out=vtm_t[:, tb, 2 * vt:2 * vt + 2, :].rearrange("p a b -> p (a b)"),
                                    in_=pbh[4][:, 0:128]), waits=[("te", TE_TV)])
                                norm_free_d = P.now("dve")
                            norm_free_te = P.now("te")
            QKV_B = P.barrier()
            P.mark("QKV")

            with nc.sbuf_tensor("wba", [128, 16, 32], BF16) as wba, \
                 nc.sbuf_tensor("braw", [16, T], F32) as braw, \
                 nc.sbuf_tensor("graw", [16, T], F32) as graw, \
                 nc.sbuf_tensor("gtm1", [16, T], F32) as gtm1, \
                 nc.sbuf_tensor("gtm2", [16, T], F32) as gtm2:
                wdm_b = P.dma_emit("gp", lambda gp: gp.dma_start(
                    out=wba[:, :, 0:16], in_=io["wb_t"].rearrange("a p c -> p a c")),
                    waits=P.bwaits(QKV_B, "te", "act", "dve"))
                wdm_a = P.dma_emit("gp", lambda gp: gp.dma_start(
                    out=wba[:, :, 16:32], in_=io["wa_t"].rearrange("a p c -> p a c")))
                for cofs, dst in ((0, braw), (16, graw)):
                    for tch in range(4):
                        for kt in range(16):
                            P.emit("te", lambda te, cofs=cofs, tch=tch, kt=kt: te.matmul(
                                out=pb[5][0:16, 0:512], lhsT=wba[:, kt, cofs:cofs + 16],
                                rhs=xn_scan[:, kt, tch * 512:(tch + 1) * 512],
                                start=(kt == 0), stop=(kt == 15 and cofs == 0)),
                                waits=[("dmag", wdm_a), ("dve", P.now("dve"))] if kt == 0 else [])
                        if cofs == 16:
                            for cc in range(4):
                                P.emit("te", lambda te, cc=cc: te.matmul(
                                    out=pb[5][0:16, cc * 128:(cc + 1) * 128], lhsT=dtrow[:],
                                    rhs=ones_f[0:1, :], start=False, stop=(cc == 3)))
                        TE_BA = P.now("te")
                        P.emit("dve", lambda v, dst=dst, tch=tch: v.tensor_copy(
                            out=dst[:, tch * 512:(tch + 1) * 512], in_=pb[5][0:16, 0:512]),
                            waits=[("te", TE_BA)])
                BA_D = P.now("dve")
                P.emit("act", lambda s: s.activation(out=gtm1[:], in_=braw[:], func=AF.Sigmoid),
                       waits=[("dve", BA_D)])
                SIG_A = P.now("act")
                P.emit("act", lambda s: s.activation(out=gtm2[:], in_=graw[:], func=AF.Abs),
                       waits=[("dve", BA_D), ("act", SIG_A)])
                D_AB = P.now("dve")
                A_ABS = P.now("act")
                P.emit("act", lambda s: s.activation(out=braw[:], in_=gtm2[:], func=AF.Exp,
                                                     scale=cst[0:16, 3:4]),
                       waits=[("act", A_ABS)])
                A_EX = P.now("act")
                P.emit("act", lambda s: s.activation(out=gtm2[:], in_=braw[:], func=AF.Ln,
                                                     bias=cst[0:16, 2:3]),
                       waits=[("act", A_EX)])
                A_LN = P.now("act")
                P.emit("act", lambda s: s.activation(out=braw[:], in_=graw[:], func=AF.Relu),
                       waits=[("act", A_LN)])
                A_RELU = P.now("act")
                P.emit("dve", lambda v: v.tensor_tensor(
                    out=graw[:], in0=braw[:], in1=gtm2[:], op=ALU.add),
                    waits=[("act", A_RELU), ("dve", P.now("dve"))])
                P.emit("dve", lambda v: v.tensor_scalar(
                    out=graw[:], in0=graw[:], scalar1=nalog[0:16, 0:1],
                    scalar2=None, op0=ALU.mult), waits=[("dve", P.now("dve"))])
                BG_D = P.now("dve")
                BG_A = P.now("act")
                lastbg = 0
                for ch in range(NCH):
                    P.emit("te", lambda te, ch=ch: te.transpose(
                        out=pb[4][:, 0:16], in_=gtm1[:, ch * 128:(ch + 1) * 128],
                        identity=idf32[0:16, 0:16]),
                        waits=[("dve", max(BG_D, lastbg)), ("act", BG_A)])
                    TE_TB = P.now("te")
                    P.emit("dve", lambda v, ch=ch: v.tensor_copy(out=b_tm[:, ch, :], in_=pb[4][:, 0:16]),
                           waits=[("te", TE_TB)])
                    DV_TB = P.now("dve")
                    P.emit("te", lambda te, ch=ch: te.transpose(
                        out=pb[4][:, 0:16], in_=graw[:, ch * 128:(ch + 1) * 128],
                        identity=idf32[0:16, 0:16]), waits=[("dve", DV_TB)])
                    TE_TG = P.now("te")
                    P.emit("dve", lambda v, ch=ch: v.tensor_copy(out=g_tm[:, ch, :], in_=pb[4][:, 0:16]),
                           waits=[("te", TE_TG)])
                    lastbg = P.now("dve")
            BG_B = P.barrier()
            P.mark("BG")

            BG2_B = P.barrier()
        es3 = ExitStack()
        emit_delta_phase(nc, P, es3, pb, pbh, dict(
            ktm=ktm_t, vtm=vtm_t, kfm=kf, qfm=qf, g_tm=g_tm, b_tm=b_tm,
            id128=id128, id64s=id64s, idf32=idf32, ones128=ones128, ones_f=ones_f,
            ones_qb=ones_qb, maskL=maskL, tricum=tricum, rowsel=rowsel, obuf=obf),
            BG2_B)
        DELTA_B = P.barrier()
        P.mark("DELTA")
        es3.close()
        es2.close()
        h_cm = sb("h_cm", [128, HT, TOWN], BF16)
        ctx_xn_own = nc.sbuf_tensor("xn_own", [128, 16, TOWN], BF16)
        xn_own = ctx_xn_own.__enter__()
        ctx_yn_cm = nc.sbuf_tensor("yn_cm", [128, 8, TOWN], BF16)
        yn_cm = ctx_yn_cm.__enter__()
        emit_xn(io["x_own_cm"], 2, rstd_o, mr_o, xn_own, "o",
                prior=P.bwaits(DELTA_B, "te", "act", "dve"))
        XN_OWN = P.now("dve")
        XNO_TE = P.now("te")
        if debug:
            P.dma_emit("sy", lambda sy: sy.dma_start(out=io["xn_dbg"][:], in_=xn_own[:]),
                       waits=[("dve", XN_OWN)])

        with nc.sbuf_tensor("ytm", [128, 2, 1024], F32) as ytm, \
             nc.sbuf_tensor("yss", [128, 2, 16], F32) as yss, \
             nc.sbuf_tensor("yrs", [128, 2, 16], F32) as yrs, \
             nc.sbuf_tensor("zsil", [128, 2, 1024], F32) as zsil, \
             nc.sbuf_tensor("wz", [128, 2, 16, 512], BF16) as wz, \
             nc.sbuf_tensor("ynt", [128, 1, 1024], BF16) as ynt:
            wzdm = [0, 0]
            for zc in range(2):
                wzdm[zc] = P.dma_emit("gp", lambda gp, zc=zc: gp.dma_start(
                    out=wz[:, zc, :, :], in_=io["wz_t"][zc].rearrange("a p c -> p a c")),
                    waits=[("te", XNO_TE), ("dve", XN_OWN), ("act", DELTA_B["act"][1])]
                    if zc == 0 else [])
            lasty = [0, 0]
            last_yn_p = [0, 0]
            red_done = [0, 0]
            last_ynt_te = 0
            for tt in range(8):
                par = tt % 2
                ch = 8 + tt
                for vp in range(8):
                    P.emit("te", lambda te, vp=vp, tt=tt: te.transpose(
                        out=pbh[4][:, 0:128], in_=obf[:, vp, tt, :], identity=id128[:]),
                        waits=([("dve", P.now("dve"))] + P.bwaits(DELTA_B, "act", "dve")) if vp == 0
                        else [("dve", P.now("dve"))])
                    TE_OT = P.now("te")
                    P.emit("dve", lambda v, vp=vp, par=par: v.tensor_copy(
                        out=ytm[:, par, vp * 128:(vp + 1) * 128], in_=pbh[4][:, 0:128]),
                        waits=[("te", TE_OT)])
                Y_D = P.now("dve")
                for zc in range(2):
                    for kt in range(16):
                        P.emit("te", lambda te, tt=tt, zc=zc, kt=kt: te.matmul(
                            out=pb[5][:, 0:512], lhsT=xn_own[:, kt, tt * 128:(tt + 1) * 128],
                            rhs=wz[:, zc, kt, :], start=(kt == 0), stop=(kt == 15)),
                            waits=[("dmag", wzdm[1]), ("act", P.now("act")), ("dve", XN_OWN)]
                            if kt == 0 else [])
                    TE_Z = P.now("te")
                    P.emit("act", lambda s, par=par, zc=zc: s.activation(
                        out=zsil[:, par, zc * 512:(zc + 1) * 512], in_=pb[5][:, 0:512], func=AF.Silu),
                        waits=[("te", TE_Z), ("dve", last_yn_p[par])])
                Z_A = P.now("act")
                for hq in range(2):
                    P.emit("act", lambda s, par=par, hq=hq: s.activation(
                        out=pb[6 + hq][:, 0:512], in_=ytm[:, par, hq * 512:(hq + 1) * 512],
                        func=AF.Square),
                        waits=[("dve", max(Y_D, red_done[hq]))])
                A_SQ = P.now("act")
                for hq in range(2):
                    P.emit("dve", lambda v, par=par, hq=hq: v.tensor_reduce(
                        out=yrs[:, par, hq * 8:(hq + 1) * 8].rearrange("p (a b) -> p a b", b=1),
                        in_=pb[6 + hq][:, 0:512].rearrange("p (a b) -> p a b", a=8),
                        axis=AX.X, op=ALU.add),
                        waits=[("act", A_SQ)] if hq == 0 else [("dve", P.now("dve"))])
                    red_done[hq] = P.now("dve")
                P.emit("dve", lambda v, par=par: v.tensor_scalar(
                    out=yrs[:, par, :], in0=yrs[:, par, :], scalar1=1.0 / DV, scalar2=None,
                    op0=ALU.mult), waits=[("dve", P.now("dve"))])
                DV_RS = P.now("dve")
                P.emit("act", lambda s, par=par: s.activation(
                    out=yss[:, par, :], in_=yrs[:, par, :], func=AF.Sqrt, bias=cst[:, 1:2]),
                    waits=[("dve", DV_RS)])
                A_RS = P.now("act")
                P.emit("dve", lambda v, par=par: v.reciprocal(
                    out=yrs[:, par, :], in_=yss[:, par, :]), waits=[("act", A_RS)])
                DV_RC = P.now("dve")
                P.emit("dve", lambda v, par=par: v.tensor_tensor(
                    out=ytm[:, par, :].rearrange("p (a b) -> p a b", a=16),
                    in0=ytm[:, par, :].rearrange("p (a b) -> p a b", a=16),
                    in1=yrs[:, par, :].rearrange("p (a b) -> p a b", b=1).broadcast_to((128, 16, 64)),
                    op=ALU.mult),
                    waits=[("dve", DV_RC)])
                P.emit("dve", lambda v, par=par: v.tensor_tensor(
                    out=ynt[:, 0, :], in0=ytm[:, par, :], in1=zsil[:, par, :], op=ALU.mult),
                    waits=[("dve", P.now("dve")), ("act", Z_A), ("te", last_ynt_te)])
                YN_D = P.now("dve")
                last_yn_p[par] = YN_D
                for vt in range(8):
                    P.emit("te", lambda te, par=par, vt=vt: te.transpose(
                        out=pbh[4][:, 0:128], in_=ynt[:, 0, vt * 128:(vt + 1) * 128],
                        identity=id128[:]), waits=[("dve", max(YN_D, P.now("dve")))])
                    TE_YT = P.now("te")
                    P.emit("dve", lambda v, tt=tt, vt=vt: v.tensor_scalar(
                        out=yn_cm[:, vt, tt * 128:(tt + 1) * 128], in0=pbh[4][:, 0:128],
                        scalar1=nwcol[:, 0:1], scalar2=None, op0=ALU.mult),
                        waits=[("te", TE_YT)])
                last_ynt_te = P.now("te")
                lasty[par] = P.now("dve")
        GATE_B = P.barrier()
        P.mark("GATE")
        if debug:
            P.dma_emit("sy", lambda sy: sy.dma_start(out=io["yn_dbg"][:], in_=yn_cm[:]),
                       waits=[GATE_B["dve"]])

        with nc.sbuf_tensor("wo", [128, 2, 512], BF16) as wo, \
             nc.sbuf_tensor("aev", [128, 2, 512], F32) as aev:
            lastw = [0, 0]
            lasta = [0, 0]
            ev_prev = 0
            for dc in range(4):
                for kt in range(8):
                    par = kt % 2
                    wdm = P.dma_emit("gp", lambda gp, dc=dc, kt=kt, par=par: gp.dma_start(
                        out=wo[:, par, :], in_=io["wo_t"][dc, kt, :, :]),
                        waits=[("te", lastw[par])] if (dc, kt) >= (0, 2)
                        else P.bwaits(GATE_B, "te", "act", "dve"))
                    for tt in range(8):
                        w = []
                        if tt == 0:
                            w.append(("dmag", wdm))
                        if kt == 0:
                            w.append(("dve", max(ev_prev, GATE_B["dve"][1])))
                            if tt == 0 and dc == 0:
                                w.append(("act", GATE_B["act"][1]))
                        P.emit("te", lambda te, kt=kt, tt=tt, par=par: te.matmul(
                            out=pb[tt][:, 0:512], lhsT=yn_cm[:, kt, tt * 128:(tt + 1) * 128],
                            rhs=wo[:, par, :], start=(kt == 0), stop=(kt == 7)),
                            waits=w)
                    lastw[par] = P.now("te")
                TE_WO = P.now("te")
                for tt in range(8):
                    epar = tt % 2
                    P.emit("dve", lambda v, tt=tt, epar=epar: v.tensor_copy(
                        out=aev[:, epar, :], in_=pb[tt][:, 0:512]),
                        waits=[("te", TE_WO), ("dma", lasta[epar])])
                    AD = P.now("dve")
                    admc = P.dma_emit("sy", lambda sy, tt=tt, dc=dc, epar=epar: sy.dma_start(
                        out=attn_sc[tt * 128:(tt + 1) * 128, dc * 512:(dc + 1) * 512],
                        in_=aev[:, epar, :]), waits=[("dve", AD)])
                    lasta[epar] = admc
                ev_prev = P.now("dve")
        WOUT_B = P.barrier()
        P.mark("WOUT")
        if debug:
            P.dma_emit("sy", lambda sy: sy.dma_start(out=io["attn_dbg"][:], in_=attn_sc[:]),
                       waits=[WOUT_B["dma"]])
        ctx_yn_cm.__exit__(None, None, None)

        with nc.sbuf_tensor("bt", [128, 3, 512], U8) as bt, \
             nc.sbuf_tensor("sstrip", [2, 1, 8, 512], F16) as sstrip, \
             nc.sbuf_tensor("sstripb", [2, 1, 8, 512], BF16) as sstripb, \
             nc.sbuf_tensor("i1", [128, 1, 128], I32) as i1, \
             nc.sbuf_tensor("i2", [128, 1, 128], I32) as i2, \
             nc.sbuf_tensor("wdq", [128, 2, 8, 2, 512], BF16) as wdq, \
             nc.sbuf_tensor("gsil", [128, 4, TOWN], F32) as gsil:
            bti = bt.ap().bitcast(I32)

            bt_free = [0, 0, 0]
            bc_free = [0, 0]
            gdq_n = [0]

            def dequant_granule(srcap, scrow_ap, par, dqslot, pp, cast_cnt, extra_w=()):
                dm = P.dma_emit("gp", lambda gp: gp.dma_start(out=bt[:, par, :], in_=srcap),
                                waits=list(extra_w) + [("dve", bt_free[par])])
                bcb = gdq_n[0] % 2
                gdq_n[0] += 1
                pbc = pb[6] if bcb == 0 else pb[4]
                P.emit("te", lambda te, pbc=pbc: te.matmul(
                    out=pbc[:, 0:512], lhsT=ones2f[:, 0:128], rhs=scrow_ap,
                    start=True, stop=True), waits=[("dve", max(bc_free[bcb], cast_cnt))])
                TE_SC = P.now("te")
                P.emit("dve", lambda v: v.tensor_scalar(
                    out=i1[:, 0, :], in0=bti[:, par, :], scalar1=0x08080808, scalar2=4,
                    op0=ALU.bitwise_and, op1=ALU.logical_shift_left), waits=[("dmag", dm)])
                P.emit("dve", lambda v: v.tensor_scalar(
                    out=i2[:, 0, :], in0=bti[:, par, :], scalar1=0x07070707, scalar2=2,
                    op0=ALU.bitwise_and, op1=ALU.logical_shift_left))
                P.emit("dve", lambda v: v.tensor_tensor(
                    out=i1[:, 0, :], in0=i1[:, 0, :], in1=i2[:, 0, :], op=ALU.bitwise_or),
                    waits=[("dve", P.now("dve"))])
                P.emit("dve", lambda v, pbc=pbc: v.tensor_tensor(
                    out=wdq[:, pp, dqslot, 0, :], in0=i1.ap().bitcast(FP8)[:, 0, :],
                    in1=pbc[:, 0:512], op=ALU.mult),
                    waits=[("te", TE_SC), ("dve", P.now("dve"))])
                P.emit("dve", lambda v: v.tensor_scalar(
                    out=i2[:, 0, :], in0=bti[:, par, :], scalar1=0x70707070, scalar2=2,
                    op0=ALU.bitwise_and, op1=ALU.logical_shift_right))
                P.emit("dve", lambda v: v.tensor_scalar(
                    out=i1[:, 0, :], in0=bti[:, par, :], scalar1=0x7F7F7F7F, scalar2=None,
                    op0=ALU.bitwise_and), waits=[("dve", P.now("dve"))])
                P.emit("dve", lambda v: v.tensor_tensor(
                    out=i1[:, 0, :], in0=bti[:, par, :], in1=i1[:, 0, :], op=ALU.bitwise_xor),
                    waits=[("dve", P.now("dve"))])
                bt_free[par] = P.now("dve")
                P.emit("dve", lambda v: v.tensor_tensor(
                    out=i2[:, 0, :], in0=i1[:, 0, :], in1=i2[:, 0, :], op=ALU.bitwise_or),
                    waits=[("dve", P.now("dve"))])
                P.emit("dve", lambda v, pbc=pbc: v.tensor_tensor(
                    out=wdq[:, pp, dqslot, 1, :], in0=i2.ap().bitcast(FP8)[:, 0, :],
                    in1=pbc[:, 0:512], op=ALU.mult), waits=[("dve", P.now("dve"))])
                bc_free[bcb] = P.now("dve")
                return P.now("dve")

            NOFFN = bool(os.environ.get("DEVK_NOFFN"))
            sc_cast = 0
            bc_te = 0
            te_mm = [WOUT_B["te"][1], WOUT_B["te"][1]]
            gsil_free = [0] * 8
            bankev = [("dve", WOUT_B["dve"][1])] * 4
            for it in range(0 if NOFFN else 22):
                h512, part = it // 2, it % 2
                pp = it % 2
                spar = part
                sdm = P.dma_emit("sy", lambda sy, part=part, h512=h512: sy.dma_start(
                    out=sstrip[:, 0, :, :], in_=io["gus"][part, :, h512, :, :]),
                    waits=[("te", te_mm[pp]), ("dve", sc_cast)]
                          + (P.bwaits(WOUT_B, "te", "act", "dve") if it == 0 else []))
                P.emit("dve", lambda v: v.tensor_copy(
                    out=sstripb[:, 0, :, :], in_=sstrip[:, 0, :, :]),
                    waits=[("dma", sdm), ("te", bc_te)])
                sc_cast = P.now("dve")
                for ib in range(8):
                    xw = [("te", te_mm[pp])] if ib == 0 else []
                    if it == 0 and ib < 2:
                        xw = xw + P.bwaits(WOUT_B, "te", "act", "dve")
                    if it == 0 and ib < 2:
                        bc_free[ib % 2] = max(bc_free[ib % 2], WOUT_B["dve"][1])
                    dequant_granule(io["gub"][part, h512, ib, :, :],
                                    sstripb[:, 0, ib, :], ib % 3, ib, pp, sc_cast,
                                    extra_w=xw)
                DQ_D = P.now("dve")
                bc_te = P.now("te")
                for o128 in range(4):
                    for tch in range(2):
                        g8 = o128 * 2 + tch
                        pbk = pb[g8 % 4]
                        P.emit("te", lambda te, o128=o128, tch=tch, pbk=pbk, pp=pp:
                               te.matmul(
                                   out=pbk[:, 0:512],
                                   lhsT=wdq[:, pp, 0, 0, o128 * 128:(o128 + 1) * 128],
                                   rhs=xn_own[:, 0, tch * 512:(tch + 1) * 512],
                                   start=True, stop=False),
                               waits=[("dve", DQ_D)] + [bankev[g8 % 4]])
                        for ib in range(8):
                            for lh in range(2):
                                if ib == 0 and lh == 0:
                                    continue
                                P.emit("te", lambda te, o128=o128, tch=tch, ib=ib, lh=lh,
                                       pbk=pbk, pp=pp:
                                       te.matmul(
                                           out=pbk[:, 0:512],
                                           lhsT=wdq[:, pp, ib, lh, o128 * 128:(o128 + 1) * 128],
                                           rhs=xn_own[:, ib * 2 + lh, tch * 512:(tch + 1) * 512],
                                           start=False, stop=(ib == 7 and lh == 1)))
                        TE_GU = P.now("te")
                        if part == 0:
                            P.emit("act", lambda s, o128=o128, tch=tch, pbk=pbk: s.activation(
                                out=gsil[:, o128, tch * 512:(tch + 1) * 512], in_=pbk[:, 0:512],
                                func=AF.Silu),
                                waits=[("te", TE_GU), ("dve", gsil_free[g8])])
                            bankev[g8 % 4] = ("act", P.now("act"))
                        else:
                            P.emit("dve", lambda v, h512=h512, o128=o128, tch=tch, pbk=pbk:
                                   v.tensor_tensor(
                                       out=h_cm[:, h512 * 4 + o128, tch * 512:(tch + 1) * 512],
                                       in0=pbk[:, 0:512], in1=gsil[:, o128, tch * 512:(tch + 1) * 512],
                                       op=ALU.mult),
                                   waits=[("te", TE_GU), ("act", P.now("act"))])
                            gsil_free[g8] = P.now("dve")
                            bankev[g8 % 4] = ("dve", P.now("dve"))
                te_mm[pp] = P.now("te")
            GU_B = P.barrier()
            P.mark("GU")
            if debug:
                P.dma_emit("sy", lambda sy: sy.dma_start(out=io["h_dbg"][:], in_=h_cm[:]),
                           waits=[GU_B["dve"], GU_B["act"]])
                P.dma_emit("sy", lambda sy: sy.dma_start(out=io["wdq_dbg"][:], in_=wdq[:, 1, :, :, :]),
                           waits=[GU_B["dve"]])
                P.dma_emit("sy", lambda sy: sy.dma_start(out=io["gsil_dbg"][:], in_=gsil[:]),
                           waits=[GU_B["act"]])
        ctx_xn_own.__exit__(None, None, None)

        with nc.sbuf_tensor("btd", [128, 4, 512], U8) as btd, \
             nc.sbuf_tensor("i1d", [128, 1, 128], I32) as i1d, \
             nc.sbuf_tensor("i2d", [128, 1, 128], I32) as i2d, \
             nc.sbuf_tensor("wdd", [128, 22, 2, 512], BF16) as wdd, \
             nc.sbuf_tensor("dsp", [2, 1, 512], F16) as dsp, \
             nc.sbuf_tensor("dspb", [2, 1, 512], BF16) as dspb, \
             nc.sbuf_tensor("ytile", [128, 8, 2048], BF16) as ytile, \
             nc.sbuf_tensor("yp", [128, 4], F32) as yp, \
             nc.sbuf_tensor("yr", [128, 2, 1], F32) as yr, \
             nc.sbuf_tensor("xres", [128, 1, 2048], F32) as xres, \
             nc.sbuf_tensor("ares", [128, 2048], F32) as ares:
            btid = btd.ap().bitcast(I32)
            NOFFN = bool(os.environ.get("DEVK_NOFFN"))
            dbankev = [0] * 4
            scst = [0, 0]
            dnfree_h = [GU_B["te"][1], GU_B["te"][1]]

            btd_free = [0, 0, 0, 0]
            dbc_free = [GU_B["dve"][1], GU_B["dve"][1]]
            ddq_n = [0]

            def dq_down(d512, hb, first=False):
                par = hb % 4
                sdm = P.dma_emit("sy", lambda sy, d512=d512, hb=hb: sy.dma_start(
                    out=dsp[:, 0, :], in_=io["dns"][d512, :, hb, :]),
                    waits=[("dve", scst[0])]
                          + (P.bwaits(GU_B, "te", "act", "dve") if first else []))
                P.emit("dve", lambda v: v.tensor_copy(
                    out=dspb[:, 0, :], in_=dsp[:, 0, :]),
                    waits=[("dma", sdm), ("te", scst[1])]
                          + (P.bwaits(GU_B, "dve") if first else []))
                scst[0] = P.now("dve")
                dm = P.dma_emit("gp", lambda gp, d512=d512, hb=hb, par=par: gp.dma_start(
                    out=btd[:, par, :], in_=io["dnb"][d512, hb, :, :]),
                    waits=([("dve", btd_free[par])]
                           + ([("te", dnfree_h[hb // 11])] if (hb % 11) < 4 else [])
                           + (P.bwaits(GU_B, "te", "act", "dve") if first else [])))
                bcb = ddq_n[0] % 2
                ddq_n[0] += 1
                pbc = pb[6] if bcb == 0 else pb[4]
                P.emit("te", lambda te, pbc=pbc: te.matmul(
                    out=pbc[:, 0:512], lhsT=ones2f[:, 0:128], rhs=dspb[:, 0, :],
                    start=True, stop=True),
                    waits=[("dve", max(dbc_free[bcb], scst[0]))]
                          + (P.bwaits(GU_B, "dve", "act") if first else []))
                TE_SC = P.now("te")
                scst[1] = TE_SC
                P.emit("dve", lambda v, par=par: v.tensor_scalar(
                    out=i1d[:, 0, :], in0=btid[:, par, :], scalar1=0x08080808, scalar2=4,
                    op0=ALU.bitwise_and, op1=ALU.logical_shift_left), waits=[("dmag", dm)])
                P.emit("dve", lambda v, par=par: v.tensor_scalar(
                    out=i2d[:, 0, :], in0=btid[:, par, :], scalar1=0x07070707, scalar2=2,
                    op0=ALU.bitwise_and, op1=ALU.logical_shift_left))
                P.emit("dve", lambda v, par=par: v.tensor_tensor(
                    out=i1d[:, 0, :], in0=i1d[:, 0, :], in1=i2d[:, 0, :], op=ALU.bitwise_or),
                    waits=[("dve", P.now("dve"))])
                P.emit("dve", lambda v, par=par, hb=hb, pbc=pbc: v.tensor_tensor(
                    out=wdd[:, hb, 0, :], in0=i1d.ap().bitcast(FP8)[:, 0, :],
                    in1=pbc[:, 0:512], op=ALU.mult),
                    waits=[("te", TE_SC), ("dve", P.now("dve"))])
                P.emit("dve", lambda v, par=par: v.tensor_scalar(
                    out=i2d[:, 0, :], in0=btid[:, par, :], scalar1=0x70707070, scalar2=2,
                    op0=ALU.bitwise_and, op1=ALU.logical_shift_right))
                P.emit("dve", lambda v, par=par: v.tensor_scalar(
                    out=i1d[:, 0, :], in0=btid[:, par, :], scalar1=0x7F7F7F7F, scalar2=None,
                    op0=ALU.bitwise_and), waits=[("dve", P.now("dve"))])
                P.emit("dve", lambda v, par=par: v.tensor_tensor(
                    out=i1d[:, 0, :], in0=btid[:, par, :], in1=i1d[:, 0, :],
                    op=ALU.bitwise_xor), waits=[("dve", P.now("dve"))])
                btd_free[par] = P.now("dve")
                P.emit("dve", lambda v, par=par: v.tensor_tensor(
                    out=i2d[:, 0, :], in0=i1d[:, 0, :], in1=i2d[:, 0, :],
                    op=ALU.bitwise_or), waits=[("dve", P.now("dve"))])
                P.emit("dve", lambda v, par=par, hb=hb, pbc=pbc: v.tensor_tensor(
                    out=wdd[:, hb, 1, :], in0=i2d.ap().bitcast(FP8)[:, 0, :],
                    in1=pbc[:, 0:512], op=ALU.mult), waits=[("dve", P.now("dve"))])
                dbc_free[bcb] = P.now("dve")

            def dn_mms(t128, hlo, hhi, dq_cnt, openg, closeg, d512):
                pbk = pb[t128 % 4]
                for hb in range(hlo, hhi):
                    for lh in range(2):
                        w = []
                        if hb == hlo and lh == 0:
                            w = [("dve", max(dq_cnt, dbankev[t128 % 4]))] if openg \
                                else [("dve", dq_cnt)]
                        P.emit("te", lambda te, t128=t128, hb=hb, lh=lh, pbk=pbk: te.matmul(
                            out=pbk[:, 0:512], lhsT=h_cm[:, hb * 2 + lh, t128 * 128:(t128 + 1) * 128],
                            rhs=wdd[:, hb, lh, :], start=(openg and hb == hlo and lh == 0),
                            stop=(closeg and hb == hhi - 1 and lh == 1)), waits=w)
                if closeg:
                    TE_DN = P.now("te")
                    P.emit("dve", lambda v, t128=t128, d512=d512, pbk=pbk: v.tensor_copy(
                        out=ytile[:, t128, d512 * 512:(d512 + 1) * 512], in_=pbk[:, 0:512]),
                        waits=[("te", TE_DN)])
                    dbankev[t128 % 4] = P.now("dve")

            if not NOFFN:
                for hb in range(11):
                    dq_down(0, hb, first=(hb == 0))
                DQ_H0 = P.now("dve")
                for d512 in range(4):
                    for t128 in range(4):
                        dn_mms(t128, 0, 11, DQ_H0, True, False, d512)
                    for hb in range(11, 22):
                        dq_down(d512, hb)
                    DQ_H1 = P.now("dve")
                    for t128 in range(4):
                        dn_mms(t128, 11, 22, DQ_H1, False, True, d512)
                    for t128 in range(4, 8):
                        dn_mms(t128, 0, 11, DQ_H0, True, False, d512)
                    dnfree_h[0] = P.now("te")
                    if d512 < 3:
                        for hb in range(11):
                            dq_down(d512 + 1, hb)
                    DQ_H0 = P.now("dve")
                    for t128 in range(4, 8):
                        dn_mms(t128, 11, 22, DQ_H1, False, True, d512)
                    dnfree_h[1] = P.now("te")
            Y_D2 = P.now("dve")
            if debug:
                P.dma_emit("sy", lambda sy: sy.dma_start(out=io["yt_dbg"][:], in_=ytile[:]),
                           waits=[("dve", Y_D2)])
            lastx = 0
            last_odm = 0
            sqfree = [0, 0]
            for t128 in range(0 if NOFFN else 8):
                par = t128 % 2
                xdm = P.dma_emit("sy", lambda sy, t128=t128: sy.dma_start(
                    out=xres[:, 0, :], in_=io["x_own_tm"][t128 * 128:(t128 + 1) * 128, :]),
                    waits=[("dve", lastx)])
                adm = P.dma_emit("sy", lambda sy, t128=t128: sy.dma_start(
                    out=ares[:], in_=attn_sc[t128 * 128:(t128 + 1) * 128, :]),
                    waits=[("dve", lastx),
                           ("dma", max(last_odm, WOUT_B["dma"][1]))])
                for g in range(4):
                    sbk = pb[5 if g % 2 == 0 else 7]
                    P.emit("act", lambda s, t128=t128, g=g, sbk=sbk: s.activation(
                        out=sbk[:, 0:512], in_=ytile[:, t128, g * 512:(g + 1) * 512],
                        func=AF.Square),
                        waits=[("dve", max(Y_D2, sqfree[g % 2]))])
                    A_S2 = P.now("act")
                    P.emit("dve", lambda v, g=g, sbk=sbk: v.tensor_reduce(
                        out=yp[:, g:g + 1], in_=sbk[:, 0:512], axis=AX.X, op=ALU.add),
                        waits=[("act", A_S2)])
                    sqfree[g % 2] = P.now("dve")
                P.emit("dve", lambda v, par=par: v.tensor_reduce(
                    out=yr[:, par, :], in_=yp[:], axis=AX.X, op=ALU.add),
                    waits=[("dve", P.now("dve"))])
                P.emit("dve", lambda v, par=par: v.tensor_scalar(
                    out=yr[:, par, :], in0=yr[:, par, :], scalar1=1.0 / D, scalar2=None,
                    op0=ALU.mult), waits=[("dve", P.now("dve"))])
                DV_Y = P.now("dve")
                P.emit("act", lambda s, par=par: s.activation(
                    out=yp[:, 0:1], in_=yr[:, par, :], func=AF.Sqrt, bias=cst[:, 1:2]),
                    waits=[("dve", DV_Y)])
                A_Y = P.now("act")
                P.emit("dve", lambda v, par=par: v.reciprocal(
                    out=yr[:, par, :], in_=yp[:, 0:1]), waits=[("act", A_Y)])
                DV_YR = P.now("dve")
                P.emit("dve", lambda v: v.tensor_tensor(
                    out=ares[:], in0=xres[:, 0, :], in1=ares[:], op=ALU.add),
                    waits=[("dma", adm), ("dma", xdm)])
                P.emit("dve", lambda v, t128=t128, par=par: v.scalar_tensor_tensor(
                    out=ares[:], in0=ytile[:, t128, :], scalar=yr[:, par, 0:1],
                    in1=ares[:], op0=ALU.mult, op1=ALU.add),
                    waits=[("dve", P.now("dve")), ("dve", DV_YR)])
                OD = P.now("dve")
                odm = P.dma_emit("sy", lambda sy, t128=t128: sy.dma_start(
                    out=io["out"][t128 * 128:(t128 + 1) * 128, :], in_=ares[:]),
                    waits=[("dve", OD)])
                last_odm = odm
                lastx = OD
            FIN_DMA = P.dma_cnt
            P.emit("sy", lambda sy: sy.nop(), waits=[("dma", FIN_DMA)], inc=False)

        blk = es.enter_context(nc.Block())

        @blk.sync
        def _(sy):
            P.replay("sy", sy)

        @blk.tensor
        def _(te):
            P.replay("te", te)

        @blk.vector
        def _(v):
            P.replay("dve", v)

        @blk.scalar
        def _(s):
            P.replay("act", s)

        @blk.gpsimd
        def _(g):
            P.replay("gp", g)

    return nc, io



def emit_delta_phase(nc, P, es, pb, pbh, tn, B):
    sb = lambda n, s, d: es.enter_context(nc.sbuf_tensor(n, s, d))
    ktm, vtm, kfm, qfm = tn["ktm"], tn["vtm"], tn["kfm"], tn["qfm"]
    g_tm, b_tm = tn["g_tm"], tn["b_tm"]
    id128, id64s, idf32 = tn["id128"], tn["id64s"], tn["idf32"]
    ones128, ones_f, ones_qb = tn["ones128"], tn["ones_f"], tn["ones_qb"]
    maskL, tricum, rowsel = tn["maskL"], tn["tricum"], tn["rowsel"]
    obuf = tn["obuf"]
    NIC = NCH * NV

    gc_tm = sb("gc_tm", [128, NIC], F32)
    gcHn = sb("gcHn", [128, NIC], F32)
    egc = sb("egc", [128, NIC], F32)
    einv_t = sb("einv_t", [128, NIC], F32)
    egcC = sb("egcC", [128, NIC], F32)
    begc = sb("begc", [128, NIC], F32)
    bneg = sb("bneg", [128, NIC], F32)
    dtmp = sb("dtmp", [128, NIC], F32)

    NROT = 2
    diagH_t = sb("d_diagH", [128, NROT, 128], F32)
    diagHn_t = sb("d_diagHn", [128, NROT, 128], F32)
    diagE_t = sb("d_diagE", [128, NROT, 128], BF16)
    expGs_t = sb("d_expGs", [128, NROT, 128], F32)
    expGi_t = sb("d_expGi", [128, NROT, 128], F32)
    abuf_t = sb("d_abuf", [128, NROT, 128], BF16)
    qbart_t = sb("d_qbart", [64, NV, 128], F32)

    YB_t = sb("d_YB", [128, NV, 2, 256], BF16)
    Ct_t = sb("d_Ct", [128, NV, 128], BF16)
    Ck_t = sb("d_Ck", [128, NV, 2, 128], BF16)
    At_t = sb("d_At", [128, NV, 128], BF16)
    khat_t = sb("d_khat", [128, NV, DK], BF16)
    AUs_t = sb("d_AUs", [64, NV, 128], BF16)
    QmAW_t = sb("d_QmAW", [64, NV, 128], BF16)
    Ptp_t = sb("d_Ptp", [64, NV, DK], BF16)
    Zs_t = sb("d_Zs", [64, NV, DV], BF16)
    IegcC_t = sb("d_IegcC", [64, NV, 64], BF16)
    Svh_t = sb("d_Svh", [64, NV, DV], BF16)

    diagH = lambda vh: diagH_t[:, vh % NROT, :]
    diagHn = lambda vh: diagHn_t[:, vh % NROT, :]
    diagE = lambda vh: diagE_t[:, vh % NROT, :]
    expGs = lambda vh: expGs_t[:, vh % NROT, :]
    expGi = lambda vh: expGi_t[:, vh % NROT, :]
    abuf = lambda vh: abuf_t[:, vh % NROT, :]
    qbart = lambda vh: qbart_t[:, vh, :]
    Bm = lambda vh: YB_t[:, vh, 0, 128:256]
    Ct = lambda vh: Ct_t[:, vh, :]
    At = lambda vh: At_t[:, vh, :]
    khat = lambda vh: khat_t[:, vh, :]
    AUs = lambda vh: AUs_t[:, vh, :]
    QmAW = lambda vh: QmAW_t[:, vh, :]
    Ptp = lambda vh: Ptp_t[:, vh, :]
    Zs = lambda vh: Zs_t[:, vh, :]
    IegcC = lambda vh: IegcC_t[:, vh, :]
    Svh = lambda vh: Svh_t[:, vh, :]

    def grel_v(i):
        return pb[i][:, 0:128]

    def gram_kk(i):
        return pb[2 + i][:, 0:128]

    def gram_qk(i):
        return pb[2 + i][:, 128:256]

    def sq_v(i):
        return pb[2 + i][:, 256:384]

    BT_V = pbh[4][:, 0:128]
    AT_V = pbh[4][:, 256:384]
    QB_V = pb[5][0:64, 0:128]
    GC_V = pb[6][:, 0:NIC]
    PT_V = pb[7][0:64, 0:64]
    Z_V = pb[7][0:64, 64:128]
    O_V = pb[6][0:64, 256:384]
    S_V = pb[6][0:64, 384:448]

    def col(t, ch, vh):
        i = ch * NV + vh
        return t[:, i:i + 1]

    bw = [B["te"], B["dve"], B["act"], B["gp"], B["dma"], B["dmag"]]

    P.emit("dve", lambda v: v.memset(Svh_t[:], 0.0), waits=bw)
    CONST = P.now("dve")
    gflat = g_tm.ap().rearrange("p a b -> p (a b)")
    bflat = b_tm.ap().rearrange("p a b -> p (a b)")
    P.emit("te", lambda te: te.matmul(out=GC_V, lhsT=tricum[:], rhs=gflat,
                                      start=True, stop=True), waits=bw + [("dve", CONST)])
    TE_GC = P.now("te")
    P.emit("dve", lambda v: v.tensor_copy(out=gc_tm[:], in_=GC_V), waits=[("te", TE_GC)])
    GC_BASE = P.now("dve")
    P.emit("act", lambda s: s.activation(out=gcHn[:], in_=gc_tm[:], func=AF.Copy, scale=-1.0),
           waits=[("dve", GC_BASE)] + bw)
    P.emit("act", lambda s: s.activation(out=egc[:], in_=gc_tm[:], func=AF.Exp))
    ACT_E = P.now("act")
    P.emit("te", lambda te: te.matmul(out=GC_V, lhsT=rowsel[:], rhs=gc_tm[:],
                                      start=True, stop=True),
           waits=[("dve", GC_BASE), ("act", ACT_E)])
    TE_GCC = P.now("te")
    P.emit("dve", lambda v: v.tensor_tensor(out=dtmp[:], in0=GC_V, in1=gc_tm[:], op=ALU.subtract),
           waits=[("te", TE_GCC)])
    DVE_EINV = P.now("dve")
    P.emit("act", lambda s: s.activation(out=einv_t[:], in_=dtmp[:], func=AF.Exp),
           waits=[("dve", DVE_EINV)])
    P.emit("act", lambda s: s.activation(out=egcC[:], in_=GC_V, func=AF.Exp))
    P.emit("dve", lambda v: v.tensor_tensor(out=begc[:], in0=bflat, in1=egc[:], op=ALU.mult),
           waits=[("act", ACT_E)])
    P.emit("act", lambda s: s.activation(out=bneg[:], in_=bflat, func=AF.Copy, scale=-1.0))
    PREP_DVE = P.now("dve")
    PREP_ACT = P.now("act")

    grel_free = [0, 0]
    grel_free_a = [0, 0]
    gram_free = [0, 0]
    sq_free = [0, 0]
    sq_free_d = [0, 0]
    sqf_act = [0, 0]
    awu_free_act = [0, 0]
    diag_free = [0, 0]
    expgs_free = [0, 0]
    abuf_free = [0, 0]
    iegc_free = [0] * NV
    b4_free_act = 0
    qb_free_dve = 0
    b5_free = dict(dve=0, act=0)
    o_free_act = 0
    s_free_act = 0
    sact = [0] * NV

    first = True
    yb0_te = [0] * NV
    khat_te = [0] * NV
    for ch in range(NCH):
        te_prev = P.now("te")
        gp_diag = [0] * NV
        dve_b = [0] * NV
        act_b = [0] * NV
        for vh in range(NV):
            w = [("act", PREP_ACT), ("dve", PREP_DVE)] if first else []
            first = False
            w = w + [("te", diag_free[vh % 2])]
            P.emit("dve", lambda v, ch=ch, vh=vh: v.tensor_scalar(
                out=diagH(vh), in0=idf32[:], scalar1=col(gc_tm, ch, vh), scalar2=None, op0=ALU.mult),
                waits=w)
            P.emit("dve", lambda v, ch=ch, vh=vh: v.tensor_scalar(
                out=diagHn(vh), in0=idf32[:], scalar1=col(gcHn, ch, vh), scalar2=None, op0=ALU.mult))
            P.emit("dve", lambda v, ch=ch, vh=vh: v.tensor_scalar(
                out=diagE(vh), in0=id128[:], scalar1=col(egc, ch, vh), scalar2=None, op0=ALU.mult))
            P.emit("dve", lambda v, ch=ch, vh=vh: v.tensor_scalar(
                out=IegcC(vh), in0=id64s[0:64, :], scalar1=col(egcC, ch, vh)[0:64, :],
                scalar2=None, op0=ALU.mult), waits=[("dve", iegc_free[vh])])
            gp_diag[vh] = P.now("dve")
            kh = vh // 2
            khs = (kh % 2) * 64
            kht = kh // 2
            par = vh % 2
            kslice = kfm[khs:khs + 64, kht, ch * C:(ch + 1) * C]
            qslice = qfm[khs:khs + 64, kht, ch * C:(ch + 1) * C]
            P.emit("te", lambda te, vh=vh, par=par: te.matmul(out=grel_v(par), lhsT=diagH(vh), rhs=ones_f[:],
                                                              start=True, stop=False),
                   waits=[("dve", max(gp_diag[vh], grel_free[par])),
                          ("act", max(awu_free_act[par], grel_free_a[par]))])
            P.emit("te", lambda te, vh=vh, par=par: te.matmul(out=grel_v(par), lhsT=ones_f[:], rhs=diagHn(vh),
                                                              start=False, stop=False))
            P.emit("te", lambda te, vh=vh, par=par: te.matmul(out=grel_v(par), lhsT=idf32[:], rhs=maskL[:],
                                                              start=False, stop=True))
            TE_GREL = P.now("te")
            P.emit("te", lambda te, ks=kslice, par=par: te.matmul(
                out=gram_kk(par), lhsT=ks, rhs=ks, start=True, stop=True),
                waits=[("dve", max(gram_free[par], sq_free_d[par])), ("act", sqf_act[par])])
            P.emit("te", lambda te, ks=kslice, qs=qslice, par=par: te.matmul(
                out=gram_qk(par), lhsT=qs, rhs=ks, start=True, stop=True))
            P.emit("te", lambda te, vh=vh: te.matmul(out=QB_V, lhsT=ones_qb[:], rhs=diagE(vh),
                                                     start=True, stop=True),
                   waits=[("dve", max(qb_free_dve, gp_diag[vh]))])
            TE_G = P.now("te")
            diag_free[par] = TE_G
            P.emit("act", lambda s, vh=vh, par=par: s.activation(out=expGs(vh), in_=grel_v(par), func=AF.Exp),
                   waits=[("te", TE_GREL), ("dve", max(expgs_free[vh % 2], grel_free[par]))])
            grel_free_a[par] = P.now("act")
            ACT_X = P.now("act")
            P.emit("dve", lambda v, vh=vh, ch=ch, par=par: v.scalar_tensor_tensor(
                out=Bm(vh), in0=gram_kk(par), scalar=col(bneg, ch, vh), in1=expGs(vh),
                op0=ALU.mult, op1=ALU.mult), waits=[("act", ACT_X), ("te", yb0_te[vh])])
            P.emit("dve", lambda v, vh=vh: v.tensor_tensor(
                out=expGi(vh), in0=expGs(vh), in1=idf32[:], op=ALU.add))
            expgs_free[vh % 2] = P.now("dve")
            P.emit("dve", lambda v, vh=vh, ch=ch: v.tensor_scalar(
                out=khat(vh), in0=ktm[:, ch, vh // 2, :], scalar1=col(einv_t, ch, vh),
                scalar2=None, op0=ALU.mult), waits=[("te", khat_te[vh])])
            P.emit("dve", lambda v, vh=vh, par=par: v.tensor_tensor(
                out=abuf(vh), in0=gram_qk(par), in1=expGi(vh), op=ALU.mult),
                waits=[("te", abuf_free[vh % 2])])
            gram_free[par] = P.now("dve")
            DVE_BA = P.now("dve")
            P.emit("te", lambda te, vh=vh: te.transpose(out=BT_V, in_=Bm(vh), identity=id128[:]),
                   waits=[("dve", DVE_BA), ("act", b4_free_act)])
            TE_BT = P.now("te")
            P.emit("act", lambda s, vh=vh: s.activation(out=Ct(vh), in_=BT_V, func=AF.Copy),
                   waits=[("te", TE_BT)])
            b4a = P.now("act")
            P.emit("te", lambda te, vh=vh: te.transpose(out=AT_V, in_=abuf(vh), identity=id128[:]),
                   waits=[("act", b4a)])
            TE_AT = P.now("te")
            abuf_free[vh % 2] = TE_AT
            P.emit("act", lambda s, vh=vh: s.activation(out=At(vh), in_=AT_V, func=AF.Copy),
                   waits=[("te", TE_AT)])
            b4_free_act = P.now("act")
            P.emit("dve", lambda v, vh=vh, ch=ch: v.tensor_scalar(
                out=YB_t[:, vh, 0, 0:64], in0=ktm[:, ch, vh // 2, :], scalar1=col(begc, ch, vh),
                scalar2=None, op0=ALU.mult), waits=[("te", yb0_te[vh])])
            P.emit("dve", lambda v, vh=vh, ch=ch: v.tensor_scalar(
                out=YB_t[:, vh, 0, 64:128], in0=vtm[:, ch, vh, :],
                scalar1=bflat[:, ch * NV + vh: ch * NV + vh + 1],
                scalar2=None, op0=ALU.mult))
            P.emit("dve", lambda v, vh=vh, ch=ch, qs=qslice: v.tensor_tensor(
                out=qbart(vh), in0=qs, in1=QB_V, op=ALU.mult), waits=[("te", TE_G)])
            qb_free_dve = P.now("dve")
            dve_b[vh] = P.now("dve")
            act_b[vh] = P.now("act")

        ybprev = [YB_t[:, vh, 0, :] for vh in range(NV)]
        cprev = [Ct(vh) for vh in range(NV)]
        ydve = list(dve_b)
        b_act = [0] * NV
        c_dve = [0] * NV
        c_act = list(act_b)
        ap_free = [0, 0]
        apb_act = [0, 0]
        sqf = [max(s, dve_b[NV - 1]) for s in sq_free]
        sqd = list(sq_free_d)
        for lev in range(NLEV):
            last = lev == NLEV - 1
            for vh in range(NV):
                par = vh % 2
                if not last:
                    P.emit("te", lambda te, c=cprev[vh], yb=ybprev[vh], par=par: te.matmul(
                        out=pb[par][:, 0:256], lhsT=c, rhs=yb, start=True, stop=True),
                        waits=[("dve", max(ydve[vh], ap_free[par], c_dve[vh])),
                               ("act", max(c_act[vh], b_act[vh], grel_free_a[par], apb_act[par]))])
                    TE_AP = P.now("te")
                    if lev == 0:
                        yb0_te[vh] = TE_AP
                    s_new = (lev + 1) % 2
                    P.emit("dve", lambda v, vh=vh, s_new=s_new, yb=ybprev[vh], par=par: v.tensor_tensor(
                        out=YB_t[:, vh, s_new, 0:128], in0=pb[par][:, 0:128],
                        in1=yb[:, 0:128], op=ALU.add), waits=[("te", TE_AP)])
                    ap_free[par] = P.now("dve")
                    grel_free[par] = P.now("dve")
                    bold_act = b_act[vh]
                    P.emit("act", lambda s, vh=vh, s_new=s_new, par=par: s.activation(
                        out=YB_t[:, vh, s_new, 128:256], in_=pb[par][:, 128:256], func=AF.Copy),
                        waits=[("te", TE_AP), ("dve", ap_free[par])])
                    apb_act[par] = P.now("act")
                    b_act[vh] = P.now("act")
                    P.emit("te", lambda te, b=ybprev[vh][:, 128:256], c=cprev[vh], par=par: te.matmul(
                        out=sq_v(par), lhsT=b, rhs=c, start=True, stop=True),
                        waits=[("dve", max(sqf[par], sqd[par], c_dve[vh])),
                               ("act", max(c_act[vh], bold_act))])
                    TE_C = P.now("te")
                    cnew = Ck_t[:, vh, lev % 2, :]
                    P.emit("dve", lambda v, cn=cnew, par=par: v.tensor_copy(out=cn, in_=sq_v(par)),
                           waits=[("te", TE_C)])
                    sqd[par] = P.now("dve")
                    c_dve[vh] = P.now("dve")
                    ybprev[vh] = YB_t[:, vh, s_new, :]
                    ydve[vh] = ap_free[par]
                    cprev[vh] = cnew
                    sqf[par] = 0
                else:
                    P.emit("te", lambda te, c=cprev[vh], yb=ybprev[vh], par=par: te.matmul(
                        out=pb[par][:, 0:128], lhsT=c, rhs=yb[:, 0:128], start=True, stop=True),
                        waits=[("dve", max(ydve[vh], ap_free[par], c_dve[vh])),
                               ("act", max(c_act[vh], grel_free_a[par], apb_act[par]))])
                    TE_AP = P.now("te")
                    s_new = (lev + 1) % 2
                    P.emit("dve", lambda v, vh=vh, s_new=s_new, yb=ybprev[vh], par=par: v.tensor_tensor(
                        out=YB_t[:, vh, s_new, 0:128], in0=pb[par][:, 0:128],
                        in1=yb[:, 0:128], op=ALU.add), waits=[("te", TE_AP)])
                    ap_free[par] = P.now("dve")
                    grel_free[par] = P.now("dve")
                    ybprev[vh] = YB_t[:, vh, s_new, :]
                    ydve[vh] = ap_free[par]
        yprev = ybprev
        sq_free[0], sq_free[1] = sqf[0], sqf[1]
        sq_free_d[0], sq_free_d[1] = sqd[0], sqd[1]
        grel_free_a[0] = max(grel_free_a[0], apb_act[0])
        grel_free_a[1] = max(grel_free_a[1], apb_act[1])

        dve_fin = [0] * NV
        act_fin = [0] * NV
        for vh in range(NV):
            par = vh % 2
            P.emit("te", lambda te, y=yprev[vh][:, 0:128], vh=vh, par=par: te.matmul(
                out=grel_v(par), lhsT=y, rhs=At(vh), start=True, stop=True),
                waits=[("dve", max(ydve[vh], grel_free[par])),
                       ("act", max(awu_free_act[par], grel_free_a[par]))])
            TE_AWU = P.now("te")
            P.emit("dve", lambda v, vh=vh, par=par: v.tensor_tensor(
                out=QmAW(vh), in0=qbart(vh), in1=grel_v(par)[0:64, :], op=ALU.subtract),
                waits=[("te", TE_AWU)])
            DVE_QM = P.now("dve")
            grel_free[par] = DVE_QM
            P.emit("act", lambda s, vh=vh, par=par: s.activation(
                out=AUs(vh), in_=grel_v(par)[64:128, :], func=AF.Copy),
                waits=[("te", TE_AWU), ("dve", DVE_QM)])
            awu_free_act[par] = P.now("act")
            P.emit("te", lambda te, y=yprev[vh], vh=vh: te.matmul(
                out=PT_V, lhsT=y[:, 0:64], rhs=khat(vh), start=True, stop=True),
                waits=[("dve", b5_free["dve"]), ("act", b5_free["act"])])
            P.emit("te", lambda te, y=yprev[vh], vh=vh: te.matmul(
                out=Z_V, lhsT=khat(vh), rhs=y[:, 64:128], start=True, stop=True))
            TE_PZ = P.now("te")
            khat_te[vh] = TE_PZ
            P.emit("dve", lambda v, vh=vh: v.scalar_tensor_tensor(
                out=Ptp(vh), in0=PT_V, scalar=-1.0, in1=IegcC(vh), op0=ALU.mult, op1=ALU.add),
                waits=[("te", TE_PZ)])
            P.emit("dve", lambda v, vh=vh: v.tensor_copy(
                out=Zs(vh), in_=Z_V), waits=[("te", TE_PZ), ("dve", P.now("dve"))])
            b5_free["dve"] = P.now("dve")
            iegc_free[vh] = P.now("dve")
            dve_fin[vh] = P.now("dve")
            act_fin[vh] = P.now("act")

        for vh in range(NV):
            vp = vh // 2
            half = (vh % 2) * 64
            if ch >= 8:
                P.emit("te", lambda te, vh=vh: te.matmul(
                    out=O_V, lhsT=Svh(vh), rhs=QmAW(vh), start=True, stop=True),
                    waits=[("dve", max(dve_fin[vh], sact[vh], o_free_act, s_free_act)),
                           ("act", act_fin[vh])])
                TE_O = P.now("te")
                P.emit("dve", lambda v, vh=vh, ch=ch, vp=vp, half=half: v.tensor_tensor(
                    out=obuf[half:half + 64, vp, ch - 8, :], in0=O_V, in1=AUs(vh), op=ALU.add),
                    waits=[("te", TE_O)])
                o_free_act = P.now("dve")
            P.emit("te", lambda te, vh=vh: te.matmul(
                out=S_V, lhsT=Ptp(vh), rhs=Svh(vh), start=True, stop=True),
                waits=[("dve", max(sact[vh], o_free_act, s_free_act))])
            TE_S = P.now("te")
            P.emit("dve", lambda v, vh=vh: v.tensor_tensor(
                out=Svh(vh), in0=S_V, in1=Zs(vh), op=ALU.add), waits=[("te", TE_S)])
            sact[vh] = P.now("dve")
            s_free_act = sact[vh]



def _make_pi(n):
    pi = np.empty(n, np.int64)
    for b0 in range(n // 256):
        pi[256 * b0:256 * b0 + 128] = 256 * b0 + 2 * np.arange(128)
        pi[256 * b0 + 128:256 * b0 + 256] = 256 * b0 + 2 * np.arange(128) + 1
    return pi


PI = _make_pi(2048)
SIG = _make_pi(5632)


def _prep_weights(inp):
    W_qkv = np.asarray(inp["L0_attn_W_qkv"], np.float32)
    W_z = np.asarray(inp["L0_attn_W_z"], np.float32)
    W_b = np.asarray(inp["L0_attn_W_b"], np.float32)
    W_a = np.asarray(inp["L0_attn_W_a"], np.float32)
    W_out = np.asarray(inp["L0_attn_W_out"], np.float32)
    conv_w = np.asarray(inp["L0_attn_conv_w"], np.float32)
    dt_bias = np.asarray(inp["L0_attn_dt_bias"], np.float32)
    A_log = np.asarray(inp["L0_attn_A_log"], np.float32)
    norm_w = np.asarray(inp["L0_attn_norm_weight"], np.float32)
    gup = np.asarray(inp["L0_ffn_gate_up_packed"])
    gus = np.asarray(inp["L0_ffn_gate_up_scales"])
    dnp_ = np.asarray(inp["L0_ffn_down_packed"])
    dns = np.asarray(inp["L0_ffn_down_scales"])

    w = {}
    wq = W_qkv[:, PI].T.reshape(16, 128, 16, 128)
    w["wqkv_t"] = np.ascontiguousarray(wq.transpose(2, 1, 0, 3))
    wz = W_z[:, PI].T.reshape(16, 128, 2, 512)
    w["wz_t"] = np.ascontiguousarray(wz.transpose(2, 0, 1, 3))
    w["wb_t"] = np.ascontiguousarray(W_b[:, PI].T.reshape(16, 128, 16))
    w["wa_t"] = np.ascontiguousarray(W_a[:, PI].T.reshape(16, 128, 16))
    wo = W_out[PI, :].T.reshape(8, 128, 4, 512)
    w["wo_t"] = np.ascontiguousarray(wo.transpose(2, 0, 1, 3))
    w["convw"] = np.ascontiguousarray(conv_w[:, 0, :].reshape(16, 128, 4))
    vecs = np.zeros((64, 3), np.float32)
    vecs[0:16, 0] = dt_bias
    vecs[0:16, 1] = A_log
    vecs[0:64, 2] = norm_w
    w["vecs"] = vecs
    sel2 = np.zeros((2, 128), ml_dtypes.bfloat16)
    sel2[0, 0:64] = 64.0
    sel2[1, 64:128] = 64.0
    w["sel2"] = sel2
    I = 5632
    gub = np.empty((2, 11, 8, 128, 512), np.uint8)
    guss = np.empty((2, 2, 11, 8, 512), np.float16)
    for part in range(2):
        rows = gup[part * I + SIG]
        bT = rows.view(np.uint8).reshape(I, 1024).T
        gub[part] = bT.reshape(8, 128, 11, 512).transpose(2, 0, 1, 3)
        sc = gus[:, part * I + SIG]
        guss[part] = sc.reshape(8, 2, 11, 512).transpose(1, 2, 0, 3)
    w["gub"] = np.ascontiguousarray(gub)
    w["gus"] = np.ascontiguousarray(guss)
    rows = dnp_[PI]
    bT2 = rows.view(np.uint8).reshape(2048, 2816).T
    w["dnb"] = np.ascontiguousarray(bT2.reshape(22, 128, 4, 512).transpose(2, 0, 1, 3))
    w["dns"] = np.ascontiguousarray(dns[:, PI].reshape(22, 2, 4, 512).transpose(2, 1, 0, 3))
    return w


def _prep_core_x(core, x_full):
    b, hf = core // 2, core % 2
    x = np.asarray(x_full, np.float32)[b]
    m = {}
    if hf == 0:
        xs = np.concatenate([np.zeros((1024, 2048), np.float32), x[0:1024]], 0)
    else:
        xs = x
    xs = np.ascontiguousarray(xs[:, PI])
    m["x_scan_tm"] = xs
    m["x_scan_cm"] = np.ascontiguousarray(xs.T)
    xo = np.ascontiguousarray(x[hf * 1024:(hf + 1) * 1024][:, PI])
    m["x_own_tm"] = xo
    m["x_own_cm"] = np.ascontiguousarray(xo.T)
    return m


def kernel(**inputs):
    w = _prep_weights(inputs)
    in_maps = []
    for core in range(8):
        m = dict(w)
        m.update(_prep_core_x(core, inputs["L0_attn_x"]))
        in_maps.append(m)
    nc, io = build()
    res = run_bass_kernel_spmd(nc, in_maps, core_ids=list(range(8)))
    out = np.empty((4, 2048, 2048), np.float32)
    for core in range(8):
        b, hf = core // 2, core % 2
        shard = np.asarray(res.results[core]["out"]).reshape(1024, 2048)
        out[b, hf * 1024:(hf + 1) * 1024][:, PI] = shard
    return out


# version 17
# speedup vs baseline: 1.3147x; 1.0671x over previous
import os
import numpy as np
import ml_dtypes
import concourse.bass as bass
import concourse.mybir as mybir
from concourse.bass_utils import run_bass_kernel_spmd
from contextlib import ExitStack

F32 = mybir.dt.float32
BF16 = mybir.dt.bfloat16
I32 = mybir.dt.int32
F16 = mybir.dt.float16
U8 = mybir.dt.uint8
FP8 = mybir.dt.float8e4
AF = mybir.ActivationFunctionType
ALU = mybir.AluOpType
AX = mybir.AxisListType

D = 2048
T = 2048
TH = 1024
_LAST_PLAN = None
TOWN = 1024
C = 128
NCH = 16
NV = 16
KH = 8
DK = DV = 64
NEG = -30000.0
NLEV = 7
I_FF = 5632
HT = 44
LN_EPS = 1e-5
EPS_A = 1e-6


class Plan:
    def __init__(self, nc, es):
        self.marks = {}
        self.nc = nc
        self.items = []
        self.cnt = {e: 0 for e in ("te", "dve", "act", "gp", "sy")}
        self.sems = {e: es.enter_context(nc.semaphore("s_" + e)) for e in self.cnt}
        self.dma = es.enter_context(nc.semaphore("s_dma"))
        self.dma_cnt = 0
        self.dmag = es.enter_context(nc.semaphore("s_dmag"))
        self.dmag_cnt = 0

    def emit(self, eng, fn, waits=(), inc=True):
        waits = tuple((we, wv) for (we, wv) in waits if wv)
        if inc:
            self.cnt[eng] += 1
        self.items.append((eng, fn, waits, "c" if inc else None))
        return self.cnt[eng]

    def dma_emit(self, eng, fn, waits=()):
        waits = tuple((we, wv) for (we, wv) in waits if wv)
        if eng == "gp":
            self.dmag_cnt += 16
            self.items.append((eng, fn, waits, "dg"))
            return self.dmag_cnt
        self.dma_cnt += 16
        self.items.append((eng, fn, waits, "d"))
        return self.dma_cnt

    def now(self, eng):
        return self.cnt[eng]

    def mark(self, name):
        self.marks[name] = {e: self.cnt[e] for e in self.cnt}

    def barrier(self):
        return dict(te=("te", self.cnt["te"]), dve=("dve", self.cnt["dve"]),
                    act=("act", self.cnt["act"]), gp=("gp", self.cnt["gp"]),
                    dma=("dma", self.dma_cnt), dmag=("dmag", self.dmag_cnt))

    def bwaits(self, b, *names):
        return [b[n] for n in names]

    def replay(self, eng_name, engine):
        for e, fn, waits, inc in self.items:
            if e != eng_name:
                continue
            for (we, wv) in waits:
                sem = self.dma if we == "dma" else (self.dmag if we == "dmag" else self.sems[we])
                engine.wait_ge(sem, wv)
            inst = fn(engine)
            if inc == "d":
                inst.then_inc(self.dma, 16)
            elif inc == "dg":
                inst.then_inc(self.dmag, 16)
            elif inc == "c":
                inst.then_inc(self.sems[e], 1)


def build(debug=False):
    nc = bass.Bass("TRN2", target_bir_lowering=False, detect_race_conditions=False)
    dt_ = nc.dram_tensor
    io = {}
    io["x_scan_tm"] = dt_("x_scan_tm", [T, D], F32, kind="ExternalInput").ap()
    io["x_scan_cm"] = dt_("x_scan_cm", [D, T], F32, kind="ExternalInput").ap()
    io["x_own_tm"] = dt_("x_own_tm", [TOWN, D], F32, kind="ExternalInput").ap()
    io["x_own_cm"] = dt_("x_own_cm", [D, TOWN], F32, kind="ExternalInput").ap()
    io["wqkv_t"] = dt_("wqkv_t", [16, 128, 16, 128], F32, kind="ExternalInput").ap()
    io["wz_t"] = dt_("wz_t", [2, 16, 128, 512], F32, kind="ExternalInput").ap()
    io["wb_t"] = dt_("wb_t", [16, 128, 16], F32, kind="ExternalInput").ap()
    io["wa_t"] = dt_("wa_t", [16, 128, 16], F32, kind="ExternalInput").ap()
    io["wo_t"] = dt_("wo_t", [4, 8, 128, 512], F32, kind="ExternalInput").ap()
    io["convw"] = dt_("convw", [16, 128, 4], F32, kind="ExternalInput").ap()
    io["vecs"] = dt_("vecs", [64, 3], F32, kind="ExternalInput").ap()
    io["sel2"] = dt_("sel2", [2, 128], BF16, kind="ExternalInput").ap()
    io["gub"] = dt_("gub", [2, 11, 8, 128, 512], U8, kind="ExternalInput").ap()
    io["gus"] = dt_("gus", [2, 11, 8, 128, 512], F16, kind="ExternalInput").ap()
    io["dnb"] = dt_("dnb", [4, 22, 128, 512], U8, kind="ExternalInput").ap()
    io["dns"] = dt_("dns", [4, 22, 128, 512], F16, kind="ExternalInput").ap()
    io["out"] = dt_("out", [TOWN, D], F32, kind="ExternalOutput").ap()
    if debug:
        io["xn_dbg"] = dt_("xn_dbg", [128, 16, TOWN], BF16, kind="ExternalOutput").ap()
        io["yn_dbg"] = dt_("yn_dbg", [128, 8, TOWN], BF16, kind="ExternalOutput").ap()
        io["attn_dbg"] = dt_("attn_dbg", [TOWN, D], F32, kind="ExternalOutput").ap()
        io["h_dbg"] = dt_("h_dbg", [128, HT, TOWN], BF16, kind="ExternalOutput").ap()
        io["wdq_dbg"] = dt_("wdq_dbg", [128, 8, 2, 512], BF16, kind="ExternalOutput").ap()
        io["gsil_dbg"] = dt_("gsil_dbg", [128, 4, TOWN], F32, kind="ExternalOutput").ap()
        io["yt_dbg"] = dt_("yt_dbg", [128, 8, 2048], BF16, kind="ExternalOutput").ap()
    attn_sc = dt_("attn_sc", [TOWN, D], F32, kind="Internal").ap()

    es = ExitStack()
    sb = lambda n, s, d: es.enter_context(nc.sbuf_tensor(n, s, d))
    ps = lambda n, s, d: es.enter_context(nc.psum_tensor(n, s, d))

    with es:
        P = Plan(nc, es)
        global _LAST_PLAN
        _LAST_PLAN = P
        id128 = sb("id128", [128, 128], BF16)
        id64s = sb("id64s", [128, 64], BF16)
        idf32 = sb("idf32", [128, 128], F32)
        ones128 = sb("ones128", [128, 128], BF16)
        ones_f = sb("ones_f", [128, 128], F32)
        ones_qb = sb("ones_qb", [128, 64], BF16)
        ones2f = sb("ones2f", [2, 512], BF16)
        maskL = sb("maskL", [128, 128], F32)
        tricum = sb("tricum", [128, 128], F32)
        rowsel = sb("rowsel", [128, 128], F32)
        vecs = sb("vecs_s", [64, 3], F32)
        nalog = sb("nalog", [16, 1], F32)
        nwcol = sb("nwcol", [128, 1], F32)
        cst = sb("cst", [128, 4], F32)
        dtrow = sb("dtrow", [1, 16], F32)
        P.emit("dve", lambda v: v.memset(cst[:, 0:1], LN_EPS))
        P.emit("dve", lambda v: v.memset(cst[:, 1:2], EPS_A))
        P.emit("dve", lambda v: v.memset(cst[:, 2:3], 1.0))
        P.emit("dve", lambda v: v.memset(cst[:, 3:4], -1.0))

        obf = sb("obuf", [128, NV // 2, NCH // 2, C], BF16)

        pb = [ps(f"pb{i}", [128, 512], F32) for i in range(8)]
        pbh = [b.ap().bitcast(BF16) for b in pb]

        LD_VEC = P.dma_emit("sy", lambda sy: sy.dma_start(out=vecs[:], in_=io["vecs"][:]))
        ctx_iota = nc.sbuf_tensor("iota_r", [128, 128], I32)
        iota_r = ctx_iota.__enter__()
        ctx_iotac = nc.sbuf_tensor("iota_c", [128, 128], I32)
        iota_c = ctx_iotac.__enter__()
        ctx_low = nc.sbuf_tensor("low01", [128, 128], F32)
        low01 = ctx_low.__enter__()
        P.emit("gp", lambda gp: gp.iota(iota_r[:], pattern=[[0, 128]], channel_multiplier=1))
        P.emit("gp", lambda gp: gp.iota(iota_c[:], pattern=[[1, 128]], channel_multiplier=0))
        GPC = P.now("gp")
        P.emit("dve", lambda v: v.tensor_tensor(out=id128[:], in0=iota_r[:], in1=iota_c[:], op=ALU.is_equal),
               waits=[("gp", GPC)])
        P.emit("dve", lambda v: v.tensor_tensor(out=idf32[:], in0=iota_r[:], in1=iota_c[:], op=ALU.is_equal))
        P.emit("dve", lambda v: v.tensor_tensor(out=low01[:], in0=iota_r[:], in1=iota_c[:], op=ALU.is_gt))
        P.emit("dve", lambda v: v.tensor_tensor(out=tricum[:], in0=iota_r[:], in1=iota_c[:], op=ALU.is_le))
        P.emit("dve", lambda v: v.tensor_scalar(out=maskL[:], in0=low01[:], scalar1=1.0, scalar2=-NEG,
                                                op0=ALU.subtract, op1=ALU.mult),
               waits=[("dve", P.now("dve"))])
        P.emit("dve", lambda v: v.tensor_scalar(out=rowsel[:], in0=iota_r[:], scalar1=127, scalar2=None,
                                                op0=ALU.is_equal))
        P.emit("dve", lambda v: v.memset(ones128[:], 1.0))
        P.emit("dve", lambda v: v.memset(ones_f[:], 1.0))
        P.emit("dve", lambda v: v.memset(ones_qb[:], 1.0))
        LD_SEL = P.dma_emit("sy", lambda sy: sy.dma_start(out=ones2f[:, 0:128],
                                                          in_=io["sel2"][:]))
        P.emit("dve", lambda v: v.tensor_copy(out=id64s[0:64, :], in_=id128[0:64, 0:64]),
               waits=[("dve", P.now("dve"))])
        P.emit("dve", lambda v: v.tensor_copy(out=id64s[64:128, :], in_=id128[0:64, 0:64]),
               waits=[("dve", P.now("dve"))])
        ctx_low.__exit__(None, None, None)
        ctx_iotac.__exit__(None, None, None)
        ctx_iota.__exit__(None, None, None)
        P.emit("act", lambda s: s.activation(out=nalog[:], in_=vecs[0:16, 1:2], func=AF.Exp),
               waits=[("dma", LD_VEC)])
        ACT_AL = P.now("act")
        P.emit("dve", lambda v: v.tensor_scalar(out=nalog[:], in0=nalog[:], scalar1=-1.0, scalar2=None,
                                                op0=ALU.mult), waits=[("act", ACT_AL), ("dve", P.now("dve"))])
        P.emit("dve", lambda v: v.tensor_copy(out=nwcol[0:64, :], in_=vecs[0:64, 2:3]),
               waits=[("dve", P.now("dve"))])
        P.emit("te", lambda te: te.transpose(
            out=pb[6][0:1, 0:16], in_=vecs[0:16, 0:1], identity=idf32[0:16, 0:16]),
            waits=[("dma", LD_VEC), ("dve", P.now("dve"))])
        TE_DT = P.now("te")
        P.emit("dve", lambda v: v.tensor_copy(out=dtrow[:], in_=pb[6][0:1, 0:16]),
               waits=[("te", TE_DT)])
        P.emit("dve", lambda v: v.tensor_copy(out=nwcol[64:128, :], in_=vecs[0:64, 2:3]),
               waits=[("dve", P.now("dve"))])
        B0 = P.barrier()
        P.mark("B0")

        ostat = sb("ostat", [128, 8, 2], F32)
        sstat = sb("sstat", [128, 16, 2], F32)
        bnt = sb("bnt", [128, 2, 4, 6], F32)
        rstd_o = sb("rstd_o", [128, 8], F32)
        mr_o = sb("mr_o", [128, 8], F32)
        rstd_s = sb("rstd_s", [128, 16], F32)
        mr_s = sb("mr_s", [128, 16], F32)
        sq_t = sb("sq_t", [128, 16], F32)
        ctx_wkt = nc.sbuf_tensor("wkt", [128, 2, 16, 128], BF16)
        wkt = ctx_wkt.__enter__()
        with nc.sbuf_tensor("xtile", [128, 2, D], F32) as xtile:
            lastd = [0, 0]
            for (xin, nt, stat) in ((io["x_scan_tm"], 16, sstat), (io["x_own_tm"], 8, ostat)):
                for i in range(nt):
                    par = i % 2
                    dm = P.dma_emit("sy", lambda sy, i=i, par=par, xin=xin: sy.dma_start(
                        out=xtile[:, par, :], in_=xin[i * 128:(i + 1) * 128, :]),
                        waits=[("dve", lastd[par])])
                    for c in range(4):
                        P.emit("dve", lambda v, par=par, c=c: v.bn_stats(
                            out=bnt[:, par, c, :], in_=xtile[:, par, c * 512:(c + 1) * 512]),
                            waits=[("dma", dm)] if c == 0 else [("dve", P.now("dve"))])
                    P.emit("dve", lambda v, i=i, par=par, stat=stat: v.bn_aggr(
                        out=stat[:, i, :], in_=bnt[:, par, :, :].rearrange("p a b -> p (a b)")),
                        waits=[("dve", P.now("dve"))])
                    lastd[par] = P.now("dve")
        STATS = P.now("dve")
        for (stat, nt, rstd, mr) in ((ostat, 8, rstd_o, mr_o), (sstat, 16, rstd_s, mr_s)):
            P.emit("act", lambda s, stat=stat, nt=nt: s.activation(
                out=sq_t[:, 0:nt], in_=stat.ap()[:, :, 1], func=AF.Sqrt, bias=cst[:, 0:1]),
                waits=[("dve", STATS), ("dve", P.now("dve"))])
            AC = P.now("act")
            P.emit("dve", lambda v, nt=nt, rstd=rstd: v.reciprocal(
                out=rstd[:], in_=sq_t[:, 0:nt]), waits=[("act", AC)])
            P.emit("dve", lambda v, stat=stat, rstd=rstd, mr=mr: v.tensor_tensor(
                out=mr[:], in0=stat.ap()[:, :, 0], in1=rstd[:], op=ALU.mult),
                waits=[("dve", P.now("dve"))])
        NORMS = P.now("dve")
        def emit_xn(xcm_d, nt512, rstd, mr, xn_dst, tag, prior=()):
            prior = list(prior)
            tc_marks = []
            with nc.sbuf_tensor("rowscr" + tag, [1, 2, 512], F32) as rowscr, \
                 nc.sbuf_tensor("xcmt" + tag, [128, 2, 512], F32) as xcmt, \
                 nc.sbuf_tensor("rbt" + tag, [128, 2, 512], F32) as rbt, \
                 nc.sbuf_tensor("mbt" + tag, [128, 2, 512], F32) as mbt, \
                 nc.sbuf_tensor("xtmp" + tag, [128, 2, 512], F32) as xtmp:
                lastd = [0, 0]
                rowscr_te = 0
                for tc in range(nt512):
                    par = tc % 2
                    for ii in range(4):
                        i = tc * 4 + ii
                        for half, src_c in ((0, rstd), (1, mr)):
                            P.emit("te", lambda te, i=i, src_c=src_c: te.matmul(
                                out=pb[6][0:1, 0:128], lhsT=src_c[:, i:i + 1], rhs=idf32[:],
                                start=True, stop=True),
                                waits=[("dve", max(NORMS, P.now("dve"))), ("te", rowscr_te)]
                                      + (prior if (tc == 0 and ii == 0 and half == 0) else []))
                            TE_R = P.now("te")
                            P.emit("dve", lambda v, ii=ii, half=half: v.tensor_copy(
                                out=rowscr[:, half, ii * 128:(ii + 1) * 128], in_=pb[6][0:1, 0:128]),
                                waits=[("te", TE_R)])
                    DV_ROW = P.now("dve")
                    P.emit("te", lambda te: te.matmul(
                        out=pb[6][:, 0:512], lhsT=ones_f[0:1, :], rhs=rowscr[:, 0, :],
                        start=True, stop=True), waits=[("dve", max(DV_ROW, lastd[par]))])
                    TE_B1 = P.now("te")
                    P.emit("dve", lambda v, par=par: v.tensor_copy(out=rbt[:, par, :], in_=pb[6][:, 0:512]),
                           waits=[("te", TE_B1)])
                    DV_B1 = P.now("dve")
                    P.emit("te", lambda te: te.matmul(
                        out=pb[6][:, 0:512], lhsT=ones_f[0:1, :], rhs=rowscr[:, 1, :],
                        start=True, stop=True), waits=[("dve", DV_B1)])
                    TE_B2 = P.now("te")
                    rowscr_te = TE_B2
                    P.emit("dve", lambda v, par=par: v.tensor_copy(out=mbt[:, par, :], in_=pb[6][:, 0:512]),
                           waits=[("te", TE_B2)])
                    for dtile in range(16):
                        dm = P.dma_emit("sy", lambda sy, dtile=dtile, tc=tc, par=par, xcm_d=xcm_d: sy.dma_start(
                            out=xcmt[:, par, :],
                            in_=xcm_d[dtile * 128:(dtile + 1) * 128, tc * 512:(tc + 1) * 512]),
                            waits=[("dve", lastd[par])] if lastd[par] else prior)
                        P.emit("dve", lambda v, par=par: v.tensor_tensor(
                            out=xtmp[:, par, :], in0=xcmt[:, par, :], in1=rbt[:, par, :], op=ALU.mult),
                            waits=[("dma", dm)])
                        P.emit("dve", lambda v, dtile=dtile, tc=tc, par=par, xn_dst=xn_dst: v.tensor_tensor(
                            out=xn_dst[:, dtile, tc * 512:(tc + 1) * 512],
                            in0=xtmp[:, par, :], in1=mbt[:, par, :], op=ALU.subtract),
                            waits=[("dve", P.now("dve"))])
                        lastd[par] = P.now("dve")
                    tc_marks.append(P.now("dve"))
            return tc_marks


        es2 = ExitStack()
        sb2 = lambda n, s, d: es2.enter_context(nc.sbuf_tensor(n, s, d))
        qf = sb2("qf", [128, 4, T], BF16)
        kf = sb2("kf", [128, 4, T], BF16)
        b_tm = sb2("b_tm", [128, NCH, NV], F32)
        g_tm = sb2("g_tm", [128, NCH, NV], F32)
        ktm_t = sb2("ktm_t", [128, NCH, KH, DK], BF16)
        vtm_t = sb2("vtm_t", [128, NCH, NV, DV], BF16)
        with nc.sbuf_tensor("xn_scan", [128, 16, T], BF16) as xn_scan:
            XN_TCM = emit_xn(io["x_scan_cm"], 4, rstd_s, mr_s, xn_scan, "s", prior=[("dve", NORMS)])
            XN_SCAN = P.now("dve")
            XN_TE = P.now("te")

            with nc.sbuf_tensor("cvt", [128, 3 + T], F32) as cvt_, \
                 nc.sbuf_tensor("cva", [128, TH], F32) as cva_, \
                 nc.sbuf_tensor("cvb", [128, TH], F32) as cvb_, \
                 nc.sbuf_tensor("convw_s", [128, 16, 4], F32) as convw_s, \
                 nc.sbuf_tensor("sil", [128, TH], F32) as sil_, \
                 nc.sbuf_tensor("ktf", [128, 2, 128], F32) as ktf, \
                 nc.sbuf_tensor("sqv", [128, 2, 128], F32) as sqv, \
                 nc.sbuf_tensor("rsq", [128, 2, 2], F32) as rsq, \
                 nc.sbuf_tensor("ktb", [128, 2, 128], BF16) as ktb:
                cvt = cvt_.ap().rearrange("p (a b) -> p a b", a=1)
                cva = cva_.ap().rearrange("p (a b) -> p a b", a=1)
                cvb = cvb_.ap().rearrange("p (a b) -> p a b", a=1)
                sil = sil_.ap().rearrange("p (a b) -> p a b", a=1)
                vsil = sil_.ap().bitcast(BF16).rearrange("p (a b) -> p a b", a=2)
                cwdm = 0
                for i in range(16):
                    cwdm = P.dma_emit("sy", lambda sy, i=i: sy.dma_start(
                        out=convw_s[:, i, :], in_=io["convw"][i, :, :]),
                        waits=[("dve", XN_SCAN), ("te", XN_TE)] if i == 0 else [])
                last_w = [0, 0]
                last_cv = [0, 0]
                last_sil = [0, 0]
                norm_free_d = 0
                norm_free_a = 0
                norm_free_te = 0
                for ot in range(16):
                    par = ot % 2
                    wdm = P.dma_emit("gp", lambda gp, ot=ot, par=par: gp.dma_start(
                        out=wkt[:, par, :, :], in_=io["wqkv_t"][ot, :, :, :]),
                        waits=[("te", last_w[par])])
                    for tch in range(4):
                        pbank = pb[par]
                        for kt in range(16):
                            P.emit("te", lambda te, par=par, tch=tch, kt=kt, pbank=pbank: te.matmul(
                                out=pbank[:, 0:512], lhsT=wkt[:, par, kt, :],
                                rhs=xn_scan[:, kt, tch * 512:(tch + 1) * 512],
                                start=(kt == 0), stop=(kt == 15)),
                                waits=([("dmag", wdm), ("dve", max(XN_TCM[tch], last_cv[par]))]
                                       if kt == 0 else []))
                        TE_P = P.now("te")
                        P.emit("dve", lambda v, par=par, tch=tch, pbank=pbank: v.tensor_copy(
                            out=cvt[:, 0, 3 + tch * 512: 3 + (tch + 1) * 512], in_=pbank[:, 0:512]),
                            waits=[("te", TE_P)])
                        last_cv[par] = P.now("dve")
                    last_w[par] = P.now("te")
                    P.emit("dve", lambda v: v.memset(cvt[:, 0, 0:3], 0.0),
                           waits=[("dve", last_cv[par]), ("dma", cwdm)])
                    for h2 in range(2):
                        P.emit("dve", lambda v, ot=ot, h2=h2: v.tensor_scalar(
                            out=cva[:, 0, :], in0=cvt[:, 0, 3 + h2 * TH:3 + h2 * TH + TH],
                            scalar1=convw_s[:, ot, 3:4], scalar2=None, op0=ALU.mult),
                            waits=[("dve", P.now("dve")),
                                   ("act", max(last_sil[0], last_sil[1]))])
                        bufs = (cva, cvb)
                        for idx, j in enumerate((2, 1, 0)):
                            src_b, dst_b = bufs[idx % 2], bufs[(idx + 1) % 2]
                            P.emit("dve", lambda v, ot=ot, h2=h2, j=j, src_b=src_b, dst_b=dst_b:
                                   v.scalar_tensor_tensor(
                                       out=dst_b[:, 0, :],
                                       in0=cvt[:, 0, j + h2 * TH:j + h2 * TH + TH],
                                       scalar=convw_s[:, ot, j:j + 1], in1=src_b[:, 0, :],
                                       op0=ALU.mult, op1=ALU.add),
                                   waits=[("dve", P.now("dve"))])
                        conv_out = bufs[1]
                        CONV_D = P.now("dve")
                        if ot < 8:
                            P.emit("act", lambda s, conv_out=conv_out: s.activation(
                                out=sil[:, 0, :], in_=conv_out[:, 0, :], func=AF.Silu),
                                waits=[("dve", CONV_D), ("te", norm_free_te)])
                            ACT_S = P.now("act")
                            last_sil[par] = ACT_S
                            dst_fm = qf if ot < 4 else kf
                            fmslot = ot % 4
                            for tb8 in range(8):
                                tb = h2 * 8 + tb8
                                tpar = tb % 2
                                P.emit("te", lambda te, tb8=tb8: te.transpose(
                                    out=pb[4][:, 0:128], in_=sil[:, 0, tb8 * 128:(tb8 + 1) * 128],
                                    identity=idf32[:]),
                                    waits=[("act", ACT_S), ("dve", norm_free_d)])
                                TE_T1 = P.now("te")
                                P.emit("dve", lambda v, tpar=tpar: v.tensor_copy(
                                    out=ktf[:, tpar, :], in_=pb[4][:, 0:128]), waits=[("te", TE_T1)])
                                DV_T1 = P.now("dve")
                                P.emit("act", lambda s, tpar=tpar: s.activation(
                                    out=sqv[:, tpar, :], in_=ktf[:, tpar, :], func=AF.Square),
                                    waits=[("dve", DV_T1), ("act", norm_free_a)])
                                ACT_Q = P.now("act")
                                P.emit("dve", lambda v, tpar=tpar: v.tensor_reduce(
                                    out=rsq[:, tpar, :].rearrange("p (a b) -> p a b", b=1),
                                    in_=sqv[:, tpar, :].rearrange("p (a b) -> p a b", a=2),
                                    axis=AX.X, op=ALU.add),
                                    waits=[("act", ACT_Q)])
                                DV_RED = P.now("dve")
                                P.emit("act", lambda s, tpar=tpar: s.activation(
                                    out=sqv[:, tpar, 0:2], in_=rsq[:, tpar, :], func=AF.Sqrt, bias=cst[:, 1:2]),
                                    waits=[("dve", DV_RED)])
                                ACT_R = P.now("act")
                                norm_free_a = ACT_R
                                P.emit("dve", lambda v, tpar=tpar: v.reciprocal(
                                    out=rsq[:, tpar, :], in_=sqv[:, tpar, 0:2]), waits=[("act", ACT_R)])
                                if ot < 4:
                                    P.emit("dve", lambda v, tpar=tpar: v.tensor_scalar(
                                        out=rsq[:, tpar, :], in0=rsq[:, tpar, :], scalar1=DK ** -0.5,
                                        scalar2=None, op0=ALU.mult), waits=[("dve", P.now("dve"))])
                                DV_R2 = P.now("dve")
                                P.emit("dve", lambda v, tpar=tpar: v.tensor_tensor(
                                    out=ktb[:, tpar, :].rearrange("p (a b) -> p a b", a=2),
                                    in0=ktf[:, tpar, :].rearrange("p (a b) -> p a b", a=2),
                                    in1=rsq[:, tpar, :].rearrange("p (a b) -> p a b", b=1)
                                        .broadcast_to((128, 2, 64)),
                                    op=ALU.mult),
                                    waits=[("dve", DV_R2)])
                                DV_N = P.now("dve")
                                if ot >= 4:
                                    ch = tb
                                    kh0 = (ot - 4) * 2
                                    P.emit("dve", lambda v, tpar=tpar, ch=ch, kh0=kh0: v.tensor_copy(
                                        out=ktm_t[:, ch, kh0, :], in_=ktb[:, tpar, 0:64]),
                                        waits=[("dve", DV_N)])
                                    P.emit("dve", lambda v, tpar=tpar, ch=ch, kh0=kh0: v.tensor_copy(
                                        out=ktm_t[:, ch, kh0 + 1, :], in_=ktb[:, tpar, 64:128]),
                                        waits=[("dve", P.now("dve"))])
                                DV_K = P.now("dve")
                                P.emit("te", lambda te, tpar=tpar: te.transpose(
                                    out=pbh[4][:, 256:384], in_=ktb[:, tpar, :], identity=id128[:]),
                                    waits=[("dve", DV_K)])
                                TE_T2 = P.now("te")
                                P.emit("dve", lambda v, dst_fm=dst_fm, fmslot=fmslot, tb=tb: v.tensor_copy(
                                    out=dst_fm[:, fmslot, tb * 128:(tb + 1) * 128], in_=pbh[4][:, 256:384]),
                                    waits=[("te", TE_T2)])
                                norm_free_d = P.now("dve")
                            norm_free_te = P.now("te")
                        else:
                            P.emit("act", lambda s, conv_out=conv_out: s.activation(
                                out=vsil[:, 0, :], in_=conv_out[:, 0, :], func=AF.Silu),
                                waits=[("dve", CONV_D), ("te", norm_free_te)])
                            ACT_V = P.now("act")
                            last_sil[par] = ACT_V
                            vt = ot - 8
                            for tb8 in range(8):
                                tb = h2 * 8 + tb8
                                P.emit("te", lambda te, tb8=tb8: te.transpose(
                                    out=pbh[4][:, 0:128], in_=vsil[:, 0, tb8 * 128:(tb8 + 1) * 128],
                                    identity=id128[:]),
                                    waits=[("act", ACT_V), ("dve", norm_free_d)])
                                TE_TV = P.now("te")
                                P.emit("dve", lambda v, vt=vt, tb=tb: v.tensor_copy(
                                    out=vtm_t[:, tb, 2 * vt:2 * vt + 2, :].rearrange("p a b -> p (a b)"),
                                    in_=pbh[4][:, 0:128]), waits=[("te", TE_TV)])
                                norm_free_d = P.now("dve")
                            norm_free_te = P.now("te")
            QKV_B = P.barrier()
            P.mark("QKV")

            with nc.sbuf_tensor("wba", [128, 16, 16], BF16) as wba, \
                 nc.sbuf_tensor("braw", [16, T], F32) as braw, \
                 nc.sbuf_tensor("graw", [16, T], F32) as graw, \
                 nc.sbuf_tensor("gtm1", [16, T], F32) as gtm1, \
                 nc.sbuf_tensor("gtm2", [16, T], F32) as gtm2:
                wdm_b = P.dma_emit("gp", lambda gp: gp.dma_start(
                    out=wba[:, :, :], in_=io["wb_t"].rearrange("a p c -> p a c")),
                    waits=P.bwaits(QKV_B, "te", "act", "dve"))
                wdm_a = wdm_b
                for cofs, dst in ((0, braw), (16, graw)):
                    if cofs == 16:
                        wdm_a = P.dma_emit("gp", lambda gp: gp.dma_start(
                            out=wba[:, :, :], in_=io["wa_t"].rearrange("a p c -> p a c")),
                            waits=[("te", P.now("te"))])
                    for tch in range(4):
                        for kt in range(16):
                            P.emit("te", lambda te, cofs=cofs, tch=tch, kt=kt: te.matmul(
                                out=pb[5][0:16, 0:512], lhsT=wba[:, kt, :],
                                rhs=xn_scan[:, kt, tch * 512:(tch + 1) * 512],
                                start=(kt == 0), stop=(kt == 15 and cofs == 0)),
                                waits=[("dmag", wdm_a), ("dve", P.now("dve"))] if kt == 0 else [])
                        if cofs == 16:
                            for cc in range(4):
                                P.emit("te", lambda te, cc=cc: te.matmul(
                                    out=pb[5][0:16, cc * 128:(cc + 1) * 128], lhsT=dtrow[:],
                                    rhs=ones_f[0:1, :], start=False, stop=(cc == 3)))
                        TE_BA = P.now("te")
                        P.emit("dve", lambda v, dst=dst, tch=tch: v.tensor_copy(
                            out=dst[:, tch * 512:(tch + 1) * 512], in_=pb[5][0:16, 0:512]),
                            waits=[("te", TE_BA)])
                BA_D = P.now("dve")
                P.emit("act", lambda s: s.activation(out=gtm1[:], in_=braw[:], func=AF.Sigmoid),
                       waits=[("dve", BA_D)])
                SIG_A = P.now("act")
                P.emit("act", lambda s: s.activation(out=gtm2[:], in_=graw[:], func=AF.Abs),
                       waits=[("dve", BA_D), ("act", SIG_A)])
                D_AB = P.now("dve")
                A_ABS = P.now("act")
                P.emit("act", lambda s: s.activation(out=braw[:], in_=gtm2[:], func=AF.Exp,
                                                     scale=cst[0:16, 3:4]),
                       waits=[("act", A_ABS)])
                A_EX = P.now("act")
                P.emit("act", lambda s: s.activation(out=gtm2[:], in_=braw[:], func=AF.Ln,
                                                     bias=cst[0:16, 2:3]),
                       waits=[("act", A_EX)])
                A_LN = P.now("act")
                P.emit("act", lambda s: s.activation(out=braw[:], in_=graw[:], func=AF.Relu),
                       waits=[("act", A_LN)])
                A_RELU = P.now("act")
                P.emit("dve", lambda v: v.tensor_tensor(
                    out=graw[:], in0=braw[:], in1=gtm2[:], op=ALU.add),
                    waits=[("act", A_RELU), ("dve", P.now("dve"))])
                P.emit("dve", lambda v: v.tensor_scalar(
                    out=graw[:], in0=graw[:], scalar1=nalog[0:16, 0:1],
                    scalar2=None, op0=ALU.mult), waits=[("dve", P.now("dve"))])
                BG_D = P.now("dve")
                BG_A = P.now("act")
                lastbg = 0
                for ch in range(NCH):
                    P.emit("te", lambda te, ch=ch: te.transpose(
                        out=pb[4][:, 0:16], in_=gtm1[:, ch * 128:(ch + 1) * 128],
                        identity=idf32[0:16, 0:16]),
                        waits=[("dve", max(BG_D, lastbg)), ("act", BG_A)])
                    TE_TB = P.now("te")
                    P.emit("dve", lambda v, ch=ch: v.tensor_copy(out=b_tm[:, ch, :], in_=pb[4][:, 0:16]),
                           waits=[("te", TE_TB)])
                    DV_TB = P.now("dve")
                    P.emit("te", lambda te, ch=ch: te.transpose(
                        out=pb[4][:, 0:16], in_=graw[:, ch * 128:(ch + 1) * 128],
                        identity=idf32[0:16, 0:16]), waits=[("dve", DV_TB)])
                    TE_TG = P.now("te")
                    P.emit("dve", lambda v, ch=ch: v.tensor_copy(out=g_tm[:, ch, :], in_=pb[4][:, 0:16]),
                           waits=[("te", TE_TG)])
                    lastbg = P.now("dve")
            BG_B = P.barrier()
            P.mark("BG")

            BG2_B = P.barrier()
        es3 = ExitStack()
        emit_delta_phase(nc, P, es3, pb, pbh, dict(
            ktm=ktm_t, vtm=vtm_t, kfm=kf, qfm=qf, g_tm=g_tm, b_tm=b_tm,
            id128=id128, id64s=id64s, idf32=idf32, ones128=ones128, ones_f=ones_f,
            ones_qb=ones_qb, maskL=maskL, tricum=tricum, rowsel=rowsel, obuf=obf),
            BG2_B)
        DELTA_B = P.barrier()
        P.mark("DELTA")
        es3.close()
        es2.close()
        ctx_wkt.__exit__(None, None, None)
        h_cm = sb("h_cm", [128, HT, TOWN], BF16)
        ctx_xn_own = nc.sbuf_tensor("xn_own", [128, 16, TOWN], BF16)
        xn_own = ctx_xn_own.__enter__()
        ctx_yn_cm = nc.sbuf_tensor("yn_cm", [128, 8, TOWN], BF16)
        yn_cm = ctx_yn_cm.__enter__()
        emit_xn(io["x_own_cm"], 2, rstd_o, mr_o, xn_own, "o",
                prior=P.bwaits(DELTA_B, "te", "act", "dve"))
        XN_OWN = P.now("dve")
        XNO_TE = P.now("te")
        if debug:
            P.dma_emit("sy", lambda sy: sy.dma_start(out=io["xn_dbg"][:], in_=xn_own[:]),
                       waits=[("dve", XN_OWN)])

        with nc.sbuf_tensor("ytm", [128, 2, 1024], F32) as ytm, \
             nc.sbuf_tensor("yss", [128, 2, 16], F32) as yss, \
             nc.sbuf_tensor("yrs", [128, 2, 16], F32) as yrs, \
             nc.sbuf_tensor("zsil", [128, 2, 1024], F32) as zsil, \
             nc.sbuf_tensor("wz", [128, 2, 16, 512], BF16) as wz, \
             nc.sbuf_tensor("ynt", [128, 1, 1024], BF16) as ynt:
            wzdm = [0, 0]
            for zc in range(2):
                wzdm[zc] = P.dma_emit("gp", lambda gp, zc=zc: gp.dma_start(
                    out=wz[:, zc, :, :], in_=io["wz_t"][zc].rearrange("a p c -> p a c")),
                    waits=[("te", XNO_TE), ("dve", XN_OWN), ("act", DELTA_B["act"][1])]
                    if zc == 0 else [])
            lasty = [0, 0]
            last_yn_p = [0, 0]
            red_done = [0, 0]
            last_ynt_te = 0
            for tt in range(8):
                par = tt % 2
                ch = 8 + tt
                for vp in range(8):
                    P.emit("te", lambda te, vp=vp, tt=tt: te.transpose(
                        out=pbh[4][:, 0:128], in_=obf[:, vp, tt, :], identity=id128[:]),
                        waits=([("dve", P.now("dve"))] + P.bwaits(DELTA_B, "act", "dve")) if vp == 0
                        else [("dve", P.now("dve"))])
                    TE_OT = P.now("te")
                    P.emit("dve", lambda v, vp=vp, par=par: v.tensor_copy(
                        out=ytm[:, par, vp * 128:(vp + 1) * 128], in_=pbh[4][:, 0:128]),
                        waits=[("te", TE_OT)])
                Y_D = P.now("dve")
                for zc in range(2):
                    for kt in range(16):
                        P.emit("te", lambda te, tt=tt, zc=zc, kt=kt: te.matmul(
                            out=pb[5][:, 0:512], lhsT=xn_own[:, kt, tt * 128:(tt + 1) * 128],
                            rhs=wz[:, zc, kt, :], start=(kt == 0), stop=(kt == 15)),
                            waits=[("dmag", wzdm[1]), ("act", P.now("act")), ("dve", XN_OWN)]
                            if kt == 0 else [])
                    TE_Z = P.now("te")
                    P.emit("act", lambda s, par=par, zc=zc: s.activation(
                        out=zsil[:, par, zc * 512:(zc + 1) * 512], in_=pb[5][:, 0:512], func=AF.Silu),
                        waits=[("te", TE_Z), ("dve", last_yn_p[par])])
                Z_A = P.now("act")
                for hq in range(2):
                    P.emit("act", lambda s, par=par, hq=hq: s.activation(
                        out=pb[6 + hq][:, 0:512], in_=ytm[:, par, hq * 512:(hq + 1) * 512],
                        func=AF.Square),
                        waits=[("dve", max(Y_D, red_done[hq]))])
                A_SQ = P.now("act")
                for hq in range(2):
                    P.emit("dve", lambda v, par=par, hq=hq: v.tensor_reduce(
                        out=yrs[:, par, hq * 8:(hq + 1) * 8].rearrange("p (a b) -> p a b", b=1),
                        in_=pb[6 + hq][:, 0:512].rearrange("p (a b) -> p a b", a=8),
                        axis=AX.X, op=ALU.add),
                        waits=[("act", A_SQ)] if hq == 0 else [("dve", P.now("dve"))])
                    red_done[hq] = P.now("dve")
                P.emit("dve", lambda v, par=par: v.tensor_scalar(
                    out=yrs[:, par, :], in0=yrs[:, par, :], scalar1=1.0 / DV, scalar2=None,
                    op0=ALU.mult), waits=[("dve", P.now("dve"))])
                DV_RS = P.now("dve")
                P.emit("act", lambda s, par=par: s.activation(
                    out=yss[:, par, :], in_=yrs[:, par, :], func=AF.Sqrt, bias=cst[:, 1:2]),
                    waits=[("dve", DV_RS)])
                A_RS = P.now("act")
                P.emit("dve", lambda v, par=par: v.reciprocal(
                    out=yrs[:, par, :], in_=yss[:, par, :]), waits=[("act", A_RS)])
                DV_RC = P.now("dve")
                P.emit("dve", lambda v, par=par: v.tensor_tensor(
                    out=ytm[:, par, :].rearrange("p (a b) -> p a b", a=16),
                    in0=ytm[:, par, :].rearrange("p (a b) -> p a b", a=16),
                    in1=yrs[:, par, :].rearrange("p (a b) -> p a b", b=1).broadcast_to((128, 16, 64)),
                    op=ALU.mult),
                    waits=[("dve", DV_RC)])
                P.emit("dve", lambda v, par=par: v.tensor_tensor(
                    out=ynt[:, 0, :], in0=ytm[:, par, :], in1=zsil[:, par, :], op=ALU.mult),
                    waits=[("dve", P.now("dve")), ("act", Z_A), ("te", last_ynt_te)])
                YN_D = P.now("dve")
                last_yn_p[par] = YN_D
                for vt in range(8):
                    P.emit("te", lambda te, par=par, vt=vt: te.transpose(
                        out=pbh[4][:, 0:128], in_=ynt[:, 0, vt * 128:(vt + 1) * 128],
                        identity=id128[:]), waits=[("dve", max(YN_D, P.now("dve")))])
                    TE_YT = P.now("te")
                    P.emit("dve", lambda v, tt=tt, vt=vt: v.tensor_scalar(
                        out=yn_cm[:, vt, tt * 128:(tt + 1) * 128], in0=pbh[4][:, 0:128],
                        scalar1=nwcol[:, 0:1], scalar2=None, op0=ALU.mult),
                        waits=[("te", TE_YT)])
                last_ynt_te = P.now("te")
                lasty[par] = P.now("dve")
        GATE_B = P.barrier()
        P.mark("GATE")
        if debug:
            P.dma_emit("sy", lambda sy: sy.dma_start(out=io["yn_dbg"][:], in_=yn_cm[:]),
                       waits=[GATE_B["dve"]])

        with nc.sbuf_tensor("wo", [128, 4, 512], BF16) as wo, \
             nc.sbuf_tensor("aev", [128, 2, 512], F32) as aev:
            lastw = [0, 0, 0, 0]
            lasta = [0, 0]
            ev_prev = 0
            for dc in range(4):
                for kt in range(8):
                    par = kt % 4
                    wdm = P.dma_emit("gp", lambda gp, dc=dc, kt=kt, par=par: gp.dma_start(
                        out=wo[:, par, :], in_=io["wo_t"][dc, kt, :, :]),
                        waits=[("te", lastw[par])] if (dc, kt) >= (0, 4)
                        else P.bwaits(GATE_B, "te", "act", "dve"))
                    for tt in range(8):
                        w = []
                        if tt == 0:
                            w.append(("dmag", wdm))
                        if kt == 0:
                            w.append(("dve", max(ev_prev, GATE_B["dve"][1])))
                            if tt == 0 and dc == 0:
                                w.append(("act", GATE_B["act"][1]))
                        P.emit("te", lambda te, kt=kt, tt=tt, par=par: te.matmul(
                            out=pb[tt][:, 0:512], lhsT=yn_cm[:, kt, tt * 128:(tt + 1) * 128],
                            rhs=wo[:, par, :], start=(kt == 0), stop=(kt == 7)),
                            waits=w)
                    lastw[par] = P.now("te")
                TE_WO = P.now("te")
                for tt in range(8):
                    epar = tt % 2
                    P.emit("dve", lambda v, tt=tt, epar=epar: v.tensor_copy(
                        out=aev[:, epar, :], in_=pb[tt][:, 0:512]),
                        waits=[("te", TE_WO), ("dma", lasta[epar])])
                    AD = P.now("dve")
                    admc = P.dma_emit("sy", lambda sy, tt=tt, dc=dc, epar=epar: sy.dma_start(
                        out=attn_sc[tt * 128:(tt + 1) * 128, dc * 512:(dc + 1) * 512],
                        in_=aev[:, epar, :]), waits=[("dve", AD)])
                    lasta[epar] = admc
                ev_prev = P.now("dve")
        WOUT_B = P.barrier()
        P.mark("WOUT")
        if debug:
            P.dma_emit("sy", lambda sy: sy.dma_start(out=io["attn_dbg"][:], in_=attn_sc[:]),
                       waits=[WOUT_B["dma"]])
        ctx_yn_cm.__exit__(None, None, None)

        with nc.sbuf_tensor("bt", [128, 3, 512], U8) as bt, \
             nc.sbuf_tensor("sct", [128, 3, 512], F16) as sct, \
             nc.sbuf_tensor("i1", [128, 1, 128], I32) as i1, \
             nc.sbuf_tensor("i2", [128, 1, 128], I32) as i2, \
             nc.sbuf_tensor("wdq", [128, 2, 8, 2, 512], BF16) as wdq, \
             nc.sbuf_tensor("gsil", [128, 4, TOWN], F32) as gsil:
            bti = bt.ap().bitcast(I32)

            bt_free = [0, 0, 0]
            sct_free = [0, 0, 0]

            def dequant_granule(srcap, scap, par, dqslot, pp, extra_w=()):
                dm = P.dma_emit("gp", lambda gp: gp.dma_start(out=bt[:, par, :], in_=srcap),
                                waits=list(extra_w) + [("dve", bt_free[par])])
                sdm2 = P.dma_emit("sy", lambda sy: sy.dma_start(out=sct[:, par, :], in_=scap),
                                  waits=list(extra_w) + [("dve", sct_free[par])])
                P.emit("dve", lambda v: v.tensor_scalar(
                    out=i1[:, 0, :], in0=bti[:, par, :], scalar1=0x08080808, scalar2=4,
                    op0=ALU.bitwise_and, op1=ALU.logical_shift_left), waits=[("dmag", dm)])
                P.emit("dve", lambda v: v.tensor_scalar(
                    out=i2[:, 0, :], in0=bti[:, par, :], scalar1=0x07070707, scalar2=2,
                    op0=ALU.bitwise_and, op1=ALU.logical_shift_left))
                P.emit("dve", lambda v: v.tensor_tensor(
                    out=i1[:, 0, :], in0=i1[:, 0, :], in1=i2[:, 0, :], op=ALU.bitwise_or),
                    waits=[("dve", P.now("dve"))])
                P.emit("dve", lambda v: v.scalar_tensor_tensor(
                    out=wdq[:, pp, dqslot, 0, :], in0=i1.ap().bitcast(FP8)[:, 0, :],
                    scalar=64.0, in1=sct[:, par, :], op0=ALU.mult, op1=ALU.mult),
                    waits=[("dma", sdm2), ("dve", P.now("dve"))])
                P.emit("dve", lambda v: v.tensor_scalar(
                    out=i2[:, 0, :], in0=bti[:, par, :], scalar1=0x70707070, scalar2=2,
                    op0=ALU.bitwise_and, op1=ALU.logical_shift_right))
                P.emit("dve", lambda v: v.tensor_scalar(
                    out=i1[:, 0, :], in0=bti[:, par, :], scalar1=0x7F7F7F7F, scalar2=None,
                    op0=ALU.bitwise_and), waits=[("dve", P.now("dve"))])
                P.emit("dve", lambda v: v.tensor_tensor(
                    out=i1[:, 0, :], in0=bti[:, par, :], in1=i1[:, 0, :], op=ALU.bitwise_xor),
                    waits=[("dve", P.now("dve"))])
                bt_free[par] = P.now("dve")
                P.emit("dve", lambda v: v.tensor_tensor(
                    out=i2[:, 0, :], in0=i1[:, 0, :], in1=i2[:, 0, :], op=ALU.bitwise_or),
                    waits=[("dve", P.now("dve"))])
                P.emit("dve", lambda v: v.scalar_tensor_tensor(
                    out=wdq[:, pp, dqslot, 1, :], in0=i2.ap().bitcast(FP8)[:, 0, :],
                    scalar=64.0, in1=sct[:, par, :], op0=ALU.mult, op1=ALU.mult),
                    waits=[("dve", P.now("dve"))])
                sct_free[par] = P.now("dve")
                return P.now("dve")

            NOFFN = bool(os.environ.get("DEVK_NOFFN"))
            te_mm = [WOUT_B["te"][1], WOUT_B["te"][1]]
            gsil_free = [0] * 8
            bankev = [("dve", WOUT_B["dve"][1])] * 4
            for it in range(0 if NOFFN else 22):
                h512, part = it // 2, it % 2
                pp = it % 2
                for ib in range(8):
                    xw = [("te", te_mm[pp])] if ib == 0 else []
                    if it == 0 and ib < 3:
                        xw = xw + P.bwaits(WOUT_B, "te", "act", "dve")
                    dequant_granule(io["gub"][part, h512, ib, :, :],
                                    io["gus"][part, h512, ib, :, :], ib % 3, ib, pp,
                                    extra_w=xw)
                DQ_D = P.now("dve")
                for o128 in range(4):
                    for tch in range(2):
                        g8 = o128 * 2 + tch
                        pbk = pb[g8 % 4]
                        P.emit("te", lambda te, o128=o128, tch=tch, pbk=pbk, pp=pp:
                               te.matmul(
                                   out=pbk[:, 0:512],
                                   lhsT=wdq[:, pp, 0, 0, o128 * 128:(o128 + 1) * 128],
                                   rhs=xn_own[:, 0, tch * 512:(tch + 1) * 512],
                                   start=True, stop=False),
                               waits=[("dve", DQ_D)] + [bankev[g8 % 4]])
                        for ib in range(8):
                            for lh in range(2):
                                if ib == 0 and lh == 0:
                                    continue
                                P.emit("te", lambda te, o128=o128, tch=tch, ib=ib, lh=lh,
                                       pbk=pbk, pp=pp:
                                       te.matmul(
                                           out=pbk[:, 0:512],
                                           lhsT=wdq[:, pp, ib, lh, o128 * 128:(o128 + 1) * 128],
                                           rhs=xn_own[:, ib * 2 + lh, tch * 512:(tch + 1) * 512],
                                           start=False, stop=(ib == 7 and lh == 1)))
                        TE_GU = P.now("te")
                        if part == 0:
                            P.emit("act", lambda s, o128=o128, tch=tch, pbk=pbk: s.activation(
                                out=gsil[:, o128, tch * 512:(tch + 1) * 512], in_=pbk[:, 0:512],
                                func=AF.Silu),
                                waits=[("te", TE_GU), ("dve", gsil_free[g8])])
                            bankev[g8 % 4] = ("act", P.now("act"))
                        else:
                            P.emit("dve", lambda v, h512=h512, o128=o128, tch=tch, pbk=pbk:
                                   v.tensor_tensor(
                                       out=h_cm[:, h512 * 4 + o128, tch * 512:(tch + 1) * 512],
                                       in0=pbk[:, 0:512], in1=gsil[:, o128, tch * 512:(tch + 1) * 512],
                                       op=ALU.mult),
                                   waits=[("te", TE_GU), ("act", P.now("act"))])
                            gsil_free[g8] = P.now("dve")
                            bankev[g8 % 4] = ("dve", P.now("dve"))
                te_mm[pp] = P.now("te")
            GU_B = P.barrier()
            P.mark("GU")
            if debug:
                P.dma_emit("sy", lambda sy: sy.dma_start(out=io["h_dbg"][:], in_=h_cm[:]),
                           waits=[GU_B["dve"], GU_B["act"]])
                P.dma_emit("sy", lambda sy: sy.dma_start(out=io["wdq_dbg"][:], in_=wdq[:, 1, :, :, :]),
                           waits=[GU_B["dve"]])
                P.dma_emit("sy", lambda sy: sy.dma_start(out=io["gsil_dbg"][:], in_=gsil[:]),
                           waits=[GU_B["act"]])
        ctx_xn_own.__exit__(None, None, None)

        with nc.sbuf_tensor("btd", [128, 4, 512], U8) as btd, \
             nc.sbuf_tensor("i1d", [128, 1, 128], I32) as i1d, \
             nc.sbuf_tensor("i2d", [128, 1, 128], I32) as i2d, \
             nc.sbuf_tensor("wdd", [128, 22, 2, 512], BF16) as wdd, \
             nc.sbuf_tensor("dsct", [128, 2, 512], F16) as dsct, \
             nc.sbuf_tensor("ytile", [128, 8, 2048], BF16) as ytile, \
             nc.sbuf_tensor("yp", [128, 4], F32) as yp, \
             nc.sbuf_tensor("yr", [128, 2, 1], F32) as yr, \
             nc.sbuf_tensor("xres", [128, 1, 2048], F32) as xres, \
             nc.sbuf_tensor("ares", [128, 2048], F32) as ares:
            btid = btd.ap().bitcast(I32)
            NOFFN = bool(os.environ.get("DEVK_NOFFN"))
            dbankev = [0] * 4
            dnfree_h = [GU_B["te"][1], GU_B["te"][1]]

            btd_free = [0, 0, 0, 0]
            dsct_free = [0, 0]

            def dq_down(d512, hb, first=False):
                par = hb % 4
                sp = hb % 2
                sdm = P.dma_emit("sy", lambda sy, d512=d512, hb=hb, sp=sp: sy.dma_start(
                    out=dsct[:, sp, :], in_=io["dns"][d512, hb, :, :]),
                    waits=[("dve", dsct_free[sp])]
                          + (P.bwaits(GU_B, "te", "act", "dve") if first else []))
                dm = P.dma_emit("gp", lambda gp, d512=d512, hb=hb, par=par: gp.dma_start(
                    out=btd[:, par, :], in_=io["dnb"][d512, hb, :, :]),
                    waits=([("dve", btd_free[par])]
                           + ([("te", dnfree_h[hb // 11])] if (hb % 11) < 4 else [])
                           + (P.bwaits(GU_B, "te", "act", "dve") if first else [])))
                P.emit("dve", lambda v, par=par: v.tensor_scalar(
                    out=i1d[:, 0, :], in0=btid[:, par, :], scalar1=0x08080808, scalar2=4,
                    op0=ALU.bitwise_and, op1=ALU.logical_shift_left), waits=[("dmag", dm)])
                P.emit("dve", lambda v, par=par: v.tensor_scalar(
                    out=i2d[:, 0, :], in0=btid[:, par, :], scalar1=0x07070707, scalar2=2,
                    op0=ALU.bitwise_and, op1=ALU.logical_shift_left))
                P.emit("dve", lambda v, par=par: v.tensor_tensor(
                    out=i1d[:, 0, :], in0=i1d[:, 0, :], in1=i2d[:, 0, :], op=ALU.bitwise_or),
                    waits=[("dve", P.now("dve"))])
                P.emit("dve", lambda v, par=par, hb=hb, sp=sp: v.scalar_tensor_tensor(
                    out=wdd[:, hb, 0, :], in0=i1d.ap().bitcast(FP8)[:, 0, :],
                    scalar=64.0, in1=dsct[:, sp, :], op0=ALU.mult, op1=ALU.mult),
                    waits=[("dma", sdm), ("dve", P.now("dve"))])
                P.emit("dve", lambda v, par=par: v.tensor_scalar(
                    out=i2d[:, 0, :], in0=btid[:, par, :], scalar1=0x70707070, scalar2=2,
                    op0=ALU.bitwise_and, op1=ALU.logical_shift_right))
                P.emit("dve", lambda v, par=par: v.tensor_scalar(
                    out=i1d[:, 0, :], in0=btid[:, par, :], scalar1=0x7F7F7F7F, scalar2=None,
                    op0=ALU.bitwise_and), waits=[("dve", P.now("dve"))])
                P.emit("dve", lambda v, par=par: v.tensor_tensor(
                    out=i1d[:, 0, :], in0=btid[:, par, :], in1=i1d[:, 0, :],
                    op=ALU.bitwise_xor), waits=[("dve", P.now("dve"))])
                btd_free[par] = P.now("dve")
                P.emit("dve", lambda v, par=par: v.tensor_tensor(
                    out=i2d[:, 0, :], in0=i1d[:, 0, :], in1=i2d[:, 0, :],
                    op=ALU.bitwise_or), waits=[("dve", P.now("dve"))])
                P.emit("dve", lambda v, par=par, hb=hb, sp=sp: v.scalar_tensor_tensor(
                    out=wdd[:, hb, 1, :], in0=i2d.ap().bitcast(FP8)[:, 0, :],
                    scalar=64.0, in1=dsct[:, sp, :], op0=ALU.mult, op1=ALU.mult),
                    waits=[("dve", P.now("dve"))])
                dsct_free[sp] = P.now("dve")

            def dn_mms(t128, hlo, hhi, dq_cnt, openg, closeg, d512):
                pbk = pb[t128 % 4]
                for hb in range(hlo, hhi):
                    for lh in range(2):
                        w = []
                        if hb == hlo and lh == 0:
                            w = [("dve", max(dq_cnt, dbankev[t128 % 4]))] if openg \
                                else [("dve", dq_cnt)]
                        P.emit("te", lambda te, t128=t128, hb=hb, lh=lh, pbk=pbk: te.matmul(
                            out=pbk[:, 0:512], lhsT=h_cm[:, hb * 2 + lh, t128 * 128:(t128 + 1) * 128],
                            rhs=wdd[:, hb, lh, :], start=(openg and hb == hlo and lh == 0),
                            stop=(closeg and hb == hhi - 1 and lh == 1)), waits=w)
                if closeg:
                    TE_DN = P.now("te")
                    P.emit("dve", lambda v, t128=t128, d512=d512, pbk=pbk: v.tensor_copy(
                        out=ytile[:, t128, d512 * 512:(d512 + 1) * 512], in_=pbk[:, 0:512]),
                        waits=[("te", TE_DN)])
                    dbankev[t128 % 4] = P.now("dve")

            if not NOFFN:
                for hb in range(11):
                    dq_down(0, hb, first=(hb == 0))
                DQ_H0 = P.now("dve")
                for d512 in range(4):
                    for t128 in range(4):
                        dn_mms(t128, 0, 11, DQ_H0, True, False, d512)
                    for hb in range(11, 22):
                        dq_down(d512, hb)
                    DQ_H1 = P.now("dve")
                    for t128 in range(4):
                        dn_mms(t128, 11, 22, DQ_H1, False, True, d512)
                    for t128 in range(4, 8):
                        dn_mms(t128, 0, 11, DQ_H0, True, False, d512)
                    dnfree_h[0] = P.now("te")
                    if d512 < 3:
                        for hb in range(11):
                            dq_down(d512 + 1, hb)
                    DQ_H0 = P.now("dve")
                    for t128 in range(4, 8):
                        dn_mms(t128, 11, 22, DQ_H1, False, True, d512)
                    dnfree_h[1] = P.now("te")
            Y_D2 = P.now("dve")
            if debug:
                P.dma_emit("sy", lambda sy: sy.dma_start(out=io["yt_dbg"][:], in_=ytile[:]),
                           waits=[("dve", Y_D2)])
            lastx = 0
            last_odm = 0
            sqfree = [0, 0]
            for t128 in range(0 if NOFFN else 8):
                par = t128 % 2
                xdm = P.dma_emit("sy", lambda sy, t128=t128: sy.dma_start(
                    out=xres[:, 0, :], in_=io["x_own_tm"][t128 * 128:(t128 + 1) * 128, :]),
                    waits=[("dve", lastx)])
                adm = P.dma_emit("sy", lambda sy, t128=t128: sy.dma_start(
                    out=ares[:], in_=attn_sc[t128 * 128:(t128 + 1) * 128, :]),
                    waits=[("dve", lastx),
                           ("dma", max(last_odm, WOUT_B["dma"][1]))])
                for g in range(4):
                    sbk = pb[5 if g % 2 == 0 else 7]
                    P.emit("act", lambda s, t128=t128, g=g, sbk=sbk: s.activation(
                        out=sbk[:, 0:512], in_=ytile[:, t128, g * 512:(g + 1) * 512],
                        func=AF.Square),
                        waits=[("dve", max(Y_D2, sqfree[g % 2]))])
                    A_S2 = P.now("act")
                    P.emit("dve", lambda v, g=g, sbk=sbk: v.tensor_reduce(
                        out=yp[:, g:g + 1], in_=sbk[:, 0:512], axis=AX.X, op=ALU.add),
                        waits=[("act", A_S2)])
                    sqfree[g % 2] = P.now("dve")
                P.emit("dve", lambda v, par=par: v.tensor_reduce(
                    out=yr[:, par, :], in_=yp[:], axis=AX.X, op=ALU.add),
                    waits=[("dve", P.now("dve"))])
                P.emit("dve", lambda v, par=par: v.tensor_scalar(
                    out=yr[:, par, :], in0=yr[:, par, :], scalar1=1.0 / D, scalar2=None,
                    op0=ALU.mult), waits=[("dve", P.now("dve"))])
                DV_Y = P.now("dve")
                P.emit("act", lambda s, par=par: s.activation(
                    out=yp[:, 0:1], in_=yr[:, par, :], func=AF.Sqrt, bias=cst[:, 1:2]),
                    waits=[("dve", DV_Y)])
                A_Y = P.now("act")
                P.emit("dve", lambda v, par=par: v.reciprocal(
                    out=yr[:, par, :], in_=yp[:, 0:1]), waits=[("act", A_Y)])
                DV_YR = P.now("dve")
                P.emit("dve", lambda v: v.tensor_tensor(
                    out=ares[:], in0=xres[:, 0, :], in1=ares[:], op=ALU.add),
                    waits=[("dma", adm), ("dma", xdm)])
                P.emit("dve", lambda v, t128=t128, par=par: v.scalar_tensor_tensor(
                    out=ares[:], in0=ytile[:, t128, :], scalar=yr[:, par, 0:1],
                    in1=ares[:], op0=ALU.mult, op1=ALU.add),
                    waits=[("dve", P.now("dve")), ("dve", DV_YR)])
                OD = P.now("dve")
                odm = P.dma_emit("sy", lambda sy, t128=t128: sy.dma_start(
                    out=io["out"][t128 * 128:(t128 + 1) * 128, :], in_=ares[:]),
                    waits=[("dve", OD)])
                last_odm = odm
                lastx = OD
            FIN_DMA = P.dma_cnt
            P.emit("sy", lambda sy: sy.nop(), waits=[("dma", FIN_DMA)], inc=False)

        blk = es.enter_context(nc.Block())

        @blk.sync
        def _(sy):
            P.replay("sy", sy)

        @blk.tensor
        def _(te):
            P.replay("te", te)

        @blk.vector
        def _(v):
            P.replay("dve", v)

        @blk.scalar
        def _(s):
            P.replay("act", s)

        @blk.gpsimd
        def _(g):
            P.replay("gp", g)

    return nc, io



def emit_delta_phase(nc, P, es, pb, pbh, tn, B):
    sb = lambda n, s, d: es.enter_context(nc.sbuf_tensor(n, s, d))
    ktm, vtm, kfm, qfm = tn["ktm"], tn["vtm"], tn["kfm"], tn["qfm"]
    g_tm, b_tm = tn["g_tm"], tn["b_tm"]
    id128, id64s, idf32 = tn["id128"], tn["id64s"], tn["idf32"]
    ones128, ones_f, ones_qb = tn["ones128"], tn["ones_f"], tn["ones_qb"]
    maskL, tricum, rowsel = tn["maskL"], tn["tricum"], tn["rowsel"]
    obuf = tn["obuf"]
    NIC = NCH * NV

    gc_tm = sb("gc_tm", [128, NIC], F32)
    gcHn = sb("gcHn", [128, NIC], F32)
    egc = sb("egc", [128, NIC], F32)
    einv_t = sb("einv_t", [128, NIC], F32)
    egcC = sb("egcC", [128, NIC], F32)
    begc = sb("begc", [128, NIC], F32)
    bneg = sb("bneg", [128, NIC], F32)
    dtmp = sb("dtmp", [128, NIC], F32)

    NROT = 2
    diagH_t = sb("d_diagH", [128, NROT, 128], F32)
    diagHn_t = sb("d_diagHn", [128, NROT, 128], F32)
    diagE_t = sb("d_diagE", [128, NROT, 128], BF16)
    expGs_t = sb("d_expGs", [128, NROT, 128], F32)
    expGi_t = sb("d_expGi", [128, NROT, 128], F32)
    abuf_t = sb("d_abuf", [128, NROT, 128], BF16)
    qbart_t = sb("d_qbart", [64, NV, 128], F32)

    YB_t = sb("d_YB", [128, NV, 2, 256], BF16)
    Ct_t = sb("d_Ct", [128, NV, 128], BF16)
    Ck_t = sb("d_Ck", [128, NV, 2, 128], BF16)
    At_t = sb("d_At", [128, NV, 128], BF16)
    khat_t = sb("d_khat", [128, NV, DK], BF16)
    AUs_t = sb("d_AUs", [64, NV, 128], BF16)
    QmAW_t = sb("d_QmAW", [64, NV, 128], BF16)
    Ptp_t = sb("d_Ptp", [64, NV, DK], BF16)
    Zs_t = sb("d_Zs", [64, NV, DV], BF16)
    IegcC_t = sb("d_IegcC", [64, NV, 64], BF16)
    Svh_t = sb("d_Svh", [64, NV, DV], BF16)

    diagH = lambda vh: diagH_t[:, vh % NROT, :]
    diagHn = lambda vh: diagHn_t[:, vh % NROT, :]
    diagE = lambda vh: diagE_t[:, vh % NROT, :]
    expGs = lambda vh: expGs_t[:, vh % NROT, :]
    expGi = lambda vh: expGi_t[:, vh % NROT, :]
    abuf = lambda vh: abuf_t[:, vh % NROT, :]
    qbart = lambda vh: qbart_t[:, vh, :]
    Bm = lambda vh: YB_t[:, vh, 0, 128:256]
    Ct = lambda vh: Ct_t[:, vh, :]
    At = lambda vh: At_t[:, vh, :]
    khat = lambda vh: khat_t[:, vh, :]
    AUs = lambda vh: AUs_t[:, vh, :]
    QmAW = lambda vh: QmAW_t[:, vh, :]
    Ptp = lambda vh: Ptp_t[:, vh, :]
    Zs = lambda vh: Zs_t[:, vh, :]
    IegcC = lambda vh: IegcC_t[:, vh, :]
    Svh = lambda vh: Svh_t[:, vh, :]

    def grel_v(i):
        return pb[i][:, 0:128]

    def gram_kk(i):
        return pb[2 + i][:, 0:128]

    def gram_qk(i):
        return pb[2 + i][:, 128:256]

    def sq_v(i):
        return pb[2 + i][:, 256:384]

    BT_V = pbh[4][:, 0:128]
    AT_V = pbh[4][:, 256:384]
    QB_V = pb[5][0:64, 0:128]
    GC_V = pb[6][:, 0:NIC]
    PT_V = pb[7][0:64, 0:64]
    Z_V = pb[7][0:64, 64:128]
    O_V = pb[6][0:64, 256:384]
    S_V = pb[6][0:64, 384:448]

    def col(t, ch, vh):
        i = ch * NV + vh
        return t[:, i:i + 1]

    bw = [B["te"], B["dve"], B["act"], B["gp"], B["dma"], B["dmag"]]

    P.emit("dve", lambda v: v.memset(Svh_t[:], 0.0), waits=bw)
    CONST = P.now("dve")
    gflat = g_tm.ap().rearrange("p a b -> p (a b)")
    bflat = b_tm.ap().rearrange("p a b -> p (a b)")
    P.emit("te", lambda te: te.matmul(out=GC_V, lhsT=tricum[:], rhs=gflat,
                                      start=True, stop=True), waits=bw + [("dve", CONST)])
    TE_GC = P.now("te")
    P.emit("dve", lambda v: v.tensor_copy(out=gc_tm[:], in_=GC_V), waits=[("te", TE_GC)])
    GC_BASE = P.now("dve")
    P.emit("act", lambda s: s.activation(out=gcHn[:], in_=gc_tm[:], func=AF.Copy, scale=-1.0),
           waits=[("dve", GC_BASE)] + bw)
    P.emit("act", lambda s: s.activation(out=egc[:], in_=gc_tm[:], func=AF.Exp))
    ACT_E = P.now("act")
    P.emit("te", lambda te: te.matmul(out=GC_V, lhsT=rowsel[:], rhs=gc_tm[:],
                                      start=True, stop=True),
           waits=[("dve", GC_BASE), ("act", ACT_E)])
    TE_GCC = P.now("te")
    P.emit("dve", lambda v: v.tensor_tensor(out=dtmp[:], in0=GC_V, in1=gc_tm[:], op=ALU.subtract),
           waits=[("te", TE_GCC)])
    DVE_EINV = P.now("dve")
    P.emit("act", lambda s: s.activation(out=einv_t[:], in_=dtmp[:], func=AF.Exp),
           waits=[("dve", DVE_EINV)])
    P.emit("act", lambda s: s.activation(out=egcC[:], in_=GC_V, func=AF.Exp))
    P.emit("dve", lambda v: v.tensor_tensor(out=begc[:], in0=bflat, in1=egc[:], op=ALU.mult),
           waits=[("act", ACT_E)])
    P.emit("act", lambda s: s.activation(out=bneg[:], in_=bflat, func=AF.Copy, scale=-1.0))
    PREP_DVE = P.now("dve")
    PREP_ACT = P.now("act")

    grel_free = [0, 0]
    grel_free_a = [0, 0]
    gram_free = [0, 0]
    sq_free = [0, 0]
    sq_free_d = [0, 0]
    sqf_act = [0, 0]
    awu_free_act = [0, 0]
    diag_free = [0, 0]
    expgs_free = [0, 0]
    abuf_free = [0, 0]
    iegc_free = [0] * NV
    b4_free_act = 0
    qb_free_dve = 0
    b5_free = dict(dve=0, act=0)
    o_free_act = 0
    s_free_act = 0
    sact = [0] * NV

    first = True
    yb0_te = [0] * NV
    khat_te = [0] * NV
    for ch in range(NCH):
        te_prev = P.now("te")
        gp_diag = [0] * NV
        dve_b = [0] * NV
        act_b = [0] * NV
        for vh in range(NV):
            w = [("act", PREP_ACT), ("dve", PREP_DVE)] if first else []
            first = False
            w = w + [("te", diag_free[vh % 2])]
            P.emit("dve", lambda v, ch=ch, vh=vh: v.tensor_scalar(
                out=diagH(vh), in0=idf32[:], scalar1=col(gc_tm, ch, vh), scalar2=None, op0=ALU.mult),
                waits=w)
            P.emit("dve", lambda v, ch=ch, vh=vh: v.tensor_scalar(
                out=diagHn(vh), in0=idf32[:], scalar1=col(gcHn, ch, vh), scalar2=None, op0=ALU.mult))
            P.emit("dve", lambda v, ch=ch, vh=vh: v.tensor_scalar(
                out=diagE(vh), in0=id128[:], scalar1=col(egc, ch, vh), scalar2=None, op0=ALU.mult))
            P.emit("dve", lambda v, ch=ch, vh=vh: v.tensor_scalar(
                out=IegcC(vh), in0=id64s[0:64, :], scalar1=col(egcC, ch, vh)[0:64, :],
                scalar2=None, op0=ALU.mult), waits=[("dve", iegc_free[vh])])
            gp_diag[vh] = P.now("dve")
            kh = vh // 2
            khs = (kh % 2) * 64
            kht = kh // 2
            par = vh % 2
            kslice = kfm[khs:khs + 64, kht, ch * C:(ch + 1) * C]
            qslice = qfm[khs:khs + 64, kht, ch * C:(ch + 1) * C]
            P.emit("te", lambda te, vh=vh, par=par: te.matmul(out=grel_v(par), lhsT=diagH(vh), rhs=ones_f[:],
                                                              start=True, stop=False),
                   waits=[("dve", max(gp_diag[vh], grel_free[par])),
                          ("act", max(awu_free_act[par], grel_free_a[par]))])
            P.emit("te", lambda te, vh=vh, par=par: te.matmul(out=grel_v(par), lhsT=ones_f[:], rhs=diagHn(vh),
                                                              start=False, stop=False))
            P.emit("te", lambda te, vh=vh, par=par: te.matmul(out=grel_v(par), lhsT=idf32[:], rhs=maskL[:],
                                                              start=False, stop=True))
            TE_GREL = P.now("te")
            P.emit("te", lambda te, ks=kslice, par=par: te.matmul(
                out=gram_kk(par), lhsT=ks, rhs=ks, start=True, stop=True),
                waits=[("dve", max(gram_free[par], sq_free_d[par])), ("act", sqf_act[par])])
            P.emit("te", lambda te, ks=kslice, qs=qslice, par=par: te.matmul(
                out=gram_qk(par), lhsT=qs, rhs=ks, start=True, stop=True))
            P.emit("te", lambda te, vh=vh: te.matmul(out=QB_V, lhsT=ones_qb[:], rhs=diagE(vh),
                                                     start=True, stop=True),
                   waits=[("dve", max(qb_free_dve, gp_diag[vh]))])
            TE_G = P.now("te")
            diag_free[par] = TE_G
            P.emit("act", lambda s, vh=vh, par=par: s.activation(out=expGs(vh), in_=grel_v(par), func=AF.Exp),
                   waits=[("te", TE_GREL), ("dve", max(expgs_free[vh % 2], grel_free[par]))])
            grel_free_a[par] = P.now("act")
            ACT_X = P.now("act")
            P.emit("dve", lambda v, vh=vh, ch=ch, par=par: v.scalar_tensor_tensor(
                out=Bm(vh), in0=gram_kk(par), scalar=col(bneg, ch, vh), in1=expGs(vh),
                op0=ALU.mult, op1=ALU.mult), waits=[("act", ACT_X), ("te", yb0_te[vh])])
            P.emit("dve", lambda v, vh=vh: v.tensor_tensor(
                out=expGi(vh), in0=expGs(vh), in1=idf32[:], op=ALU.add))
            expgs_free[vh % 2] = P.now("dve")
            P.emit("dve", lambda v, vh=vh, ch=ch: v.tensor_scalar(
                out=khat(vh), in0=ktm[:, ch, vh // 2, :], scalar1=col(einv_t, ch, vh),
                scalar2=None, op0=ALU.mult), waits=[("te", khat_te[vh])])
            P.emit("dve", lambda v, vh=vh, par=par: v.tensor_tensor(
                out=abuf(vh), in0=gram_qk(par), in1=expGi(vh), op=ALU.mult),
                waits=[("te", abuf_free[vh % 2])])
            gram_free[par] = P.now("dve")
            DVE_BA = P.now("dve")
            P.emit("te", lambda te, vh=vh: te.transpose(out=BT_V, in_=Bm(vh), identity=id128[:]),
                   waits=[("dve", DVE_BA), ("act", b4_free_act)])
            TE_BT = P.now("te")
            P.emit("act", lambda s, vh=vh: s.activation(out=Ct(vh), in_=BT_V, func=AF.Copy),
                   waits=[("te", TE_BT)])
            b4a = P.now("act")
            P.emit("te", lambda te, vh=vh: te.transpose(out=AT_V, in_=abuf(vh), identity=id128[:]),
                   waits=[("act", b4a)])
            TE_AT = P.now("te")
            abuf_free[vh % 2] = TE_AT
            P.emit("act", lambda s, vh=vh: s.activation(out=At(vh), in_=AT_V, func=AF.Copy),
                   waits=[("te", TE_AT)])
            b4_free_act = P.now("act")
            P.emit("dve", lambda v, vh=vh, ch=ch: v.tensor_scalar(
                out=YB_t[:, vh, 0, 0:64], in0=ktm[:, ch, vh // 2, :], scalar1=col(begc, ch, vh),
                scalar2=None, op0=ALU.mult), waits=[("te", yb0_te[vh])])
            P.emit("dve", lambda v, vh=vh, ch=ch: v.tensor_scalar(
                out=YB_t[:, vh, 0, 64:128], in0=vtm[:, ch, vh, :],
                scalar1=bflat[:, ch * NV + vh: ch * NV + vh + 1],
                scalar2=None, op0=ALU.mult))
            P.emit("dve", lambda v, vh=vh, ch=ch, qs=qslice: v.tensor_tensor(
                out=qbart(vh), in0=qs, in1=QB_V, op=ALU.mult), waits=[("te", TE_G)])
            qb_free_dve = P.now("dve")
            dve_b[vh] = P.now("dve")
            act_b[vh] = P.now("act")

        ybprev = [YB_t[:, vh, 0, :] for vh in range(NV)]
        cprev = [Ct(vh) for vh in range(NV)]
        ydve = list(dve_b)
        b_act = [0] * NV
        c_dve = [0] * NV
        c_act = list(act_b)
        ap_free = [0, 0]
        apb_act = [0, 0]
        sqf = [max(s, dve_b[NV - 1]) for s in sq_free]
        sqd = list(sq_free_d)
        for lev in range(NLEV):
            last = lev == NLEV - 1
            for vh in range(NV):
                par = vh % 2
                if not last:
                    P.emit("te", lambda te, c=cprev[vh], yb=ybprev[vh], par=par: te.matmul(
                        out=pb[par][:, 0:256], lhsT=c, rhs=yb, start=True, stop=True),
                        waits=[("dve", max(ydve[vh], ap_free[par], c_dve[vh])),
                               ("act", max(c_act[vh], b_act[vh], grel_free_a[par], apb_act[par]))])
                    TE_AP = P.now("te")
                    if lev == 0:
                        yb0_te[vh] = TE_AP
                    s_new = (lev + 1) % 2
                    P.emit("dve", lambda v, vh=vh, s_new=s_new, yb=ybprev[vh], par=par: v.tensor_tensor(
                        out=YB_t[:, vh, s_new, 0:128], in0=pb[par][:, 0:128],
                        in1=yb[:, 0:128], op=ALU.add), waits=[("te", TE_AP)])
                    ap_free[par] = P.now("dve")
                    grel_free[par] = P.now("dve")
                    bold_act = b_act[vh]
                    P.emit("act", lambda s, vh=vh, s_new=s_new, par=par: s.activation(
                        out=YB_t[:, vh, s_new, 128:256], in_=pb[par][:, 128:256], func=AF.Copy),
                        waits=[("te", TE_AP), ("dve", ap_free[par])])
                    apb_act[par] = P.now("act")
                    b_act[vh] = P.now("act")
                    P.emit("te", lambda te, b=ybprev[vh][:, 128:256], c=cprev[vh], par=par: te.matmul(
                        out=sq_v(par), lhsT=b, rhs=c, start=True, stop=True),
                        waits=[("dve", max(sqf[par], sqd[par], c_dve[vh])),
                               ("act", max(c_act[vh], bold_act))])
                    TE_C = P.now("te")
                    cnew = Ck_t[:, vh, lev % 2, :]
                    P.emit("dve", lambda v, cn=cnew, par=par: v.tensor_copy(out=cn, in_=sq_v(par)),
                           waits=[("te", TE_C)])
                    sqd[par] = P.now("dve")
                    c_dve[vh] = P.now("dve")
                    ybprev[vh] = YB_t[:, vh, s_new, :]
                    ydve[vh] = ap_free[par]
                    cprev[vh] = cnew
                    sqf[par] = 0
                else:
                    P.emit("te", lambda te, c=cprev[vh], yb=ybprev[vh], par=par: te.matmul(
                        out=pb[par][:, 0:128], lhsT=c, rhs=yb[:, 0:128], start=True, stop=True),
                        waits=[("dve", max(ydve[vh], ap_free[par], c_dve[vh])),
                               ("act", max(c_act[vh], grel_free_a[par], apb_act[par]))])
                    TE_AP = P.now("te")
                    s_new = (lev + 1) % 2
                    P.emit("dve", lambda v, vh=vh, s_new=s_new, yb=ybprev[vh], par=par: v.tensor_tensor(
                        out=YB_t[:, vh, s_new, 0:128], in0=pb[par][:, 0:128],
                        in1=yb[:, 0:128], op=ALU.add), waits=[("te", TE_AP)])
                    ap_free[par] = P.now("dve")
                    grel_free[par] = P.now("dve")
                    ybprev[vh] = YB_t[:, vh, s_new, :]
                    ydve[vh] = ap_free[par]
        yprev = ybprev
        sq_free[0], sq_free[1] = sqf[0], sqf[1]
        sq_free_d[0], sq_free_d[1] = sqd[0], sqd[1]
        grel_free_a[0] = max(grel_free_a[0], apb_act[0])
        grel_free_a[1] = max(grel_free_a[1], apb_act[1])

        dve_fin = [0] * NV
        act_fin = [0] * NV
        for vh in range(NV):
            par = vh % 2
            P.emit("te", lambda te, y=yprev[vh][:, 0:128], vh=vh, par=par: te.matmul(
                out=grel_v(par), lhsT=y, rhs=At(vh), start=True, stop=True),
                waits=[("dve", max(ydve[vh], grel_free[par])),
                       ("act", max(awu_free_act[par], grel_free_a[par]))])
            TE_AWU = P.now("te")
            P.emit("dve", lambda v, vh=vh, par=par: v.tensor_tensor(
                out=QmAW(vh), in0=qbart(vh), in1=grel_v(par)[0:64, :], op=ALU.subtract),
                waits=[("te", TE_AWU)])
            DVE_QM = P.now("dve")
            grel_free[par] = DVE_QM
            P.emit("act", lambda s, vh=vh, par=par: s.activation(
                out=AUs(vh), in_=grel_v(par)[64:128, :], func=AF.Copy),
                waits=[("te", TE_AWU), ("dve", DVE_QM)])
            awu_free_act[par] = P.now("act")
            P.emit("te", lambda te, y=yprev[vh], vh=vh: te.matmul(
                out=PT_V, lhsT=y[:, 0:64], rhs=khat(vh), start=True, stop=True),
                waits=[("dve", b5_free["dve"]), ("act", b5_free["act"])])
            P.emit("te", lambda te, y=yprev[vh], vh=vh: te.matmul(
                out=Z_V, lhsT=khat(vh), rhs=y[:, 64:128], start=True, stop=True))
            TE_PZ = P.now("te")
            khat_te[vh] = TE_PZ
            P.emit("dve", lambda v, vh=vh: v.scalar_tensor_tensor(
                out=Ptp(vh), in0=PT_V, scalar=-1.0, in1=IegcC(vh), op0=ALU.mult, op1=ALU.add),
                waits=[("te", TE_PZ)])
            P.emit("dve", lambda v, vh=vh: v.tensor_copy(
                out=Zs(vh), in_=Z_V), waits=[("te", TE_PZ), ("dve", P.now("dve"))])
            b5_free["dve"] = P.now("dve")
            iegc_free[vh] = P.now("dve")
            dve_fin[vh] = P.now("dve")
            act_fin[vh] = P.now("act")

        for vh in range(NV):
            vp = vh // 2
            half = (vh % 2) * 64
            if ch >= 8:
                P.emit("te", lambda te, vh=vh: te.matmul(
                    out=O_V, lhsT=Svh(vh), rhs=QmAW(vh), start=True, stop=True),
                    waits=[("dve", max(dve_fin[vh], sact[vh], o_free_act, s_free_act)),
                           ("act", act_fin[vh])])
                TE_O = P.now("te")
                P.emit("dve", lambda v, vh=vh, ch=ch, vp=vp, half=half: v.tensor_tensor(
                    out=obuf[half:half + 64, vp, ch - 8, :], in0=O_V, in1=AUs(vh), op=ALU.add),
                    waits=[("te", TE_O)])
                o_free_act = P.now("dve")
            P.emit("te", lambda te, vh=vh: te.matmul(
                out=S_V, lhsT=Ptp(vh), rhs=Svh(vh), start=True, stop=True),
                waits=[("dve", max(sact[vh], o_free_act, s_free_act))])
            TE_S = P.now("te")
            P.emit("dve", lambda v, vh=vh: v.tensor_tensor(
                out=Svh(vh), in0=S_V, in1=Zs(vh), op=ALU.add), waits=[("te", TE_S)])
            sact[vh] = P.now("dve")
            s_free_act = sact[vh]



def _make_pi(n):
    pi = np.empty(n, np.int64)
    for b0 in range(n // 256):
        pi[256 * b0:256 * b0 + 128] = 256 * b0 + 2 * np.arange(128)
        pi[256 * b0 + 128:256 * b0 + 256] = 256 * b0 + 2 * np.arange(128) + 1
    return pi


PI = _make_pi(2048)
SIG = _make_pi(5632)


def _prep_weights(inp):
    W_qkv = np.asarray(inp["L0_attn_W_qkv"], np.float32)
    W_z = np.asarray(inp["L0_attn_W_z"], np.float32)
    W_b = np.asarray(inp["L0_attn_W_b"], np.float32)
    W_a = np.asarray(inp["L0_attn_W_a"], np.float32)
    W_out = np.asarray(inp["L0_attn_W_out"], np.float32)
    conv_w = np.asarray(inp["L0_attn_conv_w"], np.float32)
    dt_bias = np.asarray(inp["L0_attn_dt_bias"], np.float32)
    A_log = np.asarray(inp["L0_attn_A_log"], np.float32)
    norm_w = np.asarray(inp["L0_attn_norm_weight"], np.float32)
    gup = np.asarray(inp["L0_ffn_gate_up_packed"])
    gus = np.asarray(inp["L0_ffn_gate_up_scales"])
    dnp_ = np.asarray(inp["L0_ffn_down_packed"])
    dns = np.asarray(inp["L0_ffn_down_scales"])

    w = {}
    wq = W_qkv[:, PI].T.reshape(16, 128, 16, 128)
    w["wqkv_t"] = np.ascontiguousarray(wq.transpose(2, 1, 0, 3))
    wz = W_z[:, PI].T.reshape(16, 128, 2, 512)
    w["wz_t"] = np.ascontiguousarray(wz.transpose(2, 0, 1, 3))
    w["wb_t"] = np.ascontiguousarray(W_b[:, PI].T.reshape(16, 128, 16))
    w["wa_t"] = np.ascontiguousarray(W_a[:, PI].T.reshape(16, 128, 16))
    wo = W_out[PI, :].T.reshape(8, 128, 4, 512)
    w["wo_t"] = np.ascontiguousarray(wo.transpose(2, 0, 1, 3))
    w["convw"] = np.ascontiguousarray(conv_w[:, 0, :].reshape(16, 128, 4))
    vecs = np.zeros((64, 3), np.float32)
    vecs[0:16, 0] = dt_bias
    vecs[0:16, 1] = A_log
    vecs[0:64, 2] = norm_w
    w["vecs"] = vecs
    sel2 = np.zeros((2, 128), ml_dtypes.bfloat16)
    sel2[0, 0:64] = 64.0
    sel2[1, 64:128] = 64.0
    w["sel2"] = sel2
    I = 5632
    gub = np.empty((2, 11, 8, 128, 512), np.uint8)
    guss = np.empty((2, 11, 8, 128, 512), np.float16)
    for part in range(2):
        rows = gup[part * I + SIG]
        bT = rows.view(np.uint8).reshape(I, 1024).T
        gub[part] = bT.reshape(8, 128, 11, 512).transpose(2, 0, 1, 3)
        sc = gus[:, part * I + SIG]
        g4 = sc.reshape(8, 2, 11, 512)
        guss[part] = np.repeat(g4.transpose(2, 0, 1, 3), 64, axis=2).reshape(11, 8, 128, 512)
    w["gub"] = np.ascontiguousarray(gub)
    w["gus"] = np.ascontiguousarray(guss)
    rows = dnp_[PI]
    bT2 = rows.view(np.uint8).reshape(2048, 2816).T
    w["dnb"] = np.ascontiguousarray(bT2.reshape(22, 128, 4, 512).transpose(2, 0, 1, 3))
    d4 = dns[:, PI].reshape(22, 2, 4, 512).transpose(2, 0, 1, 3)
    w["dns"] = np.ascontiguousarray(np.repeat(d4, 64, axis=2).reshape(4, 22, 128, 512))
    return w


def _prep_core_x(core, x_full):
    b, hf = core // 2, core % 2
    x = np.asarray(x_full, np.float32)[b]
    m = {}
    if hf == 0:
        xs = np.concatenate([np.zeros((1024, 2048), np.float32), x[0:1024]], 0)
    else:
        xs = x
    xs = np.ascontiguousarray(xs[:, PI])
    m["x_scan_tm"] = xs
    m["x_scan_cm"] = np.ascontiguousarray(xs.T)
    xo = np.ascontiguousarray(x[hf * 1024:(hf + 1) * 1024][:, PI])
    m["x_own_tm"] = xo
    m["x_own_cm"] = np.ascontiguousarray(xo.T)
    return m


def kernel(**inputs):
    w = _prep_weights(inputs)
    in_maps = []
    for core in range(8):
        m = dict(w)
        m.update(_prep_core_x(core, inputs["L0_attn_x"]))
        in_maps.append(m)
    nc, io = build()
    res = run_bass_kernel_spmd(nc, in_maps, core_ids=list(range(8)))
    out = np.empty((4, 2048, 2048), np.float32)
    for core in range(8):
        b, hf = core // 2, core % 2
        shard = np.asarray(res.results[core]["out"]).reshape(1024, 2048)
        out[b, hf * 1024:(hf + 1) * 1024][:, PI] = shard
    return out
